# version 08
# baseline (speedup 1.0000x reference)
import math
from contextlib import ExitStack

import numpy as np
import ml_dtypes

import concourse.bass as bass
import concourse.mybir as mybir
from concourse.bass_utils import run_bass_kernel_spmd
from concourse.aot_env import allow_ici_all_to_all

F32 = mybir.dt.float32
BF16 = mybir.dt.bfloat16
AF = mybir.ActivationFunctionType
ALU = mybir.AluOpType
AX = mybir.AxisListType
BF = ml_dtypes.bfloat16

R = 8
SEQ = 4096
NBATCH = 2
NT = NBATCH * SEQ
DM = 4096
DI = 11008
NHL = 4
TOWN = NT // R
NTC = NT // 128
NIC = DM // 128
NGC = DI // 128
EPS_LN = 1e-5
EPS_Q = 1e-5
EXP_C = 20.0
ISQ = 1.0 / math.sqrt(128.0)
NEG_BIG = -1.0e30
NW = float(DI * DM)

ENGS = ("tensor", "vector", "scalar", "gpsimd", "sync")


class Sched:

    def __init__(self, sem_names):
        self.cnt = {n: 0 for n in sem_names}
        self.ops = {e: [] for e in ENGS}

    def v(self, sem):
        return self.cnt[sem]

    def emit(self, eng, fn, waits=(), inc=None, n=1):
        waits = tuple((s, v) for (s, v) in waits if v > 0)
        if inc is not None:
            self.cnt[inc] += n
        self.ops[eng].append((fn, waits, inc, n))
        return self.cnt[inc] if inc is not None else None

    def dma(self, eng, out, in_, waits=(), transpose=False, sem=None):
        def fn(e, out=out, in_=in_, transpose=transpose):
            if transpose:
                return e.dma_start(out=out, in_=in_, transpose=True)
            return e.dma_start(out=out, in_=in_)
        return self.emit(eng, fn, waits, sem or ("dq_" + eng), 16)

    def run(self, block, handles):
        for eng in ENGS:
            ops = self.ops[eng]
            def make(eng_ops):
                def body(e):
                    seen = {}
                    for fn, waits, inc, n in eng_ops:
                        for s, v in waits:
                            if seen.get(s, 0) < v:
                                e.wait_ge(handles[s], v)
                                seen[s] = v
                        inst = fn(e)
                        if inc is not None:
                            inst.then_inc(handles[inc], n)
                return body
            getattr(block, eng)(make(ops))


def rope_tables():
    f32 = np.float32
    d = np.arange(0, 128, 2, dtype=f32)
    inv_freq = (f32(1.0) / (f32(500000.0) ** (d / f32(128.0)))).astype(f32)
    wavelen = f32(2.0 * math.pi) / inv_freq
    smooth = (f32(8192.0) / wavelen - f32(1.0)) / f32(3.0)
    smooth = np.clip(smooth, f32(0.0), f32(1.0)).astype(f32)
    new_inv = ((f32(1.0) - smooth) * (inv_freq / f32(8.0)) + smooth * inv_freq).astype(f32)
    t = np.arange(SEQ, dtype=f32)
    freqs = np.outer(t, new_inv).astype(f32)
    emb = np.concatenate([freqs, freqs], axis=-1)
    cos = np.cos(emb).astype(f32)
    sin = np.sin(emb).astype(f32)
    sinr = np.concatenate([-sin[:, :64], sin[:, 64:]], axis=-1).astype(f32)
    return cos, sinr


def causal_masks():
    m = np.zeros((128, 4, 512), dtype=np.float32)
    k = np.arange(128)[:, None]
    q = np.arange(512)[None, :]
    for v in range(4):
        m[:, v, :] = np.where(128 * v + k <= q, 0.0, NEG_BIG)
    return m.reshape(128, 2048)


def build_nc(debug=False):
    nc = bass.Bass("TRN2", target_bir_lowering=False)
    dt = nc.dram_tensor

    x_fm = dt("x_fm", [DM, NT], F32, kind="ExternalInput").ap()
    x_own = dt("x_own", [TOWN, DM], F32, kind="ExternalInput").ap()
    wqkv = dt("wqkv", [DM, 768], F32, kind="ExternalInput").ap()
    woT = dt("woT", [DM, DM], F32, kind="ExternalInput").ap()
    wg_tiles = dt("wg_tiles", [NGC, 128, NIC, 128], F32, kind="ExternalInput").ap()
    wu_tiles = dt("wu_tiles", [NGC, 128, NIC, 128], F32, kind="ExternalInput").ap()
    wd_tiles = dt("wd_tiles", [8, NGC, 128, 512], F32, kind="ExternalInput").ap()
    wg_s = dt("wg_s", [128, 44032], F32, kind="ExternalInput").ap()
    wu_s = dt("wu_s", [128, 44032], F32, kind="ExternalInput").ap()
    wd_s = dt("wd_s", [128, 44032], F32, kind="ExternalInput").ap()
    cosT = dt("cosT", [SEQ, 128], F32, kind="ExternalInput").ap()
    sinrT = dt("sinrT", [SEQ, 128], F32, kind="ExternalInput").ap()
    maskT = dt("maskT", [128, 2048], F32, kind="ExternalInput").ap()
    idf = dt("idf", [128, 128], F32, kind="ExternalInput").ap()
    y = dt("y", [TOWN, DM], F32, kind="ExternalOutput").ap()

    dbg = "ExternalOutput" if debug else "Internal"
    qkv_spill = dt("qkv_spill", [NT, 768], BF16, kind=dbg).ap()
    a2a_in = dt("a2a_in", [NT, 512], BF16, kind="Internal")
    a2a_out = dt("a2a_out", [NT, 512], BF16, kind="Internal")
    stats_in = dt("stats_in", [TOWN, 2], F32, kind="Internal")
    stats_ag = dt("stats_ag", [NT, 2], F32, kind="Internal")
    scales_in = dt("scales_in", [4, 1], F32, kind="Internal")
    scales_out = dt("scales_out", [32, 1], F32, kind="Internal")
    if debug:
        absacc_d = dt("absacc_d", [128, 4], F32, kind="ExternalOutput").ap()
        slabd_d = dt("slabd_d", [128, 36], F32, kind="ExternalOutput").ap()
        stat8_d = dt("stat8_d", [128, 32], F32, kind="ExternalOutput").ap()
        stats_ag_d = dt("stats_ag_d", [NT, 2], F32, kind="ExternalOutput").ap()
        scales_d = dt("scales_d", [1, 8], F32, kind="ExternalOutput").ap()
        a2a_in_d = dt("a2a_in_d", [NT, 512], BF16, kind="ExternalOutput").ap()
    nrow_dram = dt("nrow_dram", [64, 128], BF16, kind="Internal").ap()
    thr_dram = dt("thr_dram", [1, 4], F32, kind="Internal").ap()
    x1_dram = dt("x1_dram", [TOWN, DM], F32, kind=dbg).ap()
    x1n_dram = dt("x1n_dram", [TOWN, DM], BF16, kind=dbg).ap()
    inter_dram = dt("inter_dram", [DI, TOWN], BF16, kind=dbg).ap()

    with ExitStack() as ctx:
        ecx = ctx.enter_context

        def sbuf(name, shape, dtype):
            return ecx(nc.sbuf_tensor(name, shape, dtype))

        A1 = sbuf("A1", [128, 32768], BF16)
        B2 = sbuf("B2", [128, 32768], BF16)
        F3 = sbuf("F3", [128, 16384], F32)
        outb = sbuf("outb", [128, 2 * 768], BF16)
        stag = sbuf("stag", [128, 128], F32)
        stat8 = sbuf("stat8", [128, 32], F32)
        nchunk = sbuf("nchunk", [1, 2 * 128], BF16)
        w1row = sbuf("w1row", [1, 768], BF16)
        onesb = sbuf("onesb", [128, 1], BF16)
        onesf = sbuf("onesf", [128, 1], F32)
        ncol = sbuf("ncol", [128, 1], F32)
        epscol = sbuf("epscol", [128, 1], F32)
        thrb = sbuf("thrb", [128, 8], F32)
        slabd = sbuf("slabd", [128, 36], F32)
        scf = sbuf("scf", [128, 256], F32)
        scr32 = sbuf("scr32", [1, 32], F32)
        absacc = sbuf("absacc", [128, 4], F32)
        scw = sbuf("scw", [1, 8], F32)
        zc = sbuf("zc", [128, 4], F32)
        r2c = sbuf("r2c", [128, 8], F32)
        n64 = sbuf("n64", [64, 128], BF16)

        P0 = ecx(nc.psum_tensor("P0", [128, 1024], F32))
        P1 = ecx(nc.psum_tensor("P1", [128, 1024], F32))
        P2 = ecx(nc.psum_tensor("P2", [128, 1024], F32))
        P3 = ecx(nc.psum_tensor("P3", [128, 1024], F32))
        PQ = [P0, P1]
        PV = [P2, P3]

        sems = ["dq_sync", "dq_gpsimd", "dq_scalar", "s_pe", "s_dve", "s_act", "s_cc",
                "d_q", "d_q1", "d_tr", "d_ost", "d_ost1", "d_n", "d_sp", "d_sp1",
                "d_xo", "d_xd0", "d_xd1", "d_x1s", "d_x1s1", "d_w", "d_w1",
                "d_i", "d_i1", "d_dbg", "d_xc", "d_xc1", "d_xf", "d_xf1",
                "d_x1n", "d_x1n1", "d_ys", "d_ys1", "d_wd", "d_wd1",
                "d_xb0", "d_xb1", "d_xb2", "d_xb3", "d_sc0", "d_sc1",
                "d_wo0", "d_wo1", "d_ib0", "d_ib1", "d_msk", "d_thr"]
        handles = {n: ecx(nc.semaphore(n)) for n in sems}
        S = Sched(sems)

        wq_v = B2[:, 0:24576].rearrange("p (i o) -> p i o", i=NIC)
        xb = [B2[:, 24576 + k * 4096:24576 + (k + 1) * 4096].rearrange("p (i t) -> p i t", i=NIC)
              for k in range(2)] + \
             [A1[:, k * 4096:(k + 1) * 4096].rearrange("p (i t) -> p i t", i=NIC)
              for k in range(2)]
        nall = A1[0:1, 8192:16384]
        cos_v = F3[:, 0:4096].rearrange("p (i d) -> p i d", i=32)
        sinr_v = F3[:, 4096:8192].rearrange("p (i d) -> p i d", i=32)
        xo = [F3[:, 8192:12288], F3[:, 12288:16384]]
        scbuf = F3[:, 12288:16384]
        idf_v = F3[:, 8192:8320]
        rps = F3[:, 8320:9600]
        ytile = F3[:, 4096:5120]
        ubf = B2[:, 24576:26624]
        ibf = B2[:, 26624:28672]
        sq_bf = B2[:, 0:4096]

        S.emit("vector", lambda e: e.memset(onesb[:, :], 1.0), inc="s_dve")
        S.emit("vector", lambda e: e.memset(onesf[:, :], 1.0), inc="s_dve")
        S.emit("vector", lambda e: e.memset(ncol[:, :], -EXP_C), inc="s_dve")
        S.emit("vector", lambda e: e.memset(epscol[:, :], EPS_LN), inc="s_dve")
        S.emit("vector", lambda e: e.memset(absacc[:, :], 0.0), inc="s_dve")
        S.emit("vector", lambda e: e.memset(scw[0:1, :], 0.0), inc="s_dve")
        v_consts = S.v("s_dve")
        S.dma("sync", idf_v, idf)
        ld_idf = S.v("dq_sync")

        xld = {}
        for tc in range(4):
            S.dma("gpsimd", xb[tc], x_fm[:, tc * 128:(tc + 1) * 128].rearrange("(i p) t -> p i t", p=128),
                  sem="d_xb%d" % tc)
            xld[tc] = S.v("d_xb%d" % tc)

        last_sq = 0
        for i in range(8):
            buf = xo[1]
            w = [("s_dve", S.v("s_dve")), ("s_act", S.v("s_act"))] if i >= 1 else []
            ldv = S.dma("sync", buf[:, 0:4096], x_own[i * 128:(i + 1) * 128, :], waits=w, sem="d_xo")
            S.emit("vector",
                   lambda e, buf=buf, i=i: e.reduce_sum(stat8[:, i:i + 1], buf[:, 0:4096], axis=AX.X),
                   waits=[("d_xo", ldv)], inc="s_dve")
            S.emit("scalar",
                   lambda e, buf=buf: e.activation(sq_bf, buf[:, 0:4096], AF.Square),
                   waits=[("d_xo", ldv), ("s_dve", last_sq)], inc="s_act")
            S.emit("vector",
                   lambda e, i=i: e.reduce_sum(stat8[:, 8 + i:9 + i], sq_bf, axis=AX.X),
                   waits=[("s_act", S.v("s_act"))], inc="s_dve")
            last_sq = S.v("s_dve")
        scr_free = [("s_dve", S.v("s_dve")), ("s_act", S.v("s_act"))]
        nr_v = stat8[:, 16:32].rearrange("p (i c) -> p i c", c=2)
        S.emit("vector", lambda e: e.tensor_scalar(nr_v[:, :, 0], stat8[:, 0:8], -1.0 / DM, None, ALU.mult),
               waits=[("s_dve", S.v("s_dve"))], inc="s_dve")
        S.emit("vector", lambda e: e.tensor_scalar(stat8[:, 8:16], stat8[:, 8:16], 1.0 / DM, None, ALU.mult),
               waits=[("s_dve", S.v("s_dve"))], inc="s_dve")
        S.emit("vector", lambda e: e.tensor_tensor(stat8[:, 0:8], nr_v[:, :, 0], nr_v[:, :, 0], ALU.mult),
               waits=[("s_dve", S.v("s_dve"))], inc="s_dve")
        S.emit("vector", lambda e: e.tensor_tensor(stat8[:, 8:16], stat8[:, 8:16], stat8[:, 0:8], ALU.subtract),
               waits=[("s_dve", S.v("s_dve"))], inc="s_dve")
        v_var = S.v("s_dve")
        S.emit("scalar", lambda e: e.activation(stat8[:, 0:8], stat8[:, 8:16], AF.Ln, bias=epscol[:, 0:1]),
               waits=[("s_dve", v_var)], inc="s_act")
        S.emit("scalar", lambda e: e.activation(stat8[:, 8:16], stat8[:, 0:8], AF.Exp, scale=-0.5),
               waits=[("s_act", S.v("s_act"))], inc="s_act")
        v_stats0 = S.v("s_act")
        S.emit("vector", lambda e: e.tensor_scalar(nr_v[:, :, 1], stat8[:, 8:16], 1.0, None, ALU.mult),
               waits=[("s_act", v_stats0)], inc="s_dve")
        v_stats = ("s_dve", S.v("s_dve"))
        S.dma("sync", stats_in.ap().rearrange("(i p) c -> p i c", p=128),
              stat8[:, 16:32].rearrange("p (i c) -> p i c", c=2),
              waits=[v_stats])
        st_stats = S.v("dq_sync")
        S.emit("gpsimd",
               lambda e: e.collective_compute("AllGather", ALU.bypass,
                                              replica_groups=[list(range(R))],
                                              ins=[stats_in.ap().opt()],
                                              outs=[stats_ag.ap().opt()]),
               waits=[("dq_sync", st_stats)], inc="s_cc")
        cc_stats = S.v("s_cc")
        S.dma("sync", stag[:, :].rearrange("p (i c) -> p i c", c=2),
              stats_ag.ap().rearrange("(i p) c -> p i c", p=128),
              waits=[("s_cc", cc_stats)])
        ld_stag = S.v("dq_sync")
        if debug:
            S.dma("sync", stats_ag_d, stats_ag.ap(), waits=[("s_cc", cc_stats)], sem="d_dbg")
            S.dma("sync", stat8_d, stat8[:, :], waits=[("s_cc", cc_stats)], sem="d_dbg")
        stag_v = stag[:, :].rearrange("p (i c) -> p i c", c=2)
        S.emit("tensor",
               lambda e: e.transpose(P0[0:64, 0:128], stag_v[:, :, 0], idf_v),
               waits=[("dq_sync", ld_stag), ("dq_sync", ld_idf)], inc="s_pe")
        S.emit("vector", lambda e: e.tensor_copy(n64[:, :], P0[0:64, 0:128]),
               waits=[("s_pe", S.v("s_pe"))], inc="s_dve")
        v_n64 = S.v("s_dve")
        S.dma("sync", nrow_dram, n64[:, :], waits=[("s_dve", v_n64)], sem="d_n")
        st_nrow = S.v("d_n")
        S.dma("sync", nall, nrow_dram.rearrange("(o a) b -> o (a b)", o=1),
              waits=[("d_n", st_nrow)], sem="d_n")
        ld_nall = S.v("d_n")

        S.dma("gpsimd", wq_v, wqkv.rearrange("(i p) o -> p i o", p=128),
              waits=[v_stats])
        ld_wqkv = S.v("dq_gpsimd")

        for ic in range(NIC):
            w = [("dq_gpsimd", ld_wqkv), ("s_dve", v_consts)] if ic == 0 else []
            S.emit("tensor",
                   lambda e, ic=ic: e.matmul(P1[0:1, 0:512], onesb[:, :], wq_v[:, ic, 0:512],
                                             start=(ic == 0), stop=(ic == NIC - 1)),
                   waits=w)
            S.emit("tensor",
                   lambda e, ic=ic: e.matmul(P1[0:1, 512:768], onesb[:, :], wq_v[:, ic, 512:768],
                                             start=(ic == 0), stop=(ic == NIC - 1)),
                   inc=("s_pe" if ic == NIC - 1 else None))
        v_w1mm = S.v("s_pe")
        S.emit("scalar", lambda e: e.copy(w1row[0:1, :], P1[0:1, 0:768]),
               waits=[("s_pe", v_w1mm)], inc="s_act")
        v_w1 = S.v("s_act")

        S.dma("sync", cos_v, cosT.rearrange("(i p) d -> p i d", p=128))
        S.dma("sync", sinr_v, sinrT.rearrange("(i p) d -> p i d", p=128))
        ld_tabs = S.v("dq_sync")

        slabs = []
        for wi, wsrc in enumerate((wg_s, wu_s, wd_s)):
            for c0 in range(0, 44032, 2048):
                slabs.append((wi, wsrc, c0, min(c0 + 2048, 44032)))
        scbufs = [scbuf[:, 0:2048], scbuf[:, 2048:4096]]
        sl_dve = {}

        def emit_scale_slab(k):
            wi, wsrc, c0, c1 = slabs[k]
            scbuf_k = scbufs[k % 2]
            scsem = "d_sc%d" % (k % 2)
            ldv = S.dma("scalar", scbuf_k[:, 0:c1 - c0], wsrc[:, c0:c1],
                        waits=(scr_free if k < 2 else [("s_dve", sl_dve.get(k - 2, 0))]),
                        sem=scsem)
            ldv = S.v(scsem)
            for a in range(0, c1 - c0, 512):
                S.emit("vector",
                       lambda e, a=a, scbuf_k=scbuf_k: e.reduce_sum(
                           absacc[:, 3:4], scbuf_k[:, a:a + 512],
                           axis=AX.X, apply_absolute_value=True),
                       waits=([(scsem, ldv)] if a == 0 else [("s_dve", S.v("s_dve"))]),
                       inc="s_dve")
                S.emit("vector",
                       lambda e, wi=wi: e.tensor_tensor(absacc[:, wi:wi + 1], absacc[:, wi:wi + 1],
                                                        absacc[:, 3:4], ALU.add),
                       waits=[("s_dve", S.v("s_dve"))], inc="s_dve")
            sl_dve[k] = S.v("s_dve")

        scale_state = {}

        def emit_scale_finish():
            S.emit("tensor", lambda e: e.transpose(P2[0:4, 0:128], absacc[:, 0:4], idf_v),
                   waits=[("s_dve", S.v("s_dve")), ("dq_sync", ld_idf)], inc="s_pe")
            S.emit("scalar", lambda e: e.copy(scf[0:4, 0:128], P2[0:4, 0:128]),
                   waits=[("s_pe", S.v("s_pe"))], inc="s_act")
            scale_state["p2done"] = S.v("s_act")
            base = 0
            n = 128
            w = [("s_act", S.v("s_act"))]
            while n > 1:
                h = n // 2
                src = scf[0:4, base:base + n].rearrange("p (a b) -> p a b", b=2)
                S.emit("vector",
                       lambda e, src=src, base=base, n=n, h=h: e.tensor_tensor(
                           scf[0:4, base + n:base + n + h], src[:, :, 0], src[:, :, 1], ALU.add),
                       waits=w + [("s_dve", S.v("s_dve"))], inc="s_dve")
                w = []
                base += n
                n = h
            S.dma("sync", scales_in.ap(), scf[0:4, 254:255], waits=[("s_dve", S.v("s_dve"))],
                  sem="d_thr")
            S.emit("gpsimd",
                   lambda e: e.collective_compute("AllGather", ALU.bypass,
                                                  replica_groups=[list(range(R))],
                                                  ins=[scales_in.ap().opt()],
                                                  outs=[scales_out.ap().opt()]),
                   waits=[("d_thr", S.v("d_thr"))], inc="s_cc")
            S.dma("sync", scr32[0:1, 0:32],
                  scales_out.ap().rearrange("(o a) b -> o (a b)", o=1),
                  waits=[("s_cc", S.v("s_cc"))], sem="d_thr")
            ldt = S.v("d_thr")
            S.emit("vector", lambda e: e.tensor_copy(scw[0:1, 0:3], scr32[0:1, 20:23]),
                   waits=[("d_thr", ldt)], inc="s_dve")
            for j in range(1, 8):
                r = (5 + j) % 8
                S.emit("vector",
                       lambda e, r=r: e.tensor_tensor(scw[0:1, 0:3], scw[0:1, 0:3],
                                                      scr32[0:1, 4 * r:4 * r + 3], ALU.add),
                       waits=[("s_dve", S.v("s_dve"))], inc="s_dve")
            if debug:
                S.emit("vector", lambda e: e.tensor_copy(scw[0:1, 4:7], scw[0:1, 0:3]),
                       waits=[("s_dve", S.v("s_dve"))], inc="s_dve")
                S.dma("sync", scales_d, scw[0:1, 0:8], waits=[("s_dve", S.v("s_dve"))], sem="d_dbg")
            S.emit("vector", lambda e: e.tensor_scalar(scw[0:1, 0:3], scw[0:1, 0:3],
                                                       float(np.float32(1.0) / np.float32(NW)), None, ALU.mult),
                   waits=[("s_dve", S.v("s_dve"))], inc="s_dve")
            S.emit("vector", lambda e: e.tensor_scalar(scw[0:1, 4:7], scw[0:1, 0:3], EPS_Q, 0.5, ALU.add, ALU.mult),
                   waits=[("s_dve", S.v("s_dve"))], inc="s_dve")
            S.emit("vector", lambda e: e.tensor_tensor(scw[0:1, 3:4], scw[0:1, 0:1], scw[0:1, 1:2], ALU.mult),
                   waits=[("s_dve", S.v("s_dve"))], inc="s_dve")
            S.emit("vector", lambda e: e.tensor_tensor(scw[0:1, 7:8], scw[0:1, 3:4], scw[0:1, 2:3], ALU.mult),
                   waits=[("s_dve", S.v("s_dve"))], inc="s_dve")
            S.dma("sync", thr_dram[0:1, 0:4], scw[0:1, 4:8], waits=[("s_dve", S.v("s_dve"))],
                  sem="d_thr")
            S.dma("sync", thrb[:, 0:4], thr_dram[0:1, 0:4].partition_broadcast(128),
                  waits=[("d_thr", S.v("d_thr"))], sem="d_thr")
            S.emit("vector", lambda e: e.tensor_scalar(thrb[:, 4:8], thrb[:, 0:4], -1.0, None, ALU.mult),
                   waits=[("d_thr", S.v("d_thr"))], inc="s_dve")
            scale_state["thr"] = S.v("s_dve")

        pe_tc = {}
        dve_rope = {}
        spill_v = {}
        n_scale = 0
        for tc in range(NTC):
            par = tc % 2
            b4 = tc % 4
            P = PQ[par]
            if tc >= 4:
                S.dma("gpsimd", xb[b4], x_fm[:, tc * 128:(tc + 1) * 128].rearrange("(i p) t -> p i t", p=128),
                      waits=[("s_pe", pe_tc.get(tc - 4, 0))], sem="d_xb%d" % b4)
                xld[tc] = S.v("d_xb%d" % b4)
            for ic in range(NIC):
                w = []
                if ic == 0:
                    w = [("d_xb%d" % b4, xld[tc]), ("d_n", ld_nall), ("s_act", v_w1),
                         ("s_dve", dve_rope.get(tc - 2, 0))]
                S.emit("tensor",
                       lambda e, ic=ic, P=P, b4=b4: e.matmul(
                           P[:, 0:512], xb[b4][:, ic, :], wq_v[:, ic, 0:512],
                           start=(ic == 0), stop=False),
                       waits=w)
                S.emit("tensor",
                       lambda e, ic=ic, P=P, b4=b4: e.matmul(
                           P[:, 512:768], xb[b4][:, ic, :], wq_v[:, ic, 512:768],
                           start=(ic == 0), stop=False))
            S.emit("tensor",
                   lambda e, P=P, tc=tc: e.matmul(P[:, 0:512], nall[0:1, tc * 128:(tc + 1) * 128],
                                                  w1row[0:1, 0:512], start=False, stop=True))
            S.emit("tensor",
                   lambda e, P=P, tc=tc: e.matmul(P[:, 512:768], nall[0:1, tc * 128:(tc + 1) * 128],
                                                  w1row[0:1, 512:768], start=False, stop=True),
                   inc="s_pe")
            pe_tc[tc] = S.v("s_pe")

            ci = tc % 32
            rA = rps[:, 0:640]
            rB = rps[:, 640:1280]
            spsem = "d_sp" if par == 0 else "d_sp1"
            wv = [("s_pe", pe_tc[tc]), ("dq_sync", ld_tabs), ("dq_sync", ld_stag),
                  (spsem, spill_v.get(tc - 2, 0)), ("s_dve", dve_rope.get(tc - 1, 0))]
            for blk in range(5):
                c0 = blk * 128
                S.emit("vector",
                       lambda e, P=P, c0=c0, ci=ci: e.tensor_tensor(
                           rps[:, c0:c0 + 128], P[:, c0:c0 + 128], cos_v[:, ci, :], ALU.mult),
                       waits=(wv if blk == 0 else ()))
                S.emit("vector",
                       lambda e, P=P, c0=c0, ci=ci: e.tensor_tensor(
                           rps[:, 640 + c0:640 + c0 + 128],
                           P[:, c0:c0 + 128].rearrange("p (g c) -> p g c", g=2)[:, ::-1, :],
                           sinr_v[:, ci, :].rearrange("p (g c) -> p g c", g=2),
                           ALU.mult),
                       inc=("s_dve" if blk == 4 else None))
            S.emit("vector", lambda e: e.tensor_tensor(rA, rA, rB, ALU.add),
                   waits=[("s_dve", S.v("s_dve"))], inc="s_dve")
            S.emit("vector",
                   lambda e, tc=tc, par=par: e.tensor_scalar(
                       outb[:, par * 768:par * 768 + 640], rA,
                       stag_v[:, tc, 1:2], None, ALU.mult),
                   waits=[("s_dve", S.v("s_dve"))])
            S.emit("vector",
                   lambda e, tc=tc, par=par, P=P: e.tensor_scalar(
                       outb[:, par * 768 + 640:par * 768 + 768], P[:, 640:768],
                       stag_v[:, tc, 1:2], None, ALU.mult),
                   inc="s_dve")
            dve_rope[tc] = S.v("s_dve")
            S.dma("sync", qkv_spill[tc * 128:(tc + 1) * 128, :], outb[:, par * 768:(par + 1) * 768],
                  waits=[("s_dve", dve_rope[tc])], sem=spsem)
            spill_v[tc] = S.v(spsem)
            if n_scale < len(slabs):
                emit_scale_slab(n_scale)
                n_scale += 1
        while n_scale < len(slabs):
            emit_scale_slab(n_scale)
            n_scale += 1
        emit_scale_finish()
        spill_done = [("d_sp", spill_v[NTC - 2]), ("d_sp1", spill_v[NTC - 1])]
        pe_A_done = pe_tc[NTC - 1]

        k_sb = A1[:, 0:8192]
        v_sb = A1[:, 8192:16640]
        v_sb3 = A1[:, 8192:16640].rearrange("p (g c) -> p g c", g=64)
        m_bf = A1[:, 16640:18688]
        qsl = A1[:, 18688:19712]
        ebuf = A1[:, 19712:21760]
        ostg = A1[:, 21760:25856]

        S.emit("vector", lambda e: e.memset(v_sb[:, :], 1.0),
               waits=[("s_pe", pe_A_done)], inc="s_dve")
        v_vones = S.v("s_dve")
        ld_mask = S.dma("sync", F3[:, 0:2048], maskT, waits=[("s_dve", dve_rope[NTC - 1])],
                        sem="d_msk")
        S.emit("vector", lambda e: e.tensor_copy(m_bf[:, :], F3[:, 0:2048]),
               waits=[("d_msk", ld_mask)], inc="s_dve")
        v_mask = S.v("s_dve")

        for j in range(16):
            S.dma("sync", k_sb[:, j * 512:(j + 1) * 512],
                  qkv_spill[j * 512:(j + 1) * 512, 512:640],
                  waits=spill_done + [("s_dve", v_vones)], transpose=True, sem="d_tr")
        ld_k = S.v("d_tr")
        for g in range(64):
            S.dma("sync", v_sb3[:, g, 0:128],
                  qkv_spill[g * 128:(g + 1) * 128, 640:768],
                  waits=[("s_dve", v_vones)] + spill_done)
        ld_v = S.v("dq_sync")

        wo_buf = [B2[:, 0:16384].rearrange("p (h o) -> p h o", h=32),
                  B2[:, 16384:32768].rearrange("p (h o) -> p h o", h=32)]
        wo_ld = {}
        for oc in range(2):
            S.dma("gpsimd", wo_buf[oc],
                  woT[:, oc * 512:(oc + 1) * 512].rearrange("(h p) o -> p h o", p=128),
                  waits=[("s_pe", pe_A_done)], sem="d_wo%d" % oc)
            wo_ld[oc] = S.v("d_wo%d" % oc)

        groups = [(b, qc, h) for b in range(NBATCH) for qc in range(8) for h in range(NHL)]
        pairs = []
        for gi, (b, qc, h) in enumerate(groups):
            npairs = 2 * (qc + 1)
            for j in range(npairs):
                pairs.append((gi, b, qc, h, j, npairs))
        NPAIR = len(pairs)

        qld = {}
        pe_st_group = {}
        exp_done = {}
        av_done = {}
        ev_po = {"v": 0}
        st_ostg = {0: 0, 1: 0}
        evac_cnt = {}

        def emit_qload(gi):
            b, qc, h = groups[gi]
            gpar = gi % 2
            qsem = "d_q" if gpar == 0 else "d_q1"
            S.dma("sync", qsl[:, gpar * 512:(gpar + 1) * 512],
                  qkv_spill[b * SEQ + qc * 512: b * SEQ + qc * 512 + 512, h * 128:(h + 1) * 128],
                  waits=[("s_pe", pe_st_group.get(gi - 2, 0)), ("d_tr", ld_k), ("dq_sync", ld_v)],
                  transpose=True, sem=qsem)
            qld[gi] = (qsem, S.v(qsem))

        def emit_st(p):
            gi, b, qc, h, j, npairs = pairs[p]
            spar = p % 2
            SP = PQ[spar]
            gpar = gi % 2
            qv = qsl[:, gpar * 512:(gpar + 1) * 512]
            w = [("s_act", exp_done.get(p - 2, 0)), qld[gi]]
            if p <= 1:
                w.append(("s_act", v_w1))
                w.append(("s_dve", dve_rope[NTC - 1]))
            kb0 = 2 * j
            S.emit("tensor",
                   lambda e, SP=SP, b=b, kb0=kb0, qv=qv: e.matmul(
                       SP[:, 0:512], k_sb[:, b * SEQ + kb0 * 128: b * SEQ + kb0 * 128 + 128],
                       qv, start=True, stop=True),
                   waits=w)
            S.emit("tensor",
                   lambda e, SP=SP, b=b, kb0=kb0, qv=qv: e.matmul(
                       SP[:, 512:1024], k_sb[:, b * SEQ + kb0 * 128 + 128: b * SEQ + kb0 * 128 + 256],
                       qv, start=True, stop=True),
                   inc="s_pe")
            pairs_pe[p] = S.v("s_pe")
            pe_st_group[gi] = S.v("s_pe")

        def emit_mask_exp(p):
            gi, b, qc, h, j, npairs = pairs[p]
            spar = p % 2
            SP = PQ[spar]
            diag = (j >= npairs - 2)
            wexp = [("s_pe", pairs_pe[p]), ("s_pe", av_done.get(p - 2, 0))]
            if diag:
                v0 = 2 * (j - (npairs - 2))
                S.emit("vector",
                       lambda e, SP=SP, v0=v0: e.tensor_tensor(
                           SP[:, 0:1024], SP[:, 0:1024], m_bf[:, v0 * 512: v0 * 512 + 1024], ALU.add),
                       waits=[("s_pe", pairs_pe[p]), ("s_dve", v_mask)], inc="s_dve")
                wexp.append(("s_dve", S.v("s_dve")))
            S.emit("scalar",
                   lambda e, SP=SP, spar=spar: e.activation(
                       ebuf[:, spar * 1024:(spar + 1) * 1024], SP[:, 0:1024], AF.Exp,
                       bias=ncol[:, 0:1], scale=ISQ),
                   waits=wexp, inc="s_act")
            exp_done[p] = S.v("s_act")

        def emit_av(p):
            gi, b, qc, h, j, npairs = pairs[p]
            spar = p % 2
            kb0 = 2 * j
            first = (j == 0)
            last = (j == npairs - 1)
            for kk in range(2):
                kbg = b * 32 + kb0 + kk
                for qs in range(4):
                    slot = PV[qs // 2]
                    c0 = (qs % 2) * 512
                    w = []
                    if first and kk == 0 and qs == 0:
                        w = [("s_act", exp_done[p]), ("s_dve", ev_po["v"])]
                    elif kk == 0 and qs == 0:
                        w = [("s_act", exp_done[p])]
                    S.emit("tensor",
                           lambda e, slot=slot, c0=c0, spar=spar, kk=kk, qs=qs, kbg=kbg,
                                  first=first, last=last: e.matmul(
                               slot[:, c0:c0 + 129],
                               ebuf[:, spar * 1024 + kk * 512 + qs * 128: spar * 1024 + kk * 512 + qs * 128 + 128],
                               v_sb3[:, kbg, 0:129],
                               start=(first and kk == 0), stop=(last and kk == 1)),
                           waits=w,
                           inc=("s_pe" if (kk == 1 and qs == 3) else None))
            av_done[p] = S.v("s_pe")

        def emit_evac(p):
            gi, b, qc, h, j, npairs = pairs[p]
            opar = (b * 8 + qc) % 2
            osem = "d_ost" if opar == 0 else "d_ost1"
            w = [("s_pe", av_done[p]), (osem, st_ostg[opar]), ("s_dve", ev_po["v"])]
            for qs in range(4):
                slot = PV[qs // 2]
                c0 = (qs % 2) * 512
                S.emit("vector",
                       lambda e, slot=slot, c0=c0, qs=qs: e.reciprocal(
                           zc[:, qs:qs + 1], slot[:, c0 + 128:c0 + 129]),
                       waits=(w if qs == 0 else ()),
                       inc=("s_dve" if qs == 3 else None))
            vrec = S.v("s_dve")
            for qs in range(4):
                slot = PV[qs // 2]
                c0 = (qs % 2) * 512
                S.emit("vector",
                       lambda e, slot=slot, c0=c0, qs=qs, opar=opar, h=h: e.tensor_scalar(
                           ostg[:, opar * 2048 + qs * 512 + h * 128: opar * 2048 + qs * 512 + h * 128 + 128],
                           slot[:, c0:c0 + 128], zc[:, qs:qs + 1], None, ALU.mult),
                       waits=([("s_dve", vrec)] if qs == 0 else ()), inc="s_dve")
            ev_po["v"] = S.v("s_dve")
            evac_cnt[(b, qc)] = S.v("s_dve")

        pairs_pe = {}
        p = 0
        cur_g = -1
        emit_qload(0)
        emit_qload(1)
        for p in range(NPAIR):
            gi, b, qc, h, j, npairs = pairs[p]
            if j == npairs - 1 and gi + 2 < len(groups):
                emit_qload(gi + 2)
            if p == 0:
                emit_st(0)
            if p + 1 < NPAIR:
                emit_st(p + 1)
            emit_mask_exp(p)
            emit_av(p)
            if j == npairs - 1:
                emit_evac(p)
                if h == NHL - 1:
                    opar = (b * 8 + qc) % 2
                    t0 = b * SEQ + qc * 512
                    osem2 = "d_ost" if opar == 0 else "d_ost1"
                    S.dma("sync",
                          a2a_in.ap()[t0:t0 + 512, :].rearrange("(qs p) o -> p qs o", p=128),
                          ostg[:, opar * 2048:(opar + 1) * 2048].rearrange("p (qs o) -> p qs o", qs=4),
                          waits=[("s_dve", evac_cnt[(b, qc)])], sem=osem2)
                    st_ostg[opar] = S.v(osem2)

        a2a_ready = [("d_ost", S.v("d_ost")), ("d_ost1", S.v("d_ost1"))]
        if debug:
            S.dma("sync", a2a_in_d, a2a_in.ap(), waits=a2a_ready, sem="d_dbg")
        S.emit("gpsimd",
               lambda e: e.collective_compute("AllToAll", ALU.bypass,
                                              replica_groups=[list(range(R))],
                                              ins=[a2a_in.ap().opt()],
                                              outs=[a2a_out.ap().opt()]),
               waits=a2a_ready, inc="s_cc")
        cc_a2a = S.v("s_cc")

        ofm = A1[:, :].rearrange("p (h t) -> p h t", h=32)
        for hd_c in range(32):
            r_ = hd_c // 4
            lb = hd_c % 4
            S.dma("sync", ofm[:, hd_c, :],
                  a2a_out.ap()[r_ * 1024:(r_ + 1) * 1024, lb * 128:(lb + 1) * 128],
                  waits=[("s_cc", cc_a2a)], transpose=True, sem="d_tr")
        ld_ofm = S.v("d_tr")

        xo3 = [xo[k].rearrange("p (t o) -> p t o", t=8) for k in range(2)]
        pe_C = {}
        dve_C = {}
        xo_ld = {}
        st_x1 = {}
        for oc in range(8):
            par = oc % 2
            xcsem = "d_xc" if par == 0 else "d_xc1"
            if oc >= 2:
                S.dma("gpsimd", wo_buf[par],
                      woT[:, oc * 512:(oc + 1) * 512].rearrange("(h p) o -> p h o", p=128),
                      waits=[("s_pe", pe_C.get(oc - 2, 0))], sem="d_wo%d" % par)
                wo_ld[oc] = S.v("d_wo%d" % par)
            S.dma("sync", xo3[par][:, :, :],
                  x_own[:, oc * 512:(oc + 1) * 512].rearrange("(t p) o -> p t o", p=128),
                  waits=[("s_dve", dve_C.get(oc - 2, 0)), ("s_dve", scale_state["thr"])], sem=xcsem)
            xo_ld[oc] = S.v(xcsem)
            for tc in range(8):
                slot = [P0, P1, P2, P3][tc // 2]
                c0 = (tc % 2) * 512
                for hd_c in range(32):
                    w = []
                    if hd_c == 0:
                        w = [("d_tr", ld_ofm), ("d_wo%d" % par, wo_ld[oc]),
                             ("s_dve", dve_C.get(oc - 1, 0)),
                             ("s_act", scale_state["p2done"])]
                    S.emit("tensor",
                           lambda e, slot=slot, c0=c0, hd_c=hd_c, tc=tc, par=par: e.matmul(
                               slot[:, c0:c0 + 512], ofm[:, hd_c, tc * 128:(tc + 1) * 128],
                               wo_buf[par][:, hd_c, :],
                               start=(hd_c == 0), stop=(hd_c == 31)),
                           waits=w,
                           inc=("s_pe" if hd_c == 31 else None))
                pe_C[oc] = S.v("s_pe")
                ypar = tc % 2
                S.emit("vector",
                       lambda e, slot=slot, c0=c0, tc=tc, par=par, ypar=ypar: e.tensor_tensor(
                           ytile[:, ypar * 512:(ypar + 1) * 512], slot[:, c0:c0 + 512],
                           xo3[par][:, tc, :], ALU.add),
                       waits=[("s_pe", pe_C[oc]), (xcsem, xo_ld[oc]),
                              ("d_x1s" if ypar == 0 else "d_x1s1",
                               st_x1.get((oc, tc - 2), st_x1.get((oc - 1, tc + 6), 0)))],
                       inc="s_dve")
                dve_C[oc] = S.v("s_dve")
                x1sem = "d_x1s" if ypar == 0 else "d_x1s1"
                S.dma("sync", x1_dram[tc * 128:(tc + 1) * 128, oc * 512:(oc + 1) * 512],
                      ytile[:, ypar * 512:(ypar + 1) * 512],
                      waits=[("s_dve", dve_C[oc])], sem=x1sem)
                st_x1[(oc, tc)] = S.v(x1sem)
        x1_done = [("d_x1s", S.v("d_x1s")), ("d_x1s1", S.v("d_x1s1"))]

        x1n_t = [B2[:, 16384 + k * 4096:16384 + (k + 1) * 4096] for k in range(2)]
        st_x1n = {}
        act_D = {}
        for tc in range(8):
            par = tc % 2
            xdsem = "d_xd%d" % par
            ldv = S.dma("sync", xo[par][:, 0:4096], x1_dram[tc * 128:(tc + 1) * 128, :],
                        waits=x1_done + [act_D.get(tc - 2, ("s_dve", 0))], sem=xdsem)
            ldv = S.v(xdsem)
            S.emit("vector", lambda e, par=par: e.reduce_sum(r2c[:, 0:1], xo[par][:, 0:4096], axis=AX.X),
                   waits=[(xdsem, ldv), act_D.get(tc - 1, ("s_dve", 0))], inc="s_dve")
            S.emit("scalar", lambda e, par=par: e.activation(sq_bf, xo[par][:, 0:4096], AF.Square),
                   waits=[(xdsem, ldv), ("s_dve", S.v("s_dve"))], inc="s_act")
            S.emit("vector", lambda e: e.reduce_sum(r2c[:, 1:2], sq_bf, axis=AX.X),
                   waits=[("s_act", S.v("s_act"))], inc="s_dve")
            S.emit("vector", lambda e: e.tensor_scalar(r2c[:, 2:3], r2c[:, 0:1], -1.0 / DM, None, ALU.mult),
                   waits=[("s_dve", S.v("s_dve"))], inc="s_dve")
            S.emit("vector", lambda e: e.tensor_scalar(r2c[:, 3:4], r2c[:, 1:2], 1.0 / DM, None, ALU.mult),
                   waits=[("s_dve", S.v("s_dve"))], inc="s_dve")
            S.emit("vector", lambda e: e.tensor_tensor(r2c[:, 4:5], r2c[:, 2:3], r2c[:, 2:3], ALU.mult),
                   waits=[("s_dve", S.v("s_dve"))], inc="s_dve")
            S.emit("vector", lambda e: e.tensor_tensor(r2c[:, 5:6], r2c[:, 3:4], r2c[:, 4:5], ALU.subtract),
                   waits=[("s_dve", S.v("s_dve"))], inc="s_dve")
            v_d = S.v("s_dve")
            S.emit("scalar", lambda e: e.activation(r2c[:, 6:7], r2c[:, 5:6], AF.Ln, bias=epscol[:, 0:1]),
                   waits=[("s_dve", v_d)], inc="s_act")
            S.emit("scalar", lambda e: e.activation(r2c[:, 7:8], r2c[:, 6:7], AF.Exp, scale=-0.5),
                   waits=[("s_act", S.v("s_act"))], inc="s_act")
            v_r2 = S.v("s_act")
            x1nsem = "d_x1n" if par == 0 else "d_x1n1"
            S.emit("vector",
                   lambda e, par=par: e.tensor_scalar(x1n_t[par], xo[par][:, 0:4096],
                                                      r2c[:, 2:3], r2c[:, 7:8], ALU.add, ALU.mult),
                   waits=[("s_act", v_r2), (x1nsem, st_x1n.get(tc - 2, 0))], inc="s_dve")
            act_D[tc] = ("s_dve", S.v("s_dve"))
            S.dma("sync", x1n_dram[tc * 128:(tc + 1) * 128, :], x1n_t[par],
                  waits=[act_D[tc]], sem=x1nsem)
            st_x1n[tc] = S.v(x1nsem)
        x1n_done = [("d_x1n", S.v("d_x1n")), ("d_x1n1", S.v("d_x1n1"))]

        xn = A1[:, :].rearrange("p (i t) -> p i t", i=32)
        for ic in range(NIC):
            S.dma("sync", xn[:, ic, :], x1n_dram[0:TOWN, ic * 128:(ic + 1) * 128],
                  waits=x1n_done + [("s_pe", pe_C[7])], transpose=True, sem="d_tr")
        ld_xn = S.v("d_tr")

        qg = [B2[:, 0:4096], B2[:, 4096:8192]]
        qu = [B2[:, 8192:12288], B2[:, 12288:16384]]
        scr1 = B2[:, 16384:20480]
        scr2 = B2[:, 20480:24576]
        wgf = [F3[:, 0:4096], F3[:, 4096:8192]]
        wuf = [F3[:, 8192:12288], F3[:, 12288:16384]]
        pe_E = {}
        dve_E = {}
        act_E = {}
        ib_st = {}
        for j in range(NGC):
            par = j % 2
            wsem = "d_w" if par == 0 else "d_w1"
            wq1 = [("s_dve", dve_E.get(j - 2, 0))]
            lg = S.dma("sync", wgf[par], wg_tiles[j].rearrange("p i g -> p (i g)"),
                       waits=wq1 + x1n_done, sem=wsem)
            lu = S.dma("sync", wuf[par], wu_tiles[j].rearrange("p i g -> p (i g)"), sem=wsem)
            lu = S.v(wsem)
            wq2 = [(wsem, lu), ("s_dve", scale_state["thr"]),
                   ("s_pe", pe_E.get(j - 2, 0)), ("s_dve", dve_E.get(j - 1, 0))]
            S.emit("vector", lambda e, par=par: e.tensor_scalar(scr1, wgf[par], thrb[:, 0:1], None, ALU.is_gt),
                   waits=wq2)
            S.emit("vector", lambda e, par=par: e.tensor_scalar(scr2, wgf[par], thrb[:, 4:5], None, ALU.is_lt),
                   inc="s_dve")
            S.emit("vector", lambda e, par=par: e.tensor_tensor(qg[par], scr1, scr2, ALU.subtract),
                   waits=[("s_dve", S.v("s_dve"))], inc="s_dve")
            vqg = S.v("s_dve")
            S.emit("vector", lambda e, par=par: e.tensor_scalar(scr1, wuf[par], thrb[:, 1:2], None, ALU.is_gt),
                   waits=[(wsem, lu), ("s_dve", vqg)])
            S.emit("vector", lambda e, par=par: e.tensor_scalar(scr2, wuf[par], thrb[:, 5:6], None, ALU.is_lt),
                   inc="s_dve")
            S.emit("vector", lambda e, par=par: e.tensor_tensor(qu[par], scr1, scr2, ALU.subtract),
                   waits=[("s_dve", S.v("s_dve"))], inc="s_dve")
            vq = S.v("s_dve")
            dve_E[j] = vq
            PG = PQ[par]
            PU = PV[par]
            for ic in range(NIC):
                w = []
                if ic == 0:
                    w = [("s_dve", vq), ("d_tr", ld_xn),
                         ("s_dve", dve_E.get(("i", j - 2), 0)),
                         ("s_act", act_E.get(j - 2, 0))]
                S.emit("tensor",
                       lambda e, PG=PG, par=par, ic=ic: e.matmul(
                           PG[:, 0:512], qg[par][:, ic * 128:(ic + 1) * 128],
                           xn[:, ic, 0:512], start=(ic == 0), stop=(ic == NIC - 1)),
                       waits=w)
                S.emit("tensor",
                       lambda e, PG=PG, par=par, ic=ic: e.matmul(
                           PG[:, 512:1024], qg[par][:, ic * 128:(ic + 1) * 128],
                           xn[:, ic, 512:1024], start=(ic == 0), stop=(ic == NIC - 1)))
            for ic in range(NIC):
                S.emit("tensor",
                       lambda e, PU=PU, par=par, ic=ic: e.matmul(
                           PU[:, 0:512], qu[par][:, ic * 128:(ic + 1) * 128],
                           xn[:, ic, 0:512], start=(ic == 0), stop=(ic == NIC - 1)))
                S.emit("tensor",
                       lambda e, PU=PU, par=par, ic=ic: e.matmul(
                           PU[:, 512:1024], qu[par][:, ic * 128:(ic + 1) * 128],
                           xn[:, ic, 512:1024], start=(ic == 0), stop=(ic == NIC - 1)),
                       inc=("s_pe" if ic == NIC - 1 else None))
            pe_E[j] = S.v("s_pe")
            S.emit("scalar", lambda e, PU=PU, par=par: e.copy(ubf[:, par * 1024:(par + 1) * 1024], PU[:, 0:1024]),
                   waits=[("s_pe", pe_E[j]), ("s_dve", dve_E.get(("i", j - 2), 0))], inc="s_act")
            act_E[j] = S.v("s_act")
            S.emit("vector",
                   lambda e, PG=PG, par=par: e.tensor_tensor(
                       ibf[:, par * 1024:(par + 1) * 1024], PG[:, 0:1024],
                       ubf[:, par * 1024:(par + 1) * 1024], ALU.mult),
                   waits=[("s_act", act_E[j]),
                          ("d_ib%d" % par, ib_st.get(j - 2, 0))],
                   inc="s_dve")
            dve_E[("i", j)] = S.v("s_dve")
            S.dma("scalar", inter_dram[j * 128:(j + 1) * 128, :], ibf[:, par * 1024:(par + 1) * 1024],
                  waits=[("s_dve", dve_E[("i", j)])], sem="d_ib%d" % par)
            ib_st[j] = S.v("d_ib%d" % par)
        ib_done = [("d_ib0", S.v("d_ib0")), ("d_ib1", S.v("d_ib1"))]
        pe_E_done = pe_E[NGC - 1]

        int_sb = [B2[:, 0:4096].rearrange("p (i t) -> p i t", i=4),
                  B2[:, 4096:8192].rearrange("p (i t) -> p i t", i=4)]
        qd = [B2[:, 8192:10240], B2[:, 10240:12288]]
        fs1 = B2[:, 12288:14336]
        fs2 = B2[:, 14336:16384]
        wdf = [F3[:, 0:2048], F3[:, 2048:4096]]
        x1o = [F3[:, 8192:12288].rearrange("p (t o) -> p t o", t=8),
               F3[:, 12288:16384].rearrange("p (t o) -> p t o", t=8)]
        batches = [(i, min(i + 4, NGC)) for i in range(0, NGC, 4)]
        last_qd = [0]
        pe_F = {}
        dve_F = {}
        st_y = {}
        ld_F = {}
        dve_E_all = dve_E[NGC - 1]
        dve_E_int = dve_E[("i", NGC - 1)]
        act_E_all = act_E[NGC - 1]
        for oc in range(8):
            opar = oc % 2
            xfsem = "d_xf" if opar == 0 else "d_xf1"
            S.dma("sync", x1o[opar][:, :, :],
                  x1_dram[:, oc * 512:(oc + 1) * 512].rearrange("(t p) o -> p t o", p=128),
                  waits=[("s_dve", dve_F.get(oc - 2, 0))] + x1n_done +
                        [("s_dve", dve_E_all)], sem=xfsem)
            x1ld = S.v(xfsem)
            for bi, (ic0, ic1) in enumerate(batches):
                n = ic1 - ic0
                bpar = bi % 2
                wq1 = [("s_dve", dve_F.get(("q", oc, bi - 2), dve_E_all if oc == 0 else 0)),
                       ("s_pe", pe_F.get((oc, bi - 2), pe_E_done if (oc == 0 and bi < 2) else 0))]
                wdsem = "d_wd" if bpar == 0 else "d_wd1"
                disem = "d_i" if bpar == 0 else "d_i1"
                lw = S.dma("sync", wdf[bpar][:, 0:n * 512].rearrange("p (i o) -> p i o", i=n),
                           wd_tiles[oc, ic0:ic1].rearrange("i p o -> p i o"),
                           waits=wq1, sem=wdsem)
                li = S.dma("sync", int_sb[bpar][:, 0:n, :],
                           inter_dram[ic0 * 128:ic1 * 128, :].rearrange("(i p) t -> p i t", p=128),
                           waits=ib_done + [("s_pe", pe_F.get((oc, bi - 2), 0))], sem=disem)
                S.emit("vector", lambda e, bpar=bpar, n=n: e.tensor_scalar(
                           fs1[:, 0:n * 512], wdf[bpar][:, 0:n * 512], thrb[:, 2:3], None, ALU.is_gt),
                       waits=[(wdsem, lw), ("s_dve", scale_state["thr"]),
                              ("s_dve", last_qd[0])])
                S.emit("vector", lambda e, bpar=bpar, n=n: e.tensor_scalar(
                           fs2[:, 0:n * 512], wdf[bpar][:, 0:n * 512], thrb[:, 6:7], None, ALU.is_lt),
                       inc="s_dve")
                S.emit("vector", lambda e, bpar=bpar, n=n: e.tensor_tensor(
                           qd[bpar][:, 0:n * 512], fs1[:, 0:n * 512], fs2[:, 0:n * 512], ALU.subtract),
                       waits=[("s_dve", S.v("s_dve"))], inc="s_dve")
                dve_F[("q", oc, bi)] = S.v("s_dve")
                last_qd[0] = S.v("s_dve")
                for i in range(n):
                    icg = ic0 + i
                    for tc in range(8):
                        slot = [P0, P1, P2, P3][tc // 2]
                        c0 = (tc % 2) * 512
                        w = []
                        if i == 0 and tc == 0:
                            w = [("s_dve", dve_F[("q", oc, bi)]), (disem, li)]
                            if bi == 0:
                                w.append(("s_dve", dve_F.get(oc - 1, dve_E_int)))
                                w.append(("s_act", act_E_all))
                        S.emit("tensor",
                               lambda e, slot=slot, c0=c0, bpar=bpar, i=i, tc=tc, icg=icg: e.matmul(
                                   slot[:, c0:c0 + 512], int_sb[bpar][:, i, tc * 128:(tc + 1) * 128],
                                   qd[bpar][:, i * 512:(i + 1) * 512],
                                   start=(icg == 0), stop=(icg == NGC - 1)),
                               waits=w,
                               inc=("s_pe" if (i == n - 1 and tc == 7) else None))
                pe_F[(oc, bi)] = S.v("s_pe")
            for tc in range(8):
                slot = [P0, P1, P2, P3][tc // 2]
                c0 = (tc % 2) * 512
                ypar = tc % 2
                S.emit("vector",
                       lambda e, slot=slot, c0=c0, opar=opar, tc=tc, ypar=ypar: e.scalar_tensor_tensor(
                           ytile[:, ypar * 512:(ypar + 1) * 512], slot[:, c0:c0 + 512],
                           thrb[:, 3:4], x1o[opar][:, tc, :], ALU.mult, ALU.add),
                       waits=[("s_pe", pe_F[(oc, len(batches) - 1)]), (xfsem, x1ld),
                              ("d_ys" if ypar == 0 else "d_ys1",
                               st_y.get((oc, tc - 2), st_y.get((oc - 1, tc + 6), 0)))],
                       inc="s_dve")
                dve_F[oc] = S.v("s_dve")
                yssem = "d_ys" if ypar == 0 else "d_ys1"
                S.dma("sync", y[tc * 128:(tc + 1) * 128, oc * 512:(oc + 1) * 512],
                      ytile[:, ypar * 512:(ypar + 1) * 512],
                      waits=[("s_dve", dve_F[oc])], sem=yssem)
                st_y[(oc, tc)] = S.v(yssem)

        S.emit("sync", lambda e: None,
               waits=[(nm, S.v(nm)) for nm in sems
                      if nm.startswith("d") and S.v(nm) > 0])

        with allow_ici_all_to_all():
            with nc.Block() as block:
                S.run(block, handles)
    return nc


_NC_CACHE = {}


def _get_nc(debug=False):
    key = bool(debug)
    if key not in _NC_CACHE:
        _NC_CACHE[key] = build_nc(debug=key)
    return _NC_CACHE[key]


def make_in_maps(inputs):
    x = np.asarray(inputs["L0_attn_hidden_states"], dtype=np.float32)
    Wq = np.asarray(inputs["L0_attn_Wq"], dtype=np.float32)
    Wk = np.asarray(inputs["L0_attn_Wk"], dtype=np.float32)
    Wv = np.asarray(inputs["L0_attn_Wv"], dtype=np.float32)
    Wo = np.asarray(inputs["L0_attn_Wo"], dtype=np.float32)
    wg = np.asarray(inputs["L0_ffn_w_gate"], dtype=np.float32)
    wu = np.asarray(inputs["L0_ffn_w_up"], dtype=np.float32)
    wd = np.asarray(inputs["L0_ffn_w_down"], dtype=np.float32)

    xr = x.reshape(NT, DM)
    x_fm = np.ascontiguousarray(xr.T)
    woT = np.ascontiguousarray(Wo.T)
    wg_tiles = np.ascontiguousarray(wg.reshape(NGC, 128, NIC, 128).transpose(0, 3, 2, 1))
    wu_tiles = np.ascontiguousarray(wu.reshape(NGC, 128, NIC, 128).transpose(0, 3, 2, 1))
    wd_tiles = np.ascontiguousarray(wd.reshape(8, 512, NGC, 128).transpose(0, 2, 3, 1))
    wg_flat = wg_tiles.reshape(-1)
    wu_flat = wu_tiles.reshape(-1)
    wd_flat = wd_tiles.reshape(-1)
    nsl = wg_flat.size // R
    cos, sinr = rope_tables()
    masks = causal_masks()
    ident = np.eye(128, dtype=np.float32)

    in_maps = []
    for c in range(R):
        wqkv_c = np.ascontiguousarray(np.concatenate(
            [Wq[512 * c:512 * (c + 1), :].T, Wk[128 * c:128 * (c + 1), :].T,
             Wv[128 * c:128 * (c + 1), :].T], axis=1))
        in_maps.append({
            "x_fm": x_fm,
            "x_own": np.ascontiguousarray(xr[TOWN * c:TOWN * (c + 1), :]),
            "wqkv": wqkv_c,
            "woT": woT,
            "wg_tiles": wg_tiles,
            "wu_tiles": wu_tiles,
            "wd_tiles": wd_tiles,
            "wg_s": np.ascontiguousarray(wg_flat[nsl * c:nsl * (c + 1)].reshape(128, 44032)),
            "wu_s": np.ascontiguousarray(wu_flat[nsl * c:nsl * (c + 1)].reshape(128, 44032)),
            "wd_s": np.ascontiguousarray(wd_flat[nsl * c:nsl * (c + 1)].reshape(128, 44032)),
            "cosT": cos,
            "sinrT": sinr,
            "maskT": masks,
            "idf": ident,
        })
    return in_maps


def kernel(**inputs):
    nc = _get_nc(debug=False)
    in_maps = make_in_maps(inputs)
    res = run_bass_kernel_spmd(nc, in_maps, core_ids=list(range(R)))
    out = np.empty((NT, DM), dtype=np.float32)
    for c in range(R):
        out[TOWN * c:TOWN * (c + 1), :] = res.results[c]["y"]
    return out.reshape(NBATCH, SEQ, DM)


# version 09
# speedup vs baseline: 1.2909x; 1.2909x over previous
import math
from contextlib import ExitStack

import numpy as np
import ml_dtypes

import concourse.bass as bass
import concourse.mybir as mybir
from concourse.bass_utils import run_bass_kernel_spmd
from concourse.aot_env import allow_ici_all_to_all

F32 = mybir.dt.float32
BF16 = mybir.dt.bfloat16
AF = mybir.ActivationFunctionType
ALU = mybir.AluOpType
AX = mybir.AxisListType
BF = ml_dtypes.bfloat16

R = 8
SEQ = 4096
NBATCH = 2
NT = NBATCH * SEQ
DM = 4096
DI = 11008
NHL = 4
TOWN = NT // R
NTC = NT // 128
NIC = DM // 128
NGC = DI // 128
EPS_LN = 1e-5
EPS_Q = 1e-5
EXP_C = 20.0
ISQ = 1.0 / math.sqrt(128.0)
NEG_BIG = -1.0e30
NW = float(DI * DM)

ENGS = ("tensor", "vector", "scalar", "gpsimd", "sync")


class Sched:

    def __init__(self, sem_names):
        self.cnt = {n: 0 for n in sem_names}
        self.ops = {e: [] for e in ENGS}

    def v(self, sem):
        return self.cnt[sem]

    def emit(self, eng, fn, waits=(), inc=None, n=1):
        waits = tuple((s, v) for (s, v) in waits if v > 0)
        if inc is not None:
            self.cnt[inc] += n
        self.ops[eng].append((fn, waits, inc, n))
        return self.cnt[inc] if inc is not None else None

    def dma(self, eng, out, in_, waits=(), transpose=False, sem=None):
        def fn(e, out=out, in_=in_, transpose=transpose):
            if transpose:
                return e.dma_start(out=out, in_=in_, transpose=True)
            return e.dma_start(out=out, in_=in_)
        return self.emit(eng, fn, waits, sem or ("dq_" + eng), 16)

    def run(self, block, handles):
        for eng in ENGS:
            ops = self.ops[eng]
            def make(eng_ops):
                def body(e):
                    seen = {}
                    for fn, waits, inc, n in eng_ops:
                        for s, v in waits:
                            if seen.get(s, 0) < v:
                                e.wait_ge(handles[s], v)
                                seen[s] = v
                        inst = fn(e)
                        if inc is not None:
                            inst.then_inc(handles[inc], n)
                return body
            getattr(block, eng)(make(ops))


def rope_tables():
    f32 = np.float32
    d = np.arange(0, 128, 2, dtype=f32)
    inv_freq = (f32(1.0) / (f32(500000.0) ** (d / f32(128.0)))).astype(f32)
    wavelen = f32(2.0 * math.pi) / inv_freq
    smooth = (f32(8192.0) / wavelen - f32(1.0)) / f32(3.0)
    smooth = np.clip(smooth, f32(0.0), f32(1.0)).astype(f32)
    new_inv = ((f32(1.0) - smooth) * (inv_freq / f32(8.0)) + smooth * inv_freq).astype(f32)
    t = np.arange(SEQ, dtype=f32)
    freqs = np.outer(t, new_inv).astype(f32)
    emb = np.concatenate([freqs, freqs], axis=-1)
    cos = np.cos(emb).astype(f32)
    sin = np.sin(emb).astype(f32)
    sinr = np.concatenate([-sin[:, :64], sin[:, 64:]], axis=-1).astype(f32)
    return cos, sinr


def causal_masks():
    m = np.zeros((128, 4, 512), dtype=np.float32)
    k = np.arange(128)[:, None]
    q = np.arange(512)[None, :]
    for v in range(4):
        m[:, v, :] = np.where(128 * v + k <= q, 0.0, NEG_BIG)
    return m.reshape(128, 2048)


def build_nc(debug=False):
    nc = bass.Bass("TRN2", target_bir_lowering=False)
    dt = nc.dram_tensor

    x_fm = dt("x_fm", [DM, NT], F32, kind="ExternalInput").ap()
    x_own = dt("x_own", [TOWN, DM], F32, kind="ExternalInput").ap()
    wqkv = dt("wqkv", [DM, 768], F32, kind="ExternalInput").ap()
    woT = dt("woT", [DM, DM], F32, kind="ExternalInput").ap()
    wg_tiles = dt("wg_tiles", [NGC, 128, NIC, 128], F32, kind="ExternalInput").ap()
    wu_tiles = dt("wu_tiles", [NGC, 128, NIC, 128], F32, kind="ExternalInput").ap()
    wd_tiles = dt("wd_tiles", [8, NGC, 128, 512], F32, kind="ExternalInput").ap()
    wg_s = dt("wg_s", [128, 44032], F32, kind="ExternalInput").ap()
    wu_s = dt("wu_s", [128, 44032], F32, kind="ExternalInput").ap()
    wd_s = dt("wd_s", [128, 44032], F32, kind="ExternalInput").ap()
    cosT = dt("cosT", [SEQ, 128], F32, kind="ExternalInput").ap()
    sinrT = dt("sinrT", [SEQ, 128], F32, kind="ExternalInput").ap()
    maskT = dt("maskT", [128, 2048], F32, kind="ExternalInput").ap()
    idf = dt("idf", [128, 128], F32, kind="ExternalInput").ap()
    y = dt("y", [TOWN, DM], F32, kind="ExternalOutput").ap()

    dbg = "ExternalOutput" if debug else "Internal"
    qkv_spill = dt("qkv_spill", [NT, 768], BF16, kind=dbg).ap()
    a2a_in = dt("a2a_in", [NT, 512], BF16, kind="Internal")
    a2a_out = dt("a2a_out", [NT, 512], BF16, kind="Internal")
    stats_in = dt("stats_in", [TOWN, 2], F32, kind="Internal")
    stats_ag = dt("stats_ag", [NT, 2], F32, kind="Internal")
    scales_in = dt("scales_in", [4, 1], F32, kind="Internal")
    scales_out = dt("scales_out", [32, 1], F32, kind="Internal")
    if debug:
        absacc_d = dt("absacc_d", [128, 4], F32, kind="ExternalOutput").ap()
        slabd_d = dt("slabd_d", [128, 36], F32, kind="ExternalOutput").ap()
        stat8_d = dt("stat8_d", [128, 32], F32, kind="ExternalOutput").ap()
        stats_ag_d = dt("stats_ag_d", [NT, 2], F32, kind="ExternalOutput").ap()
        scales_d = dt("scales_d", [1, 8], F32, kind="ExternalOutput").ap()
        a2a_in_d = dt("a2a_in_d", [NT, 512], BF16, kind="ExternalOutput").ap()
    nrow_dram = dt("nrow_dram", [64, 128], BF16, kind="Internal").ap()
    thr_dram = dt("thr_dram", [1, 4], F32, kind="Internal").ap()
    x1_dram = dt("x1_dram", [TOWN, DM], F32, kind=dbg).ap()
    x1n_dram = dt("x1n_dram", [TOWN, DM], BF16, kind=dbg).ap()
    inter_dram = dt("inter_dram", [DI, TOWN], BF16, kind=dbg).ap()

    with ExitStack() as ctx:
        ecx = ctx.enter_context

        def sbuf(name, shape, dtype):
            return ecx(nc.sbuf_tensor(name, shape, dtype))

        A1 = sbuf("A1", [128, 32768], BF16)
        B2 = sbuf("B2", [128, 32768], BF16)
        F3 = sbuf("F3", [128, 16384], F32)
        outb = sbuf("outb", [128, 2 * 768], BF16)
        stag = sbuf("stag", [128, 128], F32)
        stat8 = sbuf("stat8", [128, 32], F32)
        nchunk = sbuf("nchunk", [1, 2 * 128], BF16)
        w1row = sbuf("w1row", [1, 768], BF16)
        onesb = sbuf("onesb", [128, 1], BF16)
        onesf = sbuf("onesf", [128, 1], F32)
        ncol = sbuf("ncol", [128, 1], F32)
        epscol = sbuf("epscol", [128, 1], F32)
        thrb = sbuf("thrb", [128, 8], F32)
        slabd = sbuf("slabd", [128, 36], F32)
        scf = sbuf("scf", [128, 256], F32)
        scr32 = sbuf("scr32", [1, 32], F32)
        absacc = sbuf("absacc", [128, 4], F32)
        scw = sbuf("scw", [1, 8], F32)
        zc = sbuf("zc", [128, 4], F32)
        r2c = sbuf("r2c", [128, 8], F32)
        n64 = sbuf("n64", [64, 128], BF16)

        P0 = ecx(nc.psum_tensor("P0", [128, 1024], F32))
        P1 = ecx(nc.psum_tensor("P1", [128, 1024], F32))
        P2 = ecx(nc.psum_tensor("P2", [128, 1024], F32))
        P3 = ecx(nc.psum_tensor("P3", [128, 1024], F32))
        PQ = [P0, P1]
        PV = [P2, P3]

        sems = ["dq_sync", "dq_gpsimd", "dq_scalar", "s_pe", "s_dve", "s_act", "s_cc",
                "d_q", "d_q1", "d_tr", "d_ost", "d_ost1", "d_n", "d_sp", "d_sp1",
                "d_xo", "d_xd0", "d_xd1", "d_x1s", "d_x1s1", "d_w", "d_w1",
                "d_i", "d_i1", "d_dbg", "d_xc", "d_xc1", "d_xf", "d_xf1",
                "d_x1n", "d_x1n1", "d_ys", "d_ys1", "d_wd", "d_wd1",
                "d_xb0", "d_xb1", "d_xb2", "d_xb3", "d_sc0", "d_sc1",
                "d_wo0", "d_wo1", "d_ib0", "d_ib1", "d_msk", "d_thr"]
        handles = {n: ecx(nc.semaphore(n)) for n in sems}
        S = Sched(sems)

        wq_v = B2[:, 0:24576].rearrange("p (i o) -> p i o", i=NIC)
        xb = [B2[:, 24576 + k * 4096:24576 + (k + 1) * 4096].rearrange("p (i t) -> p i t", i=NIC)
              for k in range(2)] + \
             [A1[:, k * 4096:(k + 1) * 4096].rearrange("p (i t) -> p i t", i=NIC)
              for k in range(2)]
        nall = A1[0:1, 8192:16384]
        cos_v = F3[:, 0:4096].rearrange("p (i d) -> p i d", i=32)
        sinr_v = F3[:, 4096:8192].rearrange("p (i d) -> p i d", i=32)
        xo = [F3[:, 8192:12288], F3[:, 12288:16384]]
        scbuf = F3[:, 12288:16384]
        idf_v = F3[:, 8192:8320]
        rps = F3[:, 8320:9600]
        ytile = F3[:, 4096:5120]
        ubf = B2[:, 24576:26624]
        ibf = B2[:, 26624:28672]
        sq_bf = B2[:, 0:4096]

        S.emit("vector", lambda e: e.memset(onesb[:, :], 1.0), inc="s_dve")
        S.emit("vector", lambda e: e.memset(onesf[:, :], 1.0), inc="s_dve")
        S.emit("vector", lambda e: e.memset(ncol[:, :], -EXP_C), inc="s_dve")
        S.emit("vector", lambda e: e.memset(epscol[:, :], EPS_LN), inc="s_dve")
        S.emit("vector", lambda e: e.memset(absacc[:, :], 0.0), inc="s_dve")
        S.emit("vector", lambda e: e.memset(scw[0:1, :], 0.0), inc="s_dve")
        v_consts = S.v("s_dve")
        S.dma("sync", idf_v, idf)
        ld_idf = S.v("dq_sync")

        xld = {}
        for tc in range(4):
            S.dma("gpsimd", xb[tc], x_fm[:, tc * 128:(tc + 1) * 128].rearrange("(i p) t -> p i t", p=128),
                  sem="d_xb%d" % tc)
            xld[tc] = S.v("d_xb%d" % tc)

        last_sq = 0
        for i in range(8):
            buf = xo[1]
            w = [("s_dve", S.v("s_dve")), ("s_act", S.v("s_act"))] if i >= 1 else []
            ldv = S.dma("sync", buf[:, 0:4096], x_own[i * 128:(i + 1) * 128, :], waits=w, sem="d_xo")
            S.emit("vector",
                   lambda e, buf=buf, i=i: e.reduce_sum(stat8[:, i:i + 1], buf[:, 0:4096], axis=AX.X),
                   waits=[("d_xo", ldv)], inc="s_dve")
            S.emit("scalar",
                   lambda e, buf=buf: e.activation(sq_bf, buf[:, 0:4096], AF.Square),
                   waits=[("d_xo", ldv), ("s_dve", last_sq)], inc="s_act")
            S.emit("vector",
                   lambda e, i=i: e.reduce_sum(stat8[:, 8 + i:9 + i], sq_bf, axis=AX.X),
                   waits=[("s_act", S.v("s_act"))], inc="s_dve")
            last_sq = S.v("s_dve")
        scr_free = [("s_dve", S.v("s_dve")), ("s_act", S.v("s_act"))]
        nr_v = stat8[:, 16:32].rearrange("p (i c) -> p i c", c=2)
        S.emit("vector", lambda e: e.tensor_scalar(nr_v[:, :, 0], stat8[:, 0:8], -1.0 / DM, None, ALU.mult),
               waits=[("s_dve", S.v("s_dve"))], inc="s_dve")
        S.emit("vector", lambda e: e.tensor_scalar(stat8[:, 8:16], stat8[:, 8:16], 1.0 / DM, None, ALU.mult),
               waits=[("s_dve", S.v("s_dve"))], inc="s_dve")
        S.emit("vector", lambda e: e.tensor_tensor(stat8[:, 0:8], nr_v[:, :, 0], nr_v[:, :, 0], ALU.mult),
               waits=[("s_dve", S.v("s_dve"))], inc="s_dve")
        S.emit("vector", lambda e: e.tensor_tensor(stat8[:, 8:16], stat8[:, 8:16], stat8[:, 0:8], ALU.subtract),
               waits=[("s_dve", S.v("s_dve"))], inc="s_dve")
        v_var = S.v("s_dve")
        S.emit("scalar", lambda e: e.activation(stat8[:, 0:8], stat8[:, 8:16], AF.Ln, bias=epscol[:, 0:1]),
               waits=[("s_dve", v_var)], inc="s_act")
        S.emit("scalar", lambda e: e.activation(stat8[:, 8:16], stat8[:, 0:8], AF.Exp, scale=-0.5),
               waits=[("s_act", S.v("s_act"))], inc="s_act")
        v_stats0 = S.v("s_act")
        S.emit("vector", lambda e: e.tensor_scalar(nr_v[:, :, 1], stat8[:, 8:16], 1.0, None, ALU.mult),
               waits=[("s_act", v_stats0)], inc="s_dve")
        v_stats = ("s_dve", S.v("s_dve"))
        S.dma("sync", stats_in.ap().rearrange("(i p) c -> p i c", p=128),
              stat8[:, 16:32].rearrange("p (i c) -> p i c", c=2),
              waits=[v_stats])
        st_stats = S.v("dq_sync")
        S.emit("gpsimd",
               lambda e: e.collective_compute("AllGather", ALU.bypass,
                                              replica_groups=[list(range(R))],
                                              ins=[stats_in.ap().opt()],
                                              outs=[stats_ag.ap().opt()]),
               waits=[("dq_sync", st_stats)], inc="s_cc")
        cc_stats = S.v("s_cc")
        S.dma("sync", stag[:, :].rearrange("p (i c) -> p i c", c=2),
              stats_ag.ap().rearrange("(i p) c -> p i c", p=128),
              waits=[("s_cc", cc_stats)])
        ld_stag = S.v("dq_sync")
        if debug:
            S.dma("sync", stats_ag_d, stats_ag.ap(), waits=[("s_cc", cc_stats)], sem="d_dbg")
            S.dma("sync", stat8_d, stat8[:, :], waits=[("s_cc", cc_stats)], sem="d_dbg")
        stag_v = stag[:, :].rearrange("p (i c) -> p i c", c=2)
        S.emit("tensor",
               lambda e: e.transpose(P0[0:64, 0:128], stag_v[:, :, 0], idf_v),
               waits=[("dq_sync", ld_stag), ("dq_sync", ld_idf)], inc="s_pe")
        S.emit("vector", lambda e: e.tensor_copy(n64[:, :], P0[0:64, 0:128]),
               waits=[("s_pe", S.v("s_pe"))], inc="s_dve")
        v_n64 = S.v("s_dve")
        S.dma("sync", nrow_dram, n64[:, :], waits=[("s_dve", v_n64)], sem="d_n")
        st_nrow = S.v("d_n")
        S.dma("sync", nall, nrow_dram.rearrange("(o a) b -> o (a b)", o=1),
              waits=[("d_n", st_nrow)], sem="d_n")
        ld_nall = S.v("d_n")

        S.dma("gpsimd", wq_v, wqkv.rearrange("(i p) o -> p i o", p=128),
              waits=[v_stats])
        ld_wqkv = S.v("dq_gpsimd")

        for ic in range(NIC):
            w = [("dq_gpsimd", ld_wqkv), ("s_dve", v_consts)] if ic == 0 else []
            S.emit("tensor",
                   lambda e, ic=ic: e.matmul(P1[0:1, 0:512], onesb[:, :], wq_v[:, ic, 0:512],
                                             start=(ic == 0), stop=(ic == NIC - 1)),
                   waits=w)
            S.emit("tensor",
                   lambda e, ic=ic: e.matmul(P1[0:1, 512:768], onesb[:, :], wq_v[:, ic, 512:768],
                                             start=(ic == 0), stop=(ic == NIC - 1)),
                   inc=("s_pe" if ic == NIC - 1 else None))
        v_w1mm = S.v("s_pe")
        S.emit("scalar", lambda e: e.copy(w1row[0:1, :], P1[0:1, 0:768]),
               waits=[("s_pe", v_w1mm)], inc="s_act")
        v_w1 = S.v("s_act")

        S.dma("sync", cos_v, cosT.rearrange("(i p) d -> p i d", p=128))
        S.dma("sync", sinr_v, sinrT.rearrange("(i p) d -> p i d", p=128))
        ld_tabs = S.v("dq_sync")

        slabs = []
        for wi, wsrc in enumerate((wg_s, wu_s, wd_s)):
            for c0 in range(0, 44032, 2048):
                slabs.append((wi, wsrc, c0, min(c0 + 2048, 44032)))
        scbufs = [scbuf[:, 0:2048], scbuf[:, 2048:4096]]
        sl_dve = {}

        def emit_scale_slab(k):
            wi, wsrc, c0, c1 = slabs[k]
            scbuf_k = scbufs[k % 2]
            scsem = "d_sc%d" % (k % 2)
            ldv = S.dma("scalar", scbuf_k[:, 0:c1 - c0], wsrc[:, c0:c1],
                        waits=(scr_free if k < 2 else [("s_dve", sl_dve.get(k - 2, 0))]),
                        sem=scsem)
            ldv = S.v(scsem)
            for a in range(0, c1 - c0, 512):
                S.emit("vector",
                       lambda e, a=a, scbuf_k=scbuf_k: e.reduce_sum(
                           absacc[:, 3:4], scbuf_k[:, a:a + 512],
                           axis=AX.X, apply_absolute_value=True),
                       waits=([(scsem, ldv)] if a == 0 else [("s_dve", S.v("s_dve"))]),
                       inc="s_dve")
                S.emit("vector",
                       lambda e, wi=wi: e.tensor_tensor(absacc[:, wi:wi + 1], absacc[:, wi:wi + 1],
                                                        absacc[:, 3:4], ALU.add),
                       waits=[("s_dve", S.v("s_dve"))], inc="s_dve")
            sl_dve[k] = S.v("s_dve")

        scale_state = {}

        def emit_scale_finish():
            S.emit("tensor", lambda e: e.transpose(P2[0:4, 0:128], absacc[:, 0:4], idf_v),
                   waits=[("s_dve", S.v("s_dve")), ("dq_sync", ld_idf)], inc="s_pe")
            S.emit("scalar", lambda e: e.copy(scf[0:4, 0:128], P2[0:4, 0:128]),
                   waits=[("s_pe", S.v("s_pe"))], inc="s_act")
            scale_state["p2done"] = S.v("s_act")
            base = 0
            n = 128
            w = [("s_act", S.v("s_act"))]
            while n > 1:
                h = n // 2
                src = scf[0:4, base:base + n].rearrange("p (a b) -> p a b", b=2)
                S.emit("vector",
                       lambda e, src=src, base=base, n=n, h=h: e.tensor_tensor(
                           scf[0:4, base + n:base + n + h], src[:, :, 0], src[:, :, 1], ALU.add),
                       waits=w + [("s_dve", S.v("s_dve"))], inc="s_dve")
                w = []
                base += n
                n = h
            S.dma("sync", scales_in.ap(), scf[0:4, 254:255], waits=[("s_dve", S.v("s_dve"))],
                  sem="d_thr")
            S.emit("gpsimd",
                   lambda e: e.collective_compute("AllGather", ALU.bypass,
                                                  replica_groups=[list(range(R))],
                                                  ins=[scales_in.ap().opt()],
                                                  outs=[scales_out.ap().opt()]),
                   waits=[("d_thr", S.v("d_thr"))], inc="s_cc")
            S.dma("sync", scr32[0:1, 0:32],
                  scales_out.ap().rearrange("(o a) b -> o (a b)", o=1),
                  waits=[("s_cc", S.v("s_cc"))], sem="d_thr")
            ldt = S.v("d_thr")
            S.emit("vector", lambda e: e.tensor_copy(scw[0:1, 0:3], scr32[0:1, 20:23]),
                   waits=[("d_thr", ldt)], inc="s_dve")
            for j in range(1, 8):
                r = (5 + j) % 8
                S.emit("vector",
                       lambda e, r=r: e.tensor_tensor(scw[0:1, 0:3], scw[0:1, 0:3],
                                                      scr32[0:1, 4 * r:4 * r + 3], ALU.add),
                       waits=[("s_dve", S.v("s_dve"))], inc="s_dve")
            if debug:
                S.emit("vector", lambda e: e.tensor_copy(scw[0:1, 4:7], scw[0:1, 0:3]),
                       waits=[("s_dve", S.v("s_dve"))], inc="s_dve")
                S.dma("sync", scales_d, scw[0:1, 0:8], waits=[("s_dve", S.v("s_dve"))], sem="d_dbg")
            S.emit("vector", lambda e: e.tensor_scalar(scw[0:1, 0:3], scw[0:1, 0:3],
                                                       float(np.float32(1.0) / np.float32(NW)), None, ALU.mult),
                   waits=[("s_dve", S.v("s_dve"))], inc="s_dve")
            S.emit("vector", lambda e: e.tensor_scalar(scw[0:1, 4:7], scw[0:1, 0:3], EPS_Q, 0.5, ALU.add, ALU.mult),
                   waits=[("s_dve", S.v("s_dve"))], inc="s_dve")
            S.emit("vector", lambda e: e.tensor_tensor(scw[0:1, 3:4], scw[0:1, 0:1], scw[0:1, 1:2], ALU.mult),
                   waits=[("s_dve", S.v("s_dve"))], inc="s_dve")
            S.emit("vector", lambda e: e.tensor_tensor(scw[0:1, 7:8], scw[0:1, 3:4], scw[0:1, 2:3], ALU.mult),
                   waits=[("s_dve", S.v("s_dve"))], inc="s_dve")
            S.dma("sync", thr_dram[0:1, 0:4], scw[0:1, 4:8], waits=[("s_dve", S.v("s_dve"))],
                  sem="d_thr")
            S.dma("sync", thrb[:, 0:4], thr_dram[0:1, 0:4].partition_broadcast(128),
                  waits=[("d_thr", S.v("d_thr"))], sem="d_thr")
            S.emit("vector", lambda e: e.tensor_scalar(thrb[:, 4:8], thrb[:, 0:4], -1.0, None, ALU.mult),
                   waits=[("d_thr", S.v("d_thr"))], inc="s_dve")
            scale_state["thr"] = S.v("s_dve")

        pe_tc = {}
        dve_rope = {}
        spill_v = {}
        n_scale = 0
        for tc in range(NTC):
            par = tc % 2
            b4 = tc % 4
            P = PQ[par]
            if tc >= 4:
                S.dma("gpsimd", xb[b4], x_fm[:, tc * 128:(tc + 1) * 128].rearrange("(i p) t -> p i t", p=128),
                      waits=[("s_pe", pe_tc.get(tc - 4, 0))], sem="d_xb%d" % b4)
                xld[tc] = S.v("d_xb%d" % b4)
            for ic in range(NIC):
                w = []
                if ic == 0:
                    w = [("d_xb%d" % b4, xld[tc]), ("d_n", ld_nall), ("s_act", v_w1),
                         ("s_dve", dve_rope.get(tc - 2, 0))]
                S.emit("tensor",
                       lambda e, ic=ic, P=P, b4=b4: e.matmul(
                           P[:, 0:512], xb[b4][:, ic, :], wq_v[:, ic, 0:512],
                           start=(ic == 0), stop=False),
                       waits=w)
                S.emit("tensor",
                       lambda e, ic=ic, P=P, b4=b4: e.matmul(
                           P[:, 512:768], xb[b4][:, ic, :], wq_v[:, ic, 512:768],
                           start=(ic == 0), stop=False))
            S.emit("tensor",
                   lambda e, P=P, tc=tc: e.matmul(P[:, 0:512], nall[0:1, tc * 128:(tc + 1) * 128],
                                                  w1row[0:1, 0:512], start=False, stop=True))
            S.emit("tensor",
                   lambda e, P=P, tc=tc: e.matmul(P[:, 512:768], nall[0:1, tc * 128:(tc + 1) * 128],
                                                  w1row[0:1, 512:768], start=False, stop=True),
                   inc="s_pe")
            pe_tc[tc] = S.v("s_pe")

            ci = tc % 32
            rA = rps[:, 0:640]
            rB = rps[:, 640:1280]
            spsem = "d_sp" if par == 0 else "d_sp1"
            wv = [("s_pe", pe_tc[tc]), ("dq_sync", ld_tabs), ("dq_sync", ld_stag),
                  (spsem, spill_v.get(tc - 2, 0)), ("s_dve", dve_rope.get(tc - 1, 0))]
            for blk in range(5):
                c0 = blk * 128
                S.emit("vector",
                       lambda e, P=P, c0=c0, ci=ci: e.tensor_tensor(
                           rps[:, c0:c0 + 128], P[:, c0:c0 + 128], cos_v[:, ci, :], ALU.mult),
                       waits=(wv if blk == 0 else ()))
                S.emit("vector",
                       lambda e, P=P, c0=c0, ci=ci: e.tensor_tensor(
                           rps[:, 640 + c0:640 + c0 + 128],
                           P[:, c0:c0 + 128].rearrange("p (g c) -> p g c", g=2)[:, ::-1, :],
                           sinr_v[:, ci, :].rearrange("p (g c) -> p g c", g=2),
                           ALU.mult),
                       inc=("s_dve" if blk == 4 else None))
            S.emit("vector", lambda e: e.tensor_tensor(rA, rA, rB, ALU.add),
                   waits=[("s_dve", S.v("s_dve"))], inc="s_dve")
            S.emit("vector",
                   lambda e, tc=tc, par=par: e.tensor_scalar(
                       outb[:, par * 768:par * 768 + 640], rA,
                       stag_v[:, tc, 1:2], None, ALU.mult),
                   waits=[("s_dve", S.v("s_dve"))])
            S.emit("vector",
                   lambda e, tc=tc, par=par, P=P: e.tensor_scalar(
                       outb[:, par * 768 + 640:par * 768 + 768], P[:, 640:768],
                       stag_v[:, tc, 1:2], None, ALU.mult),
                   inc="s_dve")
            dve_rope[tc] = S.v("s_dve")
            S.dma("sync", qkv_spill[tc * 128:(tc + 1) * 128, :], outb[:, par * 768:(par + 1) * 768],
                  waits=[("s_dve", dve_rope[tc])], sem=spsem)
            spill_v[tc] = S.v(spsem)
            if n_scale < len(slabs):
                emit_scale_slab(n_scale)
                n_scale += 1
        while n_scale < len(slabs):
            emit_scale_slab(n_scale)
            n_scale += 1
        emit_scale_finish()
        spill_done = [("d_sp", spill_v[NTC - 2]), ("d_sp1", spill_v[NTC - 1])]
        pe_A_done = pe_tc[NTC - 1]

        k_sb = A1[:, 0:8192]
        v_sb = A1[:, 8192:16640]
        v_sb3 = A1[:, 8192:16640].rearrange("p (g c) -> p g c", g=64)
        m_bf = A1[:, 16640:18688]
        qsl = A1[:, 18688:19712]
        ebuf = A1[:, 19712:21760]
        ostg = A1[:, 21760:25856]

        S.emit("vector", lambda e: e.memset(v_sb[:, :], 1.0),
               waits=[("s_pe", pe_A_done)], inc="s_dve")
        v_vones = S.v("s_dve")
        ld_mask = S.dma("sync", F3[:, 0:2048], maskT, waits=[("s_dve", dve_rope[NTC - 1])],
                        sem="d_msk")
        S.emit("vector", lambda e: e.tensor_copy(m_bf[:, :], F3[:, 0:2048]),
               waits=[("d_msk", ld_mask)], inc="s_dve")
        v_mask = S.v("s_dve")

        for j in range(16):
            S.dma("sync", k_sb[:, j * 512:(j + 1) * 512],
                  qkv_spill[j * 512:(j + 1) * 512, 512:640],
                  waits=spill_done + [("s_dve", v_vones)], transpose=True, sem="d_tr")
        ld_k = S.v("d_tr")
        for g in range(64):
            S.dma("sync", v_sb3[:, g, 0:128],
                  qkv_spill[g * 128:(g + 1) * 128, 640:768],
                  waits=[("s_dve", v_vones)] + spill_done)
        ld_v = S.v("dq_sync")

        wo_buf = [B2[:, 0:16384].rearrange("p (h o) -> p h o", h=32),
                  B2[:, 16384:32768].rearrange("p (h o) -> p h o", h=32)]
        wo_ld = {}
        for oc in range(2):
            S.dma("gpsimd", wo_buf[oc],
                  woT[:, oc * 512:(oc + 1) * 512].rearrange("(h p) o -> p h o", p=128),
                  waits=[("s_pe", pe_A_done)], sem="d_wo%d" % oc)
            wo_ld[oc] = S.v("d_wo%d" % oc)

        groups = [(b, qc, h) for b in range(NBATCH) for qc in range(8) for h in range(NHL)]
        pairs = []
        for gi, (b, qc, h) in enumerate(groups):
            npairs = 2 * (qc + 1)
            for j in range(npairs):
                pairs.append((gi, b, qc, h, j, npairs))
        NPAIR = len(pairs)

        qld = {}
        pe_st_group = {}
        exp_done = {}
        av_done = {}
        ev_po = {"v": 0}
        st_ostg = {0: 0, 1: 0}
        evac_cnt = {}

        def emit_qload(gi):
            b, qc, h = groups[gi]
            gpar = gi % 2
            qsem = "d_q" if gpar == 0 else "d_q1"
            S.dma("sync", qsl[:, gpar * 512:(gpar + 1) * 512],
                  qkv_spill[b * SEQ + qc * 512: b * SEQ + qc * 512 + 512, h * 128:(h + 1) * 128],
                  waits=[("s_pe", pe_st_group.get(gi - 2, 0)), ("d_tr", ld_k), ("dq_sync", ld_v)],
                  transpose=True, sem=qsem)
            qld[gi] = (qsem, S.v(qsem))

        def emit_st(p):
            gi, b, qc, h, j, npairs = pairs[p]
            spar = p % 2
            SP = PQ[spar]
            gpar = gi % 2
            qv = qsl[:, gpar * 512:(gpar + 1) * 512]
            w = [("s_act", exp_done.get(p - 2, 0)), qld[gi]]
            if p <= 1:
                w.append(("s_act", v_w1))
                w.append(("s_dve", dve_rope[NTC - 1]))
            kb0 = 2 * j
            S.emit("tensor",
                   lambda e, SP=SP, b=b, kb0=kb0, qv=qv: e.matmul(
                       SP[:, 0:512], k_sb[:, b * SEQ + kb0 * 128: b * SEQ + kb0 * 128 + 128],
                       qv, start=True, stop=True),
                   waits=w)
            S.emit("tensor",
                   lambda e, SP=SP, b=b, kb0=kb0, qv=qv: e.matmul(
                       SP[:, 512:1024], k_sb[:, b * SEQ + kb0 * 128 + 128: b * SEQ + kb0 * 128 + 256],
                       qv, start=True, stop=True),
                   inc="s_pe")
            pairs_pe[p] = S.v("s_pe")
            pe_st_group[gi] = S.v("s_pe")

        def emit_mask_exp(p):
            gi, b, qc, h, j, npairs = pairs[p]
            spar = p % 2
            SP = PQ[spar]
            diag = (j >= npairs - 2)
            wexp = [("s_pe", pairs_pe[p]), ("s_pe", av_done.get(p - 2, 0))]
            if diag:
                v0 = 2 * (j - (npairs - 2))
                S.emit("vector",
                       lambda e, SP=SP, v0=v0: e.tensor_tensor(
                           SP[:, 0:1024], SP[:, 0:1024], m_bf[:, v0 * 512: v0 * 512 + 1024], ALU.add),
                       waits=[("s_pe", pairs_pe[p]), ("s_dve", v_mask)], inc="s_dve")
                wexp.append(("s_dve", S.v("s_dve")))
            S.emit("scalar",
                   lambda e, SP=SP, spar=spar: e.activation(
                       ebuf[:, spar * 1024:(spar + 1) * 1024], SP[:, 0:1024], AF.Exp,
                       bias=ncol[:, 0:1], scale=ISQ),
                   waits=wexp, inc="s_act")
            exp_done[p] = S.v("s_act")

        def emit_av(p):
            gi, b, qc, h, j, npairs = pairs[p]
            spar = p % 2
            kb0 = 2 * j
            first = (j == 0)
            last = (j == npairs - 1)
            for kk in range(2):
                kbg = b * 32 + kb0 + kk
                for qs in range(4):
                    slot = PV[qs // 2]
                    c0 = (qs % 2) * 512
                    w = []
                    if first and kk == 0 and qs == 0:
                        w = [("s_act", exp_done[p]), ("s_dve", ev_po["v"])]
                    elif kk == 0 and qs == 0:
                        w = [("s_act", exp_done[p])]
                    S.emit("tensor",
                           lambda e, slot=slot, c0=c0, spar=spar, kk=kk, qs=qs, kbg=kbg,
                                  first=first, last=last: e.matmul(
                               slot[:, c0:c0 + 129],
                               ebuf[:, spar * 1024 + kk * 512 + qs * 128: spar * 1024 + kk * 512 + qs * 128 + 128],
                               v_sb3[:, kbg, 0:129],
                               start=(first and kk == 0), stop=(last and kk == 1)),
                           waits=w,
                           inc=("s_pe" if (kk == 1 and qs == 3) else None))
            av_done[p] = S.v("s_pe")

        def emit_evac(p):
            gi, b, qc, h, j, npairs = pairs[p]
            opar = (b * 8 + qc) % 2
            osem = "d_ost" if opar == 0 else "d_ost1"
            w = [("s_pe", av_done[p]), (osem, st_ostg[opar]), ("s_dve", ev_po["v"])]
            for qs in range(4):
                slot = PV[qs // 2]
                c0 = (qs % 2) * 512
                S.emit("vector",
                       lambda e, slot=slot, c0=c0, qs=qs: e.reciprocal(
                           zc[:, qs:qs + 1], slot[:, c0 + 128:c0 + 129]),
                       waits=(w if qs == 0 else ()),
                       inc=("s_dve" if qs == 3 else None))
            vrec = S.v("s_dve")
            for qs in range(4):
                slot = PV[qs // 2]
                c0 = (qs % 2) * 512
                S.emit("vector",
                       lambda e, slot=slot, c0=c0, qs=qs, opar=opar, h=h: e.tensor_scalar(
                           ostg[:, opar * 2048 + qs * 512 + h * 128: opar * 2048 + qs * 512 + h * 128 + 128],
                           slot[:, c0:c0 + 128], zc[:, qs:qs + 1], None, ALU.mult),
                       waits=([("s_dve", vrec)] if qs == 0 else ()), inc="s_dve")
            ev_po["v"] = S.v("s_dve")
            evac_cnt[(b, qc)] = S.v("s_dve")

        pairs_pe = {}
        p = 0
        cur_g = -1
        emit_qload(0)
        emit_qload(1)
        for p in range(NPAIR):
            gi, b, qc, h, j, npairs = pairs[p]
            if j == npairs - 1 and gi + 2 < len(groups):
                emit_qload(gi + 2)
            if p == 0:
                emit_st(0)
            if p + 1 < NPAIR:
                emit_st(p + 1)
            emit_mask_exp(p)
            emit_av(p)
            if j == npairs - 1:
                emit_evac(p)
                if h == NHL - 1:
                    opar = (b * 8 + qc) % 2
                    t0 = b * SEQ + qc * 512
                    osem2 = "d_ost" if opar == 0 else "d_ost1"
                    S.dma("sync",
                          a2a_in.ap()[t0:t0 + 512, :].rearrange("(qs p) o -> p qs o", p=128),
                          ostg[:, opar * 2048:(opar + 1) * 2048].rearrange("p (qs o) -> p qs o", qs=4),
                          waits=[("s_dve", evac_cnt[(b, qc)])], sem=osem2)
                    st_ostg[opar] = S.v(osem2)

        a2a_ready = [("d_ost", S.v("d_ost")), ("d_ost1", S.v("d_ost1"))]
        if debug:
            S.dma("sync", a2a_in_d, a2a_in.ap(), waits=a2a_ready, sem="d_dbg")
        S.emit("gpsimd",
               lambda e: e.collective_compute("AllToAll", ALU.bypass,
                                              replica_groups=[list(range(R))],
                                              ins=[a2a_in.ap().opt()],
                                              outs=[a2a_out.ap().opt()]),
               waits=a2a_ready, inc="s_cc")
        cc_a2a = S.v("s_cc")

        ofm = A1[:, :].rearrange("p (h t) -> p h t", h=32)
        for hd_c in range(32):
            r_ = hd_c // 4
            lb = hd_c % 4
            S.dma("sync", ofm[:, hd_c, :],
                  a2a_out.ap()[r_ * 1024:(r_ + 1) * 1024, lb * 128:(lb + 1) * 128],
                  waits=[("s_cc", cc_a2a)], transpose=True, sem="d_tr")
        ld_ofm = S.v("d_tr")

        xo3 = [xo[k].rearrange("p (t o) -> p t o", t=8) for k in range(2)]
        pe_C = {}
        dve_C = {}
        xo_ld = {}
        st_x1 = {}
        for oc in range(8):
            par = oc % 2
            xcsem = "d_xc" if par == 0 else "d_xc1"
            if oc >= 2:
                S.dma("gpsimd", wo_buf[par],
                      woT[:, oc * 512:(oc + 1) * 512].rearrange("(h p) o -> p h o", p=128),
                      waits=[("s_pe", pe_C.get(oc - 2, 0))], sem="d_wo%d" % par)
                wo_ld[oc] = S.v("d_wo%d" % par)
            S.dma("sync", xo3[par][:, :, :],
                  x_own[:, oc * 512:(oc + 1) * 512].rearrange("(t p) o -> p t o", p=128),
                  waits=[("s_dve", dve_C.get(oc - 2, 0)), ("s_dve", scale_state["thr"])], sem=xcsem)
            xo_ld[oc] = S.v(xcsem)
            for tc in range(8):
                slot = [P0, P1, P2, P3][tc // 2]
                c0 = (tc % 2) * 512
                for hd_c in range(32):
                    w = []
                    if hd_c == 0:
                        w = [("d_tr", ld_ofm), ("d_wo%d" % par, wo_ld[oc]),
                             ("s_dve", dve_C.get(oc - 1, 0)),
                             ("s_act", scale_state["p2done"])]
                    S.emit("tensor",
                           lambda e, slot=slot, c0=c0, hd_c=hd_c, tc=tc, par=par: e.matmul(
                               slot[:, c0:c0 + 512], ofm[:, hd_c, tc * 128:(tc + 1) * 128],
                               wo_buf[par][:, hd_c, :],
                               start=(hd_c == 0), stop=(hd_c == 31)),
                           waits=w,
                           inc=("s_pe" if hd_c == 31 else None))
                pe_C[oc] = S.v("s_pe")
                ypar = tc % 2
                S.emit("vector",
                       lambda e, slot=slot, c0=c0, tc=tc, par=par, ypar=ypar: e.tensor_tensor(
                           ytile[:, ypar * 512:(ypar + 1) * 512], slot[:, c0:c0 + 512],
                           xo3[par][:, tc, :], ALU.add),
                       waits=[("s_pe", pe_C[oc]), (xcsem, xo_ld[oc]),
                              ("d_x1s" if ypar == 0 else "d_x1s1",
                               st_x1.get((oc, tc - 2), st_x1.get((oc - 1, tc + 6), 0)))],
                       inc="s_dve")
                dve_C[oc] = S.v("s_dve")
                x1sem = "d_x1s" if ypar == 0 else "d_x1s1"
                S.dma("sync", x1_dram[tc * 128:(tc + 1) * 128, oc * 512:(oc + 1) * 512],
                      ytile[:, ypar * 512:(ypar + 1) * 512],
                      waits=[("s_dve", dve_C[oc])], sem=x1sem)
                st_x1[(oc, tc)] = S.v(x1sem)
        x1_done = [("d_x1s", S.v("d_x1s")), ("d_x1s1", S.v("d_x1s1"))]

        x1n_t = [B2[:, 16384 + k * 4096:16384 + (k + 1) * 4096] for k in range(2)]
        st_x1n = {}
        act_D = {}
        for tc in range(8):
            par = tc % 2
            xdsem = "d_xd%d" % par
            ldv = S.dma("sync", xo[par][:, 0:4096], x1_dram[tc * 128:(tc + 1) * 128, :],
                        waits=x1_done + [act_D.get(tc - 2, ("s_dve", 0))], sem=xdsem)
            ldv = S.v(xdsem)
            S.emit("vector", lambda e, par=par: e.reduce_sum(r2c[:, 0:1], xo[par][:, 0:4096], axis=AX.X),
                   waits=[(xdsem, ldv), act_D.get(tc - 1, ("s_dve", 0))], inc="s_dve")
            S.emit("scalar", lambda e, par=par: e.activation(sq_bf, xo[par][:, 0:4096], AF.Square),
                   waits=[(xdsem, ldv), ("s_dve", S.v("s_dve"))], inc="s_act")
            S.emit("vector", lambda e: e.reduce_sum(r2c[:, 1:2], sq_bf, axis=AX.X),
                   waits=[("s_act", S.v("s_act"))], inc="s_dve")
            S.emit("vector", lambda e: e.tensor_scalar(r2c[:, 2:3], r2c[:, 0:1], -1.0 / DM, None, ALU.mult),
                   waits=[("s_dve", S.v("s_dve"))], inc="s_dve")
            S.emit("vector", lambda e: e.tensor_scalar(r2c[:, 3:4], r2c[:, 1:2], 1.0 / DM, None, ALU.mult),
                   waits=[("s_dve", S.v("s_dve"))], inc="s_dve")
            S.emit("vector", lambda e: e.tensor_tensor(r2c[:, 4:5], r2c[:, 2:3], r2c[:, 2:3], ALU.mult),
                   waits=[("s_dve", S.v("s_dve"))], inc="s_dve")
            S.emit("vector", lambda e: e.tensor_tensor(r2c[:, 5:6], r2c[:, 3:4], r2c[:, 4:5], ALU.subtract),
                   waits=[("s_dve", S.v("s_dve"))], inc="s_dve")
            v_d = S.v("s_dve")
            S.emit("scalar", lambda e: e.activation(r2c[:, 6:7], r2c[:, 5:6], AF.Ln, bias=epscol[:, 0:1]),
                   waits=[("s_dve", v_d)], inc="s_act")
            S.emit("scalar", lambda e: e.activation(r2c[:, 7:8], r2c[:, 6:7], AF.Exp, scale=-0.5),
                   waits=[("s_act", S.v("s_act"))], inc="s_act")
            v_r2 = S.v("s_act")
            x1nsem = "d_x1n" if par == 0 else "d_x1n1"
            S.emit("vector",
                   lambda e, par=par: e.tensor_scalar(x1n_t[par], xo[par][:, 0:4096],
                                                      r2c[:, 2:3], r2c[:, 7:8], ALU.add, ALU.mult),
                   waits=[("s_act", v_r2), (x1nsem, st_x1n.get(tc - 2, 0))], inc="s_dve")
            act_D[tc] = ("s_dve", S.v("s_dve"))
            S.dma("sync", x1n_dram[tc * 128:(tc + 1) * 128, :], x1n_t[par],
                  waits=[act_D[tc]], sem=x1nsem)
            st_x1n[tc] = S.v(x1nsem)
        x1n_done = [("d_x1n", S.v("d_x1n")), ("d_x1n1", S.v("d_x1n1"))]

        xn = A1[:, :].rearrange("p (i t) -> p i t", i=32)
        for ic in range(NIC):
            S.dma("sync", xn[:, ic, :], x1n_dram[0:TOWN, ic * 128:(ic + 1) * 128],
                  waits=x1n_done + [("s_pe", pe_C[7])], transpose=True, sem="d_tr")
        ld_xn = S.v("d_tr")

        qg = [B2[:, 0:4096], B2[:, 4096:8192]]
        qu = [B2[:, 8192:12288], B2[:, 12288:16384]]
        scr1 = B2[:, 16384:20480]
        scr2 = B2[:, 20480:24576]
        wgf = [F3[:, 0:4096], F3[:, 4096:8192]]
        wuf = [F3[:, 8192:12288], F3[:, 12288:16384]]
        pe_E = {}
        dve_E = {}
        act_E = {}
        ib_st = {}
        ew_ld = {}

        def emit_E_load(j):
            par = j % 2
            wsem = "d_w" if par == 0 else "d_w1"
            S.dma("sync", wgf[par], wg_tiles[j].rearrange("p i g -> p (i g)"),
                  waits=[("s_dve", dve_E.get(j - 2, 0))] + (x1n_done if j < 2 else []),
                  sem=wsem)
            S.dma("sync", wuf[par], wu_tiles[j].rearrange("p i g -> p (i g)"), sem=wsem)
            ew_ld[j] = (wsem, S.v(wsem))

        def emit_E_quant(j):
            par = j % 2
            wq2 = [ew_ld[j], ("s_dve", scale_state["thr"]),
                   ("s_pe", pe_E.get(j - 2, 0)), ("s_dve", dve_E.get(j - 1, 0))]
            S.emit("vector", lambda e, par=par: e.tensor_scalar(scr1, wgf[par], thrb[:, 0:1], None, ALU.is_gt),
                   waits=wq2)
            S.emit("vector", lambda e, par=par: e.tensor_scalar(scr2, wgf[par], thrb[:, 4:5], None, ALU.is_lt),
                   inc="s_dve")
            S.emit("vector", lambda e, par=par: e.tensor_tensor(qg[par], scr1, scr2, ALU.subtract),
                   waits=[("s_dve", S.v("s_dve"))], inc="s_dve")
            vqg = S.v("s_dve")
            S.emit("vector", lambda e, par=par: e.tensor_scalar(scr1, wuf[par], thrb[:, 1:2], None, ALU.is_gt),
                   waits=[ew_ld[j], ("s_dve", vqg)])
            S.emit("vector", lambda e, par=par: e.tensor_scalar(scr2, wuf[par], thrb[:, 5:6], None, ALU.is_lt),
                   inc="s_dve")
            S.emit("vector", lambda e, par=par: e.tensor_tensor(qu[par], scr1, scr2, ALU.subtract),
                   waits=[("s_dve", S.v("s_dve"))], inc="s_dve")
            dve_E[j] = S.v("s_dve")

        emit_E_load(0)
        emit_E_load(1)
        emit_E_quant(0)
        for j in range(NGC):
            par = j % 2
            if j + 2 < NGC:
                emit_E_load(j + 2)
            if j + 1 < NGC:
                emit_E_quant(j + 1)
            vq = dve_E[j]
            PG = PQ[par]
            PU = PV[par]
            for ic in range(NIC):
                w = []
                if ic == 0:
                    w = [("s_dve", vq), ("d_tr", ld_xn),
                         ("s_dve", dve_E.get(("i", j - 2), 0)),
                         ("s_act", act_E.get(j - 2, 0))]
                S.emit("tensor",
                       lambda e, PG=PG, par=par, ic=ic: e.matmul(
                           PG[:, 0:512], qg[par][:, ic * 128:(ic + 1) * 128],
                           xn[:, ic, 0:512], start=(ic == 0), stop=(ic == NIC - 1)),
                       waits=w)
                S.emit("tensor",
                       lambda e, PG=PG, par=par, ic=ic: e.matmul(
                           PG[:, 512:1024], qg[par][:, ic * 128:(ic + 1) * 128],
                           xn[:, ic, 512:1024], start=(ic == 0), stop=(ic == NIC - 1)))
            for ic in range(NIC):
                S.emit("tensor",
                       lambda e, PU=PU, par=par, ic=ic: e.matmul(
                           PU[:, 0:512], qu[par][:, ic * 128:(ic + 1) * 128],
                           xn[:, ic, 0:512], start=(ic == 0), stop=(ic == NIC - 1)))
                S.emit("tensor",
                       lambda e, PU=PU, par=par, ic=ic: e.matmul(
                           PU[:, 512:1024], qu[par][:, ic * 128:(ic + 1) * 128],
                           xn[:, ic, 512:1024], start=(ic == 0), stop=(ic == NIC - 1)),
                       inc=("s_pe" if ic == NIC - 1 else None))
            pe_E[j] = S.v("s_pe")
            S.emit("scalar", lambda e, PU=PU, par=par: e.copy(ubf[:, par * 1024:(par + 1) * 1024], PU[:, 0:1024]),
                   waits=[("s_pe", pe_E[j]), ("s_dve", dve_E.get(("i", j - 2), 0))], inc="s_act")
            act_E[j] = S.v("s_act")
            S.emit("vector",
                   lambda e, PG=PG, par=par: e.tensor_tensor(
                       ibf[:, par * 1024:(par + 1) * 1024], PG[:, 0:1024],
                       ubf[:, par * 1024:(par + 1) * 1024], ALU.mult),
                   waits=[("s_act", act_E[j]),
                          ("d_ib%d" % par, ib_st.get(j - 2, 0))],
                   inc="s_dve")
            dve_E[("i", j)] = S.v("s_dve")
            S.dma("scalar", inter_dram[j * 128:(j + 1) * 128, :], ibf[:, par * 1024:(par + 1) * 1024],
                  waits=[("s_dve", dve_E[("i", j)])], sem="d_ib%d" % par)
            ib_st[j] = S.v("d_ib%d" % par)
        ib_done = [("d_ib0", S.v("d_ib0")), ("d_ib1", S.v("d_ib1"))]
        pe_E_done = pe_E[NGC - 1]

        int_sb = [B2[:, 0:4096].rearrange("p (i t) -> p i t", i=4),
                  B2[:, 4096:8192].rearrange("p (i t) -> p i t", i=4)]
        qd = [B2[:, 8192:10240], B2[:, 10240:12288]]
        fs1 = B2[:, 12288:14336]
        fs2 = B2[:, 14336:16384]
        wdf = [F3[:, 0:2048], F3[:, 2048:4096]]
        x1o = [F3[:, 8192:12288].rearrange("p (t o) -> p t o", t=8),
               F3[:, 12288:16384].rearrange("p (t o) -> p t o", t=8)]
        batches = [(i, min(i + 4, NGC)) for i in range(0, NGC, 4)]
        last_qd = [0]
        pe_F = {}
        dve_F = {}
        st_y = {}
        ld_F = {}
        dve_E_all = dve_E[NGC - 1]
        dve_E_int = dve_E[("i", NGC - 1)]
        act_E_all = act_E[NGC - 1]
        for oc in range(8):
            opar = oc % 2
            xfsem = "d_xf" if opar == 0 else "d_xf1"
            S.dma("sync", x1o[opar][:, :, :],
                  x1_dram[:, oc * 512:(oc + 1) * 512].rearrange("(t p) o -> p t o", p=128),
                  waits=[("s_dve", dve_F.get(oc - 2, 0))] + x1n_done +
                        [("s_dve", dve_E_all)], sem=xfsem)
            x1ld = S.v(xfsem)
            for bi, (ic0, ic1) in enumerate(batches):
                n = ic1 - ic0
                bpar = bi % 2
                wq1 = [("s_dve", dve_F.get(("q", oc, bi - 2), dve_E_all if oc == 0 else 0)),
                       ("s_pe", pe_F.get((oc, bi - 2), pe_E_done if (oc == 0 and bi < 2) else 0))]
                wdsem = "d_wd" if bpar == 0 else "d_wd1"
                disem = "d_i" if bpar == 0 else "d_i1"
                lw = S.dma("sync", wdf[bpar][:, 0:n * 512].rearrange("p (i o) -> p i o", i=n),
                           wd_tiles[oc, ic0:ic1].rearrange("i p o -> p i o"),
                           waits=wq1, sem=wdsem)
                li = S.dma("sync", int_sb[bpar][:, 0:n, :],
                           inter_dram[ic0 * 128:ic1 * 128, :].rearrange("(i p) t -> p i t", p=128),
                           waits=ib_done + [("s_pe", pe_F.get((oc, bi - 2), 0))], sem=disem)
                S.emit("vector", lambda e, bpar=bpar, n=n: e.tensor_scalar(
                           fs1[:, 0:n * 512], wdf[bpar][:, 0:n * 512], thrb[:, 2:3], None, ALU.is_gt),
                       waits=[(wdsem, lw), ("s_dve", scale_state["thr"]),
                              ("s_dve", last_qd[0])])
                S.emit("vector", lambda e, bpar=bpar, n=n: e.tensor_scalar(
                           fs2[:, 0:n * 512], wdf[bpar][:, 0:n * 512], thrb[:, 6:7], None, ALU.is_lt),
                       inc="s_dve")
                S.emit("vector", lambda e, bpar=bpar, n=n: e.tensor_tensor(
                           qd[bpar][:, 0:n * 512], fs1[:, 0:n * 512], fs2[:, 0:n * 512], ALU.subtract),
                       waits=[("s_dve", S.v("s_dve"))], inc="s_dve")
                dve_F[("q", oc, bi)] = S.v("s_dve")
                last_qd[0] = S.v("s_dve")
                for i in range(n):
                    icg = ic0 + i
                    for tc in range(8):
                        slot = [P0, P1, P2, P3][tc // 2]
                        c0 = (tc % 2) * 512
                        w = []
                        if i == 0 and tc == 0:
                            w = [("s_dve", dve_F[("q", oc, bi)]), (disem, li)]
                            if bi == 0:
                                w.append(("s_dve", dve_F.get(oc - 1, dve_E_int)))
                                w.append(("s_act", act_E_all))
                        S.emit("tensor",
                               lambda e, slot=slot, c0=c0, bpar=bpar, i=i, tc=tc, icg=icg: e.matmul(
                                   slot[:, c0:c0 + 512], int_sb[bpar][:, i, tc * 128:(tc + 1) * 128],
                                   qd[bpar][:, i * 512:(i + 1) * 512],
                                   start=(icg == 0), stop=(icg == NGC - 1)),
                               waits=w,
                               inc=("s_pe" if (i == n - 1 and tc == 7) else None))
                pe_F[(oc, bi)] = S.v("s_pe")
            for tc in range(8):
                slot = [P0, P1, P2, P3][tc // 2]
                c0 = (tc % 2) * 512
                ypar = tc % 2
                S.emit("vector",
                       lambda e, slot=slot, c0=c0, opar=opar, tc=tc, ypar=ypar: e.scalar_tensor_tensor(
                           ytile[:, ypar * 512:(ypar + 1) * 512], slot[:, c0:c0 + 512],
                           thrb[:, 3:4], x1o[opar][:, tc, :], ALU.mult, ALU.add),
                       waits=[("s_pe", pe_F[(oc, len(batches) - 1)]), (xfsem, x1ld),
                              ("d_ys" if ypar == 0 else "d_ys1",
                               st_y.get((oc, tc - 2), st_y.get((oc - 1, tc + 6), 0)))],
                       inc="s_dve")
                dve_F[oc] = S.v("s_dve")
                yssem = "d_ys" if ypar == 0 else "d_ys1"
                S.dma("sync", y[tc * 128:(tc + 1) * 128, oc * 512:(oc + 1) * 512],
                      ytile[:, ypar * 512:(ypar + 1) * 512],
                      waits=[("s_dve", dve_F[oc])], sem=yssem)
                st_y[(oc, tc)] = S.v(yssem)

        S.emit("sync", lambda e: None,
               waits=[(nm, S.v(nm)) for nm in sems
                      if nm.startswith("d") and S.v(nm) > 0])

        with allow_ici_all_to_all():
            with nc.Block() as block:
                S.run(block, handles)
    return nc


_NC_CACHE = {}


def _get_nc(debug=False):
    key = bool(debug)
    if key not in _NC_CACHE:
        _NC_CACHE[key] = build_nc(debug=key)
    return _NC_CACHE[key]


def make_in_maps(inputs):
    x = np.asarray(inputs["L0_attn_hidden_states"], dtype=np.float32)
    Wq = np.asarray(inputs["L0_attn_Wq"], dtype=np.float32)
    Wk = np.asarray(inputs["L0_attn_Wk"], dtype=np.float32)
    Wv = np.asarray(inputs["L0_attn_Wv"], dtype=np.float32)
    Wo = np.asarray(inputs["L0_attn_Wo"], dtype=np.float32)
    wg = np.asarray(inputs["L0_ffn_w_gate"], dtype=np.float32)
    wu = np.asarray(inputs["L0_ffn_w_up"], dtype=np.float32)
    wd = np.asarray(inputs["L0_ffn_w_down"], dtype=np.float32)

    xr = x.reshape(NT, DM)
    x_fm = np.ascontiguousarray(xr.T)
    woT = np.ascontiguousarray(Wo.T)
    wg_tiles = np.ascontiguousarray(wg.reshape(NGC, 128, NIC, 128).transpose(0, 3, 2, 1))
    wu_tiles = np.ascontiguousarray(wu.reshape(NGC, 128, NIC, 128).transpose(0, 3, 2, 1))
    wd_tiles = np.ascontiguousarray(wd.reshape(8, 512, NGC, 128).transpose(0, 2, 3, 1))
    wg_flat = wg_tiles.reshape(-1)
    wu_flat = wu_tiles.reshape(-1)
    wd_flat = wd_tiles.reshape(-1)
    nsl = wg_flat.size // R
    cos, sinr = rope_tables()
    masks = causal_masks()
    ident = np.eye(128, dtype=np.float32)

    in_maps = []
    for c in range(R):
        wqkv_c = np.ascontiguousarray(np.concatenate(
            [Wq[512 * c:512 * (c + 1), :].T, Wk[128 * c:128 * (c + 1), :].T,
             Wv[128 * c:128 * (c + 1), :].T], axis=1))
        in_maps.append({
            "x_fm": x_fm,
            "x_own": np.ascontiguousarray(xr[TOWN * c:TOWN * (c + 1), :]),
            "wqkv": wqkv_c,
            "woT": woT,
            "wg_tiles": wg_tiles,
            "wu_tiles": wu_tiles,
            "wd_tiles": wd_tiles,
            "wg_s": np.ascontiguousarray(wg_flat[nsl * c:nsl * (c + 1)].reshape(128, 44032)),
            "wu_s": np.ascontiguousarray(wu_flat[nsl * c:nsl * (c + 1)].reshape(128, 44032)),
            "wd_s": np.ascontiguousarray(wd_flat[nsl * c:nsl * (c + 1)].reshape(128, 44032)),
            "cosT": cos,
            "sinrT": sinr,
            "maskT": masks,
            "idf": ident,
        })
    return in_maps


def kernel(**inputs):
    nc = _get_nc(debug=False)
    in_maps = make_in_maps(inputs)
    res = run_bass_kernel_spmd(nc, in_maps, core_ids=list(range(R)))
    out = np.empty((NT, DM), dtype=np.float32)
    for c in range(R):
        out[TOWN * c:TOWN * (c + 1), :] = res.results[c]["y"]
    return out.reshape(NBATCH, SEQ, DM)


# version 10
# speedup vs baseline: 1.3179x; 1.0209x over previous
import math
from contextlib import ExitStack

import numpy as np
import ml_dtypes

import concourse.bass as bass
import concourse.mybir as mybir
from concourse.bass_utils import run_bass_kernel_spmd
from concourse.aot_env import allow_ici_all_to_all

F32 = mybir.dt.float32
BF16 = mybir.dt.bfloat16
AF = mybir.ActivationFunctionType
ALU = mybir.AluOpType
AX = mybir.AxisListType
BF = ml_dtypes.bfloat16

R = 8
SEQ = 4096
NBATCH = 2
NT = NBATCH * SEQ
DM = 4096
DI = 11008
NHL = 4
TOWN = NT // R
NTC = NT // 128
NIC = DM // 128
NGC = DI // 128
EPS_LN = 1e-5
EPS_Q = 1e-5
EXP_C = 20.0
ISQ = 1.0 / math.sqrt(128.0)
NEG_BIG = -1.0e30
NW = float(DI * DM)

ENGS = ("tensor", "vector", "scalar", "gpsimd", "sync")


class Sched:

    def __init__(self, sem_names):
        self.cnt = {n: 0 for n in sem_names}
        self.ops = {e: [] for e in ENGS}

    def v(self, sem):
        return self.cnt[sem]

    def emit(self, eng, fn, waits=(), inc=None, n=1):
        waits = tuple((s, v) for (s, v) in waits if v > 0)
        if inc is not None:
            self.cnt[inc] += n
        self.ops[eng].append((fn, waits, inc, n))
        return self.cnt[inc] if inc is not None else None

    def dma(self, eng, out, in_, waits=(), transpose=False, sem=None):
        def fn(e, out=out, in_=in_, transpose=transpose):
            if transpose:
                return e.dma_start(out=out, in_=in_, transpose=True)
            return e.dma_start(out=out, in_=in_)
        return self.emit(eng, fn, waits, sem or ("dq_" + eng), 16)

    def run(self, block, handles):
        for eng in ENGS:
            ops = self.ops[eng]
            def make(eng_ops):
                def body(e):
                    seen = {}
                    for fn, waits, inc, n in eng_ops:
                        for s, v in waits:
                            if seen.get(s, 0) < v:
                                e.wait_ge(handles[s], v)
                                seen[s] = v
                        inst = fn(e)
                        if inc is not None:
                            inst.then_inc(handles[inc], n)
                return body
            getattr(block, eng)(make(ops))


def rope_tables():
    f32 = np.float32
    d = np.arange(0, 128, 2, dtype=f32)
    inv_freq = (f32(1.0) / (f32(500000.0) ** (d / f32(128.0)))).astype(f32)
    wavelen = f32(2.0 * math.pi) / inv_freq
    smooth = (f32(8192.0) / wavelen - f32(1.0)) / f32(3.0)
    smooth = np.clip(smooth, f32(0.0), f32(1.0)).astype(f32)
    new_inv = ((f32(1.0) - smooth) * (inv_freq / f32(8.0)) + smooth * inv_freq).astype(f32)
    t = np.arange(SEQ, dtype=f32)
    freqs = np.outer(t, new_inv).astype(f32)
    emb = np.concatenate([freqs, freqs], axis=-1)
    cos = np.cos(emb).astype(f32)
    sin = np.sin(emb).astype(f32)
    sinr = np.concatenate([-sin[:, :64], sin[:, 64:]], axis=-1).astype(f32)
    return cos, sinr


def causal_masks():
    m = np.zeros((128, 4, 512), dtype=np.float32)
    k = np.arange(128)[:, None]
    q = np.arange(512)[None, :]
    for v in range(4):
        m[:, v, :] = np.where(128 * v + k <= q, 0.0, NEG_BIG)
    return m.reshape(128, 2048)


def build_nc(debug=False):
    nc = bass.Bass("TRN2", target_bir_lowering=False)
    dt = nc.dram_tensor

    x_fm = dt("x_fm", [DM, NT], F32, kind="ExternalInput").ap()
    x_own = dt("x_own", [TOWN, DM], F32, kind="ExternalInput").ap()
    wqkv = dt("wqkv", [DM, 768], F32, kind="ExternalInput").ap()
    woT = dt("woT", [DM, DM], F32, kind="ExternalInput").ap()
    wg_tiles = dt("wg_tiles", [NGC, 128, NIC, 128], F32, kind="ExternalInput").ap()
    wu_tiles = dt("wu_tiles", [NGC, 128, NIC, 128], F32, kind="ExternalInput").ap()
    wd_tiles = dt("wd_tiles", [8, NGC, 128, 512], F32, kind="ExternalInput").ap()
    wg_s = dt("wg_s", [128, 44032], F32, kind="ExternalInput").ap()
    wu_s = dt("wu_s", [128, 44032], F32, kind="ExternalInput").ap()
    wd_s = dt("wd_s", [128, 44032], F32, kind="ExternalInput").ap()
    cosT = dt("cosT", [SEQ, 128], F32, kind="ExternalInput").ap()
    sinrT = dt("sinrT", [SEQ, 128], F32, kind="ExternalInput").ap()
    maskT = dt("maskT", [128, 2048], F32, kind="ExternalInput").ap()
    idf = dt("idf", [128, 128], F32, kind="ExternalInput").ap()
    y = dt("y", [TOWN, DM], F32, kind="ExternalOutput").ap()

    dbg = "ExternalOutput" if debug else "Internal"
    qkv_spill = dt("qkv_spill", [NT, 768], BF16, kind=dbg).ap()
    a2a_in = dt("a2a_in", [NT, 512], BF16, kind="Internal")
    a2a_out = dt("a2a_out", [NT, 512], BF16, kind="Internal")
    stats_in = dt("stats_in", [TOWN, 2], F32, kind="Internal")
    stats_ag = dt("stats_ag", [NT, 2], F32, kind="Internal")
    scales_in = dt("scales_in", [4, 1], F32, kind="Internal")
    scales_out = dt("scales_out", [32, 1], F32, kind="Internal")
    if debug:
        absacc_d = dt("absacc_d", [128, 4], F32, kind="ExternalOutput").ap()
        slabd_d = dt("slabd_d", [128, 36], F32, kind="ExternalOutput").ap()
        stat8_d = dt("stat8_d", [128, 32], F32, kind="ExternalOutput").ap()
        stats_ag_d = dt("stats_ag_d", [NT, 2], F32, kind="ExternalOutput").ap()
        scales_d = dt("scales_d", [1, 8], F32, kind="ExternalOutput").ap()
        a2a_in_d = dt("a2a_in_d", [NT, 512], BF16, kind="ExternalOutput").ap()
    nrow_dram = dt("nrow_dram", [64, 128], BF16, kind="Internal").ap()
    thr_dram = dt("thr_dram", [1, 4], F32, kind="Internal").ap()
    x1_dram = dt("x1_dram", [TOWN, DM], F32, kind=dbg).ap()
    x1n_dram = dt("x1n_dram", [TOWN, DM], BF16, kind=dbg).ap()
    inter_dram = dt("inter_dram", [DI, TOWN], BF16, kind=dbg).ap()

    with ExitStack() as ctx:
        ecx = ctx.enter_context

        def sbuf(name, shape, dtype):
            return ecx(nc.sbuf_tensor(name, shape, dtype))

        A1 = sbuf("A1", [128, 32768], BF16)
        B2 = sbuf("B2", [128, 32768], BF16)
        F3 = sbuf("F3", [128, 16384], F32)
        outb = sbuf("outb", [128, 2 * 768], BF16)
        stag = sbuf("stag", [128, 128], F32)
        stat8 = sbuf("stat8", [128, 32], F32)
        nchunk = sbuf("nchunk", [1, 2 * 128], BF16)
        w1row = sbuf("w1row", [1, 768], BF16)
        onesb = sbuf("onesb", [128, 1], BF16)
        onesf = sbuf("onesf", [128, 1], F32)
        ncol = sbuf("ncol", [128, 1], F32)
        epscol = sbuf("epscol", [128, 1], F32)
        thrb = sbuf("thrb", [128, 8], F32)
        slabd = sbuf("slabd", [128, 36], F32)
        scf = sbuf("scf", [128, 256], F32)
        scr32 = sbuf("scr32", [1, 32], F32)
        absacc = sbuf("absacc", [128, 4], F32)
        scw = sbuf("scw", [1, 8], F32)
        zc = sbuf("zc", [128, 4], F32)
        r2c = sbuf("r2c", [128, 8], F32)
        n64 = sbuf("n64", [64, 128], BF16)

        P0 = ecx(nc.psum_tensor("P0", [128, 1024], F32))
        P1 = ecx(nc.psum_tensor("P1", [128, 1024], F32))
        P2 = ecx(nc.psum_tensor("P2", [128, 1024], F32))
        P3 = ecx(nc.psum_tensor("P3", [128, 1024], F32))
        PQ = [P0, P1]
        PV = [P2, P3]

        sems = ["dq_sync", "dq_gpsimd", "dq_scalar", "s_pe", "s_dve", "s_act", "s_cc",
                "d_q", "d_q1", "d_tr", "d_ost", "d_ost1", "d_n", "d_sp", "d_sp1",
                "d_xo", "d_xd0", "d_xd1", "d_x1s", "d_x1s1", "d_w", "d_w1",
                "d_i", "d_i1", "d_dbg", "d_xc", "d_xc1", "d_xf", "d_xf1",
                "d_x1n", "d_x1n1", "d_ys", "d_ys1", "d_wd", "d_wd1",
                "d_xb0", "d_xb1", "d_xb2", "d_xb3", "d_sc0", "d_sc1",
                "d_wo0", "d_wo1", "d_ib0", "d_ib1", "d_msk", "d_thr",
                "d_i2", "d_i3", "d_x1s2", "d_x1s3", "d_ys2", "d_ys3"]
        handles = {n: ecx(nc.semaphore(n)) for n in sems}
        S = Sched(sems)

        wq_v = B2[:, 0:24576].rearrange("p (i o) -> p i o", i=NIC)
        xb = [B2[:, 24576 + k * 4096:24576 + (k + 1) * 4096].rearrange("p (i t) -> p i t", i=NIC)
              for k in range(2)] + \
             [A1[:, k * 4096:(k + 1) * 4096].rearrange("p (i t) -> p i t", i=NIC)
              for k in range(2)]
        nall = A1[0:1, 8192:16384]
        cos_v = F3[:, 0:4096].rearrange("p (i d) -> p i d", i=32)
        sinr_v = F3[:, 4096:8192].rearrange("p (i d) -> p i d", i=32)
        xo = [F3[:, 8192:12288], F3[:, 12288:16384]]
        scbuf = F3[:, 12288:16384]
        idf_v = F3[:, 8192:8320]
        rps = F3[:, 8320:9600]
        ytile = F3[:, 4096:6144]
        ubf = B2[:, 24576:26624]
        ibf = B2[:, 26624:28672]
        sq_bf = B2[:, 0:4096]

        S.emit("vector", lambda e: e.memset(onesb[:, :], 1.0), inc="s_dve")
        S.emit("vector", lambda e: e.memset(onesf[:, :], 1.0), inc="s_dve")
        S.emit("vector", lambda e: e.memset(ncol[:, :], -EXP_C), inc="s_dve")
        S.emit("vector", lambda e: e.memset(epscol[:, :], EPS_LN), inc="s_dve")
        S.emit("vector", lambda e: e.memset(absacc[:, :], 0.0), inc="s_dve")
        S.emit("vector", lambda e: e.memset(scw[0:1, :], 0.0), inc="s_dve")
        v_consts = S.v("s_dve")
        S.dma("sync", idf_v, idf)
        ld_idf = S.v("dq_sync")

        xld = {}
        for tc in range(4):
            S.dma("gpsimd", xb[tc], x_fm[:, tc * 128:(tc + 1) * 128].rearrange("(i p) t -> p i t", p=128),
                  sem="d_xb%d" % tc)
            xld[tc] = S.v("d_xb%d" % tc)

        last_sq = 0
        for i in range(8):
            buf = xo[1]
            w = [("s_dve", S.v("s_dve")), ("s_act", S.v("s_act"))] if i >= 1 else []
            ldv = S.dma("sync", buf[:, 0:4096], x_own[i * 128:(i + 1) * 128, :], waits=w, sem="d_xo")
            S.emit("vector",
                   lambda e, buf=buf, i=i: e.reduce_sum(stat8[:, i:i + 1], buf[:, 0:4096], axis=AX.X),
                   waits=[("d_xo", ldv)], inc="s_dve")
            S.emit("scalar",
                   lambda e, buf=buf: e.activation(sq_bf, buf[:, 0:4096], AF.Square),
                   waits=[("d_xo", ldv), ("s_dve", last_sq)], inc="s_act")
            S.emit("vector",
                   lambda e, i=i: e.reduce_sum(stat8[:, 8 + i:9 + i], sq_bf, axis=AX.X),
                   waits=[("s_act", S.v("s_act"))], inc="s_dve")
            last_sq = S.v("s_dve")
        scr_free = [("s_dve", S.v("s_dve")), ("s_act", S.v("s_act"))]
        nr_v = stat8[:, 16:32].rearrange("p (i c) -> p i c", c=2)
        S.emit("vector", lambda e: e.tensor_scalar(nr_v[:, :, 0], stat8[:, 0:8], -1.0 / DM, None, ALU.mult),
               waits=[("s_dve", S.v("s_dve"))], inc="s_dve")
        S.emit("vector", lambda e: e.tensor_scalar(stat8[:, 8:16], stat8[:, 8:16], 1.0 / DM, None, ALU.mult),
               waits=[("s_dve", S.v("s_dve"))], inc="s_dve")
        S.emit("vector", lambda e: e.tensor_tensor(stat8[:, 0:8], nr_v[:, :, 0], nr_v[:, :, 0], ALU.mult),
               waits=[("s_dve", S.v("s_dve"))], inc="s_dve")
        S.emit("vector", lambda e: e.tensor_tensor(stat8[:, 8:16], stat8[:, 8:16], stat8[:, 0:8], ALU.subtract),
               waits=[("s_dve", S.v("s_dve"))], inc="s_dve")
        v_var = S.v("s_dve")
        S.emit("scalar", lambda e: e.activation(stat8[:, 0:8], stat8[:, 8:16], AF.Ln, bias=epscol[:, 0:1]),
               waits=[("s_dve", v_var)], inc="s_act")
        S.emit("scalar", lambda e: e.activation(stat8[:, 8:16], stat8[:, 0:8], AF.Exp, scale=-0.5),
               waits=[("s_act", S.v("s_act"))], inc="s_act")
        v_stats0 = S.v("s_act")
        S.emit("vector", lambda e: e.tensor_scalar(nr_v[:, :, 1], stat8[:, 8:16], 1.0, None, ALU.mult),
               waits=[("s_act", v_stats0)], inc="s_dve")
        v_stats = ("s_dve", S.v("s_dve"))
        S.dma("sync", stats_in.ap().rearrange("(i p) c -> p i c", p=128),
              stat8[:, 16:32].rearrange("p (i c) -> p i c", c=2),
              waits=[v_stats])
        st_stats = S.v("dq_sync")
        S.emit("gpsimd",
               lambda e: e.collective_compute("AllGather", ALU.bypass,
                                              replica_groups=[list(range(R))],
                                              ins=[stats_in.ap().opt()],
                                              outs=[stats_ag.ap().opt()]),
               waits=[("dq_sync", st_stats)], inc="s_cc")
        cc_stats = S.v("s_cc")
        S.dma("sync", stag[:, :].rearrange("p (i c) -> p i c", c=2),
              stats_ag.ap().rearrange("(i p) c -> p i c", p=128),
              waits=[("s_cc", cc_stats)])
        ld_stag = S.v("dq_sync")
        if debug:
            S.dma("sync", stats_ag_d, stats_ag.ap(), waits=[("s_cc", cc_stats)], sem="d_dbg")
            S.dma("sync", stat8_d, stat8[:, :], waits=[("s_cc", cc_stats)], sem="d_dbg")
        stag_v = stag[:, :].rearrange("p (i c) -> p i c", c=2)
        S.emit("tensor",
               lambda e: e.transpose(P0[0:64, 0:128], stag_v[:, :, 0], idf_v),
               waits=[("dq_sync", ld_stag), ("dq_sync", ld_idf)], inc="s_pe")
        S.emit("vector", lambda e: e.tensor_copy(n64[:, :], P0[0:64, 0:128]),
               waits=[("s_pe", S.v("s_pe"))], inc="s_dve")
        v_n64 = S.v("s_dve")
        S.dma("sync", nrow_dram, n64[:, :], waits=[("s_dve", v_n64)], sem="d_n")
        st_nrow = S.v("d_n")
        S.dma("sync", nall, nrow_dram.rearrange("(o a) b -> o (a b)", o=1),
              waits=[("d_n", st_nrow)], sem="d_n")
        ld_nall = S.v("d_n")

        S.dma("gpsimd", wq_v, wqkv.rearrange("(i p) o -> p i o", p=128),
              waits=[v_stats])
        ld_wqkv = S.v("dq_gpsimd")

        for ic in range(NIC):
            w = [("dq_gpsimd", ld_wqkv), ("s_dve", v_consts)] if ic == 0 else []
            S.emit("tensor",
                   lambda e, ic=ic: e.matmul(P1[0:1, 0:512], onesb[:, :], wq_v[:, ic, 0:512],
                                             start=(ic == 0), stop=(ic == NIC - 1)),
                   waits=w)
            S.emit("tensor",
                   lambda e, ic=ic: e.matmul(P1[0:1, 512:768], onesb[:, :], wq_v[:, ic, 512:768],
                                             start=(ic == 0), stop=(ic == NIC - 1)),
                   inc=("s_pe" if ic == NIC - 1 else None))
        v_w1mm = S.v("s_pe")
        S.emit("scalar", lambda e: e.copy(w1row[0:1, :], P1[0:1, 0:768]),
               waits=[("s_pe", v_w1mm)], inc="s_act")
        v_w1 = S.v("s_act")

        S.dma("sync", cos_v, cosT.rearrange("(i p) d -> p i d", p=128))
        S.dma("sync", sinr_v, sinrT.rearrange("(i p) d -> p i d", p=128))
        ld_tabs = S.v("dq_sync")

        slabs = []
        for wi, wsrc in enumerate((wg_s, wu_s, wd_s)):
            for c0 in range(0, 44032, 2048):
                slabs.append((wi, wsrc, c0, min(c0 + 2048, 44032)))
        scbufs = [scbuf[:, 0:2048], scbuf[:, 2048:4096]]
        sl_dve = {}

        def emit_scale_slab(k):
            wi, wsrc, c0, c1 = slabs[k]
            scbuf_k = scbufs[k % 2]
            scsem = "d_sc%d" % (k % 2)
            ldv = S.dma("scalar", scbuf_k[:, 0:c1 - c0], wsrc[:, c0:c1],
                        waits=(scr_free if k < 2 else [("s_dve", sl_dve.get(k - 2, 0))]),
                        sem=scsem)
            ldv = S.v(scsem)
            for a in range(0, c1 - c0, 512):
                S.emit("vector",
                       lambda e, a=a, scbuf_k=scbuf_k: e.reduce_sum(
                           absacc[:, 3:4], scbuf_k[:, a:a + 512],
                           axis=AX.X, apply_absolute_value=True),
                       waits=([(scsem, ldv)] if a == 0 else [("s_dve", S.v("s_dve"))]),
                       inc="s_dve")
                S.emit("vector",
                       lambda e, wi=wi: e.tensor_tensor(absacc[:, wi:wi + 1], absacc[:, wi:wi + 1],
                                                        absacc[:, 3:4], ALU.add),
                       waits=[("s_dve", S.v("s_dve"))], inc="s_dve")
            sl_dve[k] = S.v("s_dve")

        scale_state = {}

        def emit_scale_finish():
            S.emit("tensor", lambda e: e.transpose(P2[0:4, 0:128], absacc[:, 0:4], idf_v),
                   waits=[("s_dve", S.v("s_dve")), ("dq_sync", ld_idf)], inc="s_pe")
            S.emit("scalar", lambda e: e.copy(scf[0:4, 0:128], P2[0:4, 0:128]),
                   waits=[("s_pe", S.v("s_pe"))], inc="s_act")
            scale_state["p2done"] = S.v("s_act")
            base = 0
            n = 128
            w = [("s_act", S.v("s_act"))]
            while n > 1:
                h = n // 2
                src = scf[0:4, base:base + n].rearrange("p (a b) -> p a b", b=2)
                S.emit("vector",
                       lambda e, src=src, base=base, n=n, h=h: e.tensor_tensor(
                           scf[0:4, base + n:base + n + h], src[:, :, 0], src[:, :, 1], ALU.add),
                       waits=w + [("s_dve", S.v("s_dve"))], inc="s_dve")
                w = []
                base += n
                n = h
            S.dma("sync", scales_in.ap(), scf[0:4, 254:255], waits=[("s_dve", S.v("s_dve"))],
                  sem="d_thr")
            S.emit("gpsimd",
                   lambda e: e.collective_compute("AllGather", ALU.bypass,
                                                  replica_groups=[list(range(R))],
                                                  ins=[scales_in.ap().opt()],
                                                  outs=[scales_out.ap().opt()]),
                   waits=[("d_thr", S.v("d_thr"))], inc="s_cc")
            S.dma("sync", scr32[0:1, 0:32],
                  scales_out.ap().rearrange("(o a) b -> o (a b)", o=1),
                  waits=[("s_cc", S.v("s_cc"))], sem="d_thr")
            ldt = S.v("d_thr")
            S.emit("vector", lambda e: e.tensor_copy(scw[0:1, 0:3], scr32[0:1, 20:23]),
                   waits=[("d_thr", ldt)], inc="s_dve")
            for j in range(1, 8):
                r = (5 + j) % 8
                S.emit("vector",
                       lambda e, r=r: e.tensor_tensor(scw[0:1, 0:3], scw[0:1, 0:3],
                                                      scr32[0:1, 4 * r:4 * r + 3], ALU.add),
                       waits=[("s_dve", S.v("s_dve"))], inc="s_dve")
            if debug:
                S.emit("vector", lambda e: e.tensor_copy(scw[0:1, 4:7], scw[0:1, 0:3]),
                       waits=[("s_dve", S.v("s_dve"))], inc="s_dve")
                S.dma("sync", scales_d, scw[0:1, 0:8], waits=[("s_dve", S.v("s_dve"))], sem="d_dbg")
            S.emit("vector", lambda e: e.tensor_scalar(scw[0:1, 0:3], scw[0:1, 0:3],
                                                       float(np.float32(1.0) / np.float32(NW)), None, ALU.mult),
                   waits=[("s_dve", S.v("s_dve"))], inc="s_dve")
            S.emit("vector", lambda e: e.tensor_scalar(scw[0:1, 4:7], scw[0:1, 0:3], EPS_Q, 0.5, ALU.add, ALU.mult),
                   waits=[("s_dve", S.v("s_dve"))], inc="s_dve")
            S.emit("vector", lambda e: e.tensor_tensor(scw[0:1, 3:4], scw[0:1, 0:1], scw[0:1, 1:2], ALU.mult),
                   waits=[("s_dve", S.v("s_dve"))], inc="s_dve")
            S.emit("vector", lambda e: e.tensor_tensor(scw[0:1, 7:8], scw[0:1, 3:4], scw[0:1, 2:3], ALU.mult),
                   waits=[("s_dve", S.v("s_dve"))], inc="s_dve")
            S.dma("sync", thr_dram[0:1, 0:4], scw[0:1, 4:8], waits=[("s_dve", S.v("s_dve"))],
                  sem="d_thr")
            S.dma("sync", thrb[:, 0:4], thr_dram[0:1, 0:4].partition_broadcast(128),
                  waits=[("d_thr", S.v("d_thr"))], sem="d_thr")
            S.emit("vector", lambda e: e.tensor_scalar(thrb[:, 4:8], thrb[:, 0:4], -1.0, None, ALU.mult),
                   waits=[("d_thr", S.v("d_thr"))], inc="s_dve")
            scale_state["thr"] = S.v("s_dve")

        pe_tc = {}
        dve_rope = {}
        spill_v = {}
        n_scale = 0
        for tc in range(NTC):
            par = tc % 2
            b4 = tc % 4
            P = PQ[par]
            if tc >= 4:
                S.dma("gpsimd", xb[b4], x_fm[:, tc * 128:(tc + 1) * 128].rearrange("(i p) t -> p i t", p=128),
                      waits=[("s_pe", pe_tc.get(tc - 4, 0))], sem="d_xb%d" % b4)
                xld[tc] = S.v("d_xb%d" % b4)
            for ic in range(NIC):
                w = []
                if ic == 0:
                    w = [("d_xb%d" % b4, xld[tc]), ("d_n", ld_nall), ("s_act", v_w1),
                         ("s_dve", dve_rope.get(tc - 2, 0))]
                S.emit("tensor",
                       lambda e, ic=ic, P=P, b4=b4: e.matmul(
                           P[:, 0:512], xb[b4][:, ic, :], wq_v[:, ic, 0:512],
                           start=(ic == 0), stop=False),
                       waits=w)
                S.emit("tensor",
                       lambda e, ic=ic, P=P, b4=b4: e.matmul(
                           P[:, 512:768], xb[b4][:, ic, :], wq_v[:, ic, 512:768],
                           start=(ic == 0), stop=False))
            S.emit("tensor",
                   lambda e, P=P, tc=tc: e.matmul(P[:, 0:512], nall[0:1, tc * 128:(tc + 1) * 128],
                                                  w1row[0:1, 0:512], start=False, stop=True))
            S.emit("tensor",
                   lambda e, P=P, tc=tc: e.matmul(P[:, 512:768], nall[0:1, tc * 128:(tc + 1) * 128],
                                                  w1row[0:1, 512:768], start=False, stop=True),
                   inc="s_pe")
            pe_tc[tc] = S.v("s_pe")

            ci = tc % 32
            rA = rps[:, 0:640]
            rB = rps[:, 640:1280]
            spsem = "d_sp" if par == 0 else "d_sp1"
            wv = [("s_pe", pe_tc[tc]), ("dq_sync", ld_tabs), ("dq_sync", ld_stag),
                  (spsem, spill_v.get(tc - 2, 0)), ("s_dve", dve_rope.get(tc - 1, 0))]
            for blk in range(5):
                c0 = blk * 128
                S.emit("vector",
                       lambda e, P=P, c0=c0, ci=ci: e.tensor_tensor(
                           rps[:, c0:c0 + 128], P[:, c0:c0 + 128], cos_v[:, ci, :], ALU.mult),
                       waits=(wv if blk == 0 else ()))
                S.emit("vector",
                       lambda e, P=P, c0=c0, ci=ci: e.tensor_tensor(
                           rps[:, 640 + c0:640 + c0 + 128],
                           P[:, c0:c0 + 128].rearrange("p (g c) -> p g c", g=2)[:, ::-1, :],
                           sinr_v[:, ci, :].rearrange("p (g c) -> p g c", g=2),
                           ALU.mult),
                       inc=("s_dve" if blk == 4 else None))
            S.emit("vector", lambda e: e.tensor_tensor(rA, rA, rB, ALU.add),
                   waits=[("s_dve", S.v("s_dve"))], inc="s_dve")
            S.emit("vector",
                   lambda e, tc=tc, par=par: e.tensor_scalar(
                       outb[:, par * 768:par * 768 + 640], rA,
                       stag_v[:, tc, 1:2], None, ALU.mult),
                   waits=[("s_dve", S.v("s_dve"))])
            S.emit("vector",
                   lambda e, tc=tc, par=par, P=P: e.tensor_scalar(
                       outb[:, par * 768 + 640:par * 768 + 768], P[:, 640:768],
                       stag_v[:, tc, 1:2], None, ALU.mult),
                   inc="s_dve")
            dve_rope[tc] = S.v("s_dve")
            S.dma("sync", qkv_spill[tc * 128:(tc + 1) * 128, :], outb[:, par * 768:(par + 1) * 768],
                  waits=[("s_dve", dve_rope[tc])], sem=spsem)
            spill_v[tc] = S.v(spsem)
            if n_scale < len(slabs):
                emit_scale_slab(n_scale)
                n_scale += 1
        while n_scale < len(slabs):
            emit_scale_slab(n_scale)
            n_scale += 1
        emit_scale_finish()
        spill_done = [("d_sp", spill_v[NTC - 2]), ("d_sp1", spill_v[NTC - 1])]
        pe_A_done = pe_tc[NTC - 1]

        k_sb = A1[:, 0:8192]
        v_sb = A1[:, 8192:16640]
        v_sb3 = A1[:, 8192:16640].rearrange("p (g c) -> p g c", g=64)
        m_bf = A1[:, 16640:18688]
        qsl = A1[:, 18688:19712]
        ebuf = A1[:, 19712:21760]
        ostg = A1[:, 21760:25856]

        S.emit("vector", lambda e: e.memset(v_sb[:, :], 1.0),
               waits=[("s_pe", pe_A_done)], inc="s_dve")
        v_vones = S.v("s_dve")
        ld_mask = S.dma("sync", F3[:, 0:2048], maskT, waits=[("s_dve", dve_rope[NTC - 1])],
                        sem="d_msk")
        S.emit("vector", lambda e: e.tensor_copy(m_bf[:, :], F3[:, 0:2048]),
               waits=[("d_msk", ld_mask)], inc="s_dve")
        v_mask = S.v("s_dve")

        for j in range(16):
            S.dma("sync", k_sb[:, j * 512:(j + 1) * 512],
                  qkv_spill[j * 512:(j + 1) * 512, 512:640],
                  waits=spill_done + [("s_dve", v_vones)], transpose=True, sem="d_tr")
        ld_k = S.v("d_tr")
        for g in range(64):
            S.dma("sync", v_sb3[:, g, 0:128],
                  qkv_spill[g * 128:(g + 1) * 128, 640:768],
                  waits=[("s_dve", v_vones)] + spill_done)
        ld_v = S.v("dq_sync")

        wo_buf = [B2[:, 0:16384].rearrange("p (h o) -> p h o", h=32),
                  B2[:, 16384:32768].rearrange("p (h o) -> p h o", h=32)]
        wo_ld = {}
        for oc in range(2):
            S.dma("gpsimd", wo_buf[oc],
                  woT[:, oc * 512:(oc + 1) * 512].rearrange("(h p) o -> p h o", p=128),
                  waits=[("s_pe", pe_A_done)], sem="d_wo%d" % oc)
            wo_ld[oc] = S.v("d_wo%d" % oc)

        groups = [(b, qc, h) for b in range(NBATCH) for qc in range(8) for h in range(NHL)]
        pairs = []
        for gi, (b, qc, h) in enumerate(groups):
            npairs = 2 * (qc + 1)
            for j in range(npairs):
                pairs.append((gi, b, qc, h, j, npairs))
        NPAIR = len(pairs)

        qld = {}
        pe_st_group = {}
        exp_done = {}
        av_done = {}
        ev_po = {"v": 0}
        st_ostg = {0: 0, 1: 0}
        evac_cnt = {}

        def emit_qload(gi):
            b, qc, h = groups[gi]
            gpar = gi % 2
            qsem = "d_q" if gpar == 0 else "d_q1"
            S.dma("sync", qsl[:, gpar * 512:(gpar + 1) * 512],
                  qkv_spill[b * SEQ + qc * 512: b * SEQ + qc * 512 + 512, h * 128:(h + 1) * 128],
                  waits=[("s_pe", pe_st_group.get(gi - 2, 0)), ("d_tr", ld_k), ("dq_sync", ld_v)],
                  transpose=True, sem=qsem)
            qld[gi] = (qsem, S.v(qsem))

        def emit_st(p):
            gi, b, qc, h, j, npairs = pairs[p]
            spar = p % 2
            SP = PQ[spar]
            gpar = gi % 2
            qv = qsl[:, gpar * 512:(gpar + 1) * 512]
            w = [("s_act", exp_done.get(p - 2, 0)), qld[gi]]
            if p <= 1:
                w.append(("s_act", v_w1))
                w.append(("s_dve", dve_rope[NTC - 1]))
            kb0 = 2 * j
            S.emit("tensor",
                   lambda e, SP=SP, b=b, kb0=kb0, qv=qv: e.matmul(
                       SP[:, 0:512], k_sb[:, b * SEQ + kb0 * 128: b * SEQ + kb0 * 128 + 128],
                       qv, start=True, stop=True),
                   waits=w)
            S.emit("tensor",
                   lambda e, SP=SP, b=b, kb0=kb0, qv=qv: e.matmul(
                       SP[:, 512:1024], k_sb[:, b * SEQ + kb0 * 128 + 128: b * SEQ + kb0 * 128 + 256],
                       qv, start=True, stop=True),
                   inc="s_pe")
            pairs_pe[p] = S.v("s_pe")
            pe_st_group[gi] = S.v("s_pe")

        def emit_mask_exp(p):
            gi, b, qc, h, j, npairs = pairs[p]
            spar = p % 2
            SP = PQ[spar]
            diag = (j >= npairs - 2)
            wexp = [("s_pe", pairs_pe[p]), ("s_pe", av_done.get(p - 2, 0))]
            if diag:
                v0 = 2 * (j - (npairs - 2))
                S.emit("vector",
                       lambda e, SP=SP, v0=v0: e.tensor_tensor(
                           SP[:, 0:1024], SP[:, 0:1024], m_bf[:, v0 * 512: v0 * 512 + 1024], ALU.add),
                       waits=[("s_pe", pairs_pe[p]), ("s_dve", v_mask)], inc="s_dve")
                wexp.append(("s_dve", S.v("s_dve")))
            S.emit("scalar",
                   lambda e, SP=SP, spar=spar: e.activation(
                       ebuf[:, spar * 1024:(spar + 1) * 1024], SP[:, 0:1024], AF.Exp,
                       bias=ncol[:, 0:1], scale=ISQ),
                   waits=wexp, inc="s_act")
            exp_done[p] = S.v("s_act")

        def emit_av(p):
            gi, b, qc, h, j, npairs = pairs[p]
            spar = p % 2
            kb0 = 2 * j
            first = (j == 0)
            last = (j == npairs - 1)
            for kk in range(2):
                kbg = b * 32 + kb0 + kk
                for qs in range(4):
                    slot = PV[qs // 2]
                    c0 = (qs % 2) * 512
                    w = []
                    if first and kk == 0 and qs == 0:
                        w = [("s_act", exp_done[p]), ("s_dve", ev_po["v"])]
                    elif kk == 0 and qs == 0:
                        w = [("s_act", exp_done[p])]
                    S.emit("tensor",
                           lambda e, slot=slot, c0=c0, spar=spar, kk=kk, qs=qs, kbg=kbg,
                                  first=first, last=last: e.matmul(
                               slot[:, c0:c0 + 129],
                               ebuf[:, spar * 1024 + kk * 512 + qs * 128: spar * 1024 + kk * 512 + qs * 128 + 128],
                               v_sb3[:, kbg, 0:129],
                               start=(first and kk == 0), stop=(last and kk == 1)),
                           waits=w,
                           inc=("s_pe" if (kk == 1 and qs == 3) else None))
            av_done[p] = S.v("s_pe")

        def emit_evac(p):
            gi, b, qc, h, j, npairs = pairs[p]
            opar = (b * 8 + qc) % 2
            osem = "d_ost" if opar == 0 else "d_ost1"
            w = [("s_pe", av_done[p]), (osem, st_ostg[opar]), ("s_dve", ev_po["v"])]
            for qs in range(4):
                slot = PV[qs // 2]
                c0 = (qs % 2) * 512
                S.emit("vector",
                       lambda e, slot=slot, c0=c0, qs=qs: e.reciprocal(
                           zc[:, qs:qs + 1], slot[:, c0 + 128:c0 + 129]),
                       waits=(w if qs == 0 else ()),
                       inc=("s_dve" if qs == 3 else None))
            vrec = S.v("s_dve")
            for qs in range(4):
                slot = PV[qs // 2]
                c0 = (qs % 2) * 512
                S.emit("vector",
                       lambda e, slot=slot, c0=c0, qs=qs, opar=opar, h=h: e.tensor_scalar(
                           ostg[:, opar * 2048 + qs * 512 + h * 128: opar * 2048 + qs * 512 + h * 128 + 128],
                           slot[:, c0:c0 + 128], zc[:, qs:qs + 1], None, ALU.mult),
                       waits=([("s_dve", vrec)] if qs == 0 else ()), inc="s_dve")
            ev_po["v"] = S.v("s_dve")
            evac_cnt[(b, qc)] = S.v("s_dve")

        pairs_pe = {}
        p = 0
        cur_g = -1
        emit_qload(0)
        emit_qload(1)
        for p in range(NPAIR):
            gi, b, qc, h, j, npairs = pairs[p]
            if j == npairs - 1 and gi + 2 < len(groups):
                emit_qload(gi + 2)
            if p == 0:
                emit_st(0)
            if p + 1 < NPAIR:
                emit_st(p + 1)
            emit_mask_exp(p)
            emit_av(p)
            if j == npairs - 1:
                emit_evac(p)
                if h == NHL - 1:
                    opar = (b * 8 + qc) % 2
                    t0 = b * SEQ + qc * 512
                    osem2 = "d_ost" if opar == 0 else "d_ost1"
                    S.dma("sync",
                          a2a_in.ap()[t0:t0 + 512, :].rearrange("(qs p) o -> p qs o", p=128),
                          ostg[:, opar * 2048:(opar + 1) * 2048].rearrange("p (qs o) -> p qs o", qs=4),
                          waits=[("s_dve", evac_cnt[(b, qc)])], sem=osem2)
                    st_ostg[opar] = S.v(osem2)

        a2a_ready = [("d_ost", S.v("d_ost")), ("d_ost1", S.v("d_ost1"))]
        if debug:
            S.dma("sync", a2a_in_d, a2a_in.ap(), waits=a2a_ready, sem="d_dbg")
        S.emit("gpsimd",
               lambda e: e.collective_compute("AllToAll", ALU.bypass,
                                              replica_groups=[list(range(R))],
                                              ins=[a2a_in.ap().opt()],
                                              outs=[a2a_out.ap().opt()]),
               waits=a2a_ready, inc="s_cc")
        cc_a2a = S.v("s_cc")

        ofm = A1[:, :].rearrange("p (h t) -> p h t", h=32)
        for hd_c in range(32):
            r_ = hd_c // 4
            lb = hd_c % 4
            S.dma("sync", ofm[:, hd_c, :],
                  a2a_out.ap()[r_ * 1024:(r_ + 1) * 1024, lb * 128:(lb + 1) * 128],
                  waits=[("s_cc", cc_a2a)], transpose=True, sem="d_tr")
        ld_ofm = S.v("d_tr")

        xo3 = [xo[k].rearrange("p (t o) -> p t o", t=8) for k in range(2)]
        pe_C = {}
        dve_C = {}
        xo_ld = {}
        st_x1 = {}
        for oc in range(8):
            par = oc % 2
            xcsem = "d_xc" if par == 0 else "d_xc1"
            if oc >= 2:
                S.dma("gpsimd", wo_buf[par],
                      woT[:, oc * 512:(oc + 1) * 512].rearrange("(h p) o -> p h o", p=128),
                      waits=[("s_pe", pe_C.get(oc - 2, 0))], sem="d_wo%d" % par)
                wo_ld[oc] = S.v("d_wo%d" % par)
            S.dma("sync", xo3[par][:, :, :],
                  x_own[:, oc * 512:(oc + 1) * 512].rearrange("(t p) o -> p t o", p=128),
                  waits=[("s_dve", dve_C.get(oc - 2, 0)), ("s_dve", scale_state["thr"])], sem=xcsem)
            xo_ld[oc] = S.v(xcsem)
            for tc in range(8):
                slot = [P0, P1, P2, P3][tc // 2]
                c0 = (tc % 2) * 512
                for hd_c in range(32):
                    w = []
                    if hd_c == 0:
                        w = [("d_tr", ld_ofm), ("d_wo%d" % par, wo_ld[oc]),
                             ("s_dve", dve_C.get(oc - 1, 0)),
                             ("s_act", scale_state["p2done"])]
                    S.emit("tensor",
                           lambda e, slot=slot, c0=c0, hd_c=hd_c, tc=tc, par=par: e.matmul(
                               slot[:, c0:c0 + 512], ofm[:, hd_c, tc * 128:(tc + 1) * 128],
                               wo_buf[par][:, hd_c, :],
                               start=(hd_c == 0), stop=(hd_c == 31)),
                           waits=w,
                           inc=("s_pe" if hd_c == 31 else None))
                pe_C[oc] = S.v("s_pe")
                ys = tc % 4
                x1sem = ["d_x1s", "d_x1s1", "d_x1s2", "d_x1s3"][ys]
                S.emit("vector",
                       lambda e, slot=slot, c0=c0, tc=tc, par=par, ys=ys: e.tensor_tensor(
                           ytile[:, ys * 512:(ys + 1) * 512], slot[:, c0:c0 + 512],
                           xo3[par][:, tc, :], ALU.add),
                       waits=[("s_pe", pe_C[oc]), (xcsem, xo_ld[oc]),
                              (x1sem,
                               st_x1.get((oc, tc - 4), st_x1.get((oc - 1, tc + 4), 0)))],
                       inc="s_dve")
                dve_C[oc] = S.v("s_dve")
                S.dma("sync", x1_dram[tc * 128:(tc + 1) * 128, oc * 512:(oc + 1) * 512],
                      ytile[:, ys * 512:(ys + 1) * 512],
                      waits=[("s_dve", dve_C[oc])], sem=x1sem)
                st_x1[(oc, tc)] = S.v(x1sem)
        x1_done = [("d_x1s", S.v("d_x1s")), ("d_x1s1", S.v("d_x1s1"))]

        x1n_t = [B2[:, 16384 + k * 4096:16384 + (k + 1) * 4096] for k in range(2)]
        st_x1n = {}
        act_D = {}
        for tc in range(8):
            par = tc % 2
            xdsem = "d_xd%d" % par
            ldv = S.dma("sync", xo[par][:, 0:4096], x1_dram[tc * 128:(tc + 1) * 128, :],
                        waits=x1_done + [act_D.get(tc - 2, ("s_dve", 0))], sem=xdsem)
            ldv = S.v(xdsem)
            S.emit("vector", lambda e, par=par: e.reduce_sum(r2c[:, 0:1], xo[par][:, 0:4096], axis=AX.X),
                   waits=[(xdsem, ldv), act_D.get(tc - 1, ("s_dve", 0))], inc="s_dve")
            S.emit("scalar", lambda e, par=par: e.activation(sq_bf, xo[par][:, 0:4096], AF.Square),
                   waits=[(xdsem, ldv), ("s_dve", S.v("s_dve"))], inc="s_act")
            S.emit("vector", lambda e: e.reduce_sum(r2c[:, 1:2], sq_bf, axis=AX.X),
                   waits=[("s_act", S.v("s_act"))], inc="s_dve")
            S.emit("vector", lambda e: e.tensor_scalar(r2c[:, 2:3], r2c[:, 0:1], -1.0 / DM, None, ALU.mult),
                   waits=[("s_dve", S.v("s_dve"))], inc="s_dve")
            S.emit("vector", lambda e: e.tensor_scalar(r2c[:, 3:4], r2c[:, 1:2], 1.0 / DM, None, ALU.mult),
                   waits=[("s_dve", S.v("s_dve"))], inc="s_dve")
            S.emit("vector", lambda e: e.tensor_tensor(r2c[:, 4:5], r2c[:, 2:3], r2c[:, 2:3], ALU.mult),
                   waits=[("s_dve", S.v("s_dve"))], inc="s_dve")
            S.emit("vector", lambda e: e.tensor_tensor(r2c[:, 5:6], r2c[:, 3:4], r2c[:, 4:5], ALU.subtract),
                   waits=[("s_dve", S.v("s_dve"))], inc="s_dve")
            v_d = S.v("s_dve")
            S.emit("scalar", lambda e: e.activation(r2c[:, 6:7], r2c[:, 5:6], AF.Ln, bias=epscol[:, 0:1]),
                   waits=[("s_dve", v_d)], inc="s_act")
            S.emit("scalar", lambda e: e.activation(r2c[:, 7:8], r2c[:, 6:7], AF.Exp, scale=-0.5),
                   waits=[("s_act", S.v("s_act"))], inc="s_act")
            v_r2 = S.v("s_act")
            x1nsem = "d_x1n" if par == 0 else "d_x1n1"
            S.emit("vector",
                   lambda e, par=par: e.tensor_scalar(x1n_t[par], xo[par][:, 0:4096],
                                                      r2c[:, 2:3], r2c[:, 7:8], ALU.add, ALU.mult),
                   waits=[("s_act", v_r2), (x1nsem, st_x1n.get(tc - 2, 0))], inc="s_dve")
            act_D[tc] = ("s_dve", S.v("s_dve"))
            S.dma("sync", x1n_dram[tc * 128:(tc + 1) * 128, :], x1n_t[par],
                  waits=[act_D[tc]], sem=x1nsem)
            st_x1n[tc] = S.v(x1nsem)
        x1n_done = [("d_x1n", S.v("d_x1n")), ("d_x1n1", S.v("d_x1n1"))]

        xn = A1[:, :].rearrange("p (i t) -> p i t", i=32)
        for ic in range(NIC):
            S.dma("sync", xn[:, ic, :], x1n_dram[0:TOWN, ic * 128:(ic + 1) * 128],
                  waits=x1n_done + [("s_pe", pe_C[7])], transpose=True, sem="d_tr")
        ld_xn = S.v("d_tr")

        qg = [B2[:, 0:4096], B2[:, 4096:8192]]
        qu = [B2[:, 8192:12288], B2[:, 12288:16384]]
        scr1 = B2[:, 16384:20480]
        scr2 = B2[:, 20480:24576]
        wgf = [F3[:, 0:4096], F3[:, 4096:8192]]
        wuf = [F3[:, 8192:12288], F3[:, 12288:16384]]
        pe_E = {}
        dve_E = {}
        act_E = {}
        ib_st = {}
        ew_ld = {}

        def emit_E_load(j):
            par = j % 2
            wsem = "d_w" if par == 0 else "d_w1"
            S.dma("sync", wgf[par], wg_tiles[j].rearrange("p i g -> p (i g)"),
                  waits=[("s_dve", dve_E.get(j - 2, 0))] + (x1n_done if j < 2 else []),
                  sem=wsem)
            S.dma("sync", wuf[par], wu_tiles[j].rearrange("p i g -> p (i g)"), sem=wsem)
            ew_ld[j] = (wsem, S.v(wsem))

        def emit_E_quant(j):
            par = j % 2
            wq2 = [ew_ld[j], ("s_dve", scale_state["thr"]),
                   ("s_pe", pe_E.get(j - 2, 0)), ("s_dve", dve_E.get(j - 1, 0))]
            S.emit("vector", lambda e, par=par: e.tensor_scalar(scr1, wgf[par], thrb[:, 0:1], None, ALU.is_gt),
                   waits=wq2)
            S.emit("vector", lambda e, par=par: e.tensor_scalar(scr2, wgf[par], thrb[:, 4:5], None, ALU.is_lt),
                   inc="s_dve")
            S.emit("vector", lambda e, par=par: e.tensor_tensor(qg[par], scr1, scr2, ALU.subtract),
                   waits=[("s_dve", S.v("s_dve"))], inc="s_dve")
            vqg = S.v("s_dve")
            S.emit("vector", lambda e, par=par: e.tensor_scalar(scr1, wuf[par], thrb[:, 1:2], None, ALU.is_gt),
                   waits=[ew_ld[j], ("s_dve", vqg)])
            S.emit("vector", lambda e, par=par: e.tensor_scalar(scr2, wuf[par], thrb[:, 5:6], None, ALU.is_lt),
                   inc="s_dve")
            S.emit("vector", lambda e, par=par: e.tensor_tensor(qu[par], scr1, scr2, ALU.subtract),
                   waits=[("s_dve", S.v("s_dve"))], inc="s_dve")
            dve_E[j] = S.v("s_dve")

        emit_E_load(0)
        emit_E_load(1)
        emit_E_quant(0)
        for j in range(NGC):
            par = j % 2
            if j + 2 < NGC:
                emit_E_load(j + 2)
            if j + 1 < NGC:
                emit_E_quant(j + 1)
            vq = dve_E[j]
            PG = PQ[par]
            PU = PV[par]
            for ic in range(NIC):
                w = []
                if ic == 0:
                    w = [("s_dve", vq), ("d_tr", ld_xn),
                         ("s_dve", dve_E.get(("i", j - 2), 0)),
                         ("s_act", act_E.get(j - 2, 0))]
                S.emit("tensor",
                       lambda e, PG=PG, par=par, ic=ic: e.matmul(
                           PG[:, 0:512], qg[par][:, ic * 128:(ic + 1) * 128],
                           xn[:, ic, 0:512], start=(ic == 0), stop=(ic == NIC - 1)),
                       waits=w)
                S.emit("tensor",
                       lambda e, PG=PG, par=par, ic=ic: e.matmul(
                           PG[:, 512:1024], qg[par][:, ic * 128:(ic + 1) * 128],
                           xn[:, ic, 512:1024], start=(ic == 0), stop=(ic == NIC - 1)))
            for ic in range(NIC):
                S.emit("tensor",
                       lambda e, PU=PU, par=par, ic=ic: e.matmul(
                           PU[:, 0:512], qu[par][:, ic * 128:(ic + 1) * 128],
                           xn[:, ic, 0:512], start=(ic == 0), stop=(ic == NIC - 1)))
                S.emit("tensor",
                       lambda e, PU=PU, par=par, ic=ic: e.matmul(
                           PU[:, 512:1024], qu[par][:, ic * 128:(ic + 1) * 128],
                           xn[:, ic, 512:1024], start=(ic == 0), stop=(ic == NIC - 1)),
                       inc=("s_pe" if ic == NIC - 1 else None))
            pe_E[j] = S.v("s_pe")
            S.emit("scalar", lambda e, PU=PU, par=par: e.copy(ubf[:, par * 1024:(par + 1) * 1024], PU[:, 0:1024]),
                   waits=[("s_pe", pe_E[j]), ("s_dve", dve_E.get(("i", j - 2), 0))], inc="s_act")
            act_E[j] = S.v("s_act")
            S.emit("vector",
                   lambda e, PG=PG, par=par: e.tensor_tensor(
                       ibf[:, par * 1024:(par + 1) * 1024], PG[:, 0:1024],
                       ubf[:, par * 1024:(par + 1) * 1024], ALU.mult),
                   waits=[("s_act", act_E[j]),
                          ("d_ib%d" % par, ib_st.get(j - 2, 0))],
                   inc="s_dve")
            dve_E[("i", j)] = S.v("s_dve")
            S.dma("scalar", inter_dram[j * 128:(j + 1) * 128, :], ibf[:, par * 1024:(par + 1) * 1024],
                  waits=[("s_dve", dve_E[("i", j)])], sem="d_ib%d" % par)
            ib_st[j] = S.v("d_ib%d" % par)
        ib_done = [("d_ib0", S.v("d_ib0")), ("d_ib1", S.v("d_ib1"))]
        pe_E_done = pe_E[NGC - 1]

        int_sb4 = [B2[:, 16384 + k * 4096:16384 + (k + 1) * 4096].rearrange("p (i t) -> p i t", i=4)
                   for k in range(4)]
        qd = [B2[:, 8192:10240], B2[:, 10240:12288]]
        fs1 = B2[:, 12288:14336]
        fs2 = B2[:, 14336:16384]
        wdf = [F3[:, 0:2048], F3[:, 2048:4096]]
        x1o = [F3[:, 8192:12288].rearrange("p (t o) -> p t o", t=8),
               F3[:, 12288:16384].rearrange("p (t o) -> p t o", t=8)]
        batches = [(i, min(i + 4, NGC)) for i in range(0, NGC, 4)]
        NB = len(batches)
        last_qd = [0]
        pe_gb = {}
        dve_q = {}
        dve_F = {}
        dve_ev = {}
        st_y = {}
        dve_E_all = dve_E[NGC - 1]
        dve_E_int = dve_E[("i", NGC - 1)]
        act_E_all = act_E[NGC - 1]
        isems = ["d_i", "d_i1", "d_i2", "d_i3"]
        ysems = ["d_ys", "d_ys1", "d_ys2", "d_ys3"]
        for oc in range(8):
            opar = oc % 2
            xfsem = "d_xf" if opar == 0 else "d_xf1"
            S.dma("scalar", x1o[opar][:, :, :],
                  x1_dram[:, oc * 512:(oc + 1) * 512].rearrange("(t p) o -> p t o", p=128),
                  waits=[("s_dve", dve_F.get(oc - 2, 0))] + x1n_done +
                        [("s_dve", dve_E_all)], sem=xfsem)
            x1ld = S.v(xfsem)
            for bi, (ic0, ic1) in enumerate(batches):
                gb = oc * NB + bi
                n = ic1 - ic0
                bpar = gb % 2
                sl4 = gb % 4
                wq1 = [("s_dve", dve_q.get(gb - 2, dve_E_all)),
                       ("s_pe", pe_gb.get(gb - 2, pe_E_done if gb < 2 else 0))]
                wdsem = "d_wd" if bpar == 0 else "d_wd1"
                disem = isems[sl4]
                lw = S.dma("sync", wdf[bpar][:, 0:n * 512].rearrange("p (i o) -> p i o", i=n),
                           wd_tiles[oc, ic0:ic1].rearrange("i p o -> p i o"),
                           waits=wq1, sem=wdsem)
                li = S.dma("sync", int_sb4[sl4][:, 0:n, :],
                           inter_dram[ic0 * 128:ic1 * 128, :].rearrange("(i p) t -> p i t", p=128),
                           waits=ib_done + [("s_pe", pe_gb.get(gb - 4, 0)),
                                            ("s_dve", dve_E_all)], sem=disem)
                li = S.v(disem)
                S.emit("vector", lambda e, bpar=bpar, n=n: e.tensor_scalar(
                           fs1[:, 0:n * 512], wdf[bpar][:, 0:n * 512], thrb[:, 2:3], None, ALU.is_gt),
                       waits=[(wdsem, lw), ("s_dve", scale_state["thr"]),
                              ("s_dve", last_qd[0])])
                S.emit("vector", lambda e, bpar=bpar, n=n: e.tensor_scalar(
                           fs2[:, 0:n * 512], wdf[bpar][:, 0:n * 512], thrb[:, 6:7], None, ALU.is_lt),
                       inc="s_dve")
                S.emit("vector", lambda e, bpar=bpar, n=n: e.tensor_tensor(
                           qd[bpar][:, 0:n * 512], fs1[:, 0:n * 512], fs2[:, 0:n * 512], ALU.subtract),
                       waits=[("s_dve", S.v("s_dve"))], inc="s_dve")
                dve_q[gb] = S.v("s_dve")
                last_qd[0] = S.v("s_dve")
                for i in range(n):
                    icg = ic0 + i
                    for tc in range(8):
                        slot = [P0, P1, P2, P3][tc // 2]
                        c0 = (tc % 2) * 512
                        w = []
                        if i == 0 and tc == 0:
                            w = [("s_dve", dve_q[gb]), (disem, li)]
                            if bi == 0 and oc == 0:
                                w.append(("s_dve", dve_E_int))
                                w.append(("s_act", act_E_all))
                        if i == 0 and bi == 0 and oc > 0:
                            w = w + [("s_dve", dve_ev[(oc - 1, tc)])]
                        S.emit("tensor",
                               lambda e, slot=slot, c0=c0, sl4=sl4, bpar=bpar, i=i, tc=tc, icg=icg: e.matmul(
                                   slot[:, c0:c0 + 512], int_sb4[sl4][:, i, tc * 128:(tc + 1) * 128],
                                   qd[bpar][:, i * 512:(i + 1) * 512],
                                   start=(icg == 0), stop=(icg == NGC - 1)),
                               waits=w,
                               inc=("s_pe" if (i == n - 1 and tc == 7) else None))
                pe_gb[gb] = S.v("s_pe")
            for tc in range(8):
                slot = [P0, P1, P2, P3][tc // 2]
                c0 = (tc % 2) * 512
                ys = tc % 4
                yssem = ysems[ys]
                S.emit("vector",
                       lambda e, slot=slot, c0=c0, opar=opar, tc=tc, ys=ys: e.scalar_tensor_tensor(
                           ytile[:, ys * 512:(ys + 1) * 512], slot[:, c0:c0 + 512],
                           thrb[:, 3:4], x1o[opar][:, tc, :], ALU.mult, ALU.add),
                       waits=[("s_pe", pe_gb[oc * NB + NB - 1]), (xfsem, x1ld),
                              (yssem,
                               st_y.get((oc, tc - 4), st_y.get((oc - 1, tc + 4), 0)))],
                       inc="s_dve")
                dve_ev[(oc, tc)] = S.v("s_dve")
                dve_F[oc] = S.v("s_dve")
                S.dma("sync", y[tc * 128:(tc + 1) * 128, oc * 512:(oc + 1) * 512],
                      ytile[:, ys * 512:(ys + 1) * 512],
                      waits=[("s_dve", dve_F[oc])], sem=yssem)
                st_y[(oc, tc)] = S.v(yssem)

        S.emit("sync", lambda e: None,
               waits=[(nm, S.v(nm)) for nm in sems
                      if nm.startswith("d") and S.v(nm) > 0])

        with allow_ici_all_to_all():
            with nc.Block() as block:
                S.run(block, handles)
    return nc


_NC_CACHE = {}


def _get_nc(debug=False):
    key = bool(debug)
    if key not in _NC_CACHE:
        _NC_CACHE[key] = build_nc(debug=key)
    return _NC_CACHE[key]


def make_in_maps(inputs):
    x = np.asarray(inputs["L0_attn_hidden_states"], dtype=np.float32)
    Wq = np.asarray(inputs["L0_attn_Wq"], dtype=np.float32)
    Wk = np.asarray(inputs["L0_attn_Wk"], dtype=np.float32)
    Wv = np.asarray(inputs["L0_attn_Wv"], dtype=np.float32)
    Wo = np.asarray(inputs["L0_attn_Wo"], dtype=np.float32)
    wg = np.asarray(inputs["L0_ffn_w_gate"], dtype=np.float32)
    wu = np.asarray(inputs["L0_ffn_w_up"], dtype=np.float32)
    wd = np.asarray(inputs["L0_ffn_w_down"], dtype=np.float32)

    xr = x.reshape(NT, DM)
    x_fm = np.ascontiguousarray(xr.T)
    woT = np.ascontiguousarray(Wo.T)
    wg_tiles = np.ascontiguousarray(wg.reshape(NGC, 128, NIC, 128).transpose(0, 3, 2, 1))
    wu_tiles = np.ascontiguousarray(wu.reshape(NGC, 128, NIC, 128).transpose(0, 3, 2, 1))
    wd_tiles = np.ascontiguousarray(wd.reshape(8, 512, NGC, 128).transpose(0, 2, 3, 1))
    wg_flat = wg_tiles.reshape(-1)
    wu_flat = wu_tiles.reshape(-1)
    wd_flat = wd_tiles.reshape(-1)
    nsl = wg_flat.size // R
    cos, sinr = rope_tables()
    masks = causal_masks()
    ident = np.eye(128, dtype=np.float32)

    in_maps = []
    for c in range(R):
        wqkv_c = np.ascontiguousarray(np.concatenate(
            [Wq[512 * c:512 * (c + 1), :].T, Wk[128 * c:128 * (c + 1), :].T,
             Wv[128 * c:128 * (c + 1), :].T], axis=1))
        in_maps.append({
            "x_fm": x_fm,
            "x_own": np.ascontiguousarray(xr[TOWN * c:TOWN * (c + 1), :]),
            "wqkv": wqkv_c,
            "woT": woT,
            "wg_tiles": wg_tiles,
            "wu_tiles": wu_tiles,
            "wd_tiles": wd_tiles,
            "wg_s": np.ascontiguousarray(wg_flat[nsl * c:nsl * (c + 1)].reshape(128, 44032)),
            "wu_s": np.ascontiguousarray(wu_flat[nsl * c:nsl * (c + 1)].reshape(128, 44032)),
            "wd_s": np.ascontiguousarray(wd_flat[nsl * c:nsl * (c + 1)].reshape(128, 44032)),
            "cosT": cos,
            "sinrT": sinr,
            "maskT": masks,
            "idf": ident,
        })
    return in_maps


def kernel(**inputs):
    nc = _get_nc(debug=False)
    in_maps = make_in_maps(inputs)
    res = run_bass_kernel_spmd(nc, in_maps, core_ids=list(range(R)))
    out = np.empty((NT, DM), dtype=np.float32)
    for c in range(R):
        out[TOWN * c:TOWN * (c + 1), :] = res.results[c]["y"]
    return out.reshape(NBATCH, SEQ, DM)


# version 13
# speedup vs baseline: 1.3518x; 1.0257x over previous
import math
from contextlib import ExitStack

import numpy as np
import ml_dtypes

import concourse.bass as bass
import concourse.mybir as mybir
from concourse.bass_utils import run_bass_kernel_spmd
from concourse.aot_env import allow_ici_all_to_all

F32 = mybir.dt.float32
BF16 = mybir.dt.bfloat16
AF = mybir.ActivationFunctionType
ALU = mybir.AluOpType
AX = mybir.AxisListType
BF = ml_dtypes.bfloat16

R = 8
SEQ = 4096
NBATCH = 2
NT = NBATCH * SEQ
DM = 4096
DI = 11008
NHL = 4
TOWN = NT // R
NTC = NT // 128
NIC = DM // 128
NGC = DI // 128
EPS_LN = 1e-5
EPS_Q = 1e-5
EXP_C = 20.0
ISQ = 1.0 / math.sqrt(128.0)
NEG_BIG = -1.0e30
NW = float(DI * DM)

ENGS = ("tensor", "vector", "scalar", "gpsimd", "sync")


class Sched:

    def __init__(self, sem_names):
        self.cnt = {n: 0 for n in sem_names}
        self.ops = {e: [] for e in ENGS}

    def v(self, sem):
        return self.cnt[sem]

    def emit(self, eng, fn, waits=(), inc=None, n=1):
        waits = tuple((s, v) for (s, v) in waits if v > 0)
        if inc is not None:
            self.cnt[inc] += n
        self.ops[eng].append((fn, waits, inc, n))
        return self.cnt[inc] if inc is not None else None

    def dma(self, eng, out, in_, waits=(), transpose=False, sem=None):
        def fn(e, out=out, in_=in_, transpose=transpose):
            if transpose:
                return e.dma_start(out=out, in_=in_, transpose=True)
            return e.dma_start(out=out, in_=in_)
        return self.emit(eng, fn, waits, sem or ("dq_" + eng), 16)

    def run(self, block, handles):
        for eng in ENGS:
            ops = self.ops[eng]
            def make(eng_ops):
                def body(e):
                    seen = {}
                    for fn, waits, inc, n in eng_ops:
                        for s, v in waits:
                            if seen.get(s, 0) < v:
                                e.wait_ge(handles[s], v)
                                seen[s] = v
                        inst = fn(e)
                        if inc is not None:
                            inst.then_inc(handles[inc], n)
                return body
            getattr(block, eng)(make(ops))


def rope_tables():
    f32 = np.float32
    d = np.arange(0, 128, 2, dtype=f32)
    inv_freq = (f32(1.0) / (f32(500000.0) ** (d / f32(128.0)))).astype(f32)
    wavelen = f32(2.0 * math.pi) / inv_freq
    smooth = (f32(8192.0) / wavelen - f32(1.0)) / f32(3.0)
    smooth = np.clip(smooth, f32(0.0), f32(1.0)).astype(f32)
    new_inv = ((f32(1.0) - smooth) * (inv_freq / f32(8.0)) + smooth * inv_freq).astype(f32)
    t = np.arange(SEQ, dtype=f32)
    freqs = np.outer(t, new_inv).astype(f32)
    emb = np.concatenate([freqs, freqs], axis=-1)
    cos = np.cos(emb).astype(f32)
    sin = np.sin(emb).astype(f32)
    sinr = np.concatenate([-sin[:, :64], sin[:, 64:]], axis=-1).astype(f32)
    return cos, sinr


def causal_masks():
    m = np.zeros((128, 4, 512), dtype=np.float32)
    k = np.arange(128)[:, None]
    q = np.arange(512)[None, :]
    for v in range(4):
        m[:, v, :] = np.where(128 * v + k <= q, 0.0, NEG_BIG)
    return m.reshape(128, 2048)


def build_nc(debug=False):
    nc = bass.Bass("TRN2", target_bir_lowering=False)
    dt = nc.dram_tensor

    x_fm = dt("x_fm", [DM, NT], F32, kind="ExternalInput").ap()
    x_own = dt("x_own", [TOWN, DM], F32, kind="ExternalInput").ap()
    wqkv = dt("wqkv", [DM, 768], F32, kind="ExternalInput").ap()
    woT = dt("woT", [DM, DM], F32, kind="ExternalInput").ap()
    wg_tiles = dt("wg_tiles", [NGC, 128, NIC, 128], F32, kind="ExternalInput").ap()
    wu_tiles = dt("wu_tiles", [NGC, 128, NIC, 128], F32, kind="ExternalInput").ap()
    wd_tiles = dt("wd_tiles", [8, NGC, 128, 512], F32, kind="ExternalInput").ap()
    wg_s = dt("wg_s", [128, 44032], F32, kind="ExternalInput").ap()
    wu_s = dt("wu_s", [128, 44032], F32, kind="ExternalInput").ap()
    wd_s = dt("wd_s", [128, 44032], F32, kind="ExternalInput").ap()
    cosT = dt("cosT", [SEQ, 128], F32, kind="ExternalInput").ap()
    sinrT = dt("sinrT", [SEQ, 128], F32, kind="ExternalInput").ap()
    maskT = dt("maskT", [128, 2048], F32, kind="ExternalInput").ap()
    idf = dt("idf", [128, 128], F32, kind="ExternalInput").ap()
    y = dt("y", [TOWN, DM], F32, kind="ExternalOutput").ap()

    dbg = "ExternalOutput" if debug else "Internal"
    qkv_spill = dt("qkv_spill", [NT, 768], BF16, kind=dbg).ap()
    a2a_in = dt("a2a_in", [NT, 512], BF16, kind="Internal")
    a2a_out = dt("a2a_out", [NT, 512], BF16, kind="Internal")
    stats_in = dt("stats_in", [TOWN, 2], F32, kind="Internal")
    stats_ag = dt("stats_ag", [NT, 2], F32, kind="Internal")
    scales_in = dt("scales_in", [4, 1], F32, kind="Internal")
    scales_out = dt("scales_out", [32, 1], F32, kind="Internal")
    if debug:
        absacc_d = dt("absacc_d", [128, 4], F32, kind="ExternalOutput").ap()
        slabd_d = dt("slabd_d", [128, 36], F32, kind="ExternalOutput").ap()
        stat8_d = dt("stat8_d", [128, 32], F32, kind="ExternalOutput").ap()
        stats_ag_d = dt("stats_ag_d", [NT, 2], F32, kind="ExternalOutput").ap()
        scales_d = dt("scales_d", [1, 8], F32, kind="ExternalOutput").ap()
        a2a_in_d = dt("a2a_in_d", [NT, 512], BF16, kind="ExternalOutput").ap()
    nrow_dram = dt("nrow_dram", [64, 128], BF16, kind="Internal").ap()
    thr_dram = dt("thr_dram", [1, 4], F32, kind="Internal").ap()
    x1_dram = dt("x1_dram", [TOWN, DM], F32, kind=dbg).ap()
    x1n_dram = dt("x1n_dram", [TOWN, DM], BF16, kind=dbg).ap()
    inter_dram = dt("inter_dram", [DI, TOWN], BF16, kind=dbg).ap()

    with ExitStack() as ctx:
        ecx = ctx.enter_context

        def sbuf(name, shape, dtype):
            return ecx(nc.sbuf_tensor(name, shape, dtype))

        A1 = sbuf("A1", [128, 32768], BF16)
        B2 = sbuf("B2", [128, 32768], BF16)
        F3 = sbuf("F3", [128, 16384], F32)
        outb = sbuf("outb", [128, 2 * 768], BF16)
        stag = sbuf("stag", [128, 128], F32)
        stat8 = sbuf("stat8", [128, 32], F32)
        nchunk = sbuf("nchunk", [1, 2 * 128], BF16)
        w1row = sbuf("w1row", [1, 768], BF16)
        onesb = sbuf("onesb", [128, 1], BF16)
        onesf = sbuf("onesf", [128, 1], F32)
        ncol = sbuf("ncol", [128, 1], F32)
        epscol = sbuf("epscol", [128, 1], F32)
        thrb = sbuf("thrb", [128, 8], F32)
        slabd = sbuf("slabd", [128, 36], F32)
        scf = sbuf("scf", [128, 256], F32)
        scr32 = sbuf("scr32", [1, 32], F32)
        absacc = sbuf("absacc", [128, 4], F32)
        scw = sbuf("scw", [1, 8], F32)
        zc = sbuf("zc", [128, 4], F32)
        r2c = sbuf("r2c", [128, 8], F32)
        n64 = sbuf("n64", [64, 128], BF16)

        P0 = ecx(nc.psum_tensor("P0", [128, 1024], F32))
        P1 = ecx(nc.psum_tensor("P1", [128, 1024], F32))
        P2 = ecx(nc.psum_tensor("P2", [128, 1024], F32))
        P3 = ecx(nc.psum_tensor("P3", [128, 1024], F32))
        PQ = [P0, P1]
        PV = [P2, P3]

        sems = ["dq_sync", "dq_gpsimd", "dq_scalar", "s_pe", "s_dve", "s_act", "s_cc",
                "d_q", "d_q1", "d_tr", "d_ost", "d_ost1", "d_n", "d_sp", "d_sp1",
                "d_xo", "d_xd0", "d_xd1", "d_x1s", "d_x1s1", "d_w", "d_w1",
                "d_i", "d_i1", "d_dbg", "d_xc", "d_xc1", "d_xf", "d_xf1",
                "d_x1n", "d_x1n1", "d_ys", "d_ys1", "d_wd", "d_wd1",
                "d_xb0", "d_xb1", "d_xb2", "d_xb3", "d_sc0", "d_sc1",
                "d_wo0", "d_wo1", "d_ib0", "d_ib1", "d_msk", "d_thr",
                "d_i2", "d_i3", "d_x1s2", "d_x1s3", "d_ys2", "d_ys3"]
        handles = {n: ecx(nc.semaphore(n)) for n in sems}
        S = Sched(sems)

        wq_v = B2[:, 0:24576].rearrange("p (i o) -> p i o", i=NIC)
        xb = [B2[:, 24576 + k * 4096:24576 + (k + 1) * 4096].rearrange("p (i t) -> p i t", i=NIC)
              for k in range(2)] + \
             [A1[:, k * 4096:(k + 1) * 4096].rearrange("p (i t) -> p i t", i=NIC)
              for k in range(2)]
        nall = A1[0:1, 8192:16384]
        cos_v = F3[:, 0:4096].rearrange("p (i d) -> p i d", i=32)
        sinr_v = F3[:, 4096:8192].rearrange("p (i d) -> p i d", i=32)
        xo = [F3[:, 8192:12288], F3[:, 12288:16384]]
        scbuf = F3[:, 12288:16384]
        idf_v = F3[:, 8192:8320]
        rps = F3[:, 8320:9600]
        ytile = F3[:, 4096:6144]
        ubf = B2[:, 24576:26624]
        ibf = B2[:, 26624:28672]
        sq_bf = B2[:, 0:4096]

        S.emit("vector", lambda e: e.memset(onesb[:, :], 1.0), inc="s_dve")
        S.emit("vector", lambda e: e.memset(onesf[:, :], 1.0), inc="s_dve")
        S.emit("vector", lambda e: e.memset(ncol[:, :], -EXP_C), inc="s_dve")
        S.emit("vector", lambda e: e.memset(epscol[:, :], EPS_LN), inc="s_dve")
        S.emit("vector", lambda e: e.memset(absacc[:, :], 0.0), inc="s_dve")
        S.emit("vector", lambda e: e.memset(scw[0:1, :], 0.0), inc="s_dve")
        v_consts = S.v("s_dve")
        S.dma("sync", idf_v, idf)
        ld_idf = S.v("dq_sync")

        xld = {}
        for tc in range(4):
            S.dma("gpsimd", xb[tc], x_fm[:, tc * 128:(tc + 1) * 128].rearrange("(i p) t -> p i t", p=128),
                  sem="d_xb%d" % tc)
            xld[tc] = S.v("d_xb%d" % tc)

        last_sq = 0
        for i in range(8):
            buf = xo[1]
            w = [("s_dve", S.v("s_dve")), ("s_act", S.v("s_act"))] if i >= 1 else []
            ldv = S.dma("sync", buf[:, 0:4096], x_own[i * 128:(i + 1) * 128, :], waits=w, sem="d_xo")
            S.emit("vector",
                   lambda e, buf=buf, i=i: e.reduce_sum(stat8[:, i:i + 1], buf[:, 0:4096], axis=AX.X),
                   waits=[("d_xo", ldv)], inc="s_dve")
            S.emit("scalar",
                   lambda e, buf=buf: e.activation(sq_bf, buf[:, 0:4096], AF.Square),
                   waits=[("d_xo", ldv), ("s_dve", last_sq)], inc="s_act")
            S.emit("vector",
                   lambda e, i=i: e.reduce_sum(stat8[:, 8 + i:9 + i], sq_bf, axis=AX.X),
                   waits=[("s_act", S.v("s_act"))], inc="s_dve")
            last_sq = S.v("s_dve")
        scr_free = [("s_dve", S.v("s_dve")), ("s_act", S.v("s_act"))]
        nr_v = stat8[:, 16:32].rearrange("p (i c) -> p i c", c=2)
        S.emit("vector", lambda e: e.tensor_scalar(nr_v[:, :, 0], stat8[:, 0:8], -1.0 / DM, None, ALU.mult),
               waits=[("s_dve", S.v("s_dve"))], inc="s_dve")
        S.emit("vector", lambda e: e.tensor_scalar(stat8[:, 8:16], stat8[:, 8:16], 1.0 / DM, None, ALU.mult),
               waits=[("s_dve", S.v("s_dve"))], inc="s_dve")
        S.emit("vector", lambda e: e.tensor_tensor(stat8[:, 0:8], nr_v[:, :, 0], nr_v[:, :, 0], ALU.mult),
               waits=[("s_dve", S.v("s_dve"))], inc="s_dve")
        S.emit("vector", lambda e: e.tensor_tensor(stat8[:, 8:16], stat8[:, 8:16], stat8[:, 0:8], ALU.subtract),
               waits=[("s_dve", S.v("s_dve"))], inc="s_dve")
        v_var = S.v("s_dve")
        S.emit("scalar", lambda e: e.activation(stat8[:, 0:8], stat8[:, 8:16], AF.Ln, bias=epscol[:, 0:1]),
               waits=[("s_dve", v_var)], inc="s_act")
        S.emit("scalar", lambda e: e.activation(stat8[:, 8:16], stat8[:, 0:8], AF.Exp, scale=-0.5),
               waits=[("s_act", S.v("s_act"))], inc="s_act")
        v_stats0 = S.v("s_act")
        S.emit("vector", lambda e: e.tensor_scalar(nr_v[:, :, 1], stat8[:, 8:16], 1.0, None, ALU.mult),
               waits=[("s_act", v_stats0)], inc="s_dve")
        v_stats = ("s_dve", S.v("s_dve"))
        S.dma("sync", stats_in.ap().rearrange("(i p) c -> p i c", p=128),
              stat8[:, 16:32].rearrange("p (i c) -> p i c", c=2),
              waits=[v_stats])
        st_stats = S.v("dq_sync")
        S.emit("gpsimd",
               lambda e: e.collective_compute("AllGather", ALU.bypass,
                                              replica_groups=[list(range(R))],
                                              ins=[stats_in.ap().opt()],
                                              outs=[stats_ag.ap().opt()]),
               waits=[("dq_sync", st_stats)], inc="s_cc")
        cc_stats = S.v("s_cc")
        S.dma("sync", stag[:, :].rearrange("p (i c) -> p i c", c=2),
              stats_ag.ap().rearrange("(i p) c -> p i c", p=128),
              waits=[("s_cc", cc_stats)])
        ld_stag = S.v("dq_sync")
        if debug:
            S.dma("sync", stats_ag_d, stats_ag.ap(), waits=[("s_cc", cc_stats)], sem="d_dbg")
            S.dma("sync", stat8_d, stat8[:, :], waits=[("s_cc", cc_stats)], sem="d_dbg")
        stag_v = stag[:, :].rearrange("p (i c) -> p i c", c=2)
        S.emit("tensor",
               lambda e: e.transpose(P0[0:64, 0:128], stag_v[:, :, 0], idf_v),
               waits=[("dq_sync", ld_stag), ("dq_sync", ld_idf)], inc="s_pe")
        S.emit("vector", lambda e: e.tensor_copy(n64[:, :], P0[0:64, 0:128]),
               waits=[("s_pe", S.v("s_pe"))], inc="s_dve")
        v_n64 = S.v("s_dve")
        S.dma("sync", nrow_dram, n64[:, :], waits=[("s_dve", v_n64)], sem="d_n")
        st_nrow = S.v("d_n")
        S.dma("sync", nall, nrow_dram.rearrange("(o a) b -> o (a b)", o=1),
              waits=[("d_n", st_nrow)], sem="d_n")
        ld_nall = S.v("d_n")

        S.dma("gpsimd", wq_v, wqkv.rearrange("(i p) o -> p i o", p=128),
              waits=[v_stats])
        ld_wqkv = S.v("dq_gpsimd")

        for ic in range(NIC):
            w = [("dq_gpsimd", ld_wqkv), ("s_dve", v_consts)] if ic == 0 else []
            S.emit("tensor",
                   lambda e, ic=ic: e.matmul(P1[0:1, 0:512], onesb[:, :], wq_v[:, ic, 0:512],
                                             start=(ic == 0), stop=(ic == NIC - 1)),
                   waits=w)
            S.emit("tensor",
                   lambda e, ic=ic: e.matmul(P1[0:1, 512:768], onesb[:, :], wq_v[:, ic, 512:768],
                                             start=(ic == 0), stop=(ic == NIC - 1)),
                   inc=("s_pe" if ic == NIC - 1 else None))
        v_w1mm = S.v("s_pe")
        S.emit("scalar", lambda e: e.copy(w1row[0:1, :], P1[0:1, 0:768]),
               waits=[("s_pe", v_w1mm)], inc="s_act")
        v_w1 = S.v("s_act")

        S.dma("sync", cos_v, cosT.rearrange("(i p) d -> p i d", p=128))
        S.dma("sync", sinr_v, sinrT.rearrange("(i p) d -> p i d", p=128))
        ld_tabs = S.v("dq_sync")

        slabs = []
        for wi, wsrc in enumerate((wg_s, wu_s, wd_s)):
            for c0 in range(0, 44032, 2048):
                slabs.append((wi, wsrc, c0, min(c0 + 2048, 44032)))
        scbufs = [scbuf[:, 0:2048], scbuf[:, 2048:4096]]
        sl_dve = {}

        def emit_scale_slab(k):
            wi, wsrc, c0, c1 = slabs[k]
            scbuf_k = scbufs[k % 2]
            scsem = "d_sc%d" % (k % 2)
            ldv = S.dma("scalar", scbuf_k[:, 0:c1 - c0], wsrc[:, c0:c1],
                        waits=(scr_free if k < 2 else [("s_dve", sl_dve.get(k - 2, 0))]),
                        sem=scsem)
            ldv = S.v(scsem)
            for a in range(0, c1 - c0, 512):
                S.emit("vector",
                       lambda e, a=a, scbuf_k=scbuf_k: e.reduce_sum(
                           absacc[:, 3:4], scbuf_k[:, a:a + 512],
                           axis=AX.X, apply_absolute_value=True),
                       waits=([(scsem, ldv)] if a == 0 else [("s_dve", S.v("s_dve"))]),
                       inc="s_dve")
                S.emit("vector",
                       lambda e, wi=wi: e.tensor_tensor(absacc[:, wi:wi + 1], absacc[:, wi:wi + 1],
                                                        absacc[:, 3:4], ALU.add),
                       waits=[("s_dve", S.v("s_dve"))], inc="s_dve")
            sl_dve[k] = S.v("s_dve")

        scale_state = {}

        def emit_scale_finish():
            S.emit("tensor", lambda e: e.transpose(P2[0:4, 0:128], absacc[:, 0:4], idf_v),
                   waits=[("s_dve", S.v("s_dve")), ("dq_sync", ld_idf)], inc="s_pe")
            S.emit("scalar", lambda e: e.copy(scf[0:4, 0:128], P2[0:4, 0:128]),
                   waits=[("s_pe", S.v("s_pe"))], inc="s_act")
            scale_state["p2done"] = S.v("s_act")
            base = 0
            n = 128
            w = [("s_act", S.v("s_act"))]
            while n > 1:
                h = n // 2
                src = scf[0:4, base:base + n].rearrange("p (a b) -> p a b", b=2)
                S.emit("vector",
                       lambda e, src=src, base=base, n=n, h=h: e.tensor_tensor(
                           scf[0:4, base + n:base + n + h], src[:, :, 0], src[:, :, 1], ALU.add),
                       waits=w + [("s_dve", S.v("s_dve"))], inc="s_dve")
                w = []
                base += n
                n = h
            S.dma("sync", scales_in.ap(), scf[0:4, 254:255], waits=[("s_dve", S.v("s_dve"))],
                  sem="d_thr")
            S.emit("gpsimd",
                   lambda e: e.collective_compute("AllGather", ALU.bypass,
                                                  replica_groups=[list(range(R))],
                                                  ins=[scales_in.ap().opt()],
                                                  outs=[scales_out.ap().opt()]),
                   waits=[("d_thr", S.v("d_thr"))], inc="s_cc")
            S.dma("sync", scr32[0:1, 0:32],
                  scales_out.ap().rearrange("(o a) b -> o (a b)", o=1),
                  waits=[("s_cc", S.v("s_cc"))], sem="d_thr")
            ldt = S.v("d_thr")
            S.emit("vector", lambda e: e.tensor_copy(scw[0:1, 0:3], scr32[0:1, 20:23]),
                   waits=[("d_thr", ldt)], inc="s_dve")
            for j in range(1, 8):
                r = (5 + j) % 8
                S.emit("vector",
                       lambda e, r=r: e.tensor_tensor(scw[0:1, 0:3], scw[0:1, 0:3],
                                                      scr32[0:1, 4 * r:4 * r + 3], ALU.add),
                       waits=[("s_dve", S.v("s_dve"))], inc="s_dve")
            if debug:
                S.emit("vector", lambda e: e.tensor_copy(scw[0:1, 4:7], scw[0:1, 0:3]),
                       waits=[("s_dve", S.v("s_dve"))], inc="s_dve")
                S.dma("sync", scales_d, scw[0:1, 0:8], waits=[("s_dve", S.v("s_dve"))], sem="d_dbg")
            S.emit("vector", lambda e: e.tensor_scalar(scw[0:1, 0:3], scw[0:1, 0:3],
                                                       float(np.float32(1.0) / np.float32(NW)), None, ALU.mult),
                   waits=[("s_dve", S.v("s_dve"))], inc="s_dve")
            S.emit("vector", lambda e: e.tensor_scalar(scw[0:1, 4:7], scw[0:1, 0:3], EPS_Q, 0.5, ALU.add, ALU.mult),
                   waits=[("s_dve", S.v("s_dve"))], inc="s_dve")
            S.emit("vector", lambda e: e.tensor_tensor(scw[0:1, 3:4], scw[0:1, 0:1], scw[0:1, 1:2], ALU.mult),
                   waits=[("s_dve", S.v("s_dve"))], inc="s_dve")
            S.emit("vector", lambda e: e.tensor_tensor(scw[0:1, 7:8], scw[0:1, 3:4], scw[0:1, 2:3], ALU.mult),
                   waits=[("s_dve", S.v("s_dve"))], inc="s_dve")
            S.dma("sync", thr_dram[0:1, 0:4], scw[0:1, 4:8], waits=[("s_dve", S.v("s_dve"))],
                  sem="d_thr")
            S.dma("sync", thrb[:, 0:4], thr_dram[0:1, 0:4].partition_broadcast(128),
                  waits=[("d_thr", S.v("d_thr"))], sem="d_thr")
            S.emit("vector", lambda e: e.tensor_scalar(thrb[:, 4:8], thrb[:, 0:4], -1.0, None, ALU.mult),
                   waits=[("d_thr", S.v("d_thr"))], inc="s_dve")
            scale_state["thr"] = S.v("s_dve")

        pe_tc = {}
        dve_rope = {}
        spill_v = {}
        n_scale = 0
        for tc in range(NTC):
            par = tc % 2
            b4 = tc % 4
            P = PQ[par]
            if tc >= 4:
                S.dma("gpsimd", xb[b4], x_fm[:, tc * 128:(tc + 1) * 128].rearrange("(i p) t -> p i t", p=128),
                      waits=[("s_pe", pe_tc.get(tc - 4, 0))], sem="d_xb%d" % b4)
                xld[tc] = S.v("d_xb%d" % b4)
            for ic in range(NIC):
                w = []
                if ic == 0:
                    w = [("d_xb%d" % b4, xld[tc]), ("d_n", ld_nall), ("s_act", v_w1),
                         ("s_dve", dve_rope.get(tc - 2, 0))]
                S.emit("tensor",
                       lambda e, ic=ic, P=P, b4=b4: e.matmul(
                           P[:, 0:512], xb[b4][:, ic, :], wq_v[:, ic, 0:512],
                           start=(ic == 0), stop=False),
                       waits=w)
                S.emit("tensor",
                       lambda e, ic=ic, P=P, b4=b4: e.matmul(
                           P[:, 512:768], xb[b4][:, ic, :], wq_v[:, ic, 512:768],
                           start=(ic == 0), stop=False))
            S.emit("tensor",
                   lambda e, P=P, tc=tc: e.matmul(P[:, 0:512], nall[0:1, tc * 128:(tc + 1) * 128],
                                                  w1row[0:1, 0:512], start=False, stop=True))
            S.emit("tensor",
                   lambda e, P=P, tc=tc: e.matmul(P[:, 512:768], nall[0:1, tc * 128:(tc + 1) * 128],
                                                  w1row[0:1, 512:768], start=False, stop=True),
                   inc="s_pe")
            pe_tc[tc] = S.v("s_pe")

            ci = tc % 32
            rA = rps[:, 0:640]
            rB = rps[:, 640:1280]
            spsem = "d_sp" if par == 0 else "d_sp1"
            wv = [("s_pe", pe_tc[tc]), ("dq_sync", ld_tabs), ("dq_sync", ld_stag),
                  (spsem, spill_v.get(tc - 2, 0)), ("s_dve", dve_rope.get(tc - 1, 0))]
            for blk in range(5):
                c0 = blk * 128
                S.emit("vector",
                       lambda e, P=P, c0=c0, ci=ci: e.tensor_tensor(
                           rps[:, c0:c0 + 128], P[:, c0:c0 + 128], cos_v[:, ci, :], ALU.mult),
                       waits=(wv if blk == 0 else ()))
                S.emit("vector",
                       lambda e, P=P, c0=c0, ci=ci: e.tensor_tensor(
                           rps[:, 640 + c0:640 + c0 + 128],
                           P[:, c0:c0 + 128].rearrange("p (g c) -> p g c", g=2)[:, ::-1, :],
                           sinr_v[:, ci, :].rearrange("p (g c) -> p g c", g=2),
                           ALU.mult),
                       inc=("s_dve" if blk == 4 else None))
            S.emit("vector", lambda e: e.tensor_tensor(rA, rA, rB, ALU.add),
                   waits=[("s_dve", S.v("s_dve"))], inc="s_dve")
            S.emit("vector",
                   lambda e, tc=tc, par=par: e.tensor_scalar(
                       outb[:, par * 768:par * 768 + 640], rA,
                       stag_v[:, tc, 1:2], None, ALU.mult),
                   waits=[("s_dve", S.v("s_dve"))])
            S.emit("vector",
                   lambda e, tc=tc, par=par, P=P: e.tensor_scalar(
                       outb[:, par * 768 + 640:par * 768 + 768], P[:, 640:768],
                       stag_v[:, tc, 1:2], None, ALU.mult),
                   inc="s_dve")
            dve_rope[tc] = S.v("s_dve")
            S.dma("sync", qkv_spill[tc * 128:(tc + 1) * 128, :], outb[:, par * 768:(par + 1) * 768],
                  waits=[("s_dve", dve_rope[tc])], sem=spsem)
            spill_v[tc] = S.v(spsem)
            if n_scale < len(slabs):
                emit_scale_slab(n_scale)
                n_scale += 1
        while n_scale < len(slabs):
            emit_scale_slab(n_scale)
            n_scale += 1
        emit_scale_finish()
        spill_done = [("d_sp", spill_v[NTC - 2]), ("d_sp1", spill_v[NTC - 1])]
        pe_A_done = pe_tc[NTC - 1]

        k_sb = A1[:, 0:8192]
        v_sb = A1[:, 8192:16640]
        v_sb3 = A1[:, 8192:16640].rearrange("p (g c) -> p g c", g=64)
        m_bf = A1[:, 16640:18688]
        qsl = A1[:, 18688:19712]
        ebuf = A1[:, 19712:21760]
        ostg = A1[:, 21760:25856]

        S.emit("vector", lambda e: e.memset(v_sb[:, :], 1.0),
               waits=[("s_pe", pe_A_done)], inc="s_dve")
        v_vones = S.v("s_dve")
        ld_mask = S.dma("sync", F3[:, 0:2048], maskT, waits=[("s_dve", dve_rope[NTC - 1])],
                        sem="d_msk")
        S.emit("vector", lambda e: e.tensor_copy(m_bf[:, :], F3[:, 0:2048]),
               waits=[("d_msk", ld_mask)], inc="s_dve")
        v_mask = S.v("s_dve")

        for j in range(16):
            S.dma("sync", k_sb[:, j * 512:(j + 1) * 512],
                  qkv_spill[j * 512:(j + 1) * 512, 512:640],
                  waits=spill_done + [("s_dve", v_vones)], transpose=True, sem="d_tr")
        ld_k = S.v("d_tr")
        for g in range(64):
            S.dma("sync", v_sb3[:, g, 0:128],
                  qkv_spill[g * 128:(g + 1) * 128, 640:768],
                  waits=[("s_dve", v_vones)] + spill_done)
        ld_v = S.v("dq_sync")

        wo_buf = [B2[:, 0:16384].rearrange("p (h o) -> p h o", h=32),
                  B2[:, 16384:32768].rearrange("p (h o) -> p h o", h=32)]
        wo_ld = {}
        for oc in range(2):
            S.dma("gpsimd", wo_buf[oc],
                  woT[:, oc * 512:(oc + 1) * 512].rearrange("(h p) o -> p h o", p=128),
                  waits=[("s_pe", pe_A_done)], sem="d_wo%d" % oc)
            wo_ld[oc] = S.v("d_wo%d" % oc)

        groups = [(b, qc, h) for b in range(NBATCH) for qc in range(8) for h in range(NHL)]
        pairs = []
        for gi, (b, qc, h) in enumerate(groups):
            npairs = 2 * (qc + 1)
            for j in range(npairs):
                pairs.append((gi, b, qc, h, j, npairs))
        NPAIR = len(pairs)

        qld = {}
        pe_st_group = {}
        exp_done = {}
        av_done = {}
        ev_po = {"v": 0}
        st_ostg = {0: 0, 1: 0}
        evac_cnt = {}

        def emit_qload(gi):
            b, qc, h = groups[gi]
            gpar = gi % 2
            qsem = "d_q" if gpar == 0 else "d_q1"
            S.dma("sync", qsl[:, gpar * 512:(gpar + 1) * 512],
                  qkv_spill[b * SEQ + qc * 512: b * SEQ + qc * 512 + 512, h * 128:(h + 1) * 128],
                  waits=[("s_pe", pe_st_group.get(gi - 2, 0)), ("d_tr", ld_k), ("dq_sync", ld_v)],
                  transpose=True, sem=qsem)
            qld[gi] = (qsem, S.v(qsem))

        def emit_st(p):
            gi, b, qc, h, j, npairs = pairs[p]
            spar = p % 2
            SP = PQ[spar]
            gpar = gi % 2
            qv = qsl[:, gpar * 512:(gpar + 1) * 512]
            w = [("s_act", exp_done.get(p - 2, 0)), qld[gi]]
            if p <= 1:
                w.append(("s_act", v_w1))
                w.append(("s_dve", dve_rope[NTC - 1]))
            kb0 = 2 * j
            S.emit("tensor",
                   lambda e, SP=SP, b=b, kb0=kb0, qv=qv: e.matmul(
                       SP[:, 0:512], k_sb[:, b * SEQ + kb0 * 128: b * SEQ + kb0 * 128 + 128],
                       qv, start=True, stop=True),
                   waits=w)
            S.emit("tensor",
                   lambda e, SP=SP, b=b, kb0=kb0, qv=qv: e.matmul(
                       SP[:, 512:1024], k_sb[:, b * SEQ + kb0 * 128 + 128: b * SEQ + kb0 * 128 + 256],
                       qv, start=True, stop=True),
                   inc="s_pe")
            pairs_pe[p] = S.v("s_pe")
            pe_st_group[gi] = S.v("s_pe")

        def emit_mask_exp(p):
            gi, b, qc, h, j, npairs = pairs[p]
            spar = p % 2
            SP = PQ[spar]
            diag = (j >= npairs - 2)
            wexp = [("s_pe", pairs_pe[p]), ("s_pe", av_done.get(p - 2, 0))]
            if diag:
                v0 = 2 * (j - (npairs - 2))
                S.emit("vector",
                       lambda e, SP=SP, v0=v0: e.tensor_tensor(
                           SP[:, 0:1024], SP[:, 0:1024], m_bf[:, v0 * 512: v0 * 512 + 1024], ALU.add),
                       waits=[("s_pe", pairs_pe[p]), ("s_dve", v_mask)], inc="s_dve")
                wexp.append(("s_dve", S.v("s_dve")))
            S.emit("scalar",
                   lambda e, SP=SP, spar=spar: e.activation(
                       ebuf[:, spar * 1024:(spar + 1) * 1024], SP[:, 0:1024], AF.Exp,
                       bias=ncol[:, 0:1], scale=ISQ),
                   waits=wexp, inc="s_act")
            exp_done[p] = S.v("s_act")

        def emit_av(p):
            gi, b, qc, h, j, npairs = pairs[p]
            spar = p % 2
            kb0 = 2 * j
            first = (j == 0)
            last = (j == npairs - 1)
            for kk in range(2):
                kbg = b * 32 + kb0 + kk
                for qs in range(4):
                    slot = PV[qs // 2]
                    c0 = (qs % 2) * 512
                    w = []
                    if first and kk == 0 and qs == 0:
                        w = [("s_act", exp_done[p]), ("s_dve", ev_po["v"])]
                    elif kk == 0 and qs == 0:
                        w = [("s_act", exp_done[p])]
                    S.emit("tensor",
                           lambda e, slot=slot, c0=c0, spar=spar, kk=kk, qs=qs, kbg=kbg,
                                  first=first, last=last: e.matmul(
                               slot[:, c0:c0 + 129],
                               ebuf[:, spar * 1024 + kk * 512 + qs * 128: spar * 1024 + kk * 512 + qs * 128 + 128],
                               v_sb3[:, kbg, 0:129],
                               start=(first and kk == 0), stop=(last and kk == 1)),
                           waits=w,
                           inc=("s_pe" if (kk == 1 and qs == 3) else None))
            av_done[p] = S.v("s_pe")

        def emit_evac(p):
            gi, b, qc, h, j, npairs = pairs[p]
            opar = (b * 8 + qc) % 2
            osem = "d_ost" if opar == 0 else "d_ost1"
            w = [("s_pe", av_done[p]), (osem, st_ostg[opar]), ("s_dve", ev_po["v"])]
            for qs in range(4):
                slot = PV[qs // 2]
                c0 = (qs % 2) * 512
                S.emit("vector",
                       lambda e, slot=slot, c0=c0, qs=qs: e.reciprocal(
                           zc[:, qs:qs + 1], slot[:, c0 + 128:c0 + 129]),
                       waits=(w if qs == 0 else ()),
                       inc=("s_dve" if qs == 3 else None))
            vrec = S.v("s_dve")
            for qs in range(4):
                slot = PV[qs // 2]
                c0 = (qs % 2) * 512
                S.emit("vector",
                       lambda e, slot=slot, c0=c0, qs=qs, opar=opar, h=h: e.tensor_scalar(
                           ostg[:, opar * 2048 + qs * 512 + h * 128: opar * 2048 + qs * 512 + h * 128 + 128],
                           slot[:, c0:c0 + 128], zc[:, qs:qs + 1], None, ALU.mult),
                       waits=([("s_dve", vrec)] if qs == 0 else ()), inc="s_dve")
            ev_po["v"] = S.v("s_dve")
            evac_cnt[(b, qc)] = S.v("s_dve")

        pairs_pe = {}
        p = 0
        cur_g = -1
        emit_qload(0)
        emit_qload(1)
        for p in range(NPAIR):
            gi, b, qc, h, j, npairs = pairs[p]
            if j == npairs - 1 and gi + 2 < len(groups):
                emit_qload(gi + 2)
            if p == 0:
                emit_st(0)
            if p + 1 < NPAIR:
                emit_st(p + 1)
            emit_mask_exp(p)
            emit_av(p)
            if j == npairs - 1:
                emit_evac(p)
                if h == NHL - 1:
                    opar = (b * 8 + qc) % 2
                    t0 = b * SEQ + qc * 512
                    osem2 = "d_ost" if opar == 0 else "d_ost1"
                    S.dma("sync",
                          a2a_in.ap()[t0:t0 + 512, :].rearrange("(qs p) o -> p qs o", p=128),
                          ostg[:, opar * 2048:(opar + 1) * 2048].rearrange("p (qs o) -> p qs o", qs=4),
                          waits=[("s_dve", evac_cnt[(b, qc)])], sem=osem2)
                    st_ostg[opar] = S.v(osem2)

        a2a_ready = [("d_ost", S.v("d_ost")), ("d_ost1", S.v("d_ost1"))]
        if debug:
            S.dma("sync", a2a_in_d, a2a_in.ap(), waits=a2a_ready, sem="d_dbg")
        S.emit("gpsimd",
               lambda e: e.collective_compute("AllToAll", ALU.bypass,
                                              replica_groups=[list(range(R))],
                                              ins=[a2a_in.ap().opt()],
                                              outs=[a2a_out.ap().opt()]),
               waits=a2a_ready, inc="s_cc")
        cc_a2a = S.v("s_cc")

        ofm = A1[:, :].rearrange("p (h t) -> p h t", h=32)
        for hd_c in range(32):
            r_ = hd_c // 4
            lb = hd_c % 4
            S.dma("sync", ofm[:, hd_c, :],
                  a2a_out.ap()[r_ * 1024:(r_ + 1) * 1024, lb * 128:(lb + 1) * 128],
                  waits=[("s_cc", cc_a2a)], transpose=True, sem="d_tr")
        ld_ofm = S.v("d_tr")

        xo3 = [xo[k].rearrange("p (t o) -> p t o", t=8) for k in range(2)]
        pe_C = {}
        dve_C = {}
        xo_ld = {}
        st_x1 = {}
        for oc in range(8):
            par = oc % 2
            xcsem = "d_xc" if par == 0 else "d_xc1"
            if oc >= 2:
                S.dma("gpsimd", wo_buf[par],
                      woT[:, oc * 512:(oc + 1) * 512].rearrange("(h p) o -> p h o", p=128),
                      waits=[("s_pe", pe_C.get(oc - 2, 0))], sem="d_wo%d" % par)
                wo_ld[oc] = S.v("d_wo%d" % par)
            S.dma("sync", xo3[par][:, :, :],
                  x_own[:, oc * 512:(oc + 1) * 512].rearrange("(t p) o -> p t o", p=128),
                  waits=[("s_dve", dve_C.get(oc - 2, 0)), ("s_dve", scale_state["thr"])], sem=xcsem)
            xo_ld[oc] = S.v(xcsem)
            for tc in range(8):
                slot = [P0, P1, P2, P3][tc // 2]
                c0 = (tc % 2) * 512
                for hd_c in range(32):
                    w = []
                    if hd_c == 0:
                        w = [("d_tr", ld_ofm), ("d_wo%d" % par, wo_ld[oc]),
                             ("s_dve", dve_C.get(oc - 1, 0)),
                             ("s_act", scale_state["p2done"])]
                    S.emit("tensor",
                           lambda e, slot=slot, c0=c0, hd_c=hd_c, tc=tc, par=par: e.matmul(
                               slot[:, c0:c0 + 512], ofm[:, hd_c, tc * 128:(tc + 1) * 128],
                               wo_buf[par][:, hd_c, :],
                               start=(hd_c == 0), stop=(hd_c == 31)),
                           waits=w,
                           inc=("s_pe" if hd_c == 31 else None))
                pe_C[oc] = S.v("s_pe")
                ys = tc % 4
                x1sem = ["d_x1s", "d_x1s1", "d_x1s2", "d_x1s3"][ys]
                S.emit("vector",
                       lambda e, slot=slot, c0=c0, tc=tc, par=par, ys=ys: e.tensor_tensor(
                           ytile[:, ys * 512:(ys + 1) * 512], slot[:, c0:c0 + 512],
                           xo3[par][:, tc, :], ALU.add),
                       waits=[("s_pe", pe_C[oc]), (xcsem, xo_ld[oc]),
                              (x1sem,
                               st_x1.get((oc, tc - 4), st_x1.get((oc - 1, tc + 4), 0)))],
                       inc="s_dve")
                dve_C[oc] = S.v("s_dve")
                S.dma("sync", x1_dram[tc * 128:(tc + 1) * 128, oc * 512:(oc + 1) * 512],
                      ytile[:, ys * 512:(ys + 1) * 512],
                      waits=[("s_dve", dve_C[oc])], sem=x1sem)
                st_x1[(oc, tc)] = S.v(x1sem)
        x1_done = [("d_x1s", S.v("d_x1s")), ("d_x1s1", S.v("d_x1s1"))]

        x1n_t = [B2[:, 16384 + k * 4096:16384 + (k + 1) * 4096] for k in range(2)]
        st_x1n = {}
        act_D = {}
        for tc in range(8):
            par = tc % 2
            xdsem = "d_xd%d" % par
            ldv = S.dma("sync", xo[par][:, 0:4096], x1_dram[tc * 128:(tc + 1) * 128, :],
                        waits=x1_done + [act_D.get(tc - 2, ("s_dve", 0))], sem=xdsem)
            ldv = S.v(xdsem)
            S.emit("vector", lambda e, par=par: e.reduce_sum(r2c[:, 0:1], xo[par][:, 0:4096], axis=AX.X),
                   waits=[(xdsem, ldv), act_D.get(tc - 1, ("s_dve", 0))], inc="s_dve")
            S.emit("scalar", lambda e, par=par: e.activation(sq_bf, xo[par][:, 0:4096], AF.Square),
                   waits=[(xdsem, ldv), ("s_dve", S.v("s_dve"))], inc="s_act")
            S.emit("vector", lambda e: e.reduce_sum(r2c[:, 1:2], sq_bf, axis=AX.X),
                   waits=[("s_act", S.v("s_act"))], inc="s_dve")
            S.emit("vector", lambda e: e.tensor_scalar(r2c[:, 2:3], r2c[:, 0:1], -1.0 / DM, None, ALU.mult),
                   waits=[("s_dve", S.v("s_dve"))], inc="s_dve")
            S.emit("vector", lambda e: e.tensor_scalar(r2c[:, 3:4], r2c[:, 1:2], 1.0 / DM, None, ALU.mult),
                   waits=[("s_dve", S.v("s_dve"))], inc="s_dve")
            S.emit("vector", lambda e: e.tensor_tensor(r2c[:, 4:5], r2c[:, 2:3], r2c[:, 2:3], ALU.mult),
                   waits=[("s_dve", S.v("s_dve"))], inc="s_dve")
            S.emit("vector", lambda e: e.tensor_tensor(r2c[:, 5:6], r2c[:, 3:4], r2c[:, 4:5], ALU.subtract),
                   waits=[("s_dve", S.v("s_dve"))], inc="s_dve")
            v_d = S.v("s_dve")
            S.emit("scalar", lambda e: e.activation(r2c[:, 6:7], r2c[:, 5:6], AF.Ln, bias=epscol[:, 0:1]),
                   waits=[("s_dve", v_d)], inc="s_act")
            S.emit("scalar", lambda e: e.activation(r2c[:, 7:8], r2c[:, 6:7], AF.Exp, scale=-0.5),
                   waits=[("s_act", S.v("s_act"))], inc="s_act")
            v_r2 = S.v("s_act")
            x1nsem = "d_x1n" if par == 0 else "d_x1n1"
            S.emit("vector",
                   lambda e, par=par: e.tensor_scalar(x1n_t[par], xo[par][:, 0:4096],
                                                      r2c[:, 2:3], r2c[:, 7:8], ALU.add, ALU.mult),
                   waits=[("s_act", v_r2), (x1nsem, st_x1n.get(tc - 2, 0))], inc="s_dve")
            act_D[tc] = ("s_dve", S.v("s_dve"))
            S.dma("sync", x1n_dram[tc * 128:(tc + 1) * 128, :], x1n_t[par],
                  waits=[act_D[tc]], sem=x1nsem)
            st_x1n[tc] = S.v(x1nsem)
        x1n_done = [("d_x1n", S.v("d_x1n")), ("d_x1n1", S.v("d_x1n1"))]

        xn = A1[:, :].rearrange("p (i t) -> p i t", i=32)
        for ic in range(NIC):
            S.dma("sync", xn[:, ic, :], x1n_dram[0:TOWN, ic * 128:(ic + 1) * 128],
                  waits=x1n_done + [("s_pe", pe_C[7])], transpose=True, sem="d_tr")
        ld_xn = S.v("d_tr")

        qg = [B2[:, 0:4096], B2[:, 4096:8192]]
        qu = [B2[:, 8192:12288], B2[:, 12288:16384]]
        scr1 = B2[:, 16384:20480]
        scr2 = B2[:, 20480:24576]
        wgf = [F3[:, 0:4096], F3[:, 4096:8192]]
        wuf = [F3[:, 8192:12288], F3[:, 12288:16384]]
        pe_E = {}
        dve_E = {}
        act_E = {}
        ib_st = {}
        ew_ld = {}

        def emit_E_load(j):
            par = j % 2
            wsem = "d_w" if par == 0 else "d_w1"
            S.dma("sync", wgf[par], wg_tiles[j].rearrange("p i g -> p (i g)"),
                  waits=[("s_dve", dve_E.get(j - 2, 0))] + (x1n_done if j < 2 else []),
                  sem=wsem)
            S.dma("sync", wuf[par], wu_tiles[j].rearrange("p i g -> p (i g)"), sem=wsem)
            ew_ld[j] = (wsem, S.v(wsem))

        def emit_E_quant(j):
            par = j % 2
            wq2 = [ew_ld[j], ("s_dve", scale_state["thr"]),
                   ("s_pe", pe_E.get(j - 2, 0)), ("s_dve", dve_E.get(j - 1, 0))]
            S.emit("vector", lambda e, par=par: e.tensor_scalar(scr1, wgf[par], thrb[:, 0:1], None, ALU.is_gt),
                   waits=wq2)
            S.emit("vector", lambda e, par=par: e.tensor_scalar(scr2, wgf[par], thrb[:, 4:5], None, ALU.is_lt),
                   inc="s_dve")
            S.emit("vector", lambda e, par=par: e.tensor_tensor(qg[par], scr1, scr2, ALU.subtract),
                   waits=[("s_dve", S.v("s_dve"))], inc="s_dve")
            vqg = S.v("s_dve")
            S.emit("vector", lambda e, par=par: e.tensor_scalar(scr1, wuf[par], thrb[:, 1:2], None, ALU.is_gt),
                   waits=[ew_ld[j], ("s_dve", vqg)])
            S.emit("vector", lambda e, par=par: e.tensor_scalar(scr2, wuf[par], thrb[:, 5:6], None, ALU.is_lt),
                   inc="s_dve")
            S.emit("vector", lambda e, par=par: e.tensor_tensor(qu[par], scr1, scr2, ALU.subtract),
                   waits=[("s_dve", S.v("s_dve"))], inc="s_dve")
            dve_E[j] = S.v("s_dve")

        emit_E_load(0)
        emit_E_load(1)
        emit_E_quant(0)
        for j in range(NGC):
            par = j % 2
            if j + 2 < NGC:
                emit_E_load(j + 2)
            if j + 1 < NGC:
                emit_E_quant(j + 1)
            vq = dve_E[j]
            PG = PQ[par]
            PU = PV[par]
            for ic in range(NIC):
                w = []
                if ic == 0:
                    w = [("s_dve", vq), ("d_tr", ld_xn),
                         ("s_dve", dve_E.get(("i", j - 2), 0)),
                         ("s_act", act_E.get(j - 2, 0))]
                S.emit("tensor",
                       lambda e, PG=PG, par=par, ic=ic: e.matmul(
                           PG[:, 0:512], qg[par][:, ic * 128:(ic + 1) * 128],
                           xn[:, ic, 0:512], start=(ic == 0), stop=(ic == NIC - 1)),
                       waits=w)
                S.emit("tensor",
                       lambda e, PG=PG, par=par, ic=ic: e.matmul(
                           PG[:, 512:1024], qg[par][:, ic * 128:(ic + 1) * 128],
                           xn[:, ic, 512:1024], start=(ic == 0), stop=(ic == NIC - 1)))
            for ic in range(NIC):
                S.emit("tensor",
                       lambda e, PU=PU, par=par, ic=ic: e.matmul(
                           PU[:, 0:512], qu[par][:, ic * 128:(ic + 1) * 128],
                           xn[:, ic, 0:512], start=(ic == 0), stop=(ic == NIC - 1)))
                S.emit("tensor",
                       lambda e, PU=PU, par=par, ic=ic: e.matmul(
                           PU[:, 512:1024], qu[par][:, ic * 128:(ic + 1) * 128],
                           xn[:, ic, 512:1024], start=(ic == 0), stop=(ic == NIC - 1)),
                       inc=("s_pe" if ic == NIC - 1 else None))
            pe_E[j] = S.v("s_pe")
            S.emit("scalar", lambda e, PU=PU, par=par: e.copy(ubf[:, par * 1024:(par + 1) * 1024], PU[:, 0:1024]),
                   waits=[("s_pe", pe_E[j]), ("s_dve", dve_E.get(("i", j - 2), 0))], inc="s_act")
            act_E[j] = S.v("s_act")
            S.emit("vector",
                   lambda e, PG=PG, par=par: e.tensor_tensor(
                       ibf[:, par * 1024:(par + 1) * 1024], PG[:, 0:1024],
                       ubf[:, par * 1024:(par + 1) * 1024], ALU.mult),
                   waits=[("s_act", act_E[j]),
                          ("d_ib%d" % par, ib_st.get(j - 2, 0))],
                   inc="s_dve")
            dve_E[("i", j)] = S.v("s_dve")
            S.dma("scalar", inter_dram[j * 128:(j + 1) * 128, :], ibf[:, par * 1024:(par + 1) * 1024],
                  waits=[("s_dve", dve_E[("i", j)])], sem="d_ib%d" % par)
            ib_st[j] = S.v("d_ib%d" % par)
        ib_done = [("d_ib0", S.v("d_ib0")), ("d_ib1", S.v("d_ib1"))]
        pe_E_done = pe_E[NGC - 1]

        int_sb4 = [B2[:, 16384 + k * 4096:16384 + (k + 1) * 4096].rearrange("p (i t) -> p i t", i=4)
                   for k in range(4)]
        qd = [B2[:, 8192:10240], B2[:, 10240:12288]]
        fs1 = B2[:, 12288:14336]
        fs2 = B2[:, 14336:16384]
        wdf = [F3[:, 0:2048], F3[:, 2048:4096]]
        x1o = [F3[:, 8192:12288].rearrange("p (t o) -> p t o", t=8),
               F3[:, 12288:16384].rearrange("p (t o) -> p t o", t=8)]
        batches = [(i, min(i + 4, NGC)) for i in range(0, NGC, 4)]
        NB = len(batches)
        NGB = 8 * NB
        last_qd = [0]
        pe_gb = {}
        dve_q = {}
        dve_F = {}
        dve_ev = {}
        st_y = {}
        lw_v = {}
        li_v = {}
        x1_ld = {}
        dve_E_all = dve_E[NGC - 1]
        dve_E_int = dve_E[("i", NGC - 1)]
        act_E_all = act_E[NGC - 1]
        isems = ["d_i", "d_i1", "d_i2", "d_i3"]
        ysems = ["d_ys", "d_ys1", "d_ys2", "d_ys3"]

        def emit_F_load(gb):
            oc, bi = gb // NB, gb % NB
            ic0, ic1 = batches[bi]
            n = ic1 - ic0
            bpar = gb % 2
            sl4 = gb % 4
            wdsem = "d_wd" if bpar == 0 else "d_wd1"
            S.dma("sync", wdf[bpar][:, 0:n * 512].rearrange("p (i o) -> p i o", i=n),
                  wd_tiles[oc, ic0:ic1].rearrange("i p o -> p i o"),
                  waits=[("s_dve", dve_q.get(gb - 2, dve_E_all)),
                         ("s_pe", pe_gb.get(gb - 2, pe_E_done if gb < 2 else 0))],
                  sem=wdsem)
            lw_v[gb] = (wdsem, S.v(wdsem))
            S.dma("sync", int_sb4[sl4][:, 0:n, :],
                  inter_dram[ic0 * 128:ic1 * 128, :].rearrange("(i p) t -> p i t", p=128),
                  waits=ib_done + [("s_pe", pe_gb.get(gb - 4, 0)), ("s_dve", dve_E_all)],
                  sem=isems[sl4])
            li_v[gb] = (isems[sl4], S.v(isems[sl4]))

        def emit_F_quant(gb):
            bpar = gb % 2
            ic0, ic1 = batches[gb % NB]
            n = ic1 - ic0
            S.emit("vector", lambda e, bpar=bpar, n=n: e.tensor_scalar(
                       fs1[:, 0:n * 512], wdf[bpar][:, 0:n * 512], thrb[:, 2:3], None, ALU.is_gt),
                   waits=[lw_v[gb], ("s_dve", scale_state["thr"]),
                          ("s_dve", last_qd[0]),
                          ("s_pe", pe_gb.get(gb - 2, 0))])
            S.emit("vector", lambda e, bpar=bpar, n=n: e.tensor_scalar(
                       fs2[:, 0:n * 512], wdf[bpar][:, 0:n * 512], thrb[:, 6:7], None, ALU.is_lt),
                   inc="s_dve")
            S.emit("vector", lambda e, bpar=bpar, n=n: e.tensor_tensor(
                       qd[bpar][:, 0:n * 512], fs1[:, 0:n * 512], fs2[:, 0:n * 512], ALU.subtract),
                   waits=[("s_dve", S.v("s_dve"))], inc="s_dve")
            dve_q[gb] = S.v("s_dve")
            last_qd[0] = S.v("s_dve")

        def emit_F_evac(oc):
            opar = oc % 2
            xfsem = "d_xf" if opar == 0 else "d_xf1"
            for tc in range(8):
                slot = [P0, P1, P2, P3][tc // 2]
                c0 = (tc % 2) * 512
                ys = tc % 4
                yssem = ysems[ys]
                S.emit("vector",
                       lambda e, slot=slot, c0=c0, opar=opar, tc=tc, ys=ys: e.scalar_tensor_tensor(
                           ytile[:, ys * 512:(ys + 1) * 512], slot[:, c0:c0 + 512],
                           thrb[:, 3:4], x1o[opar][:, tc, :], ALU.mult, ALU.add),
                       waits=[("s_pe", pe_gb[oc * NB + NB - 1]), (xfsem, x1_ld[oc]),
                              (yssem,
                               st_y.get((oc, tc - 4), st_y.get((oc - 1, tc + 4), 0)))],
                       inc="s_dve")
                dve_ev[(oc, tc)] = S.v("s_dve")
                dve_F[oc] = S.v("s_dve")
                S.dma("sync", y[tc * 128:(tc + 1) * 128, oc * 512:(oc + 1) * 512],
                      ytile[:, ys * 512:(ys + 1) * 512],
                      waits=[("s_dve", dve_F[oc])], sem=yssem)
                st_y[(oc, tc)] = S.v(yssem)

        emit_F_load(0)
        emit_F_load(1)
        for gb in range(NGB):
            oc, bi = gb // NB, gb % NB
            if bi == 0:
                xfsem = "d_xf" if oc % 2 == 0 else "d_xf1"
                S.dma("scalar", x1o[oc % 2][:, :, :],
                      x1_dram[:, oc * 512:(oc + 1) * 512].rearrange("(t p) o -> p t o", p=128),
                      waits=[("s_dve", dve_F.get(oc - 2, 0))] + x1n_done +
                            [("s_dve", dve_E_all)], sem=xfsem)
                x1_ld[oc] = S.v(xfsem)
            emit_F_quant(gb)
            if gb + 2 < NGB:
                emit_F_load(gb + 2)
            if bi == 0 and oc > 0:
                emit_F_evac(oc - 1)
            ic0, ic1 = batches[bi]
            n = ic1 - ic0
            bpar = gb % 2
            sl4 = gb % 4
            for i in range(n):
                icg = ic0 + i
                for tc in range(8):
                    slot = [P0, P1, P2, P3][tc // 2]
                    c0 = (tc % 2) * 512
                    w = []
                    if i == 0 and tc == 0:
                        w = [("s_dve", dve_q[gb]), li_v[gb]]
                        if gb == 0:
                            w.append(("s_dve", dve_E_int))
                            w.append(("s_act", act_E_all))
                    if i == 0 and bi == 0 and oc > 0:
                        w = w + [("s_dve", dve_ev[(oc - 1, tc)])]
                    S.emit("tensor",
                           lambda e, slot=slot, c0=c0, sl4=sl4, bpar=bpar, i=i, tc=tc, icg=icg: e.matmul(
                               slot[:, c0:c0 + 512], int_sb4[sl4][:, i, tc * 128:(tc + 1) * 128],
                               qd[bpar][:, i * 512:(i + 1) * 512],
                               start=(icg == 0), stop=(icg == NGC - 1)),
                           waits=w,
                           inc=("s_pe" if (i == n - 1 and tc == 7) else None))
            pe_gb[gb] = S.v("s_pe")
        emit_F_evac(7)

        S.emit("sync", lambda e: None,
               waits=[(nm, S.v(nm)) for nm in sems
                      if nm.startswith("d") and S.v(nm) > 0])

        with allow_ici_all_to_all():
            with nc.Block() as block:
                S.run(block, handles)
    return nc


_NC_CACHE = {}


def _get_nc(debug=False):
    key = bool(debug)
    if key not in _NC_CACHE:
        _NC_CACHE[key] = build_nc(debug=key)
    return _NC_CACHE[key]


def make_in_maps(inputs):
    x = np.asarray(inputs["L0_attn_hidden_states"], dtype=np.float32)
    Wq = np.asarray(inputs["L0_attn_Wq"], dtype=np.float32)
    Wk = np.asarray(inputs["L0_attn_Wk"], dtype=np.float32)
    Wv = np.asarray(inputs["L0_attn_Wv"], dtype=np.float32)
    Wo = np.asarray(inputs["L0_attn_Wo"], dtype=np.float32)
    wg = np.asarray(inputs["L0_ffn_w_gate"], dtype=np.float32)
    wu = np.asarray(inputs["L0_ffn_w_up"], dtype=np.float32)
    wd = np.asarray(inputs["L0_ffn_w_down"], dtype=np.float32)

    xr = x.reshape(NT, DM)
    x_fm = np.ascontiguousarray(xr.T)
    woT = np.ascontiguousarray(Wo.T)
    wg_tiles = np.ascontiguousarray(wg.reshape(NGC, 128, NIC, 128).transpose(0, 3, 2, 1))
    wu_tiles = np.ascontiguousarray(wu.reshape(NGC, 128, NIC, 128).transpose(0, 3, 2, 1))
    wd_tiles = np.ascontiguousarray(wd.reshape(8, 512, NGC, 128).transpose(0, 2, 3, 1))
    wg_flat = wg_tiles.reshape(-1)
    wu_flat = wu_tiles.reshape(-1)
    wd_flat = wd_tiles.reshape(-1)
    nsl = wg_flat.size // R
    cos, sinr = rope_tables()
    masks = causal_masks()
    ident = np.eye(128, dtype=np.float32)

    in_maps = []
    for c in range(R):
        wqkv_c = np.ascontiguousarray(np.concatenate(
            [Wq[512 * c:512 * (c + 1), :].T, Wk[128 * c:128 * (c + 1), :].T,
             Wv[128 * c:128 * (c + 1), :].T], axis=1))
        in_maps.append({
            "x_fm": x_fm,
            "x_own": np.ascontiguousarray(xr[TOWN * c:TOWN * (c + 1), :]),
            "wqkv": wqkv_c,
            "woT": woT,
            "wg_tiles": wg_tiles,
            "wu_tiles": wu_tiles,
            "wd_tiles": wd_tiles,
            "wg_s": np.ascontiguousarray(wg_flat[nsl * c:nsl * (c + 1)].reshape(128, 44032)),
            "wu_s": np.ascontiguousarray(wu_flat[nsl * c:nsl * (c + 1)].reshape(128, 44032)),
            "wd_s": np.ascontiguousarray(wd_flat[nsl * c:nsl * (c + 1)].reshape(128, 44032)),
            "cosT": cos,
            "sinrT": sinr,
            "maskT": masks,
            "idf": ident,
        })
    return in_maps


def kernel(**inputs):
    nc = _get_nc(debug=False)
    in_maps = make_in_maps(inputs)
    res = run_bass_kernel_spmd(nc, in_maps, core_ids=list(range(R)))
    out = np.empty((NT, DM), dtype=np.float32)
    for c in range(R):
        out[TOWN * c:TOWN * (c + 1), :] = res.results[c]["y"]
    return out.reshape(NBATCH, SEQ, DM)


# version 14
# speedup vs baseline: 1.4405x; 1.0656x over previous
import math
from contextlib import ExitStack

import numpy as np
import ml_dtypes

import concourse.bass as bass
import concourse.mybir as mybir
from concourse.bass_utils import run_bass_kernel_spmd
from concourse.aot_env import allow_ici_all_to_all

F32 = mybir.dt.float32
BF16 = mybir.dt.bfloat16
AF = mybir.ActivationFunctionType
ALU = mybir.AluOpType
AX = mybir.AxisListType
BF = ml_dtypes.bfloat16

R = 8
SEQ = 4096
NBATCH = 2
NT = NBATCH * SEQ
DM = 4096
DI = 11008
NHL = 4
TOWN = NT // R
NTC = NT // 128
NIC = DM // 128
NGC = DI // 128
EPS_LN = 1e-5
EPS_Q = 1e-5
EXP_C = 20.0
ISQ = 1.0 / math.sqrt(128.0)
NEG_BIG = -1.0e30
NW = float(DI * DM)

ENGS = ("tensor", "vector", "scalar", "gpsimd", "sync")


class Sched:

    def __init__(self, sem_names):
        self.cnt = {n: 0 for n in sem_names}
        self.ops = {e: [] for e in ENGS}

    def v(self, sem):
        return self.cnt[sem]

    def emit(self, eng, fn, waits=(), inc=None, n=1):
        waits = tuple((s, v) for (s, v) in waits if v > 0)
        if inc is not None:
            self.cnt[inc] += n
        self.ops[eng].append((fn, waits, inc, n))
        return self.cnt[inc] if inc is not None else None

    def dma(self, eng, out, in_, waits=(), transpose=False, sem=None):
        def fn(e, out=out, in_=in_, transpose=transpose):
            if transpose:
                return e.dma_start(out=out, in_=in_, transpose=True)
            return e.dma_start(out=out, in_=in_)
        return self.emit(eng, fn, waits, sem or ("dq_" + eng), 16)

    def run(self, block, handles):
        for eng in ENGS:
            ops = self.ops[eng]
            def make(eng_ops):
                def body(e):
                    seen = {}
                    for fn, waits, inc, n in eng_ops:
                        for s, v in waits:
                            if seen.get(s, 0) < v:
                                e.wait_ge(handles[s], v)
                                seen[s] = v
                        inst = fn(e)
                        if inc is not None:
                            inst.then_inc(handles[inc], n)
                return body
            getattr(block, eng)(make(ops))


def rope_tables():
    f32 = np.float32
    d = np.arange(0, 128, 2, dtype=f32)
    inv_freq = (f32(1.0) / (f32(500000.0) ** (d / f32(128.0)))).astype(f32)
    wavelen = f32(2.0 * math.pi) / inv_freq
    smooth = (f32(8192.0) / wavelen - f32(1.0)) / f32(3.0)
    smooth = np.clip(smooth, f32(0.0), f32(1.0)).astype(f32)
    new_inv = ((f32(1.0) - smooth) * (inv_freq / f32(8.0)) + smooth * inv_freq).astype(f32)
    t = np.arange(SEQ, dtype=f32)
    freqs = np.outer(t, new_inv).astype(f32)
    emb = np.concatenate([freqs, freqs], axis=-1)
    cos = np.cos(emb).astype(f32)
    sin = np.sin(emb).astype(f32)
    sinr = np.concatenate([-sin[:, :64], sin[:, 64:]], axis=-1).astype(f32)
    return cos, sinr


def causal_masks():
    m = np.zeros((128, 4, 512), dtype=np.float32)
    k = np.arange(128)[:, None]
    q = np.arange(512)[None, :]
    for v in range(4):
        m[:, v, :] = np.where(128 * v + k <= q, 0.0, NEG_BIG)
    return m.reshape(128, 2048)


def build_nc(debug=False):
    nc = bass.Bass("TRN2", target_bir_lowering=False)
    dt = nc.dram_tensor

    x_pm = dt("x_pm", [NTC, 128, DM], F32, kind="ExternalInput").ap()
    x_own = dt("x_own", [TOWN, DM], F32, kind="ExternalInput").ap()
    wqkv = dt("wqkv", [DM, 768], F32, kind="ExternalInput").ap()
    woT = dt("woT", [DM, DM], F32, kind="ExternalInput").ap()
    wg_tiles = dt("wg_tiles", [NGC, 128, NIC, 128], F32, kind="ExternalInput").ap()
    wu_tiles = dt("wu_tiles", [NGC, 128, NIC, 128], F32, kind="ExternalInput").ap()
    wd_tiles = dt("wd_tiles", [8, NGC, 128, 512], F32, kind="ExternalInput").ap()
    wg_s = dt("wg_s", [128, 44032], F32, kind="ExternalInput").ap()
    wu_s = dt("wu_s", [128, 44032], F32, kind="ExternalInput").ap()
    wd_s = dt("wd_s", [128, 44032], F32, kind="ExternalInput").ap()
    cosT = dt("cosT", [SEQ, 128], F32, kind="ExternalInput").ap()
    sinrT = dt("sinrT", [SEQ, 128], F32, kind="ExternalInput").ap()
    maskT = dt("maskT", [128, 2048], F32, kind="ExternalInput").ap()
    idf = dt("idf", [128, 128], F32, kind="ExternalInput").ap()
    y = dt("y", [TOWN, DM], F32, kind="ExternalOutput").ap()

    dbg = "ExternalOutput" if debug else "Internal"
    qkv_spill = dt("qkv_spill", [NT, 768], BF16, kind=dbg).ap()
    a2a_in = dt("a2a_in", [NT, 512], BF16, kind="Internal")
    a2a_out = dt("a2a_out", [NT, 512], BF16, kind="Internal")
    stats_in = dt("stats_in", [TOWN, 2], F32, kind="Internal")
    stats_ag = dt("stats_ag", [NT, 2], F32, kind="Internal")
    scales_in = dt("scales_in", [4, 1], F32, kind="Internal")
    scales_out = dt("scales_out", [32, 1], F32, kind="Internal")
    if debug:
        absacc_d = dt("absacc_d", [128, 4], F32, kind="ExternalOutput").ap()
        slabd_d = dt("slabd_d", [128, 36], F32, kind="ExternalOutput").ap()
        stat8_d = dt("stat8_d", [128, 32], F32, kind="ExternalOutput").ap()
        stats_ag_d = dt("stats_ag_d", [NT, 2], F32, kind="ExternalOutput").ap()
        scales_d = dt("scales_d", [1, 8], F32, kind="ExternalOutput").ap()
        a2a_in_d = dt("a2a_in_d", [NT, 512], BF16, kind="ExternalOutput").ap()
    nrow_dram = dt("nrow_dram", [64, 128], BF16, kind="Internal").ap()
    thr_dram = dt("thr_dram", [1, 4], F32, kind="Internal").ap()
    x1_dram = dt("x1_dram", [TOWN, DM], F32, kind=dbg).ap()
    x1n_dram = dt("x1n_dram", [TOWN, DM], BF16, kind=dbg).ap()
    inter_dram = dt("inter_dram", [DI, TOWN], BF16, kind=dbg).ap()

    with ExitStack() as ctx:
        ecx = ctx.enter_context

        def sbuf(name, shape, dtype):
            return ecx(nc.sbuf_tensor(name, shape, dtype))

        A1 = sbuf("A1", [128, 32768], BF16)
        B2 = sbuf("B2", [128, 32768], BF16)
        F3 = sbuf("F3", [128, 16384], F32)
        outb = sbuf("outb", [128, 2 * 768], BF16)
        stag = sbuf("stag", [128, 128], F32)
        stat8 = sbuf("stat8", [128, 32], F32)
        nchunk = sbuf("nchunk", [1, 2 * 128], BF16)
        w1row = sbuf("w1row", [1, 768], BF16)
        onesb = sbuf("onesb", [128, 1], BF16)
        onesf = sbuf("onesf", [128, 1], F32)
        ncol = sbuf("ncol", [128, 1], F32)
        epscol = sbuf("epscol", [128, 1], F32)
        thrb = sbuf("thrb", [128, 8], F32)
        slabd = sbuf("slabd", [128, 36], F32)
        scf = sbuf("scf", [128, 256], F32)
        scr32 = sbuf("scr32", [1, 32], F32)
        absacc = sbuf("absacc", [128, 4], F32)
        scw = sbuf("scw", [1, 8], F32)
        zc = sbuf("zc", [128, 4], F32)
        r2c = sbuf("r2c", [128, 8], F32)
        n64 = sbuf("n64", [64, 128], BF16)

        P0 = ecx(nc.psum_tensor("P0", [128, 1024], F32))
        P1 = ecx(nc.psum_tensor("P1", [128, 1024], F32))
        P2 = ecx(nc.psum_tensor("P2", [128, 1024], F32))
        P3 = ecx(nc.psum_tensor("P3", [128, 1024], F32))
        PQ = [P0, P1]
        PV = [P2, P3]

        sems = ["dq_sync", "dq_gpsimd", "dq_scalar", "s_pe", "s_dve", "s_act", "s_cc",
                "d_q", "d_q1", "d_tr", "d_ost", "d_ost1", "d_n", "d_sp", "d_sp1",
                "d_xo", "d_xd0", "d_xd1", "d_x1s", "d_x1s1", "d_w", "d_w1",
                "d_i", "d_i1", "d_dbg", "d_xc", "d_xc1", "d_xf", "d_xf1",
                "d_x1n", "d_x1n1", "d_ys", "d_ys1", "d_wd", "d_wd1",
                "d_xb0", "d_xb1", "d_xb2", "d_xb3", "d_sc0", "d_sc1",
                "d_wo0", "d_wo1", "d_ib0", "d_ib1", "d_msk", "d_thr",
                "d_i2", "d_i3", "d_x1s2", "d_x1s3", "d_ys2", "d_ys3"]
        handles = {n: ecx(nc.semaphore(n)) for n in sems}
        S = Sched(sems)

        wq_v = B2[:, 0:24576].rearrange("p (i o) -> p i o", i=NIC)
        xb = [B2[:, 24576 + k * 4096:24576 + (k + 1) * 4096].rearrange("p (i t) -> p i t", i=NIC)
              for k in range(2)] + \
             [A1[:, k * 4096:(k + 1) * 4096].rearrange("p (i t) -> p i t", i=NIC)
              for k in range(2)]
        nall = A1[0:1, 8192:16384]
        cos_v = F3[:, 0:4096].rearrange("p (i d) -> p i d", i=32)
        sinr_v = F3[:, 4096:8192].rearrange("p (i d) -> p i d", i=32)
        xo = [F3[:, 8192:12288], F3[:, 12288:16384]]
        scbuf = F3[:, 12288:16384]
        idf_v = F3[:, 8192:8320]
        rps = F3[:, 8320:9600]
        ytile = F3[:, 4096:6144]
        ubf = B2[:, 24576:26624]
        ibf = B2[:, 26624:28672]
        sq_bf = B2[:, 0:4096]

        S.emit("vector", lambda e: e.memset(onesb[:, :], 1.0), inc="s_dve")
        S.emit("vector", lambda e: e.memset(onesf[:, :], 1.0), inc="s_dve")
        S.emit("vector", lambda e: e.memset(ncol[:, :], -EXP_C), inc="s_dve")
        S.emit("vector", lambda e: e.memset(epscol[:, :], EPS_LN), inc="s_dve")
        S.emit("vector", lambda e: e.memset(absacc[:, :], 0.0), inc="s_dve")
        S.emit("vector", lambda e: e.memset(scw[0:1, :], 0.0), inc="s_dve")
        v_consts = S.v("s_dve")
        S.dma("sync", idf_v, idf)
        ld_idf = S.v("dq_sync")

        xld = {}
        for tc in range(4):
            S.dma("gpsimd", xb[tc], x_pm[tc].rearrange("p (i t) -> p i t", i=NIC),
                  sem="d_xb%d" % tc)
            xld[tc] = S.v("d_xb%d" % tc)

        last_sq = 0
        for i in range(8):
            buf = xo[1]
            w = [("s_dve", S.v("s_dve")), ("s_act", S.v("s_act"))] if i >= 1 else []
            ldv = S.dma("sync", buf[:, 0:4096], x_own[i * 128:(i + 1) * 128, :], waits=w, sem="d_xo")
            S.emit("vector",
                   lambda e, buf=buf, i=i: e.reduce_sum(stat8[:, i:i + 1], buf[:, 0:4096], axis=AX.X),
                   waits=[("d_xo", ldv)], inc="s_dve")
            S.emit("scalar",
                   lambda e, buf=buf: e.activation(sq_bf, buf[:, 0:4096], AF.Square),
                   waits=[("d_xo", ldv), ("s_dve", last_sq)], inc="s_act")
            S.emit("vector",
                   lambda e, i=i: e.reduce_sum(stat8[:, 8 + i:9 + i], sq_bf, axis=AX.X),
                   waits=[("s_act", S.v("s_act"))], inc="s_dve")
            last_sq = S.v("s_dve")
        scr_free = [("s_dve", S.v("s_dve")), ("s_act", S.v("s_act"))]
        nr_v = stat8[:, 16:32].rearrange("p (i c) -> p i c", c=2)
        S.emit("vector", lambda e: e.tensor_scalar(nr_v[:, :, 0], stat8[:, 0:8], -1.0 / DM, None, ALU.mult),
               waits=[("s_dve", S.v("s_dve"))], inc="s_dve")
        S.emit("vector", lambda e: e.tensor_scalar(stat8[:, 8:16], stat8[:, 8:16], 1.0 / DM, None, ALU.mult),
               waits=[("s_dve", S.v("s_dve"))], inc="s_dve")
        S.emit("vector", lambda e: e.tensor_tensor(stat8[:, 0:8], nr_v[:, :, 0], nr_v[:, :, 0], ALU.mult),
               waits=[("s_dve", S.v("s_dve"))], inc="s_dve")
        S.emit("vector", lambda e: e.tensor_tensor(stat8[:, 8:16], stat8[:, 8:16], stat8[:, 0:8], ALU.subtract),
               waits=[("s_dve", S.v("s_dve"))], inc="s_dve")
        v_var = S.v("s_dve")
        S.emit("scalar", lambda e: e.activation(stat8[:, 0:8], stat8[:, 8:16], AF.Ln, bias=epscol[:, 0:1]),
               waits=[("s_dve", v_var)], inc="s_act")
        S.emit("scalar", lambda e: e.activation(stat8[:, 8:16], stat8[:, 0:8], AF.Exp, scale=-0.5),
               waits=[("s_act", S.v("s_act"))], inc="s_act")
        v_stats0 = S.v("s_act")
        S.emit("vector", lambda e: e.tensor_scalar(nr_v[:, :, 1], stat8[:, 8:16], 1.0, None, ALU.mult),
               waits=[("s_act", v_stats0)], inc="s_dve")
        v_stats = ("s_dve", S.v("s_dve"))
        S.dma("sync", stats_in.ap().rearrange("(i p) c -> p i c", p=128),
              stat8[:, 16:32].rearrange("p (i c) -> p i c", c=2),
              waits=[v_stats])
        st_stats = S.v("dq_sync")
        S.emit("gpsimd",
               lambda e: e.collective_compute("AllGather", ALU.bypass,
                                              replica_groups=[list(range(R))],
                                              ins=[stats_in.ap().opt()],
                                              outs=[stats_ag.ap().opt()]),
               waits=[("dq_sync", st_stats)], inc="s_cc")
        cc_stats = S.v("s_cc")
        S.dma("sync", stag[:, :].rearrange("p (i c) -> p i c", c=2),
              stats_ag.ap().rearrange("(i p) c -> p i c", p=128),
              waits=[("s_cc", cc_stats)])
        ld_stag = S.v("dq_sync")
        if debug:
            S.dma("sync", stats_ag_d, stats_ag.ap(), waits=[("s_cc", cc_stats)], sem="d_dbg")
            S.dma("sync", stat8_d, stat8[:, :], waits=[("s_cc", cc_stats)], sem="d_dbg")
        stag_v = stag[:, :].rearrange("p (i c) -> p i c", c=2)
        S.emit("tensor",
               lambda e: e.transpose(P0[0:64, 0:128], stag_v[:, :, 0], idf_v),
               waits=[("dq_sync", ld_stag), ("dq_sync", ld_idf)], inc="s_pe")
        S.emit("vector", lambda e: e.tensor_copy(n64[:, :], P0[0:64, 0:128]),
               waits=[("s_pe", S.v("s_pe"))], inc="s_dve")
        v_n64 = S.v("s_dve")
        S.dma("sync", nrow_dram, n64[:, :], waits=[("s_dve", v_n64)], sem="d_n")
        st_nrow = S.v("d_n")
        S.dma("sync", nall, nrow_dram.rearrange("(o a) b -> o (a b)", o=1),
              waits=[("d_n", st_nrow)], sem="d_n")
        ld_nall = S.v("d_n")

        S.dma("gpsimd", wq_v, wqkv.rearrange("(i p) o -> p i o", p=128),
              waits=[v_stats])
        ld_wqkv = S.v("dq_gpsimd")

        for ic in range(NIC):
            w = [("dq_gpsimd", ld_wqkv), ("s_dve", v_consts)] if ic == 0 else []
            S.emit("tensor",
                   lambda e, ic=ic: e.matmul(P1[0:1, 0:512], onesb[:, :], wq_v[:, ic, 0:512],
                                             start=(ic == 0), stop=(ic == NIC - 1)),
                   waits=w)
            S.emit("tensor",
                   lambda e, ic=ic: e.matmul(P1[0:1, 512:768], onesb[:, :], wq_v[:, ic, 512:768],
                                             start=(ic == 0), stop=(ic == NIC - 1)),
                   inc=("s_pe" if ic == NIC - 1 else None))
        v_w1mm = S.v("s_pe")
        S.emit("scalar", lambda e: e.copy(w1row[0:1, :], P1[0:1, 0:768]),
               waits=[("s_pe", v_w1mm)], inc="s_act")
        v_w1 = S.v("s_act")

        S.dma("sync", cos_v, cosT.rearrange("(i p) d -> p i d", p=128))
        S.dma("sync", sinr_v, sinrT.rearrange("(i p) d -> p i d", p=128))
        ld_tabs = S.v("dq_sync")

        slabs = []
        for wi, wsrc in enumerate((wg_s, wu_s, wd_s)):
            for c0 in range(0, 44032, 2048):
                slabs.append((wi, wsrc, c0, min(c0 + 2048, 44032)))
        scbufs = [scbuf[:, 0:2048], scbuf[:, 2048:4096]]
        sl_dve = {}

        def emit_scale_slab(k):
            wi, wsrc, c0, c1 = slabs[k]
            scbuf_k = scbufs[k % 2]
            scsem = "d_sc%d" % (k % 2)
            ldv = S.dma("scalar", scbuf_k[:, 0:c1 - c0], wsrc[:, c0:c1],
                        waits=(scr_free if k < 2 else [("s_dve", sl_dve.get(k - 2, 0))]),
                        sem=scsem)
            ldv = S.v(scsem)
            for a in range(0, c1 - c0, 512):
                S.emit("vector",
                       lambda e, a=a, scbuf_k=scbuf_k: e.reduce_sum(
                           absacc[:, 3:4], scbuf_k[:, a:a + 512],
                           axis=AX.X, apply_absolute_value=True),
                       waits=([(scsem, ldv)] if a == 0 else [("s_dve", S.v("s_dve"))]),
                       inc="s_dve")
                S.emit("vector",
                       lambda e, wi=wi: e.tensor_tensor(absacc[:, wi:wi + 1], absacc[:, wi:wi + 1],
                                                        absacc[:, 3:4], ALU.add),
                       waits=[("s_dve", S.v("s_dve"))], inc="s_dve")
            sl_dve[k] = S.v("s_dve")

        scale_state = {}

        def emit_scale_finish():
            S.emit("tensor", lambda e: e.transpose(P2[0:4, 0:128], absacc[:, 0:4], idf_v),
                   waits=[("s_dve", S.v("s_dve")), ("dq_sync", ld_idf)], inc="s_pe")
            S.emit("scalar", lambda e: e.copy(scf[0:4, 0:128], P2[0:4, 0:128]),
                   waits=[("s_pe", S.v("s_pe"))], inc="s_act")
            scale_state["p2done"] = S.v("s_act")
            base = 0
            n = 128
            w = [("s_act", S.v("s_act"))]
            while n > 1:
                h = n // 2
                src = scf[0:4, base:base + n].rearrange("p (a b) -> p a b", b=2)
                S.emit("vector",
                       lambda e, src=src, base=base, n=n, h=h: e.tensor_tensor(
                           scf[0:4, base + n:base + n + h], src[:, :, 0], src[:, :, 1], ALU.add),
                       waits=w + [("s_dve", S.v("s_dve"))], inc="s_dve")
                w = []
                base += n
                n = h
            S.dma("sync", scales_in.ap(), scf[0:4, 254:255], waits=[("s_dve", S.v("s_dve"))],
                  sem="d_thr")
            S.emit("gpsimd",
                   lambda e: e.collective_compute("AllGather", ALU.bypass,
                                                  replica_groups=[list(range(R))],
                                                  ins=[scales_in.ap().opt()],
                                                  outs=[scales_out.ap().opt()]),
                   waits=[("d_thr", S.v("d_thr"))], inc="s_cc")
            S.dma("sync", scr32[0:1, 0:32],
                  scales_out.ap().rearrange("(o a) b -> o (a b)", o=1),
                  waits=[("s_cc", S.v("s_cc"))], sem="d_thr")
            ldt = S.v("d_thr")
            S.emit("vector", lambda e: e.tensor_copy(scw[0:1, 0:3], scr32[0:1, 20:23]),
                   waits=[("d_thr", ldt)], inc="s_dve")
            for j in range(1, 8):
                r = (5 + j) % 8
                S.emit("vector",
                       lambda e, r=r: e.tensor_tensor(scw[0:1, 0:3], scw[0:1, 0:3],
                                                      scr32[0:1, 4 * r:4 * r + 3], ALU.add),
                       waits=[("s_dve", S.v("s_dve"))], inc="s_dve")
            if debug:
                S.emit("vector", lambda e: e.tensor_copy(scw[0:1, 4:7], scw[0:1, 0:3]),
                       waits=[("s_dve", S.v("s_dve"))], inc="s_dve")
                S.dma("sync", scales_d, scw[0:1, 0:8], waits=[("s_dve", S.v("s_dve"))], sem="d_dbg")
            S.emit("vector", lambda e: e.tensor_scalar(scw[0:1, 0:3], scw[0:1, 0:3],
                                                       float(np.float32(1.0) / np.float32(NW)), None, ALU.mult),
                   waits=[("s_dve", S.v("s_dve"))], inc="s_dve")
            S.emit("vector", lambda e: e.tensor_scalar(scw[0:1, 4:7], scw[0:1, 0:3], EPS_Q, 0.5, ALU.add, ALU.mult),
                   waits=[("s_dve", S.v("s_dve"))], inc="s_dve")
            S.emit("vector", lambda e: e.tensor_tensor(scw[0:1, 3:4], scw[0:1, 0:1], scw[0:1, 1:2], ALU.mult),
                   waits=[("s_dve", S.v("s_dve"))], inc="s_dve")
            S.emit("vector", lambda e: e.tensor_tensor(scw[0:1, 7:8], scw[0:1, 3:4], scw[0:1, 2:3], ALU.mult),
                   waits=[("s_dve", S.v("s_dve"))], inc="s_dve")
            S.dma("sync", thr_dram[0:1, 0:4], scw[0:1, 4:8], waits=[("s_dve", S.v("s_dve"))],
                  sem="d_thr")
            S.dma("sync", thrb[:, 0:4], thr_dram[0:1, 0:4].partition_broadcast(128),
                  waits=[("d_thr", S.v("d_thr"))], sem="d_thr")
            S.emit("vector", lambda e: e.tensor_scalar(thrb[:, 4:8], thrb[:, 0:4], -1.0, None, ALU.mult),
                   waits=[("d_thr", S.v("d_thr"))], inc="s_dve")
            scale_state["thr"] = S.v("s_dve")

        pe_tc = {}
        dve_rope = {}
        spill_v = {}
        n_scale = 0
        for tc in range(NTC):
            par = tc % 2
            b4 = tc % 4
            P = PQ[par]
            if tc >= 4:
                S.dma("gpsimd", xb[b4], x_pm[tc].rearrange("p (i t) -> p i t", i=NIC),
                      waits=[("s_pe", pe_tc.get(tc - 4, 0))], sem="d_xb%d" % b4)
                xld[tc] = S.v("d_xb%d" % b4)
            for ic in range(NIC):
                w = []
                if ic == 0:
                    w = [("d_xb%d" % b4, xld[tc]), ("d_n", ld_nall), ("s_act", v_w1),
                         ("s_dve", dve_rope.get(tc - 2, 0))]
                S.emit("tensor",
                       lambda e, ic=ic, P=P, b4=b4: e.matmul(
                           P[:, 0:512], xb[b4][:, ic, :], wq_v[:, ic, 0:512],
                           start=(ic == 0), stop=False),
                       waits=w)
                S.emit("tensor",
                       lambda e, ic=ic, P=P, b4=b4: e.matmul(
                           P[:, 512:768], xb[b4][:, ic, :], wq_v[:, ic, 512:768],
                           start=(ic == 0), stop=False))
            S.emit("tensor",
                   lambda e, P=P, tc=tc: e.matmul(P[:, 0:512], nall[0:1, tc * 128:(tc + 1) * 128],
                                                  w1row[0:1, 0:512], start=False, stop=True))
            S.emit("tensor",
                   lambda e, P=P, tc=tc: e.matmul(P[:, 512:768], nall[0:1, tc * 128:(tc + 1) * 128],
                                                  w1row[0:1, 512:768], start=False, stop=True),
                   inc="s_pe")
            pe_tc[tc] = S.v("s_pe")

            ci = tc % 32
            rA = rps[:, 0:640]
            rB = rps[:, 640:1280]
            spsem = "d_sp" if par == 0 else "d_sp1"
            wv = [("s_pe", pe_tc[tc]), ("dq_sync", ld_tabs), ("dq_sync", ld_stag),
                  (spsem, spill_v.get(tc - 2, 0)), ("s_dve", dve_rope.get(tc - 1, 0))]
            for blk in range(5):
                c0 = blk * 128
                S.emit("vector",
                       lambda e, P=P, c0=c0, ci=ci: e.tensor_tensor(
                           rps[:, c0:c0 + 128], P[:, c0:c0 + 128], cos_v[:, ci, :], ALU.mult),
                       waits=(wv if blk == 0 else ()))
                S.emit("vector",
                       lambda e, P=P, c0=c0, ci=ci: e.tensor_tensor(
                           rps[:, 640 + c0:640 + c0 + 128],
                           P[:, c0:c0 + 128].rearrange("p (g c) -> p g c", g=2)[:, ::-1, :],
                           sinr_v[:, ci, :].rearrange("p (g c) -> p g c", g=2),
                           ALU.mult),
                       inc=("s_dve" if blk == 4 else None))
            S.emit("vector", lambda e: e.tensor_tensor(rA, rA, rB, ALU.add),
                   waits=[("s_dve", S.v("s_dve"))], inc="s_dve")
            S.emit("vector",
                   lambda e, tc=tc, par=par: e.tensor_scalar(
                       outb[:, par * 768:par * 768 + 640], rA,
                       stag_v[:, tc, 1:2], None, ALU.mult),
                   waits=[("s_dve", S.v("s_dve"))])
            S.emit("vector",
                   lambda e, tc=tc, par=par, P=P: e.tensor_scalar(
                       outb[:, par * 768 + 640:par * 768 + 768], P[:, 640:768],
                       stag_v[:, tc, 1:2], None, ALU.mult),
                   inc="s_dve")
            dve_rope[tc] = S.v("s_dve")
            S.dma("sync", qkv_spill[tc * 128:(tc + 1) * 128, :], outb[:, par * 768:(par + 1) * 768],
                  waits=[("s_dve", dve_rope[tc])], sem=spsem)
            spill_v[tc] = S.v(spsem)
            if n_scale < len(slabs):
                emit_scale_slab(n_scale)
                n_scale += 1
        while n_scale < len(slabs):
            emit_scale_slab(n_scale)
            n_scale += 1
        emit_scale_finish()
        spill_done = [("d_sp", spill_v[NTC - 2]), ("d_sp1", spill_v[NTC - 1])]
        pe_A_done = pe_tc[NTC - 1]

        k_sb = A1[:, 0:8192]
        v_sb = A1[:, 8192:16640]
        v_sb3 = A1[:, 8192:16640].rearrange("p (g c) -> p g c", g=64)
        m_bf = A1[:, 16640:18688]
        qsl = A1[:, 18688:19712]
        ebuf = A1[:, 19712:21760]
        ostg = A1[:, 21760:25856]

        S.emit("vector", lambda e: e.memset(v_sb[:, :], 1.0),
               waits=[("s_pe", pe_A_done)], inc="s_dve")
        v_vones = S.v("s_dve")
        ld_mask = S.dma("sync", F3[:, 0:2048], maskT, waits=[("s_dve", dve_rope[NTC - 1])],
                        sem="d_msk")
        S.emit("vector", lambda e: e.tensor_copy(m_bf[:, :], F3[:, 0:2048]),
               waits=[("d_msk", ld_mask)], inc="s_dve")
        v_mask = S.v("s_dve")

        for j in range(16):
            S.dma("sync", k_sb[:, j * 512:(j + 1) * 512],
                  qkv_spill[j * 512:(j + 1) * 512, 512:640],
                  waits=spill_done + [("s_dve", v_vones)], transpose=True, sem="d_tr")
        ld_k = S.v("d_tr")
        for g in range(64):
            S.dma("sync", v_sb3[:, g, 0:128],
                  qkv_spill[g * 128:(g + 1) * 128, 640:768],
                  waits=[("s_dve", v_vones)] + spill_done)
        ld_v = S.v("dq_sync")

        wo_buf = [B2[:, 0:16384].rearrange("p (h o) -> p h o", h=32),
                  B2[:, 16384:32768].rearrange("p (h o) -> p h o", h=32)]
        wo_ld = {}
        for oc in range(2):
            S.dma("gpsimd", wo_buf[oc],
                  woT[:, oc * 512:(oc + 1) * 512].rearrange("(h p) o -> p h o", p=128),
                  waits=[("s_pe", pe_A_done)], sem="d_wo%d" % oc)
            wo_ld[oc] = S.v("d_wo%d" % oc)

        groups = [(b, qc, h) for b in range(NBATCH) for qc in range(8) for h in range(NHL)]
        pairs = []
        for gi, (b, qc, h) in enumerate(groups):
            npairs = 2 * (qc + 1)
            for j in range(npairs):
                pairs.append((gi, b, qc, h, j, npairs))
        NPAIR = len(pairs)

        qld = {}
        pe_st_group = {}
        exp_done = {}
        av_done = {}
        ev_po = {"v": 0}
        st_ostg = {0: 0, 1: 0}
        evac_cnt = {}

        def emit_qload(gi):
            b, qc, h = groups[gi]
            gpar = gi % 2
            qsem = "d_q" if gpar == 0 else "d_q1"
            S.dma("sync", qsl[:, gpar * 512:(gpar + 1) * 512],
                  qkv_spill[b * SEQ + qc * 512: b * SEQ + qc * 512 + 512, h * 128:(h + 1) * 128],
                  waits=[("s_pe", pe_st_group.get(gi - 2, 0)), ("d_tr", ld_k), ("dq_sync", ld_v)],
                  transpose=True, sem=qsem)
            qld[gi] = (qsem, S.v(qsem))

        def emit_st(p):
            gi, b, qc, h, j, npairs = pairs[p]
            spar = p % 2
            SP = PQ[spar]
            gpar = gi % 2
            qv = qsl[:, gpar * 512:(gpar + 1) * 512]
            w = [("s_act", exp_done.get(p - 2, 0)), qld[gi]]
            if p <= 1:
                w.append(("s_act", v_w1))
                w.append(("s_dve", dve_rope[NTC - 1]))
            kb0 = 2 * j
            S.emit("tensor",
                   lambda e, SP=SP, b=b, kb0=kb0, qv=qv: e.matmul(
                       SP[:, 0:512], k_sb[:, b * SEQ + kb0 * 128: b * SEQ + kb0 * 128 + 128],
                       qv, start=True, stop=True),
                   waits=w)
            S.emit("tensor",
                   lambda e, SP=SP, b=b, kb0=kb0, qv=qv: e.matmul(
                       SP[:, 512:1024], k_sb[:, b * SEQ + kb0 * 128 + 128: b * SEQ + kb0 * 128 + 256],
                       qv, start=True, stop=True),
                   inc="s_pe")
            pairs_pe[p] = S.v("s_pe")
            pe_st_group[gi] = S.v("s_pe")

        def emit_mask_exp(p):
            gi, b, qc, h, j, npairs = pairs[p]
            spar = p % 2
            SP = PQ[spar]
            diag = (j >= npairs - 2)
            wexp = [("s_pe", pairs_pe[p]), ("s_pe", av_done.get(p - 2, 0))]
            if diag:
                v0 = 2 * (j - (npairs - 2))
                S.emit("vector",
                       lambda e, SP=SP, v0=v0: e.tensor_tensor(
                           SP[:, 0:1024], SP[:, 0:1024], m_bf[:, v0 * 512: v0 * 512 + 1024], ALU.add),
                       waits=[("s_pe", pairs_pe[p]), ("s_dve", v_mask)], inc="s_dve")
                wexp.append(("s_dve", S.v("s_dve")))
            S.emit("scalar",
                   lambda e, SP=SP, spar=spar: e.activation(
                       ebuf[:, spar * 1024:(spar + 1) * 1024], SP[:, 0:1024], AF.Exp,
                       bias=ncol[:, 0:1], scale=ISQ),
                   waits=wexp, inc="s_act")
            exp_done[p] = S.v("s_act")

        def emit_av(p):
            gi, b, qc, h, j, npairs = pairs[p]
            spar = p % 2
            kb0 = 2 * j
            first = (j == 0)
            last = (j == npairs - 1)
            for kk in range(2):
                kbg = b * 32 + kb0 + kk
                for qs in range(4):
                    slot = PV[qs // 2]
                    c0 = (qs % 2) * 512
                    w = []
                    if first and kk == 0 and qs == 0:
                        w = [("s_act", exp_done[p]), ("s_dve", ev_po["v"])]
                    elif kk == 0 and qs == 0:
                        w = [("s_act", exp_done[p])]
                    S.emit("tensor",
                           lambda e, slot=slot, c0=c0, spar=spar, kk=kk, qs=qs, kbg=kbg,
                                  first=first, last=last: e.matmul(
                               slot[:, c0:c0 + 129],
                               ebuf[:, spar * 1024 + kk * 512 + qs * 128: spar * 1024 + kk * 512 + qs * 128 + 128],
                               v_sb3[:, kbg, 0:129],
                               start=(first and kk == 0), stop=(last and kk == 1)),
                           waits=w,
                           inc=("s_pe" if (kk == 1 and qs == 3) else None))
            av_done[p] = S.v("s_pe")

        def emit_evac(p):
            gi, b, qc, h, j, npairs = pairs[p]
            opar = (b * 8 + qc) % 2
            osem = "d_ost" if opar == 0 else "d_ost1"
            w = [("s_pe", av_done[p]), (osem, st_ostg[opar]), ("s_dve", ev_po["v"])]
            for qs in range(4):
                slot = PV[qs // 2]
                c0 = (qs % 2) * 512
                S.emit("vector",
                       lambda e, slot=slot, c0=c0, qs=qs: e.reciprocal(
                           zc[:, qs:qs + 1], slot[:, c0 + 128:c0 + 129]),
                       waits=(w if qs == 0 else ()),
                       inc=("s_dve" if qs == 3 else None))
            vrec = S.v("s_dve")
            for qs in range(4):
                slot = PV[qs // 2]
                c0 = (qs % 2) * 512
                S.emit("vector",
                       lambda e, slot=slot, c0=c0, qs=qs, opar=opar, h=h: e.tensor_scalar(
                           ostg[:, opar * 2048 + qs * 512 + h * 128: opar * 2048 + qs * 512 + h * 128 + 128],
                           slot[:, c0:c0 + 128], zc[:, qs:qs + 1], None, ALU.mult),
                       waits=([("s_dve", vrec)] if qs == 0 else ()), inc="s_dve")
            ev_po["v"] = S.v("s_dve")
            evac_cnt[(b, qc)] = S.v("s_dve")

        pairs_pe = {}
        p = 0
        cur_g = -1
        emit_qload(0)
        emit_qload(1)
        for p in range(NPAIR):
            gi, b, qc, h, j, npairs = pairs[p]
            if j == npairs - 1 and gi + 2 < len(groups):
                emit_qload(gi + 2)
            if p == 0:
                emit_st(0)
            if p + 1 < NPAIR:
                emit_st(p + 1)
            emit_mask_exp(p)
            emit_av(p)
            if j == npairs - 1:
                emit_evac(p)
                if h == NHL - 1:
                    opar = (b * 8 + qc) % 2
                    t0 = b * SEQ + qc * 512
                    osem2 = "d_ost" if opar == 0 else "d_ost1"
                    S.dma("sync",
                          a2a_in.ap()[t0:t0 + 512, :].rearrange("(qs p) o -> p qs o", p=128),
                          ostg[:, opar * 2048:(opar + 1) * 2048].rearrange("p (qs o) -> p qs o", qs=4),
                          waits=[("s_dve", evac_cnt[(b, qc)])], sem=osem2)
                    st_ostg[opar] = S.v(osem2)

        a2a_ready = [("d_ost", S.v("d_ost")), ("d_ost1", S.v("d_ost1"))]
        if debug:
            S.dma("sync", a2a_in_d, a2a_in.ap(), waits=a2a_ready, sem="d_dbg")
        S.emit("gpsimd",
               lambda e: e.collective_compute("AllToAll", ALU.bypass,
                                              replica_groups=[list(range(R))],
                                              ins=[a2a_in.ap().opt()],
                                              outs=[a2a_out.ap().opt()]),
               waits=a2a_ready, inc="s_cc")
        cc_a2a = S.v("s_cc")

        ofm = A1[:, :].rearrange("p (h t) -> p h t", h=32)
        for hd_c in range(32):
            r_ = hd_c // 4
            lb = hd_c % 4
            S.dma("sync", ofm[:, hd_c, :],
                  a2a_out.ap()[r_ * 1024:(r_ + 1) * 1024, lb * 128:(lb + 1) * 128],
                  waits=[("s_cc", cc_a2a)], transpose=True, sem="d_tr")
        ld_ofm = S.v("d_tr")

        xo3 = [xo[k].rearrange("p (t o) -> p t o", t=8) for k in range(2)]
        pe_C = {}
        dve_C = {}
        xo_ld = {}
        st_x1 = {}
        for oc in range(8):
            par = oc % 2
            xcsem = "d_xc" if par == 0 else "d_xc1"
            if oc >= 2:
                S.dma("gpsimd", wo_buf[par],
                      woT[:, oc * 512:(oc + 1) * 512].rearrange("(h p) o -> p h o", p=128),
                      waits=[("s_pe", pe_C.get(oc - 2, 0))], sem="d_wo%d" % par)
                wo_ld[oc] = S.v("d_wo%d" % par)
            S.dma("sync", xo3[par][:, :, :],
                  x_own[:, oc * 512:(oc + 1) * 512].rearrange("(t p) o -> p t o", p=128),
                  waits=[("s_dve", dve_C.get(oc - 2, 0)), ("s_dve", scale_state["thr"])], sem=xcsem)
            xo_ld[oc] = S.v(xcsem)
            for tc in range(8):
                slot = [P0, P1, P2, P3][tc // 2]
                c0 = (tc % 2) * 512
                for hd_c in range(32):
                    w = []
                    if hd_c == 0:
                        w = [("d_tr", ld_ofm), ("d_wo%d" % par, wo_ld[oc]),
                             ("s_dve", dve_C.get(oc - 1, 0)),
                             ("s_act", scale_state["p2done"])]
                    S.emit("tensor",
                           lambda e, slot=slot, c0=c0, hd_c=hd_c, tc=tc, par=par: e.matmul(
                               slot[:, c0:c0 + 512], ofm[:, hd_c, tc * 128:(tc + 1) * 128],
                               wo_buf[par][:, hd_c, :],
                               start=(hd_c == 0), stop=(hd_c == 31)),
                           waits=w,
                           inc=("s_pe" if hd_c == 31 else None))
                pe_C[oc] = S.v("s_pe")
                ys = tc % 4
                x1sem = ["d_x1s", "d_x1s1", "d_x1s2", "d_x1s3"][ys]
                S.emit("vector",
                       lambda e, slot=slot, c0=c0, tc=tc, par=par, ys=ys: e.tensor_tensor(
                           ytile[:, ys * 512:(ys + 1) * 512], slot[:, c0:c0 + 512],
                           xo3[par][:, tc, :], ALU.add),
                       waits=[("s_pe", pe_C[oc]), (xcsem, xo_ld[oc]),
                              (x1sem,
                               st_x1.get((oc, tc - 4), st_x1.get((oc - 1, tc + 4), 0)))],
                       inc="s_dve")
                dve_C[oc] = S.v("s_dve")
                S.dma("sync", x1_dram[tc * 128:(tc + 1) * 128, oc * 512:(oc + 1) * 512],
                      ytile[:, ys * 512:(ys + 1) * 512],
                      waits=[("s_dve", dve_C[oc])], sem=x1sem)
                st_x1[(oc, tc)] = S.v(x1sem)
        x1_done = [("d_x1s", S.v("d_x1s")), ("d_x1s1", S.v("d_x1s1"))]

        x1n_t = [B2[:, 16384 + k * 4096:16384 + (k + 1) * 4096] for k in range(2)]
        st_x1n = {}
        act_D = {}
        for tc in range(8):
            par = tc % 2
            xdsem = "d_xd%d" % par
            ldv = S.dma("sync", xo[par][:, 0:4096], x1_dram[tc * 128:(tc + 1) * 128, :],
                        waits=x1_done + [act_D.get(tc - 2, ("s_dve", 0))], sem=xdsem)
            ldv = S.v(xdsem)
            S.emit("vector", lambda e, par=par: e.reduce_sum(r2c[:, 0:1], xo[par][:, 0:4096], axis=AX.X),
                   waits=[(xdsem, ldv), act_D.get(tc - 1, ("s_dve", 0))], inc="s_dve")
            S.emit("scalar", lambda e, par=par: e.activation(sq_bf, xo[par][:, 0:4096], AF.Square),
                   waits=[(xdsem, ldv), ("s_dve", S.v("s_dve"))], inc="s_act")
            S.emit("vector", lambda e: e.reduce_sum(r2c[:, 1:2], sq_bf, axis=AX.X),
                   waits=[("s_act", S.v("s_act"))], inc="s_dve")
            S.emit("vector", lambda e: e.tensor_scalar(r2c[:, 2:3], r2c[:, 0:1], -1.0 / DM, None, ALU.mult),
                   waits=[("s_dve", S.v("s_dve"))], inc="s_dve")
            S.emit("vector", lambda e: e.tensor_scalar(r2c[:, 3:4], r2c[:, 1:2], 1.0 / DM, None, ALU.mult),
                   waits=[("s_dve", S.v("s_dve"))], inc="s_dve")
            S.emit("vector", lambda e: e.tensor_tensor(r2c[:, 4:5], r2c[:, 2:3], r2c[:, 2:3], ALU.mult),
                   waits=[("s_dve", S.v("s_dve"))], inc="s_dve")
            S.emit("vector", lambda e: e.tensor_tensor(r2c[:, 5:6], r2c[:, 3:4], r2c[:, 4:5], ALU.subtract),
                   waits=[("s_dve", S.v("s_dve"))], inc="s_dve")
            v_d = S.v("s_dve")
            S.emit("scalar", lambda e: e.activation(r2c[:, 6:7], r2c[:, 5:6], AF.Ln, bias=epscol[:, 0:1]),
                   waits=[("s_dve", v_d)], inc="s_act")
            S.emit("scalar", lambda e: e.activation(r2c[:, 7:8], r2c[:, 6:7], AF.Exp, scale=-0.5),
                   waits=[("s_act", S.v("s_act"))], inc="s_act")
            v_r2 = S.v("s_act")
            x1nsem = "d_x1n" if par == 0 else "d_x1n1"
            S.emit("vector",
                   lambda e, par=par: e.tensor_scalar(x1n_t[par], xo[par][:, 0:4096],
                                                      r2c[:, 2:3], r2c[:, 7:8], ALU.add, ALU.mult),
                   waits=[("s_act", v_r2), (x1nsem, st_x1n.get(tc - 2, 0))], inc="s_dve")
            act_D[tc] = ("s_dve", S.v("s_dve"))
            S.dma("sync", x1n_dram[tc * 128:(tc + 1) * 128, :], x1n_t[par],
                  waits=[act_D[tc]], sem=x1nsem)
            st_x1n[tc] = S.v(x1nsem)
        x1n_done = [("d_x1n", S.v("d_x1n")), ("d_x1n1", S.v("d_x1n1"))]

        xn = A1[:, :].rearrange("p (i t) -> p i t", i=32)
        for ic in range(NIC):
            S.dma("sync", xn[:, ic, :], x1n_dram[0:TOWN, ic * 128:(ic + 1) * 128],
                  waits=x1n_done + [("s_pe", pe_C[7])], transpose=True, sem="d_tr")
        ld_xn = S.v("d_tr")

        qg = [B2[:, 0:4096], B2[:, 4096:8192]]
        qu = [B2[:, 8192:12288], B2[:, 12288:16384]]
        scr1 = B2[:, 16384:20480]
        scr2 = B2[:, 20480:24576]
        wgf = [F3[:, 0:4096], F3[:, 4096:8192]]
        wuf = [F3[:, 8192:12288], F3[:, 12288:16384]]
        pe_E = {}
        dve_E = {}
        act_E = {}
        ib_st = {}
        ew_ld = {}

        def emit_E_load(j):
            par = j % 2
            wsem = "d_w" if par == 0 else "d_w1"
            S.dma("sync", wgf[par], wg_tiles[j].rearrange("p i g -> p (i g)"),
                  waits=[("s_dve", dve_E.get(j - 2, 0))] + (x1n_done if j < 2 else []),
                  sem=wsem)
            S.dma("sync", wuf[par], wu_tiles[j].rearrange("p i g -> p (i g)"), sem=wsem)
            ew_ld[j] = (wsem, S.v(wsem))

        def emit_E_quant(j):
            par = j % 2
            wq2 = [ew_ld[j], ("s_dve", scale_state["thr"]),
                   ("s_pe", pe_E.get(j - 2, 0)), ("s_dve", dve_E.get(j - 1, 0))]
            S.emit("vector", lambda e, par=par: e.tensor_scalar(scr1, wgf[par], thrb[:, 0:1], None, ALU.is_gt),
                   waits=wq2)
            S.emit("vector", lambda e, par=par: e.tensor_scalar(scr2, wgf[par], thrb[:, 4:5], None, ALU.is_lt),
                   inc="s_dve")
            S.emit("vector", lambda e, par=par: e.tensor_tensor(qg[par], scr1, scr2, ALU.subtract),
                   waits=[("s_dve", S.v("s_dve"))], inc="s_dve")
            vqg = S.v("s_dve")
            S.emit("vector", lambda e, par=par: e.tensor_scalar(scr1, wuf[par], thrb[:, 1:2], None, ALU.is_gt),
                   waits=[ew_ld[j], ("s_dve", vqg)])
            S.emit("vector", lambda e, par=par: e.tensor_scalar(scr2, wuf[par], thrb[:, 5:6], None, ALU.is_lt),
                   inc="s_dve")
            S.emit("vector", lambda e, par=par: e.tensor_tensor(qu[par], scr1, scr2, ALU.subtract),
                   waits=[("s_dve", S.v("s_dve"))], inc="s_dve")
            dve_E[j] = S.v("s_dve")

        emit_E_load(0)
        emit_E_load(1)
        emit_E_quant(0)
        for j in range(NGC):
            par = j % 2
            if j + 2 < NGC:
                emit_E_load(j + 2)
            if j + 1 < NGC:
                emit_E_quant(j + 1)
            vq = dve_E[j]
            PG = PQ[par]
            PU = PV[par]
            for ic in range(NIC):
                w = []
                if ic == 0:
                    w = [("s_dve", vq), ("d_tr", ld_xn),
                         ("s_dve", dve_E.get(("i", j - 2), 0)),
                         ("s_act", act_E.get(j - 2, 0))]
                S.emit("tensor",
                       lambda e, PG=PG, par=par, ic=ic: e.matmul(
                           PG[:, 0:512], qg[par][:, ic * 128:(ic + 1) * 128],
                           xn[:, ic, 0:512], start=(ic == 0), stop=(ic == NIC - 1)),
                       waits=w)
                S.emit("tensor",
                       lambda e, PG=PG, par=par, ic=ic: e.matmul(
                           PG[:, 512:1024], qg[par][:, ic * 128:(ic + 1) * 128],
                           xn[:, ic, 512:1024], start=(ic == 0), stop=(ic == NIC - 1)))
            for ic in range(NIC):
                S.emit("tensor",
                       lambda e, PU=PU, par=par, ic=ic: e.matmul(
                           PU[:, 0:512], qu[par][:, ic * 128:(ic + 1) * 128],
                           xn[:, ic, 0:512], start=(ic == 0), stop=(ic == NIC - 1)))
                S.emit("tensor",
                       lambda e, PU=PU, par=par, ic=ic: e.matmul(
                           PU[:, 512:1024], qu[par][:, ic * 128:(ic + 1) * 128],
                           xn[:, ic, 512:1024], start=(ic == 0), stop=(ic == NIC - 1)),
                       inc=("s_pe" if ic == NIC - 1 else None))
            pe_E[j] = S.v("s_pe")
            S.emit("scalar", lambda e, PU=PU, par=par: e.copy(ubf[:, par * 1024:(par + 1) * 1024], PU[:, 0:1024]),
                   waits=[("s_pe", pe_E[j]), ("s_dve", dve_E.get(("i", j - 2), 0))], inc="s_act")
            act_E[j] = S.v("s_act")
            S.emit("vector",
                   lambda e, PG=PG, par=par: e.tensor_tensor(
                       ibf[:, par * 1024:(par + 1) * 1024], PG[:, 0:1024],
                       ubf[:, par * 1024:(par + 1) * 1024], ALU.mult),
                   waits=[("s_act", act_E[j]),
                          ("d_ib%d" % par, ib_st.get(j - 2, 0))],
                   inc="s_dve")
            dve_E[("i", j)] = S.v("s_dve")
            S.dma("scalar", inter_dram[j * 128:(j + 1) * 128, :], ibf[:, par * 1024:(par + 1) * 1024],
                  waits=[("s_dve", dve_E[("i", j)])], sem="d_ib%d" % par)
            ib_st[j] = S.v("d_ib%d" % par)
        ib_done = [("d_ib0", S.v("d_ib0")), ("d_ib1", S.v("d_ib1"))]
        pe_E_done = pe_E[NGC - 1]

        int_sb4 = [B2[:, 16384 + k * 4096:16384 + (k + 1) * 4096].rearrange("p (i t) -> p i t", i=4)
                   for k in range(4)]
        qd = [B2[:, 8192:10240], B2[:, 10240:12288]]
        fs1 = B2[:, 12288:14336]
        fs2 = B2[:, 14336:16384]
        wdf = [F3[:, 0:2048], F3[:, 2048:4096]]
        x1o = [F3[:, 8192:12288].rearrange("p (t o) -> p t o", t=8),
               F3[:, 12288:16384].rearrange("p (t o) -> p t o", t=8)]
        batches = [(i, min(i + 4, NGC)) for i in range(0, NGC, 4)]
        NB = len(batches)
        NGB = 8 * NB
        last_qd = [0]
        pe_gb = {}
        dve_q = {}
        dve_F = {}
        dve_ev = {}
        st_y = {}
        lw_v = {}
        li_v = {}
        x1_ld = {}
        dve_E_all = dve_E[NGC - 1]
        dve_E_int = dve_E[("i", NGC - 1)]
        act_E_all = act_E[NGC - 1]
        isems = ["d_i", "d_i1", "d_i2", "d_i3"]
        ysems = ["d_ys", "d_ys1", "d_ys2", "d_ys3"]

        def emit_F_load(gb):
            oc, bi = gb // NB, gb % NB
            ic0, ic1 = batches[bi]
            n = ic1 - ic0
            bpar = gb % 2
            sl4 = gb % 4
            wdsem = "d_wd" if bpar == 0 else "d_wd1"
            S.dma("sync", wdf[bpar][:, 0:n * 512].rearrange("p (i o) -> p i o", i=n),
                  wd_tiles[oc, ic0:ic1].rearrange("i p o -> p i o"),
                  waits=[("s_dve", dve_q.get(gb - 2, dve_E_all)),
                         ("s_pe", pe_gb.get(gb - 2, pe_E_done if gb < 2 else 0))],
                  sem=wdsem)
            lw_v[gb] = (wdsem, S.v(wdsem))
            S.dma("sync", int_sb4[sl4][:, 0:n, :],
                  inter_dram[ic0 * 128:ic1 * 128, :].rearrange("(i p) t -> p i t", p=128),
                  waits=ib_done + [("s_pe", pe_gb.get(gb - 4, 0)), ("s_dve", dve_E_all)],
                  sem=isems[sl4])
            li_v[gb] = (isems[sl4], S.v(isems[sl4]))

        def emit_F_quant(gb):
            bpar = gb % 2
            ic0, ic1 = batches[gb % NB]
            n = ic1 - ic0
            S.emit("vector", lambda e, bpar=bpar, n=n: e.tensor_scalar(
                       fs1[:, 0:n * 512], wdf[bpar][:, 0:n * 512], thrb[:, 2:3], None, ALU.is_gt),
                   waits=[lw_v[gb], ("s_dve", scale_state["thr"]),
                          ("s_dve", last_qd[0]),
                          ("s_pe", pe_gb.get(gb - 2, 0))])
            S.emit("vector", lambda e, bpar=bpar, n=n: e.tensor_scalar(
                       fs2[:, 0:n * 512], wdf[bpar][:, 0:n * 512], thrb[:, 6:7], None, ALU.is_lt),
                   inc="s_dve")
            S.emit("vector", lambda e, bpar=bpar, n=n: e.tensor_tensor(
                       qd[bpar][:, 0:n * 512], fs1[:, 0:n * 512], fs2[:, 0:n * 512], ALU.subtract),
                   waits=[("s_dve", S.v("s_dve"))], inc="s_dve")
            dve_q[gb] = S.v("s_dve")
            last_qd[0] = S.v("s_dve")

        def emit_F_evac(oc):
            opar = oc % 2
            xfsem = "d_xf" if opar == 0 else "d_xf1"
            for tc in range(8):
                slot = [P0, P1, P2, P3][tc // 2]
                c0 = (tc % 2) * 512
                ys = tc % 4
                yssem = ysems[ys]
                S.emit("vector",
                       lambda e, slot=slot, c0=c0, opar=opar, tc=tc, ys=ys: e.scalar_tensor_tensor(
                           ytile[:, ys * 512:(ys + 1) * 512], slot[:, c0:c0 + 512],
                           thrb[:, 3:4], x1o[opar][:, tc, :], ALU.mult, ALU.add),
                       waits=[("s_pe", pe_gb[oc * NB + NB - 1]), (xfsem, x1_ld[oc]),
                              (yssem,
                               st_y.get((oc, tc - 4), st_y.get((oc - 1, tc + 4), 0)))],
                       inc="s_dve")
                dve_ev[(oc, tc)] = S.v("s_dve")
                dve_F[oc] = S.v("s_dve")
                S.dma("sync", y[tc * 128:(tc + 1) * 128, oc * 512:(oc + 1) * 512],
                      ytile[:, ys * 512:(ys + 1) * 512],
                      waits=[("s_dve", dve_F[oc])], sem=yssem)
                st_y[(oc, tc)] = S.v(yssem)

        emit_F_load(0)
        emit_F_load(1)
        for gb in range(NGB):
            oc, bi = gb // NB, gb % NB
            if bi == 0:
                xfsem = "d_xf" if oc % 2 == 0 else "d_xf1"
                S.dma("scalar", x1o[oc % 2][:, :, :],
                      x1_dram[:, oc * 512:(oc + 1) * 512].rearrange("(t p) o -> p t o", p=128),
                      waits=[("s_dve", dve_F.get(oc - 2, 0))] + x1n_done +
                            [("s_dve", dve_E_all)], sem=xfsem)
                x1_ld[oc] = S.v(xfsem)
            emit_F_quant(gb)
            if gb + 2 < NGB:
                emit_F_load(gb + 2)
            if bi == 0 and oc > 0:
                emit_F_evac(oc - 1)
            ic0, ic1 = batches[bi]
            n = ic1 - ic0
            bpar = gb % 2
            sl4 = gb % 4
            for i in range(n):
                icg = ic0 + i
                for tc in range(8):
                    slot = [P0, P1, P2, P3][tc // 2]
                    c0 = (tc % 2) * 512
                    w = []
                    if i == 0 and tc == 0:
                        w = [("s_dve", dve_q[gb]), li_v[gb]]
                        if gb == 0:
                            w.append(("s_dve", dve_E_int))
                            w.append(("s_act", act_E_all))
                    if i == 0 and bi == 0 and oc > 0:
                        w = w + [("s_dve", dve_ev[(oc - 1, tc)])]
                    S.emit("tensor",
                           lambda e, slot=slot, c0=c0, sl4=sl4, bpar=bpar, i=i, tc=tc, icg=icg: e.matmul(
                               slot[:, c0:c0 + 512], int_sb4[sl4][:, i, tc * 128:(tc + 1) * 128],
                               qd[bpar][:, i * 512:(i + 1) * 512],
                               start=(icg == 0), stop=(icg == NGC - 1)),
                           waits=w,
                           inc=("s_pe" if (i == n - 1 and tc == 7) else None))
            pe_gb[gb] = S.v("s_pe")
        emit_F_evac(7)

        S.emit("sync", lambda e: None,
               waits=[(nm, S.v(nm)) for nm in sems
                      if nm.startswith("d") and S.v(nm) > 0])

        with allow_ici_all_to_all():
            with nc.Block() as block:
                S.run(block, handles)
    return nc


_NC_CACHE = {}


def _get_nc(debug=False):
    key = bool(debug)
    if key not in _NC_CACHE:
        _NC_CACHE[key] = build_nc(debug=key)
    return _NC_CACHE[key]


def make_in_maps(inputs):
    x = np.asarray(inputs["L0_attn_hidden_states"], dtype=np.float32)
    Wq = np.asarray(inputs["L0_attn_Wq"], dtype=np.float32)
    Wk = np.asarray(inputs["L0_attn_Wk"], dtype=np.float32)
    Wv = np.asarray(inputs["L0_attn_Wv"], dtype=np.float32)
    Wo = np.asarray(inputs["L0_attn_Wo"], dtype=np.float32)
    wg = np.asarray(inputs["L0_ffn_w_gate"], dtype=np.float32)
    wu = np.asarray(inputs["L0_ffn_w_up"], dtype=np.float32)
    wd = np.asarray(inputs["L0_ffn_w_down"], dtype=np.float32)

    xr = x.reshape(NT, DM)
    x_pm = np.ascontiguousarray(
        xr.reshape(NTC, 128, NIC, 128).transpose(0, 3, 2, 1)).reshape(NTC, 128, DM)
    woT = np.ascontiguousarray(Wo.T)
    wg_tiles = np.ascontiguousarray(wg.reshape(NGC, 128, NIC, 128).transpose(0, 3, 2, 1))
    wu_tiles = np.ascontiguousarray(wu.reshape(NGC, 128, NIC, 128).transpose(0, 3, 2, 1))
    wd_tiles = np.ascontiguousarray(wd.reshape(8, 512, NGC, 128).transpose(0, 2, 3, 1))
    wg_flat = wg_tiles.reshape(-1)
    wu_flat = wu_tiles.reshape(-1)
    wd_flat = wd_tiles.reshape(-1)
    nsl = wg_flat.size // R
    cos, sinr = rope_tables()
    masks = causal_masks()
    ident = np.eye(128, dtype=np.float32)

    in_maps = []
    for c in range(R):
        wqkv_c = np.ascontiguousarray(np.concatenate(
            [Wq[512 * c:512 * (c + 1), :].T, Wk[128 * c:128 * (c + 1), :].T,
             Wv[128 * c:128 * (c + 1), :].T], axis=1))
        in_maps.append({
            "x_pm": x_pm,
            "x_own": np.ascontiguousarray(xr[TOWN * c:TOWN * (c + 1), :]),
            "wqkv": wqkv_c,
            "woT": woT,
            "wg_tiles": wg_tiles,
            "wu_tiles": wu_tiles,
            "wd_tiles": wd_tiles,
            "wg_s": np.ascontiguousarray(wg_flat[nsl * c:nsl * (c + 1)].reshape(128, 44032)),
            "wu_s": np.ascontiguousarray(wu_flat[nsl * c:nsl * (c + 1)].reshape(128, 44032)),
            "wd_s": np.ascontiguousarray(wd_flat[nsl * c:nsl * (c + 1)].reshape(128, 44032)),
            "cosT": cos,
            "sinrT": sinr,
            "maskT": masks,
            "idf": ident,
        })
    return in_maps


def kernel(**inputs):
    nc = _get_nc(debug=False)
    in_maps = make_in_maps(inputs)
    res = run_bass_kernel_spmd(nc, in_maps, core_ids=list(range(R)))
    out = np.empty((NT, DM), dtype=np.float32)
    for c in range(R):
        out[TOWN * c:TOWN * (c + 1), :] = res.results[c]["y"]
    return out.reshape(NBATCH, SEQ, DM)


# version 15
# speedup vs baseline: 1.4600x; 1.0136x over previous
import math
from contextlib import ExitStack

import numpy as np
import ml_dtypes

import concourse.bass as bass
import concourse.mybir as mybir
from concourse.bass_utils import run_bass_kernel_spmd
from concourse.aot_env import allow_ici_all_to_all

F32 = mybir.dt.float32
BF16 = mybir.dt.bfloat16
AF = mybir.ActivationFunctionType
ALU = mybir.AluOpType
AX = mybir.AxisListType
BF = ml_dtypes.bfloat16

R = 8
SEQ = 4096
NBATCH = 2
NT = NBATCH * SEQ
DM = 4096
DI = 11008
NHL = 4
TOWN = NT // R
NTC = NT // 128
NIC = DM // 128
NGC = DI // 128
EPS_LN = 1e-5
EPS_Q = 1e-5
EXP_C = 20.0
ISQ = 1.0 / math.sqrt(128.0)
NEG_BIG = -1.0e30
NW = float(DI * DM)

ENGS = ("tensor", "vector", "scalar", "gpsimd", "sync")


class Sched:

    def __init__(self, sem_names):
        self.cnt = {n: 0 for n in sem_names}
        self.ops = {e: [] for e in ENGS}

    def v(self, sem):
        return self.cnt[sem]

    def emit(self, eng, fn, waits=(), inc=None, n=1):
        waits = tuple((s, v) for (s, v) in waits if v > 0)
        if inc is not None:
            self.cnt[inc] += n
        self.ops[eng].append((fn, waits, inc, n))
        return self.cnt[inc] if inc is not None else None

    def dma(self, eng, out, in_, waits=(), transpose=False, sem=None):
        def fn(e, out=out, in_=in_, transpose=transpose):
            if transpose:
                return e.dma_start(out=out, in_=in_, transpose=True)
            return e.dma_start(out=out, in_=in_)
        return self.emit(eng, fn, waits, sem or ("dq_" + eng), 16)

    def run(self, block, handles):
        for eng in ENGS:
            ops = self.ops[eng]
            def make(eng_ops):
                def body(e):
                    seen = {}
                    for fn, waits, inc, n in eng_ops:
                        for s, v in waits:
                            if seen.get(s, 0) < v:
                                e.wait_ge(handles[s], v)
                                seen[s] = v
                        inst = fn(e)
                        if inc is not None:
                            inst.then_inc(handles[inc], n)
                return body
            getattr(block, eng)(make(ops))


def rope_tables():
    f32 = np.float32
    d = np.arange(0, 128, 2, dtype=f32)
    inv_freq = (f32(1.0) / (f32(500000.0) ** (d / f32(128.0)))).astype(f32)
    wavelen = f32(2.0 * math.pi) / inv_freq
    smooth = (f32(8192.0) / wavelen - f32(1.0)) / f32(3.0)
    smooth = np.clip(smooth, f32(0.0), f32(1.0)).astype(f32)
    new_inv = ((f32(1.0) - smooth) * (inv_freq / f32(8.0)) + smooth * inv_freq).astype(f32)
    t = np.arange(SEQ, dtype=f32)
    freqs = np.outer(t, new_inv).astype(f32)
    emb = np.concatenate([freqs, freqs], axis=-1)
    cos = np.cos(emb).astype(f32)
    sin = np.sin(emb).astype(f32)
    sinr = np.concatenate([-sin[:, :64], sin[:, 64:]], axis=-1).astype(f32)
    return cos, sinr


def causal_masks():
    m = np.zeros((128, 4, 512), dtype=np.float32)
    k = np.arange(128)[:, None]
    q = np.arange(512)[None, :]
    for v in range(4):
        m[:, v, :] = np.where(128 * v + k <= q, 0.0, NEG_BIG)
    return m.reshape(128, 2048)


def build_nc(debug=False):
    nc = bass.Bass("TRN2", target_bir_lowering=False)
    dt = nc.dram_tensor

    x_pm = dt("x_pm", [NTC, 128, DM], F32, kind="ExternalInput").ap()
    x_own = dt("x_own", [TOWN, DM], F32, kind="ExternalInput").ap()
    wqkv = dt("wqkv", [DM, 768], F32, kind="ExternalInput").ap()
    woT = dt("woT", [DM, DM], F32, kind="ExternalInput").ap()
    wg_tiles = dt("wg_tiles", [NGC, 128, NIC, 128], F32, kind="ExternalInput").ap()
    wu_tiles = dt("wu_tiles", [NGC, 128, NIC, 128], F32, kind="ExternalInput").ap()
    wd_tiles = dt("wd_tiles", [8, NGC, 128, 512], F32, kind="ExternalInput").ap()
    wg_s = dt("wg_s", [128, 44032], F32, kind="ExternalInput").ap()
    wu_s = dt("wu_s", [128, 44032], F32, kind="ExternalInput").ap()
    wd_s = dt("wd_s", [128, 44032], F32, kind="ExternalInput").ap()
    cosT = dt("cosT", [SEQ, 128], F32, kind="ExternalInput").ap()
    sinrT = dt("sinrT", [SEQ, 128], F32, kind="ExternalInput").ap()
    maskT = dt("maskT", [128, 2048], F32, kind="ExternalInput").ap()
    idf = dt("idf", [128, 128], F32, kind="ExternalInput").ap()
    y = dt("y", [TOWN, DM], F32, kind="ExternalOutput").ap()

    dbg = "ExternalOutput" if debug else "Internal"
    qkv_spill = dt("qkv_spill", [NT, 768], BF16, kind=dbg).ap()
    a2a_in = dt("a2a_in", [NT, 512], BF16, kind="Internal")
    a2a_out = dt("a2a_out", [NT, 512], BF16, kind="Internal")
    stats_in = dt("stats_in", [TOWN, 2], F32, kind="Internal")
    stats_ag = dt("stats_ag", [NT, 2], F32, kind="Internal")
    scales_in = dt("scales_in", [4, 1], F32, kind="Internal")
    scales_out = dt("scales_out", [32, 1], F32, kind="Internal")
    if debug:
        absacc_d = dt("absacc_d", [128, 4], F32, kind="ExternalOutput").ap()
        slabd_d = dt("slabd_d", [128, 36], F32, kind="ExternalOutput").ap()
        stat8_d = dt("stat8_d", [128, 32], F32, kind="ExternalOutput").ap()
        stats_ag_d = dt("stats_ag_d", [NT, 2], F32, kind="ExternalOutput").ap()
        scales_d = dt("scales_d", [1, 8], F32, kind="ExternalOutput").ap()
        a2a_in_d = dt("a2a_in_d", [NT, 512], BF16, kind="ExternalOutput").ap()
    nrow_dram = dt("nrow_dram", [64, 128], BF16, kind="Internal").ap()
    thr_dram = dt("thr_dram", [1, 4], F32, kind="Internal").ap()
    x1_dram = dt("x1_dram", [TOWN, DM], F32, kind=dbg).ap()
    x1n_dram = dt("x1n_dram", [TOWN, DM], BF16, kind=dbg).ap()
    inter_dram = dt("inter_dram", [DI, TOWN], BF16, kind=dbg).ap()

    with ExitStack() as ctx:
        ecx = ctx.enter_context

        def sbuf(name, shape, dtype):
            return ecx(nc.sbuf_tensor(name, shape, dtype))

        A1 = sbuf("A1", [128, 32768], BF16)
        B2 = sbuf("B2", [128, 32768], BF16)
        F3 = sbuf("F3", [128, 16384], F32)
        outb = sbuf("outb", [128, 2 * 768], BF16)
        stag = sbuf("stag", [128, 128], F32)
        stat8 = sbuf("stat8", [128, 32], F32)
        nchunk = sbuf("nchunk", [1, 2 * 128], BF16)
        w1row = sbuf("w1row", [1, 768], BF16)
        onesb = sbuf("onesb", [128, 1], BF16)
        onesf = sbuf("onesf", [128, 1], F32)
        ncol = sbuf("ncol", [128, 1], F32)
        epscol = sbuf("epscol", [128, 1], F32)
        thrb = sbuf("thrb", [128, 8], F32)
        slabd = sbuf("slabd", [128, 36], F32)
        scf = sbuf("scf", [128, 256], F32)
        scr32 = sbuf("scr32", [1, 32], F32)
        absacc = sbuf("absacc", [128, 4], F32)
        scw = sbuf("scw", [1, 8], F32)
        zc = sbuf("zc", [128, 4], F32)
        r2c = sbuf("r2c", [128, 8], F32)
        n64 = sbuf("n64", [64, 128], BF16)
        ln2s = sbuf("ln2s", [128, 40], F32)

        P0 = ecx(nc.psum_tensor("P0", [128, 1024], F32))
        P1 = ecx(nc.psum_tensor("P1", [128, 1024], F32))
        P2 = ecx(nc.psum_tensor("P2", [128, 1024], F32))
        P3 = ecx(nc.psum_tensor("P3", [128, 1024], F32))
        PQ = [P0, P1]
        PV = [P2, P3]

        sems = ["dq_sync", "dq_gpsimd", "dq_scalar", "s_pe", "s_dve", "s_act", "s_cc",
                "d_q", "d_q1", "d_tr", "d_ost", "d_ost1", "d_n", "d_sp", "d_sp1",
                "d_xo", "d_xd0", "d_xd1", "d_x1s", "d_x1s1", "d_w", "d_w1",
                "d_i", "d_i1", "d_dbg", "d_xc", "d_xc1", "d_xf", "d_xf1",
                "d_x1n", "d_x1n1", "d_ys", "d_ys1", "d_wd", "d_wd1",
                "d_xb0", "d_xb1", "d_xb2", "d_xb3", "d_sc0", "d_sc1",
                "d_wo0", "d_wo1", "d_ib0", "d_ib1", "d_msk", "d_thr",
                "d_i2", "d_i3", "d_x1s2", "d_x1s3", "d_ys2", "d_ys3"]
        handles = {n: ecx(nc.semaphore(n)) for n in sems}
        S = Sched(sems)

        wq_v = B2[:, 0:24576].rearrange("p (i o) -> p i o", i=NIC)
        xb = [B2[:, 24576 + k * 4096:24576 + (k + 1) * 4096].rearrange("p (i t) -> p i t", i=NIC)
              for k in range(2)] + \
             [A1[:, k * 4096:(k + 1) * 4096].rearrange("p (i t) -> p i t", i=NIC)
              for k in range(2)]
        nall = A1[0:1, 8192:16384]
        cos_v = F3[:, 0:4096].rearrange("p (i d) -> p i d", i=32)
        sinr_v = F3[:, 4096:8192].rearrange("p (i d) -> p i d", i=32)
        xo = [F3[:, 8192:12288], F3[:, 12288:16384]]
        scbuf = F3[:, 12288:16384]
        idf_v = F3[:, 8192:8320]
        rps = F3[:, 8320:9600]
        ytile = F3[:, 4096:6144]
        ubf = B2[:, 24576:26624]
        ibf = B2[:, 26624:28672]
        sq_bf = B2[:, 0:4096]

        S.emit("vector", lambda e: e.memset(onesb[:, :], 1.0), inc="s_dve")
        S.emit("vector", lambda e: e.memset(onesf[:, :], 1.0), inc="s_dve")
        S.emit("vector", lambda e: e.memset(ncol[:, :], -EXP_C), inc="s_dve")
        S.emit("vector", lambda e: e.memset(epscol[:, :], EPS_LN), inc="s_dve")
        S.emit("vector", lambda e: e.memset(absacc[:, :], 0.0), inc="s_dve")
        S.emit("vector", lambda e: e.memset(scw[0:1, :], 0.0), inc="s_dve")
        S.emit("vector", lambda e: e.memset(ln2s[:, 0:16], 0.0), inc="s_dve")
        v_consts = S.v("s_dve")
        S.dma("sync", idf_v, idf)
        ld_idf = S.v("dq_sync")

        xld = {}
        for tc in range(4):
            S.dma("gpsimd", xb[tc], x_pm[tc].rearrange("p (i t) -> p i t", i=NIC),
                  sem="d_xb%d" % tc)
            xld[tc] = S.v("d_xb%d" % tc)

        last_sq = 0
        for i in range(8):
            buf = xo[1]
            w = [("s_dve", S.v("s_dve")), ("s_act", S.v("s_act"))] if i >= 1 else []
            ldv = S.dma("sync", buf[:, 0:4096], x_own[i * 128:(i + 1) * 128, :], waits=w, sem="d_xo")
            S.emit("vector",
                   lambda e, buf=buf, i=i: e.reduce_sum(stat8[:, i:i + 1], buf[:, 0:4096], axis=AX.X),
                   waits=[("d_xo", ldv)], inc="s_dve")
            S.emit("scalar",
                   lambda e, buf=buf: e.activation(sq_bf, buf[:, 0:4096], AF.Square),
                   waits=[("d_xo", ldv), ("s_dve", last_sq)], inc="s_act")
            S.emit("vector",
                   lambda e, i=i: e.reduce_sum(stat8[:, 8 + i:9 + i], sq_bf, axis=AX.X),
                   waits=[("s_act", S.v("s_act"))], inc="s_dve")
            last_sq = S.v("s_dve")
        scr_free = [("s_dve", S.v("s_dve")), ("s_act", S.v("s_act"))]
        nr_v = stat8[:, 16:32].rearrange("p (i c) -> p i c", c=2)
        S.emit("vector", lambda e: e.tensor_scalar(nr_v[:, :, 0], stat8[:, 0:8], -1.0 / DM, None, ALU.mult),
               waits=[("s_dve", S.v("s_dve"))], inc="s_dve")
        S.emit("vector", lambda e: e.tensor_scalar(stat8[:, 8:16], stat8[:, 8:16], 1.0 / DM, None, ALU.mult),
               waits=[("s_dve", S.v("s_dve"))], inc="s_dve")
        S.emit("vector", lambda e: e.tensor_tensor(stat8[:, 0:8], nr_v[:, :, 0], nr_v[:, :, 0], ALU.mult),
               waits=[("s_dve", S.v("s_dve"))], inc="s_dve")
        S.emit("vector", lambda e: e.tensor_tensor(stat8[:, 8:16], stat8[:, 8:16], stat8[:, 0:8], ALU.subtract),
               waits=[("s_dve", S.v("s_dve"))], inc="s_dve")
        v_var = S.v("s_dve")
        S.emit("scalar", lambda e: e.activation(stat8[:, 0:8], stat8[:, 8:16], AF.Ln, bias=epscol[:, 0:1]),
               waits=[("s_dve", v_var)], inc="s_act")
        S.emit("scalar", lambda e: e.activation(stat8[:, 8:16], stat8[:, 0:8], AF.Exp, scale=-0.5),
               waits=[("s_act", S.v("s_act"))], inc="s_act")
        v_stats0 = S.v("s_act")
        S.emit("vector", lambda e: e.tensor_scalar(nr_v[:, :, 1], stat8[:, 8:16], 1.0, None, ALU.mult),
               waits=[("s_act", v_stats0)], inc="s_dve")
        v_stats = ("s_dve", S.v("s_dve"))
        S.dma("sync", stats_in.ap().rearrange("(i p) c -> p i c", p=128),
              stat8[:, 16:32].rearrange("p (i c) -> p i c", c=2),
              waits=[v_stats])
        st_stats = S.v("dq_sync")
        S.emit("gpsimd",
               lambda e: e.collective_compute("AllGather", ALU.bypass,
                                              replica_groups=[list(range(R))],
                                              ins=[stats_in.ap().opt()],
                                              outs=[stats_ag.ap().opt()]),
               waits=[("dq_sync", st_stats)], inc="s_cc")
        cc_stats = S.v("s_cc")
        S.dma("sync", stag[:, :].rearrange("p (i c) -> p i c", c=2),
              stats_ag.ap().rearrange("(i p) c -> p i c", p=128),
              waits=[("s_cc", cc_stats)])
        ld_stag = S.v("dq_sync")
        if debug:
            S.dma("sync", stats_ag_d, stats_ag.ap(), waits=[("s_cc", cc_stats)], sem="d_dbg")
            S.dma("sync", stat8_d, stat8[:, :], waits=[("s_cc", cc_stats)], sem="d_dbg")
        stag_v = stag[:, :].rearrange("p (i c) -> p i c", c=2)
        S.emit("tensor",
               lambda e: e.transpose(P0[0:64, 0:128], stag_v[:, :, 0], idf_v),
               waits=[("dq_sync", ld_stag), ("dq_sync", ld_idf)], inc="s_pe")
        S.emit("vector", lambda e: e.tensor_copy(n64[:, :], P0[0:64, 0:128]),
               waits=[("s_pe", S.v("s_pe"))], inc="s_dve")
        v_n64 = S.v("s_dve")
        S.dma("sync", nrow_dram, n64[:, :], waits=[("s_dve", v_n64)], sem="d_n")
        st_nrow = S.v("d_n")
        S.dma("sync", nall, nrow_dram.rearrange("(o a) b -> o (a b)", o=1),
              waits=[("d_n", st_nrow)], sem="d_n")
        ld_nall = S.v("d_n")

        S.dma("gpsimd", wq_v, wqkv.rearrange("(i p) o -> p i o", p=128),
              waits=[v_stats])
        ld_wqkv = S.v("dq_gpsimd")

        for ic in range(NIC):
            w = [("dq_gpsimd", ld_wqkv), ("s_dve", v_consts)] if ic == 0 else []
            S.emit("tensor",
                   lambda e, ic=ic: e.matmul(P1[0:1, 0:512], onesb[:, :], wq_v[:, ic, 0:512],
                                             start=(ic == 0), stop=(ic == NIC - 1)),
                   waits=w)
            S.emit("tensor",
                   lambda e, ic=ic: e.matmul(P1[0:1, 512:768], onesb[:, :], wq_v[:, ic, 512:768],
                                             start=(ic == 0), stop=(ic == NIC - 1)),
                   inc=("s_pe" if ic == NIC - 1 else None))
        v_w1mm = S.v("s_pe")
        S.emit("scalar", lambda e: e.copy(w1row[0:1, :], P1[0:1, 0:768]),
               waits=[("s_pe", v_w1mm)], inc="s_act")
        v_w1 = S.v("s_act")

        S.dma("sync", cos_v, cosT.rearrange("(i p) d -> p i d", p=128))
        S.dma("sync", sinr_v, sinrT.rearrange("(i p) d -> p i d", p=128))
        ld_tabs = S.v("dq_sync")

        slabs = []
        for wi, wsrc in enumerate((wg_s, wu_s, wd_s)):
            for c0 in range(0, 44032, 2048):
                slabs.append((wi, wsrc, c0, min(c0 + 2048, 44032)))
        scbufs = [scbuf[:, 0:2048], scbuf[:, 2048:4096]]
        sl_dve = {}

        def emit_scale_slab(k):
            wi, wsrc, c0, c1 = slabs[k]
            scbuf_k = scbufs[k % 2]
            scsem = "d_sc%d" % (k % 2)
            ldv = S.dma("scalar", scbuf_k[:, 0:c1 - c0], wsrc[:, c0:c1],
                        waits=(scr_free if k < 2 else [("s_dve", sl_dve.get(k - 2, 0))]),
                        sem=scsem)
            ldv = S.v(scsem)
            for a in range(0, c1 - c0, 512):
                S.emit("vector",
                       lambda e, a=a, scbuf_k=scbuf_k: e.reduce_sum(
                           absacc[:, 3:4], scbuf_k[:, a:a + 512],
                           axis=AX.X, apply_absolute_value=True),
                       waits=([(scsem, ldv)] if a == 0 else [("s_dve", S.v("s_dve"))]),
                       inc="s_dve")
                S.emit("vector",
                       lambda e, wi=wi: e.tensor_tensor(absacc[:, wi:wi + 1], absacc[:, wi:wi + 1],
                                                        absacc[:, 3:4], ALU.add),
                       waits=[("s_dve", S.v("s_dve"))], inc="s_dve")
            sl_dve[k] = S.v("s_dve")

        scale_state = {}

        def emit_scale_finish():
            S.emit("tensor", lambda e: e.transpose(P2[0:4, 0:128], absacc[:, 0:4], idf_v),
                   waits=[("s_dve", S.v("s_dve")), ("dq_sync", ld_idf)], inc="s_pe")
            S.emit("scalar", lambda e: e.copy(scf[0:4, 0:128], P2[0:4, 0:128]),
                   waits=[("s_pe", S.v("s_pe"))], inc="s_act")
            scale_state["p2done"] = S.v("s_act")
            base = 0
            n = 128
            w = [("s_act", S.v("s_act"))]
            while n > 1:
                h = n // 2
                src = scf[0:4, base:base + n].rearrange("p (a b) -> p a b", b=2)
                S.emit("vector",
                       lambda e, src=src, base=base, n=n, h=h: e.tensor_tensor(
                           scf[0:4, base + n:base + n + h], src[:, :, 0], src[:, :, 1], ALU.add),
                       waits=w + [("s_dve", S.v("s_dve"))], inc="s_dve")
                w = []
                base += n
                n = h
            S.dma("sync", scales_in.ap(), scf[0:4, 254:255], waits=[("s_dve", S.v("s_dve"))],
                  sem="d_thr")
            S.emit("gpsimd",
                   lambda e: e.collective_compute("AllGather", ALU.bypass,
                                                  replica_groups=[list(range(R))],
                                                  ins=[scales_in.ap().opt()],
                                                  outs=[scales_out.ap().opt()]),
                   waits=[("d_thr", S.v("d_thr"))], inc="s_cc")
            S.dma("sync", scr32[0:1, 0:32],
                  scales_out.ap().rearrange("(o a) b -> o (a b)", o=1),
                  waits=[("s_cc", S.v("s_cc"))], sem="d_thr")
            ldt = S.v("d_thr")
            S.emit("vector", lambda e: e.tensor_copy(scw[0:1, 0:3], scr32[0:1, 20:23]),
                   waits=[("d_thr", ldt)], inc="s_dve")
            for j in range(1, 8):
                r = (5 + j) % 8
                S.emit("vector",
                       lambda e, r=r: e.tensor_tensor(scw[0:1, 0:3], scw[0:1, 0:3],
                                                      scr32[0:1, 4 * r:4 * r + 3], ALU.add),
                       waits=[("s_dve", S.v("s_dve"))], inc="s_dve")
            if debug:
                S.emit("vector", lambda e: e.tensor_copy(scw[0:1, 4:7], scw[0:1, 0:3]),
                       waits=[("s_dve", S.v("s_dve"))], inc="s_dve")
                S.dma("sync", scales_d, scw[0:1, 0:8], waits=[("s_dve", S.v("s_dve"))], sem="d_dbg")
            S.emit("vector", lambda e: e.tensor_scalar(scw[0:1, 0:3], scw[0:1, 0:3],
                                                       float(np.float32(1.0) / np.float32(NW)), None, ALU.mult),
                   waits=[("s_dve", S.v("s_dve"))], inc="s_dve")
            S.emit("vector", lambda e: e.tensor_scalar(scw[0:1, 4:7], scw[0:1, 0:3], EPS_Q, 0.5, ALU.add, ALU.mult),
                   waits=[("s_dve", S.v("s_dve"))], inc="s_dve")
            S.emit("vector", lambda e: e.tensor_tensor(scw[0:1, 3:4], scw[0:1, 0:1], scw[0:1, 1:2], ALU.mult),
                   waits=[("s_dve", S.v("s_dve"))], inc="s_dve")
            S.emit("vector", lambda e: e.tensor_tensor(scw[0:1, 7:8], scw[0:1, 3:4], scw[0:1, 2:3], ALU.mult),
                   waits=[("s_dve", S.v("s_dve"))], inc="s_dve")
            S.dma("sync", thr_dram[0:1, 0:4], scw[0:1, 4:8], waits=[("s_dve", S.v("s_dve"))],
                  sem="d_thr")
            S.dma("sync", thrb[:, 0:4], thr_dram[0:1, 0:4].partition_broadcast(128),
                  waits=[("d_thr", S.v("d_thr"))], sem="d_thr")
            S.emit("vector", lambda e: e.tensor_scalar(thrb[:, 4:8], thrb[:, 0:4], -1.0, None, ALU.mult),
                   waits=[("d_thr", S.v("d_thr"))], inc="s_dve")
            scale_state["thr"] = S.v("s_dve")

        pe_tc = {}
        dve_rope = {}
        spill_v = {}
        n_scale = 0
        for tc in range(NTC):
            par = tc % 2
            b4 = tc % 4
            P = PQ[par]
            if tc >= 4:
                S.dma("gpsimd", xb[b4], x_pm[tc].rearrange("p (i t) -> p i t", i=NIC),
                      waits=[("s_pe", pe_tc.get(tc - 4, 0))], sem="d_xb%d" % b4)
                xld[tc] = S.v("d_xb%d" % b4)
            for ic in range(NIC):
                w = []
                if ic == 0:
                    w = [("d_xb%d" % b4, xld[tc]), ("d_n", ld_nall), ("s_act", v_w1),
                         ("s_dve", dve_rope.get(tc - 2, 0))]
                S.emit("tensor",
                       lambda e, ic=ic, P=P, b4=b4: e.matmul(
                           P[:, 0:512], xb[b4][:, ic, :], wq_v[:, ic, 0:512],
                           start=(ic == 0), stop=False),
                       waits=w)
                S.emit("tensor",
                       lambda e, ic=ic, P=P, b4=b4: e.matmul(
                           P[:, 512:768], xb[b4][:, ic, :], wq_v[:, ic, 512:768],
                           start=(ic == 0), stop=False))
            S.emit("tensor",
                   lambda e, P=P, tc=tc: e.matmul(P[:, 0:512], nall[0:1, tc * 128:(tc + 1) * 128],
                                                  w1row[0:1, 0:512], start=False, stop=True))
            S.emit("tensor",
                   lambda e, P=P, tc=tc: e.matmul(P[:, 512:768], nall[0:1, tc * 128:(tc + 1) * 128],
                                                  w1row[0:1, 512:768], start=False, stop=True),
                   inc="s_pe")
            pe_tc[tc] = S.v("s_pe")

            ci = tc % 32
            rA = rps[:, 0:640]
            rB = rps[:, 640:1280]
            spsem = "d_sp" if par == 0 else "d_sp1"
            wv = [("s_pe", pe_tc[tc]), ("dq_sync", ld_tabs), ("dq_sync", ld_stag),
                  (spsem, spill_v.get(tc - 2, 0)), ("s_dve", dve_rope.get(tc - 1, 0))]
            for blk in range(5):
                c0 = blk * 128
                S.emit("vector",
                       lambda e, P=P, c0=c0, ci=ci: e.tensor_tensor(
                           rps[:, c0:c0 + 128], P[:, c0:c0 + 128], cos_v[:, ci, :], ALU.mult),
                       waits=(wv if blk == 0 else ()))
                S.emit("vector",
                       lambda e, P=P, c0=c0, ci=ci: e.tensor_tensor(
                           rps[:, 640 + c0:640 + c0 + 128],
                           P[:, c0:c0 + 128].rearrange("p (g c) -> p g c", g=2)[:, ::-1, :],
                           sinr_v[:, ci, :].rearrange("p (g c) -> p g c", g=2),
                           ALU.mult),
                       inc=("s_dve" if blk == 4 else None))
            S.emit("vector", lambda e: e.tensor_tensor(rA, rA, rB, ALU.add),
                   waits=[("s_dve", S.v("s_dve"))], inc="s_dve")
            S.emit("vector",
                   lambda e, tc=tc, par=par: e.tensor_scalar(
                       outb[:, par * 768:par * 768 + 640], rA,
                       stag_v[:, tc, 1:2], None, ALU.mult),
                   waits=[("s_dve", S.v("s_dve"))])
            S.emit("vector",
                   lambda e, tc=tc, par=par, P=P: e.tensor_scalar(
                       outb[:, par * 768 + 640:par * 768 + 768], P[:, 640:768],
                       stag_v[:, tc, 1:2], None, ALU.mult),
                   inc="s_dve")
            dve_rope[tc] = S.v("s_dve")
            S.dma("sync", qkv_spill[tc * 128:(tc + 1) * 128, :], outb[:, par * 768:(par + 1) * 768],
                  waits=[("s_dve", dve_rope[tc])], sem=spsem)
            spill_v[tc] = S.v(spsem)
            if n_scale < len(slabs):
                emit_scale_slab(n_scale)
                n_scale += 1
        while n_scale < len(slabs):
            emit_scale_slab(n_scale)
            n_scale += 1
        emit_scale_finish()
        spill_done = [("d_sp", spill_v[NTC - 2]), ("d_sp1", spill_v[NTC - 1])]
        pe_A_done = pe_tc[NTC - 1]

        k_sb = A1[:, 0:8192]
        v_sb = A1[:, 8192:16640]
        v_sb3 = A1[:, 8192:16640].rearrange("p (g c) -> p g c", g=64)
        m_bf = A1[:, 16640:18688]
        qsl = A1[:, 18688:19712]
        ebuf = A1[:, 19712:21760]
        ostg = A1[:, 21760:25856]

        S.emit("vector", lambda e: e.memset(v_sb[:, :], 1.0),
               waits=[("s_pe", pe_A_done)], inc="s_dve")
        v_vones = S.v("s_dve")
        ld_mask = S.dma("sync", F3[:, 0:2048], maskT, waits=[("s_dve", dve_rope[NTC - 1])],
                        sem="d_msk")
        S.emit("vector", lambda e: e.tensor_copy(m_bf[:, :], F3[:, 0:2048]),
               waits=[("d_msk", ld_mask)], inc="s_dve")
        v_mask = S.v("s_dve")

        for j in range(16):
            S.dma("sync", k_sb[:, j * 512:(j + 1) * 512],
                  qkv_spill[j * 512:(j + 1) * 512, 512:640],
                  waits=spill_done + [("s_dve", v_vones)], transpose=True, sem="d_tr")
        ld_k = S.v("d_tr")
        for g in range(64):
            S.dma("sync", v_sb3[:, g, 0:128],
                  qkv_spill[g * 128:(g + 1) * 128, 640:768],
                  waits=[("s_dve", v_vones)] + spill_done)
        ld_v = S.v("dq_sync")

        wo_buf = [B2[:, 0:16384].rearrange("p (h o) -> p h o", h=32),
                  B2[:, 16384:32768].rearrange("p (h o) -> p h o", h=32)]
        wo_ld = {}
        for oc in range(2):
            S.dma("gpsimd", wo_buf[oc],
                  woT[:, oc * 512:(oc + 1) * 512].rearrange("(h p) o -> p h o", p=128),
                  waits=[("s_pe", pe_A_done)], sem="d_wo%d" % oc)
            wo_ld[oc] = S.v("d_wo%d" % oc)

        groups = [(b, qc, h) for b in range(NBATCH) for qc in range(8) for h in range(NHL)]
        pairs = []
        for gi, (b, qc, h) in enumerate(groups):
            npairs = 2 * (qc + 1)
            for j in range(npairs):
                pairs.append((gi, b, qc, h, j, npairs))
        NPAIR = len(pairs)

        qld = {}
        pe_st_group = {}
        exp_done = {}
        av_done = {}
        ev_po = {"v": 0}
        st_ostg = {0: 0, 1: 0}
        evac_cnt = {}

        def emit_qload(gi):
            b, qc, h = groups[gi]
            gpar = gi % 2
            qsem = "d_q" if gpar == 0 else "d_q1"
            S.dma("sync", qsl[:, gpar * 512:(gpar + 1) * 512],
                  qkv_spill[b * SEQ + qc * 512: b * SEQ + qc * 512 + 512, h * 128:(h + 1) * 128],
                  waits=[("s_pe", pe_st_group.get(gi - 2, 0)), ("d_tr", ld_k), ("dq_sync", ld_v)],
                  transpose=True, sem=qsem)
            qld[gi] = (qsem, S.v(qsem))

        def emit_st(p):
            gi, b, qc, h, j, npairs = pairs[p]
            spar = p % 2
            SP = PQ[spar]
            gpar = gi % 2
            qv = qsl[:, gpar * 512:(gpar + 1) * 512]
            w = [("s_act", exp_done.get(p - 2, 0)), qld[gi]]
            if p <= 1:
                w.append(("s_act", v_w1))
                w.append(("s_dve", dve_rope[NTC - 1]))
            kb0 = 2 * j
            S.emit("tensor",
                   lambda e, SP=SP, b=b, kb0=kb0, qv=qv: e.matmul(
                       SP[:, 0:512], k_sb[:, b * SEQ + kb0 * 128: b * SEQ + kb0 * 128 + 128],
                       qv, start=True, stop=True),
                   waits=w)
            S.emit("tensor",
                   lambda e, SP=SP, b=b, kb0=kb0, qv=qv: e.matmul(
                       SP[:, 512:1024], k_sb[:, b * SEQ + kb0 * 128 + 128: b * SEQ + kb0 * 128 + 256],
                       qv, start=True, stop=True),
                   inc="s_pe")
            pairs_pe[p] = S.v("s_pe")
            pe_st_group[gi] = S.v("s_pe")

        def emit_mask_exp(p):
            gi, b, qc, h, j, npairs = pairs[p]
            spar = p % 2
            SP = PQ[spar]
            diag = (j >= npairs - 2)
            wexp = [("s_pe", pairs_pe[p]), ("s_pe", av_done.get(p - 2, 0))]
            if diag:
                v0 = 2 * (j - (npairs - 2))
                S.emit("vector",
                       lambda e, SP=SP, v0=v0: e.tensor_tensor(
                           SP[:, 0:1024], SP[:, 0:1024], m_bf[:, v0 * 512: v0 * 512 + 1024], ALU.add),
                       waits=[("s_pe", pairs_pe[p]), ("s_dve", v_mask)], inc="s_dve")
                wexp.append(("s_dve", S.v("s_dve")))
            S.emit("scalar",
                   lambda e, SP=SP, spar=spar: e.activation(
                       ebuf[:, spar * 1024:(spar + 1) * 1024], SP[:, 0:1024], AF.Exp,
                       bias=ncol[:, 0:1], scale=ISQ),
                   waits=wexp, inc="s_act")
            exp_done[p] = S.v("s_act")

        def emit_av(p):
            gi, b, qc, h, j, npairs = pairs[p]
            spar = p % 2
            kb0 = 2 * j
            first = (j == 0)
            last = (j == npairs - 1)
            for kk in range(2):
                kbg = b * 32 + kb0 + kk
                for qs in range(4):
                    slot = PV[qs // 2]
                    c0 = (qs % 2) * 512
                    w = []
                    if first and kk == 0 and qs == 0:
                        w = [("s_act", exp_done[p]), ("s_dve", ev_po["v"])]
                    elif kk == 0 and qs == 0:
                        w = [("s_act", exp_done[p])]
                    S.emit("tensor",
                           lambda e, slot=slot, c0=c0, spar=spar, kk=kk, qs=qs, kbg=kbg,
                                  first=first, last=last: e.matmul(
                               slot[:, c0:c0 + 129],
                               ebuf[:, spar * 1024 + kk * 512 + qs * 128: spar * 1024 + kk * 512 + qs * 128 + 128],
                               v_sb3[:, kbg, 0:129],
                               start=(first and kk == 0), stop=(last and kk == 1)),
                           waits=w,
                           inc=("s_pe" if (kk == 1 and qs == 3) else None))
            av_done[p] = S.v("s_pe")

        def emit_evac(p):
            gi, b, qc, h, j, npairs = pairs[p]
            opar = (b * 8 + qc) % 2
            osem = "d_ost" if opar == 0 else "d_ost1"
            w = [("s_pe", av_done[p]), (osem, st_ostg[opar]), ("s_dve", ev_po["v"])]
            for qs in range(4):
                slot = PV[qs // 2]
                c0 = (qs % 2) * 512
                S.emit("vector",
                       lambda e, slot=slot, c0=c0, qs=qs: e.reciprocal(
                           zc[:, qs:qs + 1], slot[:, c0 + 128:c0 + 129]),
                       waits=(w if qs == 0 else ()),
                       inc=("s_dve" if qs == 3 else None))
            vrec = S.v("s_dve")
            for qs in range(4):
                slot = PV[qs // 2]
                c0 = (qs % 2) * 512
                S.emit("vector",
                       lambda e, slot=slot, c0=c0, qs=qs, opar=opar, h=h: e.tensor_scalar(
                           ostg[:, opar * 2048 + qs * 512 + h * 128: opar * 2048 + qs * 512 + h * 128 + 128],
                           slot[:, c0:c0 + 128], zc[:, qs:qs + 1], None, ALU.mult),
                       waits=([("s_dve", vrec)] if qs == 0 else ()), inc="s_dve")
            ev_po["v"] = S.v("s_dve")
            evac_cnt[(b, qc)] = S.v("s_dve")

        pairs_pe = {}
        p = 0
        cur_g = -1
        emit_qload(0)
        emit_qload(1)
        for p in range(NPAIR):
            gi, b, qc, h, j, npairs = pairs[p]
            if j == npairs - 1 and gi + 2 < len(groups):
                emit_qload(gi + 2)
            if p == 0:
                emit_st(0)
            if p + 1 < NPAIR:
                emit_st(p + 1)
            emit_mask_exp(p)
            emit_av(p)
            if j == npairs - 1:
                emit_evac(p)
                if h == NHL - 1:
                    opar = (b * 8 + qc) % 2
                    t0 = b * SEQ + qc * 512
                    osem2 = "d_ost" if opar == 0 else "d_ost1"
                    S.dma("sync",
                          a2a_in.ap()[t0:t0 + 512, :].rearrange("(qs p) o -> p qs o", p=128),
                          ostg[:, opar * 2048:(opar + 1) * 2048].rearrange("p (qs o) -> p qs o", qs=4),
                          waits=[("s_dve", evac_cnt[(b, qc)])], sem=osem2)
                    st_ostg[opar] = S.v(osem2)

        a2a_ready = [("d_ost", S.v("d_ost")), ("d_ost1", S.v("d_ost1"))]
        if debug:
            S.dma("sync", a2a_in_d, a2a_in.ap(), waits=a2a_ready, sem="d_dbg")
        S.emit("gpsimd",
               lambda e: e.collective_compute("AllToAll", ALU.bypass,
                                              replica_groups=[list(range(R))],
                                              ins=[a2a_in.ap().opt()],
                                              outs=[a2a_out.ap().opt()]),
               waits=a2a_ready, inc="s_cc")
        cc_a2a = S.v("s_cc")

        ofm = A1[:, :].rearrange("p (h t) -> p h t", h=32)
        for hd_c in range(32):
            r_ = hd_c // 4
            lb = hd_c % 4
            S.dma("sync", ofm[:, hd_c, :],
                  a2a_out.ap()[r_ * 1024:(r_ + 1) * 1024, lb * 128:(lb + 1) * 128],
                  waits=[("s_cc", cc_a2a)], transpose=True, sem="d_tr")
        ld_ofm = S.v("d_tr")

        xo3 = [xo[k].rearrange("p (t o) -> p t o", t=8) for k in range(2)]
        ln2_rq = [0]
        sq_st = {}
        pe_C = {}
        dve_C = {}
        xo_ld = {}
        st_x1 = {}
        for oc in range(8):
            par = oc % 2
            xcsem = "d_xc" if par == 0 else "d_xc1"
            if oc >= 2:
                S.dma("gpsimd", wo_buf[par],
                      woT[:, oc * 512:(oc + 1) * 512].rearrange("(h p) o -> p h o", p=128),
                      waits=[("s_pe", pe_C.get(oc - 2, 0))], sem="d_wo%d" % par)
                wo_ld[oc] = S.v("d_wo%d" % par)
            S.dma("sync", xo3[par][:, :, :],
                  x_own[:, oc * 512:(oc + 1) * 512].rearrange("(t p) o -> p t o", p=128),
                  waits=[("s_dve", dve_C.get(oc - 2, 0)), ("s_dve", scale_state["thr"])], sem=xcsem)
            xo_ld[oc] = S.v(xcsem)
            for tc in range(8):
                slot = [P0, P1, P2, P3][tc // 2]
                c0 = (tc % 2) * 512
                for hd_c in range(32):
                    w = []
                    if hd_c == 0:
                        w = [("d_tr", ld_ofm), ("d_wo%d" % par, wo_ld[oc]),
                             ("s_dve", dve_C.get(oc - 1, 0)),
                             ("s_act", scale_state["p2done"])]
                    S.emit("tensor",
                           lambda e, slot=slot, c0=c0, hd_c=hd_c, tc=tc, par=par: e.matmul(
                               slot[:, c0:c0 + 512], ofm[:, hd_c, tc * 128:(tc + 1) * 128],
                               wo_buf[par][:, hd_c, :],
                               start=(hd_c == 0), stop=(hd_c == 31)),
                           waits=w,
                           inc=("s_pe" if hd_c == 31 else None))
                pe_C[oc] = S.v("s_pe")
                ys = tc % 4
                x1sem = ["d_x1s", "d_x1s1", "d_x1s2", "d_x1s3"][ys]
                S.emit("vector",
                       lambda e, slot=slot, c0=c0, tc=tc, par=par, ys=ys: e.tensor_tensor(
                           ytile[:, ys * 512:(ys + 1) * 512], slot[:, c0:c0 + 512],
                           xo3[par][:, tc, :], ALU.add),
                       waits=[("s_pe", pe_C[oc]), (xcsem, xo_ld[oc]),
                              ("s_act", sq_st.get((oc, tc - 4), sq_st.get((oc - 1, tc + 4), 0))),
                              (x1sem,
                               st_x1.get((oc, tc - 4), st_x1.get((oc - 1, tc + 4), 0)))],
                       inc="s_dve")
                v_ev = S.v("s_dve")
                dve_C[oc] = v_ev
                S.dma("sync", x1_dram[tc * 128:(tc + 1) * 128, oc * 512:(oc + 1) * 512],
                      ytile[:, ys * 512:(ys + 1) * 512],
                      waits=[("s_dve", v_ev)], sem=x1sem)
                st_x1[(oc, tc)] = S.v(x1sem)
                S.emit("scalar",
                       lambda e, ys=ys: e.activation(F3[:, 6144:6656],
                                                     ytile[:, ys * 512:(ys + 1) * 512], AF.Square),
                       waits=[("s_dve", v_ev), ("s_dve", ln2_rq[0])], inc="s_act")
                sq_v = S.v("s_act")
                sq_st[(oc, tc)] = sq_v
                S.emit("vector",
                       lambda e, ys=ys: e.reduce_sum(ln2s[:, 32:33],
                                                     ytile[:, ys * 512:(ys + 1) * 512], axis=AX.X),
                       waits=[("s_dve", S.v("s_dve"))], inc="s_dve")
                S.emit("vector",
                       lambda e, tc=tc: e.tensor_tensor(ln2s[:, tc:tc + 1], ln2s[:, tc:tc + 1],
                                                        ln2s[:, 32:33], ALU.add),
                       waits=[("s_dve", S.v("s_dve"))], inc="s_dve")
                S.emit("vector",
                       lambda e: e.reduce_sum(ln2s[:, 33:34], F3[:, 6144:6656], axis=AX.X),
                       waits=[("s_dve", S.v("s_dve")), ("s_act", sq_v)], inc="s_dve")
                ln2_rq[0] = S.v("s_dve")
                S.emit("vector",
                       lambda e, tc=tc: e.tensor_tensor(ln2s[:, 8 + tc:9 + tc], ln2s[:, 8 + tc:9 + tc],
                                                        ln2s[:, 33:34], ALU.add),
                       waits=[("s_dve", S.v("s_dve"))], inc="s_dve")
                dve_C[oc] = S.v("s_dve")
        x1_done = [(nm, S.v(nm)) for nm in ("d_x1s", "d_x1s1", "d_x1s2", "d_x1s3")]
        S.emit("vector", lambda e: e.tensor_scalar(ln2s[:, 16:24], ln2s[:, 0:8], -1.0 / DM, None, ALU.mult),
               waits=[("s_dve", S.v("s_dve"))], inc="s_dve")
        S.emit("vector", lambda e: e.tensor_scalar(ln2s[:, 32:40], ln2s[:, 8:16], 1.0 / DM, None, ALU.mult),
               waits=[("s_dve", S.v("s_dve"))], inc="s_dve")
        S.emit("vector", lambda e: e.tensor_tensor(ln2s[:, 8:16], ln2s[:, 16:24], ln2s[:, 16:24], ALU.mult),
               waits=[("s_dve", S.v("s_dve"))], inc="s_dve")
        S.emit("vector", lambda e: e.tensor_tensor(ln2s[:, 32:40], ln2s[:, 32:40], ln2s[:, 8:16], ALU.subtract),
               waits=[("s_dve", S.v("s_dve"))], inc="s_dve")
        S.emit("scalar", lambda e: e.activation(ln2s[:, 8:16], ln2s[:, 32:40], AF.Ln, bias=epscol[:, 0:1]),
               waits=[("s_dve", S.v("s_dve"))], inc="s_act")
        S.emit("scalar", lambda e: e.activation(ln2s[:, 24:32], ln2s[:, 8:16], AF.Exp, scale=-0.5),
               waits=[("s_act", S.v("s_act"))], inc="s_act")
        r2all_done = S.v("s_act")

        x1n_t = [B2[:, 16384 + k * 4096:16384 + (k + 1) * 4096] for k in range(2)]
        st_x1n = {}
        act_D = {}
        for tc in range(8):
            par = tc % 2
            xdsem = "d_xd%d" % par
            ldv = S.dma("sync", xo[par][:, 0:4096], x1_dram[tc * 128:(tc + 1) * 128, :],
                        waits=x1_done + [act_D.get(tc - 2, ("s_dve", 0))], sem=xdsem)
            ldv = S.v(xdsem)
            x1nsem = "d_x1n" if par == 0 else "d_x1n1"
            S.emit("vector",
                   lambda e, par=par, tc=tc: e.tensor_scalar(
                       x1n_t[par], xo[par][:, 0:4096],
                       ln2s[:, 16 + tc:17 + tc], ln2s[:, 24 + tc:25 + tc], ALU.add, ALU.mult),
                   waits=[(xdsem, ldv), ("s_act", r2all_done),
                          (x1nsem, st_x1n.get(tc - 2, 0)),
                          act_D.get(tc - 1, ("s_dve", 0))], inc="s_dve")
            act_D[tc] = ("s_dve", S.v("s_dve"))
            S.dma("sync", x1n_dram[tc * 128:(tc + 1) * 128, :], x1n_t[par],
                  waits=[act_D[tc]], sem=x1nsem)
            st_x1n[tc] = S.v(x1nsem)
        x1n_done = [("d_x1n", S.v("d_x1n")), ("d_x1n1", S.v("d_x1n1"))]

        xn = A1[:, :].rearrange("p (i t) -> p i t", i=32)
        for ic in range(NIC):
            S.dma("sync", xn[:, ic, :], x1n_dram[0:TOWN, ic * 128:(ic + 1) * 128],
                  waits=x1n_done + [("s_pe", pe_C[7])], transpose=True, sem="d_tr")
        ld_xn = S.v("d_tr")

        qg = [B2[:, 0:4096], B2[:, 4096:8192]]
        qu = [B2[:, 8192:12288], B2[:, 12288:16384]]
        scr1 = B2[:, 16384:20480]
        scr2 = B2[:, 20480:24576]
        wgf = [F3[:, 0:4096], F3[:, 4096:8192]]
        wuf = [F3[:, 8192:12288], F3[:, 12288:16384]]
        pe_E = {}
        dve_E = {}
        act_E = {}
        ib_st = {}
        ew_ld = {}

        def emit_E_load(j):
            par = j % 2
            wsem = "d_w" if par == 0 else "d_w1"
            S.dma("sync", wgf[par], wg_tiles[j].rearrange("p i g -> p (i g)"),
                  waits=[("s_dve", dve_E.get(j - 2, 0))] + (x1n_done if j < 2 else []),
                  sem=wsem)
            S.dma("sync", wuf[par], wu_tiles[j].rearrange("p i g -> p (i g)"), sem=wsem)
            ew_ld[j] = (wsem, S.v(wsem))

        def emit_E_quant(j):
            par = j % 2
            wq2 = [ew_ld[j], ("s_dve", scale_state["thr"]),
                   ("s_pe", pe_E.get(j - 2, 0)), ("s_dve", dve_E.get(j - 1, 0))]
            S.emit("vector", lambda e, par=par: e.tensor_scalar(scr1, wgf[par], thrb[:, 0:1], None, ALU.is_gt),
                   waits=wq2)
            S.emit("vector", lambda e, par=par: e.tensor_scalar(scr2, wgf[par], thrb[:, 4:5], None, ALU.is_lt),
                   inc="s_dve")
            S.emit("vector", lambda e, par=par: e.tensor_tensor(qg[par], scr1, scr2, ALU.subtract),
                   waits=[("s_dve", S.v("s_dve"))], inc="s_dve")
            vqg = S.v("s_dve")
            S.emit("vector", lambda e, par=par: e.tensor_scalar(scr1, wuf[par], thrb[:, 1:2], None, ALU.is_gt),
                   waits=[ew_ld[j], ("s_dve", vqg)])
            S.emit("vector", lambda e, par=par: e.tensor_scalar(scr2, wuf[par], thrb[:, 5:6], None, ALU.is_lt),
                   inc="s_dve")
            S.emit("vector", lambda e, par=par: e.tensor_tensor(qu[par], scr1, scr2, ALU.subtract),
                   waits=[("s_dve", S.v("s_dve"))], inc="s_dve")
            dve_E[j] = S.v("s_dve")

        emit_E_load(0)
        emit_E_load(1)
        emit_E_quant(0)
        for j in range(NGC):
            par = j % 2
            if j + 2 < NGC:
                emit_E_load(j + 2)
            if j + 1 < NGC:
                emit_E_quant(j + 1)
            vq = dve_E[j]
            PG = PQ[par]
            PU = PV[par]
            for ic in range(NIC):
                w = []
                if ic == 0:
                    w = [("s_dve", vq), ("d_tr", ld_xn),
                         ("s_dve", dve_E.get(("i", j - 2), 0)),
                         ("s_act", act_E.get(j - 2, 0))]
                S.emit("tensor",
                       lambda e, PG=PG, par=par, ic=ic: e.matmul(
                           PG[:, 0:512], qg[par][:, ic * 128:(ic + 1) * 128],
                           xn[:, ic, 0:512], start=(ic == 0), stop=(ic == NIC - 1)),
                       waits=w)
                S.emit("tensor",
                       lambda e, PG=PG, par=par, ic=ic: e.matmul(
                           PG[:, 512:1024], qg[par][:, ic * 128:(ic + 1) * 128],
                           xn[:, ic, 512:1024], start=(ic == 0), stop=(ic == NIC - 1)))
            for ic in range(NIC):
                S.emit("tensor",
                       lambda e, PU=PU, par=par, ic=ic: e.matmul(
                           PU[:, 0:512], qu[par][:, ic * 128:(ic + 1) * 128],
                           xn[:, ic, 0:512], start=(ic == 0), stop=(ic == NIC - 1)))
                S.emit("tensor",
                       lambda e, PU=PU, par=par, ic=ic: e.matmul(
                           PU[:, 512:1024], qu[par][:, ic * 128:(ic + 1) * 128],
                           xn[:, ic, 512:1024], start=(ic == 0), stop=(ic == NIC - 1)),
                       inc=("s_pe" if ic == NIC - 1 else None))
            pe_E[j] = S.v("s_pe")
            S.emit("scalar", lambda e, PU=PU, par=par: e.copy(ubf[:, par * 1024:(par + 1) * 1024], PU[:, 0:1024]),
                   waits=[("s_pe", pe_E[j]), ("s_dve", dve_E.get(("i", j - 2), 0))], inc="s_act")
            act_E[j] = S.v("s_act")
            S.emit("vector",
                   lambda e, PG=PG, par=par: e.tensor_tensor(
                       ibf[:, par * 1024:(par + 1) * 1024], PG[:, 0:1024],
                       ubf[:, par * 1024:(par + 1) * 1024], ALU.mult),
                   waits=[("s_act", act_E[j]),
                          ("d_ib%d" % par, ib_st.get(j - 2, 0))],
                   inc="s_dve")
            dve_E[("i", j)] = S.v("s_dve")
            S.dma("scalar", inter_dram[j * 128:(j + 1) * 128, :], ibf[:, par * 1024:(par + 1) * 1024],
                  waits=[("s_dve", dve_E[("i", j)])], sem="d_ib%d" % par)
            ib_st[j] = S.v("d_ib%d" % par)
        ib_done = [("d_ib0", S.v("d_ib0")), ("d_ib1", S.v("d_ib1"))]
        pe_E_done = pe_E[NGC - 1]

        int_sb4 = [B2[:, 16384 + k * 4096:16384 + (k + 1) * 4096].rearrange("p (i t) -> p i t", i=4)
                   for k in range(4)]
        qd = [B2[:, 8192:10240], B2[:, 10240:12288]]
        fs1 = B2[:, 12288:14336]
        fs2 = B2[:, 14336:16384]
        wdf = [F3[:, 0:2048], F3[:, 2048:4096]]
        x1o = [F3[:, 8192:12288].rearrange("p (t o) -> p t o", t=8),
               F3[:, 12288:16384].rearrange("p (t o) -> p t o", t=8)]
        batches = [(i, min(i + 4, NGC)) for i in range(0, NGC, 4)]
        NB = len(batches)
        NGB = 8 * NB
        last_qd = [0]
        pe_gb = {}
        dve_q = {}
        dve_F = {}
        dve_ev = {}
        st_y = {}
        lw_v = {}
        li_v = {}
        x1_ld = {}
        dve_E_all = dve_E[NGC - 1]
        dve_E_int = dve_E[("i", NGC - 1)]
        act_E_all = act_E[NGC - 1]
        isems = ["d_i", "d_i1", "d_i2", "d_i3"]
        ysems = ["d_ys", "d_ys1", "d_ys2", "d_ys3"]

        def emit_F_load(gb):
            oc, bi = gb // NB, gb % NB
            ic0, ic1 = batches[bi]
            n = ic1 - ic0
            bpar = gb % 2
            sl4 = gb % 4
            wdsem = "d_wd" if bpar == 0 else "d_wd1"
            S.dma("sync", wdf[bpar][:, 0:n * 512].rearrange("p (i o) -> p i o", i=n),
                  wd_tiles[oc, ic0:ic1].rearrange("i p o -> p i o"),
                  waits=[("s_dve", dve_q.get(gb - 2, dve_E_all)),
                         ("s_pe", pe_gb.get(gb - 2, pe_E_done if gb < 2 else 0))],
                  sem=wdsem)
            lw_v[gb] = (wdsem, S.v(wdsem))
            S.dma("sync", int_sb4[sl4][:, 0:n, :],
                  inter_dram[ic0 * 128:ic1 * 128, :].rearrange("(i p) t -> p i t", p=128),
                  waits=ib_done + [("s_pe", pe_gb.get(gb - 4, 0)), ("s_dve", dve_E_all)],
                  sem=isems[sl4])
            li_v[gb] = (isems[sl4], S.v(isems[sl4]))

        def emit_F_quant(gb):
            bpar = gb % 2
            ic0, ic1 = batches[gb % NB]
            n = ic1 - ic0
            S.emit("vector", lambda e, bpar=bpar, n=n: e.tensor_scalar(
                       fs1[:, 0:n * 512], wdf[bpar][:, 0:n * 512], thrb[:, 2:3], None, ALU.is_gt),
                   waits=[lw_v[gb], ("s_dve", scale_state["thr"]),
                          ("s_dve", last_qd[0]),
                          ("s_pe", pe_gb.get(gb - 2, 0))])
            S.emit("vector", lambda e, bpar=bpar, n=n: e.tensor_scalar(
                       fs2[:, 0:n * 512], wdf[bpar][:, 0:n * 512], thrb[:, 6:7], None, ALU.is_lt),
                   inc="s_dve")
            S.emit("vector", lambda e, bpar=bpar, n=n: e.tensor_tensor(
                       qd[bpar][:, 0:n * 512], fs1[:, 0:n * 512], fs2[:, 0:n * 512], ALU.subtract),
                   waits=[("s_dve", S.v("s_dve"))], inc="s_dve")
            dve_q[gb] = S.v("s_dve")
            last_qd[0] = S.v("s_dve")

        def emit_F_evac(oc):
            opar = oc % 2
            xfsem = "d_xf" if opar == 0 else "d_xf1"
            for tc in range(8):
                slot = [P0, P1, P2, P3][tc // 2]
                c0 = (tc % 2) * 512
                ys = tc % 4
                yssem = ysems[ys]
                S.emit("vector",
                       lambda e, slot=slot, c0=c0, opar=opar, tc=tc, ys=ys: e.scalar_tensor_tensor(
                           ytile[:, ys * 512:(ys + 1) * 512], slot[:, c0:c0 + 512],
                           thrb[:, 3:4], x1o[opar][:, tc, :], ALU.mult, ALU.add),
                       waits=[("s_pe", pe_gb[oc * NB + NB - 1]), (xfsem, x1_ld[oc]),
                              (yssem,
                               st_y.get((oc, tc - 4), st_y.get((oc - 1, tc + 4), 0)))],
                       inc="s_dve")
                dve_ev[(oc, tc)] = S.v("s_dve")
                dve_F[oc] = S.v("s_dve")
                S.dma("sync", y[tc * 128:(tc + 1) * 128, oc * 512:(oc + 1) * 512],
                      ytile[:, ys * 512:(ys + 1) * 512],
                      waits=[("s_dve", dve_F[oc])], sem=yssem)
                st_y[(oc, tc)] = S.v(yssem)

        emit_F_load(0)
        emit_F_load(1)
        for gb in range(NGB):
            oc, bi = gb // NB, gb % NB
            if bi == 0:
                xfsem = "d_xf" if oc % 2 == 0 else "d_xf1"
                S.dma("scalar", x1o[oc % 2][:, :, :],
                      x1_dram[:, oc * 512:(oc + 1) * 512].rearrange("(t p) o -> p t o", p=128),
                      waits=[("s_dve", dve_F.get(oc - 2, 0))] + x1n_done +
                            [("s_dve", dve_E_all)], sem=xfsem)
                x1_ld[oc] = S.v(xfsem)
            emit_F_quant(gb)
            if gb + 2 < NGB:
                emit_F_load(gb + 2)
            if bi == 0 and oc > 0:
                emit_F_evac(oc - 1)
            ic0, ic1 = batches[bi]
            n = ic1 - ic0
            bpar = gb % 2
            sl4 = gb % 4
            for i in range(n):
                icg = ic0 + i
                for tc in range(8):
                    slot = [P0, P1, P2, P3][tc // 2]
                    c0 = (tc % 2) * 512
                    w = []
                    if i == 0 and tc == 0:
                        w = [("s_dve", dve_q[gb]), li_v[gb]]
                        if gb == 0:
                            w.append(("s_dve", dve_E_int))
                            w.append(("s_act", act_E_all))
                    if i == 0 and bi == 0 and oc > 0:
                        w = w + [("s_dve", dve_ev[(oc - 1, tc)])]
                    S.emit("tensor",
                           lambda e, slot=slot, c0=c0, sl4=sl4, bpar=bpar, i=i, tc=tc, icg=icg: e.matmul(
                               slot[:, c0:c0 + 512], int_sb4[sl4][:, i, tc * 128:(tc + 1) * 128],
                               qd[bpar][:, i * 512:(i + 1) * 512],
                               start=(icg == 0), stop=(icg == NGC - 1)),
                           waits=w,
                           inc=("s_pe" if (i == n - 1 and tc == 7) else None))
            pe_gb[gb] = S.v("s_pe")
        emit_F_evac(7)

        S.emit("sync", lambda e: None,
               waits=[(nm, S.v(nm)) for nm in sems
                      if nm.startswith("d") and S.v(nm) > 0])

        with allow_ici_all_to_all():
            with nc.Block() as block:
                S.run(block, handles)
    return nc


_NC_CACHE = {}


def _get_nc(debug=False):
    key = bool(debug)
    if key not in _NC_CACHE:
        _NC_CACHE[key] = build_nc(debug=key)
    return _NC_CACHE[key]


def make_in_maps(inputs):
    x = np.asarray(inputs["L0_attn_hidden_states"], dtype=np.float32)
    Wq = np.asarray(inputs["L0_attn_Wq"], dtype=np.float32)
    Wk = np.asarray(inputs["L0_attn_Wk"], dtype=np.float32)
    Wv = np.asarray(inputs["L0_attn_Wv"], dtype=np.float32)
    Wo = np.asarray(inputs["L0_attn_Wo"], dtype=np.float32)
    wg = np.asarray(inputs["L0_ffn_w_gate"], dtype=np.float32)
    wu = np.asarray(inputs["L0_ffn_w_up"], dtype=np.float32)
    wd = np.asarray(inputs["L0_ffn_w_down"], dtype=np.float32)

    xr = x.reshape(NT, DM)
    x_pm = np.ascontiguousarray(
        xr.reshape(NTC, 128, NIC, 128).transpose(0, 3, 2, 1)).reshape(NTC, 128, DM)
    woT = np.ascontiguousarray(Wo.T)
    wg_tiles = np.ascontiguousarray(wg.reshape(NGC, 128, NIC, 128).transpose(0, 3, 2, 1))
    wu_tiles = np.ascontiguousarray(wu.reshape(NGC, 128, NIC, 128).transpose(0, 3, 2, 1))
    wd_tiles = np.ascontiguousarray(wd.reshape(8, 512, NGC, 128).transpose(0, 2, 3, 1))
    wg_flat = wg_tiles.reshape(-1)
    wu_flat = wu_tiles.reshape(-1)
    wd_flat = wd_tiles.reshape(-1)
    nsl = wg_flat.size // R
    cos, sinr = rope_tables()
    masks = causal_masks()
    ident = np.eye(128, dtype=np.float32)

    in_maps = []
    for c in range(R):
        wqkv_c = np.ascontiguousarray(np.concatenate(
            [Wq[512 * c:512 * (c + 1), :].T, Wk[128 * c:128 * (c + 1), :].T,
             Wv[128 * c:128 * (c + 1), :].T], axis=1))
        in_maps.append({
            "x_pm": x_pm,
            "x_own": np.ascontiguousarray(xr[TOWN * c:TOWN * (c + 1), :]),
            "wqkv": wqkv_c,
            "woT": woT,
            "wg_tiles": wg_tiles,
            "wu_tiles": wu_tiles,
            "wd_tiles": wd_tiles,
            "wg_s": np.ascontiguousarray(wg_flat[nsl * c:nsl * (c + 1)].reshape(128, 44032)),
            "wu_s": np.ascontiguousarray(wu_flat[nsl * c:nsl * (c + 1)].reshape(128, 44032)),
            "wd_s": np.ascontiguousarray(wd_flat[nsl * c:nsl * (c + 1)].reshape(128, 44032)),
            "cosT": cos,
            "sinrT": sinr,
            "maskT": masks,
            "idf": ident,
        })
    return in_maps


def kernel(**inputs):
    nc = _get_nc(debug=False)
    in_maps = make_in_maps(inputs)
    res = run_bass_kernel_spmd(nc, in_maps, core_ids=list(range(R)))
    out = np.empty((NT, DM), dtype=np.float32)
    for c in range(R):
        out[TOWN * c:TOWN * (c + 1), :] = res.results[c]["y"]
    return out.reshape(NBATCH, SEQ, DM)


# version 16 (final)
# speedup vs baseline: 1.4626x; 1.0017x over previous
import math
from contextlib import ExitStack

import numpy as np
import ml_dtypes

import concourse.bass as bass
import concourse.mybir as mybir
from concourse.bass_utils import run_bass_kernel_spmd
from concourse.aot_env import allow_ici_all_to_all

F32 = mybir.dt.float32
BF16 = mybir.dt.bfloat16
AF = mybir.ActivationFunctionType
ALU = mybir.AluOpType
AX = mybir.AxisListType
BF = ml_dtypes.bfloat16

R = 8
SEQ = 4096
NBATCH = 2
NT = NBATCH * SEQ
DM = 4096
DI = 11008
NHL = 4
TOWN = NT // R
NTC = NT // 128
NIC = DM // 128
NGC = DI // 128
EPS_LN = 1e-5
EPS_Q = 1e-5
EXP_C = 20.0
ISQ = 1.0 / math.sqrt(128.0)
NEG_BIG = -1.0e30
NW = float(DI * DM)

ENGS = ("tensor", "vector", "scalar", "gpsimd", "sync")


class Sched:

    def __init__(self, sem_names):
        self.cnt = {n: 0 for n in sem_names}
        self.ops = {e: [] for e in ENGS}

    def v(self, sem):
        return self.cnt[sem]

    def emit(self, eng, fn, waits=(), inc=None, n=1):
        waits = tuple((s, v) for (s, v) in waits if v > 0)
        if inc is not None:
            self.cnt[inc] += n
        self.ops[eng].append((fn, waits, inc, n))
        return self.cnt[inc] if inc is not None else None

    def dma(self, eng, out, in_, waits=(), transpose=False, sem=None):
        def fn(e, out=out, in_=in_, transpose=transpose):
            if transpose:
                return e.dma_start(out=out, in_=in_, transpose=True)
            return e.dma_start(out=out, in_=in_)
        return self.emit(eng, fn, waits, sem or ("dq_" + eng), 16)

    def run(self, block, handles):
        for eng in ENGS:
            ops = self.ops[eng]
            def make(eng_ops):
                def body(e):
                    seen = {}
                    for fn, waits, inc, n in eng_ops:
                        for s, v in waits:
                            if seen.get(s, 0) < v:
                                e.wait_ge(handles[s], v)
                                seen[s] = v
                        inst = fn(e)
                        if inc is not None:
                            inst.then_inc(handles[inc], n)
                return body
            getattr(block, eng)(make(ops))


def rope_tables():
    f32 = np.float32
    d = np.arange(0, 128, 2, dtype=f32)
    inv_freq = (f32(1.0) / (f32(500000.0) ** (d / f32(128.0)))).astype(f32)
    wavelen = f32(2.0 * math.pi) / inv_freq
    smooth = (f32(8192.0) / wavelen - f32(1.0)) / f32(3.0)
    smooth = np.clip(smooth, f32(0.0), f32(1.0)).astype(f32)
    new_inv = ((f32(1.0) - smooth) * (inv_freq / f32(8.0)) + smooth * inv_freq).astype(f32)
    t = np.arange(SEQ, dtype=f32)
    freqs = np.outer(t, new_inv).astype(f32)
    emb = np.concatenate([freqs, freqs], axis=-1)
    cos = np.cos(emb).astype(f32)
    sin = np.sin(emb).astype(f32)
    sinr = np.concatenate([-sin[:, :64], sin[:, 64:]], axis=-1).astype(f32)
    return cos, sinr


def causal_masks():
    m = np.zeros((128, 4, 512), dtype=np.float32)
    k = np.arange(128)[:, None]
    q = np.arange(512)[None, :]
    for v in range(4):
        m[:, v, :] = np.where(128 * v + k <= q, 0.0, NEG_BIG)
    return m.reshape(128, 2048)


def build_nc(debug=False):
    nc = bass.Bass("TRN2", target_bir_lowering=False)
    dt = nc.dram_tensor

    x_pm = dt("x_pm", [NTC, 128, DM], F32, kind="ExternalInput").ap()
    x_own = dt("x_own", [TOWN, DM], F32, kind="ExternalInput").ap()
    wqkv = dt("wqkv", [DM, 768], F32, kind="ExternalInput").ap()
    woT = dt("woT", [DM, DM], F32, kind="ExternalInput").ap()
    wg_tiles = dt("wg_tiles", [NGC, 128, NIC, 128], F32, kind="ExternalInput").ap()
    wu_tiles = dt("wu_tiles", [NGC, 128, NIC, 128], F32, kind="ExternalInput").ap()
    wd_tiles = dt("wd_tiles", [8, NGC, 128, 512], F32, kind="ExternalInput").ap()
    wg_s = dt("wg_s", [128, 44032], F32, kind="ExternalInput").ap()
    wu_s = dt("wu_s", [128, 44032], F32, kind="ExternalInput").ap()
    wd_s = dt("wd_s", [128, 44032], F32, kind="ExternalInput").ap()
    cosT = dt("cosT", [SEQ, 128], F32, kind="ExternalInput").ap()
    sinrT = dt("sinrT", [SEQ, 128], F32, kind="ExternalInput").ap()
    maskT = dt("maskT", [128, 2048], F32, kind="ExternalInput").ap()
    idf = dt("idf", [128, 128], F32, kind="ExternalInput").ap()
    y = dt("y", [TOWN, DM], F32, kind="ExternalOutput").ap()

    dbg = "ExternalOutput" if debug else "Internal"
    qkv_spill = dt("qkv_spill", [NT, 768], BF16, kind=dbg).ap()
    a2a_in = dt("a2a_in", [NT, 512], BF16, kind="Internal")
    a2a_out = dt("a2a_out", [NT, 512], BF16, kind="Internal")
    stats_in = dt("stats_in", [TOWN, 2], F32, kind="Internal")
    stats_ag = dt("stats_ag", [NT, 2], F32, kind="Internal")
    scales_in = dt("scales_in", [4, 1], F32, kind="Internal")
    scales_out = dt("scales_out", [32, 1], F32, kind="Internal")
    if debug:
        absacc_d = dt("absacc_d", [128, 4], F32, kind="ExternalOutput").ap()
        slabd_d = dt("slabd_d", [128, 36], F32, kind="ExternalOutput").ap()
        stat8_d = dt("stat8_d", [128, 32], F32, kind="ExternalOutput").ap()
        stats_ag_d = dt("stats_ag_d", [NT, 2], F32, kind="ExternalOutput").ap()
        scales_d = dt("scales_d", [1, 8], F32, kind="ExternalOutput").ap()
        a2a_in_d = dt("a2a_in_d", [NT, 512], BF16, kind="ExternalOutput").ap()
    nrow_dram = dt("nrow_dram", [64, 128], BF16, kind="Internal").ap()
    thr_dram = dt("thr_dram", [1, 4], F32, kind="Internal").ap()
    x1_dram = dt("x1_dram", [TOWN, DM], F32, kind=dbg).ap()
    x1n_dram = dt("x1n_dram", [TOWN, DM], BF16, kind=dbg).ap()
    inter_dram = dt("inter_dram", [DI, TOWN], BF16, kind=dbg).ap()

    with ExitStack() as ctx:
        ecx = ctx.enter_context

        def sbuf(name, shape, dtype):
            return ecx(nc.sbuf_tensor(name, shape, dtype))

        A1 = sbuf("A1", [128, 32768], BF16)
        B2 = sbuf("B2", [128, 32768], BF16)
        F3 = sbuf("F3", [128, 16384], F32)
        outb = sbuf("outb", [128, 2 * 768], BF16)
        stag = sbuf("stag", [128, 128], F32)
        stat8 = sbuf("stat8", [128, 32], F32)
        nchunk = sbuf("nchunk", [1, 2 * 128], BF16)
        w1row = sbuf("w1row", [1, 768], BF16)
        onesb = sbuf("onesb", [128, 1], BF16)
        onesf = sbuf("onesf", [128, 1], F32)
        ncol = sbuf("ncol", [128, 1], F32)
        epscol = sbuf("epscol", [128, 1], F32)
        thrb = sbuf("thrb", [128, 8], F32)
        slabd = sbuf("slabd", [128, 36], F32)
        scf = sbuf("scf", [128, 256], F32)
        scr32 = sbuf("scr32", [1, 32], F32)
        absacc = sbuf("absacc", [128, 4], F32)
        scw = sbuf("scw", [1, 8], F32)
        zc = sbuf("zc", [128, 4], F32)
        r2c = sbuf("r2c", [128, 8], F32)
        n64 = sbuf("n64", [64, 128], BF16)
        ln2s = sbuf("ln2s", [128, 40], F32)

        P0 = ecx(nc.psum_tensor("P0", [128, 1024], F32))
        P1 = ecx(nc.psum_tensor("P1", [128, 1024], F32))
        P2 = ecx(nc.psum_tensor("P2", [128, 1024], F32))
        P3 = ecx(nc.psum_tensor("P3", [128, 1024], F32))
        PQ = [P0, P1]
        PV = [P2, P3]

        sems = ["dq_sync", "dq_gpsimd", "dq_scalar", "s_pe", "s_dve", "s_act", "s_cc",
                "d_q", "d_q1", "d_tr", "d_ost", "d_ost1", "d_n", "d_sp", "d_sp1",
                "d_xo", "d_xd0", "d_xd1", "d_x1s", "d_x1s1", "d_w", "d_w1",
                "d_i", "d_i1", "d_dbg", "d_xc", "d_xc1", "d_xf", "d_xf1",
                "d_x1n", "d_x1n1", "d_ys", "d_ys1", "d_wd", "d_wd1",
                "d_xb0", "d_xb1", "d_xb2", "d_xb3", "d_sc0", "d_sc1",
                "d_wo0", "d_wo1", "d_ib0", "d_ib1", "d_msk", "d_thr",
                "d_i2", "d_i3", "d_x1s2", "d_x1s3", "d_ys2", "d_ys3"]
        handles = {n: ecx(nc.semaphore(n)) for n in sems}
        S = Sched(sems)

        wq_v = B2[:, 0:24576].rearrange("p (i o) -> p i o", i=NIC)
        xb = [B2[:, 24576 + k * 4096:24576 + (k + 1) * 4096].rearrange("p (i t) -> p i t", i=NIC)
              for k in range(2)] + \
             [A1[:, k * 4096:(k + 1) * 4096].rearrange("p (i t) -> p i t", i=NIC)
              for k in range(2)]
        nall = A1[0:1, 8192:16384]
        cos_v = F3[:, 0:4096].rearrange("p (i d) -> p i d", i=32)
        sinr_v = F3[:, 4096:8192].rearrange("p (i d) -> p i d", i=32)
        xo = [F3[:, 8192:12288], F3[:, 12288:16384]]
        scbuf = F3[:, 12288:16384]
        idf_v = F3[:, 8192:8320]
        rps = F3[:, 8320:9600]
        ytile = F3[:, 4096:6144]
        ubf = B2[:, 24576:26624]
        ibf = B2[:, 26624:28672]
        sq_bf = B2[:, 0:4096]

        S.emit("vector", lambda e: e.memset(onesb[:, :], 1.0), inc="s_dve")
        S.emit("vector", lambda e: e.memset(onesf[:, :], 1.0), inc="s_dve")
        S.emit("vector", lambda e: e.memset(ncol[:, :], -EXP_C), inc="s_dve")
        S.emit("vector", lambda e: e.memset(epscol[:, :], EPS_LN), inc="s_dve")
        S.emit("vector", lambda e: e.memset(absacc[:, :], 0.0), inc="s_dve")
        S.emit("vector", lambda e: e.memset(scw[0:1, :], 0.0), inc="s_dve")
        S.emit("vector", lambda e: e.memset(ln2s[:, 0:16], 0.0), inc="s_dve")
        v_consts = S.v("s_dve")
        S.dma("sync", idf_v, idf)
        ld_idf = S.v("dq_sync")

        xld = {}
        for tc in range(4):
            S.dma("gpsimd", xb[tc], x_pm[tc].rearrange("p (i t) -> p i t", i=NIC),
                  sem="d_xb%d" % tc)
            xld[tc] = S.v("d_xb%d" % tc)

        last_sq = 0
        for i in range(8):
            buf = xo[1]
            w = [("s_dve", S.v("s_dve")), ("s_act", S.v("s_act"))] if i >= 1 else []
            ldv = S.dma("sync", buf[:, 0:4096], x_own[i * 128:(i + 1) * 128, :], waits=w, sem="d_xo")
            S.emit("vector",
                   lambda e, buf=buf, i=i: e.reduce_sum(stat8[:, i:i + 1], buf[:, 0:4096], axis=AX.X),
                   waits=[("d_xo", ldv)], inc="s_dve")
            S.emit("scalar",
                   lambda e, buf=buf: e.activation(sq_bf, buf[:, 0:4096], AF.Square),
                   waits=[("d_xo", ldv), ("s_dve", last_sq)], inc="s_act")
            S.emit("vector",
                   lambda e, i=i: e.reduce_sum(stat8[:, 8 + i:9 + i], sq_bf, axis=AX.X),
                   waits=[("s_act", S.v("s_act"))], inc="s_dve")
            last_sq = S.v("s_dve")
        scr_free = [("s_dve", S.v("s_dve")), ("s_act", S.v("s_act"))]
        nr_v = stat8[:, 16:32].rearrange("p (i c) -> p i c", c=2)
        S.emit("vector", lambda e: e.tensor_scalar(nr_v[:, :, 0], stat8[:, 0:8], -1.0 / DM, None, ALU.mult),
               waits=[("s_dve", S.v("s_dve"))], inc="s_dve")
        S.emit("vector", lambda e: e.tensor_scalar(stat8[:, 8:16], stat8[:, 8:16], 1.0 / DM, None, ALU.mult),
               waits=[("s_dve", S.v("s_dve"))], inc="s_dve")
        S.emit("vector", lambda e: e.tensor_tensor(stat8[:, 0:8], nr_v[:, :, 0], nr_v[:, :, 0], ALU.mult),
               waits=[("s_dve", S.v("s_dve"))], inc="s_dve")
        S.emit("vector", lambda e: e.tensor_tensor(stat8[:, 8:16], stat8[:, 8:16], stat8[:, 0:8], ALU.subtract),
               waits=[("s_dve", S.v("s_dve"))], inc="s_dve")
        v_var = S.v("s_dve")
        S.emit("scalar", lambda e: e.activation(stat8[:, 0:8], stat8[:, 8:16], AF.Ln, bias=epscol[:, 0:1]),
               waits=[("s_dve", v_var)], inc="s_act")
        S.emit("scalar", lambda e: e.activation(stat8[:, 8:16], stat8[:, 0:8], AF.Exp, scale=-0.5),
               waits=[("s_act", S.v("s_act"))], inc="s_act")
        v_stats0 = S.v("s_act")
        S.emit("vector", lambda e: e.tensor_scalar(nr_v[:, :, 1], stat8[:, 8:16], 1.0, None, ALU.mult),
               waits=[("s_act", v_stats0)], inc="s_dve")
        v_stats = ("s_dve", S.v("s_dve"))
        S.dma("sync", stats_in.ap().rearrange("(i p) c -> p i c", p=128),
              stat8[:, 16:32].rearrange("p (i c) -> p i c", c=2),
              waits=[v_stats])
        st_stats = S.v("dq_sync")
        S.emit("gpsimd",
               lambda e: e.collective_compute("AllGather", ALU.bypass,
                                              replica_groups=[list(range(R))],
                                              ins=[stats_in.ap().opt()],
                                              outs=[stats_ag.ap().opt()]),
               waits=[("dq_sync", st_stats)], inc="s_cc")
        cc_stats = S.v("s_cc")
        S.dma("sync", stag[:, :].rearrange("p (i c) -> p i c", c=2),
              stats_ag.ap().rearrange("(i p) c -> p i c", p=128),
              waits=[("s_cc", cc_stats)])
        ld_stag = S.v("dq_sync")
        if debug:
            S.dma("sync", stats_ag_d, stats_ag.ap(), waits=[("s_cc", cc_stats)], sem="d_dbg")
            S.dma("sync", stat8_d, stat8[:, :], waits=[("s_cc", cc_stats)], sem="d_dbg")
        stag_v = stag[:, :].rearrange("p (i c) -> p i c", c=2)
        S.emit("tensor",
               lambda e: e.transpose(P0[0:64, 0:128], stag_v[:, :, 0], idf_v),
               waits=[("dq_sync", ld_stag), ("dq_sync", ld_idf)], inc="s_pe")
        S.emit("vector", lambda e: e.tensor_copy(n64[:, :], P0[0:64, 0:128]),
               waits=[("s_pe", S.v("s_pe"))], inc="s_dve")
        v_n64 = S.v("s_dve")
        S.dma("sync", nrow_dram, n64[:, :], waits=[("s_dve", v_n64)], sem="d_n")
        st_nrow = S.v("d_n")
        S.dma("sync", nall, nrow_dram.rearrange("(o a) b -> o (a b)", o=1),
              waits=[("d_n", st_nrow)], sem="d_n")
        ld_nall = S.v("d_n")

        S.dma("gpsimd", wq_v, wqkv.rearrange("(i p) o -> p i o", p=128),
              waits=[v_stats])
        ld_wqkv = S.v("dq_gpsimd")

        for ic in range(NIC):
            w = [("dq_gpsimd", ld_wqkv), ("s_dve", v_consts)] if ic == 0 else []
            S.emit("tensor",
                   lambda e, ic=ic: e.matmul(P1[0:1, 0:512], onesb[:, :], wq_v[:, ic, 0:512],
                                             start=(ic == 0), stop=(ic == NIC - 1)),
                   waits=w)
            S.emit("tensor",
                   lambda e, ic=ic: e.matmul(P1[0:1, 512:768], onesb[:, :], wq_v[:, ic, 512:768],
                                             start=(ic == 0), stop=(ic == NIC - 1)),
                   inc=("s_pe" if ic == NIC - 1 else None))
        v_w1mm = S.v("s_pe")
        S.emit("scalar", lambda e: e.copy(w1row[0:1, :], P1[0:1, 0:768]),
               waits=[("s_pe", v_w1mm)], inc="s_act")
        v_w1 = S.v("s_act")

        S.dma("sync", cos_v, cosT.rearrange("(i p) d -> p i d", p=128))
        S.dma("sync", sinr_v, sinrT.rearrange("(i p) d -> p i d", p=128))
        ld_tabs = S.v("dq_sync")

        slabs = []
        for wi, wsrc in enumerate((wg_s, wu_s, wd_s)):
            for c0 in range(0, 44032, 2048):
                slabs.append((wi, wsrc, c0, min(c0 + 2048, 44032)))
        scbufs = [scbuf[:, 0:2048], scbuf[:, 2048:4096]]
        sl_dve = {}

        def emit_scale_slab(k):
            wi, wsrc, c0, c1 = slabs[k]
            scbuf_k = scbufs[k % 2]
            scsem = "d_sc%d" % (k % 2)
            ldv = S.dma("scalar", scbuf_k[:, 0:c1 - c0], wsrc[:, c0:c1],
                        waits=(scr_free if k < 2 else [("s_dve", sl_dve.get(k - 2, 0))]),
                        sem=scsem)
            ldv = S.v(scsem)
            for a in range(0, c1 - c0, 512):
                S.emit("vector",
                       lambda e, a=a, scbuf_k=scbuf_k: e.reduce_sum(
                           absacc[:, 3:4], scbuf_k[:, a:a + 512],
                           axis=AX.X, apply_absolute_value=True),
                       waits=([(scsem, ldv)] if a == 0 else [("s_dve", S.v("s_dve"))]),
                       inc="s_dve")
                S.emit("vector",
                       lambda e, wi=wi: e.tensor_tensor(absacc[:, wi:wi + 1], absacc[:, wi:wi + 1],
                                                        absacc[:, 3:4], ALU.add),
                       waits=[("s_dve", S.v("s_dve"))], inc="s_dve")
            sl_dve[k] = S.v("s_dve")

        scale_state = {}

        def emit_scale_finish():
            S.emit("tensor", lambda e: e.transpose(P2[0:4, 0:128], absacc[:, 0:4], idf_v),
                   waits=[("s_dve", S.v("s_dve")), ("dq_sync", ld_idf)], inc="s_pe")
            S.emit("scalar", lambda e: e.copy(scf[0:4, 0:128], P2[0:4, 0:128]),
                   waits=[("s_pe", S.v("s_pe"))], inc="s_act")
            scale_state["p2done"] = S.v("s_act")
            base = 0
            n = 128
            w = [("s_act", S.v("s_act"))]
            while n > 1:
                h = n // 2
                src = scf[0:4, base:base + n].rearrange("p (a b) -> p a b", b=2)
                S.emit("vector",
                       lambda e, src=src, base=base, n=n, h=h: e.tensor_tensor(
                           scf[0:4, base + n:base + n + h], src[:, :, 0], src[:, :, 1], ALU.add),
                       waits=w + [("s_dve", S.v("s_dve"))], inc="s_dve")
                w = []
                base += n
                n = h
            S.dma("sync", scales_in.ap(), scf[0:4, 254:255], waits=[("s_dve", S.v("s_dve"))],
                  sem="d_thr")
            S.emit("gpsimd",
                   lambda e: e.collective_compute("AllGather", ALU.bypass,
                                                  replica_groups=[list(range(R))],
                                                  ins=[scales_in.ap().opt()],
                                                  outs=[scales_out.ap().opt()]),
                   waits=[("d_thr", S.v("d_thr"))], inc="s_cc")
            S.dma("sync", scr32[0:1, 0:32],
                  scales_out.ap().rearrange("(o a) b -> o (a b)", o=1),
                  waits=[("s_cc", S.v("s_cc"))], sem="d_thr")
            ldt = S.v("d_thr")
            S.emit("vector", lambda e: e.tensor_copy(scw[0:1, 0:3], scr32[0:1, 20:23]),
                   waits=[("d_thr", ldt)], inc="s_dve")
            for j in range(1, 8):
                r = (5 + j) % 8
                S.emit("vector",
                       lambda e, r=r: e.tensor_tensor(scw[0:1, 0:3], scw[0:1, 0:3],
                                                      scr32[0:1, 4 * r:4 * r + 3], ALU.add),
                       waits=[("s_dve", S.v("s_dve"))], inc="s_dve")
            if debug:
                S.emit("vector", lambda e: e.tensor_copy(scw[0:1, 4:7], scw[0:1, 0:3]),
                       waits=[("s_dve", S.v("s_dve"))], inc="s_dve")
                S.dma("sync", scales_d, scw[0:1, 0:8], waits=[("s_dve", S.v("s_dve"))], sem="d_dbg")
            S.emit("vector", lambda e: e.tensor_scalar(scw[0:1, 0:3], scw[0:1, 0:3],
                                                       float(np.float32(1.0) / np.float32(NW)), None, ALU.mult),
                   waits=[("s_dve", S.v("s_dve"))], inc="s_dve")
            S.emit("vector", lambda e: e.tensor_scalar(scw[0:1, 4:7], scw[0:1, 0:3], EPS_Q, 0.5, ALU.add, ALU.mult),
                   waits=[("s_dve", S.v("s_dve"))], inc="s_dve")
            S.emit("vector", lambda e: e.tensor_tensor(scw[0:1, 3:4], scw[0:1, 0:1], scw[0:1, 1:2], ALU.mult),
                   waits=[("s_dve", S.v("s_dve"))], inc="s_dve")
            S.emit("vector", lambda e: e.tensor_tensor(scw[0:1, 7:8], scw[0:1, 3:4], scw[0:1, 2:3], ALU.mult),
                   waits=[("s_dve", S.v("s_dve"))], inc="s_dve")
            S.dma("sync", thr_dram[0:1, 0:4], scw[0:1, 4:8], waits=[("s_dve", S.v("s_dve"))],
                  sem="d_thr")
            S.dma("sync", thrb[:, 0:4], thr_dram[0:1, 0:4].partition_broadcast(128),
                  waits=[("d_thr", S.v("d_thr"))], sem="d_thr")
            S.emit("vector", lambda e: e.tensor_scalar(thrb[:, 4:8], thrb[:, 0:4], -1.0, None, ALU.mult),
                   waits=[("d_thr", S.v("d_thr"))], inc="s_dve")
            scale_state["thr"] = S.v("s_dve")

        pe_tc = {}
        dve_rope = {}
        spill_v = {}
        n_scale = 0
        for tc in range(NTC):
            par = tc % 2
            b4 = tc % 4
            P = PQ[par]
            if tc >= 4:
                S.dma("gpsimd", xb[b4], x_pm[tc].rearrange("p (i t) -> p i t", i=NIC),
                      waits=[("s_pe", pe_tc.get(tc - 4, 0))], sem="d_xb%d" % b4)
                xld[tc] = S.v("d_xb%d" % b4)
            for ic in range(NIC):
                w = []
                if ic == 0:
                    w = [("d_xb%d" % b4, xld[tc]), ("d_n", ld_nall), ("s_act", v_w1),
                         ("s_dve", dve_rope.get(tc - 2, 0))]
                S.emit("tensor",
                       lambda e, ic=ic, P=P, b4=b4: e.matmul(
                           P[:, 0:512], xb[b4][:, ic, :], wq_v[:, ic, 0:512],
                           start=(ic == 0), stop=False),
                       waits=w)
                S.emit("tensor",
                       lambda e, ic=ic, P=P, b4=b4: e.matmul(
                           P[:, 512:768], xb[b4][:, ic, :], wq_v[:, ic, 512:768],
                           start=(ic == 0), stop=False))
            S.emit("tensor",
                   lambda e, P=P, tc=tc: e.matmul(P[:, 0:512], nall[0:1, tc * 128:(tc + 1) * 128],
                                                  w1row[0:1, 0:512], start=False, stop=True))
            S.emit("tensor",
                   lambda e, P=P, tc=tc: e.matmul(P[:, 512:768], nall[0:1, tc * 128:(tc + 1) * 128],
                                                  w1row[0:1, 512:768], start=False, stop=True),
                   inc="s_pe")
            pe_tc[tc] = S.v("s_pe")

            ci = tc % 32
            rA = rps[:, 0:640]
            rB = rps[:, 640:1280]
            spsem = "d_sp" if par == 0 else "d_sp1"
            wv = [("s_pe", pe_tc[tc]), ("dq_sync", ld_tabs), ("dq_sync", ld_stag),
                  (spsem, spill_v.get(tc - 2, 0)), ("s_dve", dve_rope.get(tc - 1, 0))]
            for blk in range(5):
                c0 = blk * 128
                S.emit("vector",
                       lambda e, P=P, c0=c0, ci=ci: e.tensor_tensor(
                           rps[:, c0:c0 + 128], P[:, c0:c0 + 128], cos_v[:, ci, :], ALU.mult),
                       waits=(wv if blk == 0 else ()))
                S.emit("vector",
                       lambda e, P=P, c0=c0, ci=ci: e.tensor_tensor(
                           rps[:, 640 + c0:640 + c0 + 128],
                           P[:, c0:c0 + 128].rearrange("p (g c) -> p g c", g=2)[:, ::-1, :],
                           sinr_v[:, ci, :].rearrange("p (g c) -> p g c", g=2),
                           ALU.mult),
                       inc=("s_dve" if blk == 4 else None))
            S.emit("vector", lambda e: e.tensor_tensor(rA, rA, rB, ALU.add),
                   waits=[("s_dve", S.v("s_dve"))], inc="s_dve")
            S.emit("vector",
                   lambda e, tc=tc, par=par: e.tensor_scalar(
                       outb[:, par * 768:par * 768 + 640], rA,
                       stag_v[:, tc, 1:2], None, ALU.mult),
                   waits=[("s_dve", S.v("s_dve"))])
            S.emit("vector",
                   lambda e, tc=tc, par=par, P=P: e.tensor_scalar(
                       outb[:, par * 768 + 640:par * 768 + 768], P[:, 640:768],
                       stag_v[:, tc, 1:2], None, ALU.mult),
                   inc="s_dve")
            dve_rope[tc] = S.v("s_dve")
            S.dma("sync", qkv_spill[tc * 128:(tc + 1) * 128, :], outb[:, par * 768:(par + 1) * 768],
                  waits=[("s_dve", dve_rope[tc])], sem=spsem)
            spill_v[tc] = S.v(spsem)
            if n_scale < len(slabs):
                emit_scale_slab(n_scale)
                n_scale += 1
        while n_scale < len(slabs):
            emit_scale_slab(n_scale)
            n_scale += 1
        emit_scale_finish()
        spill_done = [("d_sp", spill_v[NTC - 2]), ("d_sp1", spill_v[NTC - 1])]
        pe_A_done = pe_tc[NTC - 1]

        k_sb = A1[:, 0:8192]
        v_sb = A1[:, 8192:16640]
        v_sb3 = A1[:, 8192:16640].rearrange("p (g c) -> p g c", g=64)
        m_bf = A1[:, 16640:18688]
        qsl = A1[:, 18688:19712]
        ebuf = A1[:, 19712:21760]
        ostg = A1[:, 21760:25856]

        S.emit("vector", lambda e: e.memset(v_sb[:, :], 1.0),
               waits=[("s_pe", pe_A_done)], inc="s_dve")
        v_vones = S.v("s_dve")
        ld_mask = S.dma("sync", F3[:, 0:2048], maskT, waits=[("s_dve", dve_rope[NTC - 1])],
                        sem="d_msk")
        S.emit("vector", lambda e: e.tensor_copy(m_bf[:, :], F3[:, 0:2048]),
               waits=[("d_msk", ld_mask)], inc="s_dve")
        v_mask = S.v("s_dve")

        for j in range(16):
            S.dma("sync", k_sb[:, j * 512:(j + 1) * 512],
                  qkv_spill[j * 512:(j + 1) * 512, 512:640],
                  waits=spill_done + [("s_dve", v_vones)], transpose=True, sem="d_tr")
        ld_k = S.v("d_tr")
        for g in range(64):
            S.dma("sync", v_sb3[:, g, 0:128],
                  qkv_spill[g * 128:(g + 1) * 128, 640:768],
                  waits=[("s_dve", v_vones)] + spill_done)
        ld_v = S.v("dq_sync")

        wo_buf = [B2[:, 0:16384].rearrange("p (h o) -> p h o", h=32),
                  B2[:, 16384:32768].rearrange("p (h o) -> p h o", h=32)]
        wo_ld = {}
        for oc in range(2):
            S.dma("gpsimd", wo_buf[oc],
                  woT[:, oc * 512:(oc + 1) * 512].rearrange("(h p) o -> p h o", p=128),
                  waits=[("s_pe", pe_A_done)], sem="d_wo%d" % oc)
            wo_ld[oc] = S.v("d_wo%d" % oc)

        groups = [(b, qc, h) for b in range(NBATCH) for qc in range(8) for h in range(NHL)]
        pairs = []
        for gi, (b, qc, h) in enumerate(groups):
            npairs = 2 * (qc + 1)
            for j in range(npairs):
                pairs.append((gi, b, qc, h, j, npairs))
        NPAIR = len(pairs)

        qld = {}
        pe_st_group = {}
        exp_done = {}
        av_done = {}
        ev_po = {"v": 0}
        st_ostg = {0: 0, 1: 0}
        evac_cnt = {}

        def emit_qload(gi):
            b, qc, h = groups[gi]
            gpar = gi % 2
            qsem = "d_q" if gpar == 0 else "d_q1"
            S.dma("sync", qsl[:, gpar * 512:(gpar + 1) * 512],
                  qkv_spill[b * SEQ + qc * 512: b * SEQ + qc * 512 + 512, h * 128:(h + 1) * 128],
                  waits=[("s_pe", pe_st_group.get(gi - 2, 0)), ("d_tr", ld_k), ("dq_sync", ld_v)],
                  transpose=True, sem=qsem)
            qld[gi] = (qsem, S.v(qsem))

        def emit_st(p):
            gi, b, qc, h, j, npairs = pairs[p]
            spar = p % 2
            SP = PQ[spar]
            gpar = gi % 2
            qv = qsl[:, gpar * 512:(gpar + 1) * 512]
            w = [("s_act", exp_done.get(p - 2, 0)), qld[gi]]
            if p <= 1:
                w.append(("s_act", v_w1))
                w.append(("s_dve", dve_rope[NTC - 1]))
            kb0 = 2 * j
            S.emit("tensor",
                   lambda e, SP=SP, b=b, kb0=kb0, qv=qv: e.matmul(
                       SP[:, 0:512], k_sb[:, b * SEQ + kb0 * 128: b * SEQ + kb0 * 128 + 128],
                       qv, start=True, stop=True),
                   waits=w)
            S.emit("tensor",
                   lambda e, SP=SP, b=b, kb0=kb0, qv=qv: e.matmul(
                       SP[:, 512:1024], k_sb[:, b * SEQ + kb0 * 128 + 128: b * SEQ + kb0 * 128 + 256],
                       qv, start=True, stop=True),
                   inc="s_pe")
            pairs_pe[p] = S.v("s_pe")
            pe_st_group[gi] = S.v("s_pe")

        def emit_mask_exp(p):
            gi, b, qc, h, j, npairs = pairs[p]
            spar = p % 2
            SP = PQ[spar]
            diag = (j >= npairs - 2)
            wexp = [("s_pe", pairs_pe[p]), ("s_pe", av_done.get(p - 2, 0))]
            if diag:
                v0 = 2 * (j - (npairs - 2))
                S.emit("vector",
                       lambda e, SP=SP, v0=v0: e.tensor_tensor(
                           SP[:, 0:1024], SP[:, 0:1024], m_bf[:, v0 * 512: v0 * 512 + 1024], ALU.add),
                       waits=[("s_pe", pairs_pe[p]), ("s_dve", v_mask)], inc="s_dve")
                wexp.append(("s_dve", S.v("s_dve")))
            S.emit("scalar",
                   lambda e, SP=SP, spar=spar: e.activation(
                       ebuf[:, spar * 1024:(spar + 1) * 1024], SP[:, 0:1024], AF.Exp,
                       bias=ncol[:, 0:1], scale=ISQ),
                   waits=wexp, inc="s_act")
            exp_done[p] = S.v("s_act")

        def emit_av(p):
            gi, b, qc, h, j, npairs = pairs[p]
            spar = p % 2
            kb0 = 2 * j
            first = (j == 0)
            last = (j == npairs - 1)
            for kk in range(2):
                kbg = b * 32 + kb0 + kk
                for qs in range(4):
                    slot = PV[qs // 2]
                    c0 = (qs % 2) * 512
                    w = []
                    if first and kk == 0 and qs == 0:
                        w = [("s_act", exp_done[p]), ("s_dve", ev_po["v"])]
                    elif kk == 0 and qs == 0:
                        w = [("s_act", exp_done[p])]
                    S.emit("tensor",
                           lambda e, slot=slot, c0=c0, spar=spar, kk=kk, qs=qs, kbg=kbg,
                                  first=first, last=last: e.matmul(
                               slot[:, c0:c0 + 129],
                               ebuf[:, spar * 1024 + kk * 512 + qs * 128: spar * 1024 + kk * 512 + qs * 128 + 128],
                               v_sb3[:, kbg, 0:129],
                               start=(first and kk == 0), stop=(last and kk == 1)),
                           waits=w,
                           inc=("s_pe" if (kk == 1 and qs == 3) else None))
            av_done[p] = S.v("s_pe")

        def emit_evac(p):
            gi, b, qc, h, j, npairs = pairs[p]
            opar = (b * 8 + qc) % 2
            osem = "d_ost" if opar == 0 else "d_ost1"
            w = [("s_pe", av_done[p]), (osem, st_ostg[opar]), ("s_dve", ev_po["v"])]
            for qs in range(4):
                slot = PV[qs // 2]
                c0 = (qs % 2) * 512
                S.emit("vector",
                       lambda e, slot=slot, c0=c0, qs=qs: e.reciprocal(
                           zc[:, qs:qs + 1], slot[:, c0 + 128:c0 + 129]),
                       waits=(w if qs == 0 else ()),
                       inc=("s_dve" if qs == 3 else None))
            vrec = S.v("s_dve")
            for qs in range(4):
                slot = PV[qs // 2]
                c0 = (qs % 2) * 512
                S.emit("vector",
                       lambda e, slot=slot, c0=c0, qs=qs, opar=opar, h=h: e.tensor_scalar(
                           ostg[:, opar * 2048 + qs * 512 + h * 128: opar * 2048 + qs * 512 + h * 128 + 128],
                           slot[:, c0:c0 + 128], zc[:, qs:qs + 1], None, ALU.mult),
                       waits=([("s_dve", vrec)] if qs == 0 else ()), inc="s_dve")
            ev_po["v"] = S.v("s_dve")
            evac_cnt[(b, qc)] = S.v("s_dve")

        pairs_pe = {}
        p = 0
        cur_g = -1
        emit_qload(0)
        emit_qload(1)
        for p in range(NPAIR):
            gi, b, qc, h, j, npairs = pairs[p]
            if j == npairs - 1 and gi + 2 < len(groups):
                emit_qload(gi + 2)
            if p == 0:
                emit_st(0)
            if p + 1 < NPAIR:
                emit_st(p + 1)
            emit_mask_exp(p)
            emit_av(p)
            if j == npairs - 1:
                emit_evac(p)
                if h == NHL - 1:
                    opar = (b * 8 + qc) % 2
                    t0 = b * SEQ + qc * 512
                    osem2 = "d_ost" if opar == 0 else "d_ost1"
                    S.dma("sync",
                          a2a_in.ap()[t0:t0 + 512, :].rearrange("(qs p) o -> p qs o", p=128),
                          ostg[:, opar * 2048:(opar + 1) * 2048].rearrange("p (qs o) -> p qs o", qs=4),
                          waits=[("s_dve", evac_cnt[(b, qc)])], sem=osem2)
                    st_ostg[opar] = S.v(osem2)

        a2a_ready = [("d_ost", S.v("d_ost")), ("d_ost1", S.v("d_ost1"))]
        if debug:
            S.dma("sync", a2a_in_d, a2a_in.ap(), waits=a2a_ready, sem="d_dbg")
        S.emit("gpsimd",
               lambda e: e.collective_compute("AllToAll", ALU.bypass,
                                              replica_groups=[list(range(R))],
                                              ins=[a2a_in.ap().opt()],
                                              outs=[a2a_out.ap().opt()]),
               waits=a2a_ready, inc="s_cc")
        cc_a2a = S.v("s_cc")

        ofm = A1[:, :].rearrange("p (h t) -> p h t", h=32)
        for hd_c in range(32):
            r_ = hd_c // 4
            lb = hd_c % 4
            S.dma("sync", ofm[:, hd_c, :],
                  a2a_out.ap()[r_ * 1024:(r_ + 1) * 1024, lb * 128:(lb + 1) * 128],
                  waits=[("s_cc", cc_a2a)], transpose=True, sem="d_tr")
        ld_ofm = S.v("d_tr")

        xo3 = [xo[k].rearrange("p (t o) -> p t o", t=8) for k in range(2)]
        ln2_rq = [0]
        sq_st = {}
        ev_C = {}
        pe_C = {}
        dve_C = {}
        xo_ld = {}
        st_x1 = {}
        for oc in range(8):
            par = oc % 2
            xcsem = "d_xc" if par == 0 else "d_xc1"
            if oc >= 2:
                S.dma("gpsimd", wo_buf[par],
                      woT[:, oc * 512:(oc + 1) * 512].rearrange("(h p) o -> p h o", p=128),
                      waits=[("s_pe", pe_C.get(oc - 2, 0))], sem="d_wo%d" % par)
                wo_ld[oc] = S.v("d_wo%d" % par)
            S.dma("sync", xo3[par][:, :, :],
                  x_own[:, oc * 512:(oc + 1) * 512].rearrange("(t p) o -> p t o", p=128),
                  waits=[("s_dve", dve_C.get(oc - 2, 0)), ("s_dve", scale_state["thr"])], sem=xcsem)
            xo_ld[oc] = S.v(xcsem)
            for tc in range(8):
                slot = [P0, P1, P2, P3][tc // 2]
                c0 = (tc % 2) * 512
                for hd_c in range(32):
                    w = []
                    if hd_c == 0:
                        w = [("d_tr", ld_ofm), ("d_wo%d" % par, wo_ld[oc]),
                             ("s_dve", ev_C.get((oc - 1, tc), 0)),
                             ("s_act", scale_state["p2done"])]
                    S.emit("tensor",
                           lambda e, slot=slot, c0=c0, hd_c=hd_c, tc=tc, par=par: e.matmul(
                               slot[:, c0:c0 + 512], ofm[:, hd_c, tc * 128:(tc + 1) * 128],
                               wo_buf[par][:, hd_c, :],
                               start=(hd_c == 0), stop=(hd_c == 31)),
                           waits=w,
                           inc=("s_pe" if hd_c == 31 else None))
                pe_C[oc] = S.v("s_pe")
                ys = tc % 4
                x1sem = ["d_x1s", "d_x1s1", "d_x1s2", "d_x1s3"][ys]
                S.emit("vector",
                       lambda e, slot=slot, c0=c0, tc=tc, par=par, ys=ys: e.tensor_tensor(
                           ytile[:, ys * 512:(ys + 1) * 512], slot[:, c0:c0 + 512],
                           xo3[par][:, tc, :], ALU.add),
                       waits=[("s_pe", pe_C[oc]), (xcsem, xo_ld[oc]),
                              ("s_act", sq_st.get((oc, tc - 4), sq_st.get((oc - 1, tc + 4), 0))),
                              (x1sem,
                               st_x1.get((oc, tc - 4), st_x1.get((oc - 1, tc + 4), 0)))],
                       inc="s_dve")
                v_ev = S.v("s_dve")
                ev_C[(oc, tc)] = v_ev
                dve_C[oc] = v_ev
                S.dma("sync", x1_dram[tc * 128:(tc + 1) * 128, oc * 512:(oc + 1) * 512],
                      ytile[:, ys * 512:(ys + 1) * 512],
                      waits=[("s_dve", v_ev)], sem=x1sem)
                st_x1[(oc, tc)] = S.v(x1sem)
                S.emit("scalar",
                       lambda e, ys=ys: e.activation(F3[:, 6144:6656],
                                                     ytile[:, ys * 512:(ys + 1) * 512], AF.Square),
                       waits=[("s_dve", v_ev), ("s_dve", ln2_rq[0])], inc="s_act")
                sq_v = S.v("s_act")
                sq_st[(oc, tc)] = sq_v
                S.emit("vector",
                       lambda e, ys=ys: e.reduce_sum(ln2s[:, 32:33],
                                                     ytile[:, ys * 512:(ys + 1) * 512], axis=AX.X),
                       waits=[("s_dve", S.v("s_dve"))], inc="s_dve")
                S.emit("vector",
                       lambda e, tc=tc: e.tensor_tensor(ln2s[:, tc:tc + 1], ln2s[:, tc:tc + 1],
                                                        ln2s[:, 32:33], ALU.add),
                       waits=[("s_dve", S.v("s_dve"))], inc="s_dve")
                S.emit("vector",
                       lambda e: e.reduce_sum(ln2s[:, 33:34], F3[:, 6144:6656], axis=AX.X),
                       waits=[("s_dve", S.v("s_dve")), ("s_act", sq_v)], inc="s_dve")
                ln2_rq[0] = S.v("s_dve")
                S.emit("vector",
                       lambda e, tc=tc: e.tensor_tensor(ln2s[:, 8 + tc:9 + tc], ln2s[:, 8 + tc:9 + tc],
                                                        ln2s[:, 33:34], ALU.add),
                       waits=[("s_dve", S.v("s_dve"))], inc="s_dve")
                dve_C[oc] = S.v("s_dve")
        x1_done = [(nm, S.v(nm)) for nm in ("d_x1s", "d_x1s1", "d_x1s2", "d_x1s3")]
        S.emit("vector", lambda e: e.tensor_scalar(ln2s[:, 16:24], ln2s[:, 0:8], -1.0 / DM, None, ALU.mult),
               waits=[("s_dve", S.v("s_dve"))], inc="s_dve")
        S.emit("vector", lambda e: e.tensor_scalar(ln2s[:, 32:40], ln2s[:, 8:16], 1.0 / DM, None, ALU.mult),
               waits=[("s_dve", S.v("s_dve"))], inc="s_dve")
        S.emit("vector", lambda e: e.tensor_tensor(ln2s[:, 8:16], ln2s[:, 16:24], ln2s[:, 16:24], ALU.mult),
               waits=[("s_dve", S.v("s_dve"))], inc="s_dve")
        S.emit("vector", lambda e: e.tensor_tensor(ln2s[:, 32:40], ln2s[:, 32:40], ln2s[:, 8:16], ALU.subtract),
               waits=[("s_dve", S.v("s_dve"))], inc="s_dve")
        S.emit("scalar", lambda e: e.activation(ln2s[:, 8:16], ln2s[:, 32:40], AF.Ln, bias=epscol[:, 0:1]),
               waits=[("s_dve", S.v("s_dve"))], inc="s_act")
        S.emit("scalar", lambda e: e.activation(ln2s[:, 24:32], ln2s[:, 8:16], AF.Exp, scale=-0.5),
               waits=[("s_act", S.v("s_act"))], inc="s_act")
        r2all_done = S.v("s_act")

        x1n_t = [B2[:, 16384 + k * 4096:16384 + (k + 1) * 4096] for k in range(2)]
        st_x1n = {}
        act_D = {}
        for tc in range(8):
            par = tc % 2
            xdsem = "d_xd%d" % par
            ldv = S.dma("sync", xo[par][:, 0:4096], x1_dram[tc * 128:(tc + 1) * 128, :],
                        waits=x1_done + [act_D.get(tc - 2, ("s_dve", 0))], sem=xdsem)
            ldv = S.v(xdsem)
            x1nsem = "d_x1n" if par == 0 else "d_x1n1"
            S.emit("vector",
                   lambda e, par=par, tc=tc: e.tensor_scalar(
                       x1n_t[par], xo[par][:, 0:4096],
                       ln2s[:, 16 + tc:17 + tc], ln2s[:, 24 + tc:25 + tc], ALU.add, ALU.mult),
                   waits=[(xdsem, ldv), ("s_act", r2all_done),
                          (x1nsem, st_x1n.get(tc - 2, 0)),
                          act_D.get(tc - 1, ("s_dve", 0))], inc="s_dve")
            act_D[tc] = ("s_dve", S.v("s_dve"))
            S.dma("sync", x1n_dram[tc * 128:(tc + 1) * 128, :], x1n_t[par],
                  waits=[act_D[tc]], sem=x1nsem)
            st_x1n[tc] = S.v(x1nsem)
        x1n_done = [("d_x1n", S.v("d_x1n")), ("d_x1n1", S.v("d_x1n1"))]

        xn = A1[:, :].rearrange("p (i t) -> p i t", i=32)
        for ic in range(NIC):
            S.dma("sync", xn[:, ic, :], x1n_dram[0:TOWN, ic * 128:(ic + 1) * 128],
                  waits=x1n_done + [("s_pe", pe_C[7])], transpose=True, sem="d_tr")
        ld_xn = S.v("d_tr")

        qg = [B2[:, 0:4096], B2[:, 4096:8192]]
        qu = [B2[:, 8192:12288], B2[:, 12288:16384]]
        scr1 = B2[:, 16384:20480]
        scr2 = B2[:, 20480:24576]
        wgf = [F3[:, 0:4096], F3[:, 4096:8192]]
        wuf = [F3[:, 8192:12288], F3[:, 12288:16384]]
        pe_E = {}
        dve_E = {}
        act_E = {}
        ib_st = {}
        ew_ld = {}

        def emit_E_load(j):
            par = j % 2
            wsem = "d_w" if par == 0 else "d_w1"
            S.dma("sync", wgf[par], wg_tiles[j].rearrange("p i g -> p (i g)"),
                  waits=[("s_dve", dve_E.get(j - 2, 0))] + (x1n_done if j < 2 else []),
                  sem=wsem)
            S.dma("sync", wuf[par], wu_tiles[j].rearrange("p i g -> p (i g)"), sem=wsem)
            ew_ld[j] = (wsem, S.v(wsem))

        def emit_E_quant(j):
            par = j % 2
            wq2 = [ew_ld[j], ("s_dve", scale_state["thr"]),
                   ("s_pe", pe_E.get(j - 2, 0)), ("s_dve", dve_E.get(j - 1, 0))]
            S.emit("vector", lambda e, par=par: e.tensor_scalar(scr1, wgf[par], thrb[:, 0:1], None, ALU.is_gt),
                   waits=wq2)
            S.emit("vector", lambda e, par=par: e.tensor_scalar(scr2, wgf[par], thrb[:, 4:5], None, ALU.is_lt),
                   inc="s_dve")
            S.emit("vector", lambda e, par=par: e.tensor_tensor(qg[par], scr1, scr2, ALU.subtract),
                   waits=[("s_dve", S.v("s_dve"))], inc="s_dve")
            vqg = S.v("s_dve")
            S.emit("vector", lambda e, par=par: e.tensor_scalar(scr1, wuf[par], thrb[:, 1:2], None, ALU.is_gt),
                   waits=[ew_ld[j], ("s_dve", vqg)])
            S.emit("vector", lambda e, par=par: e.tensor_scalar(scr2, wuf[par], thrb[:, 5:6], None, ALU.is_lt),
                   inc="s_dve")
            S.emit("vector", lambda e, par=par: e.tensor_tensor(qu[par], scr1, scr2, ALU.subtract),
                   waits=[("s_dve", S.v("s_dve"))], inc="s_dve")
            dve_E[j] = S.v("s_dve")

        emit_E_load(0)
        emit_E_load(1)
        emit_E_quant(0)
        for j in range(NGC):
            par = j % 2
            if j + 2 < NGC:
                emit_E_load(j + 2)
            if j + 1 < NGC:
                emit_E_quant(j + 1)
            vq = dve_E[j]
            PG = PQ[par]
            PU = PV[par]
            for ic in range(NIC):
                w = []
                if ic == 0:
                    w = [("s_dve", vq), ("d_tr", ld_xn),
                         ("s_dve", dve_E.get(("i", j - 2), 0)),
                         ("s_act", act_E.get(j - 2, 0))]
                S.emit("tensor",
                       lambda e, PG=PG, par=par, ic=ic: e.matmul(
                           PG[:, 0:512], qg[par][:, ic * 128:(ic + 1) * 128],
                           xn[:, ic, 0:512], start=(ic == 0), stop=(ic == NIC - 1)),
                       waits=w)
                S.emit("tensor",
                       lambda e, PG=PG, par=par, ic=ic: e.matmul(
                           PG[:, 512:1024], qg[par][:, ic * 128:(ic + 1) * 128],
                           xn[:, ic, 512:1024], start=(ic == 0), stop=(ic == NIC - 1)))
            for ic in range(NIC):
                S.emit("tensor",
                       lambda e, PU=PU, par=par, ic=ic: e.matmul(
                           PU[:, 0:512], qu[par][:, ic * 128:(ic + 1) * 128],
                           xn[:, ic, 0:512], start=(ic == 0), stop=(ic == NIC - 1)))
                S.emit("tensor",
                       lambda e, PU=PU, par=par, ic=ic: e.matmul(
                           PU[:, 512:1024], qu[par][:, ic * 128:(ic + 1) * 128],
                           xn[:, ic, 512:1024], start=(ic == 0), stop=(ic == NIC - 1)),
                       inc=("s_pe" if ic == NIC - 1 else None))
            pe_E[j] = S.v("s_pe")
            S.emit("scalar", lambda e, PU=PU, par=par: e.copy(ubf[:, par * 1024:(par + 1) * 1024], PU[:, 0:1024]),
                   waits=[("s_pe", pe_E[j]), ("s_dve", dve_E.get(("i", j - 2), 0))], inc="s_act")
            act_E[j] = S.v("s_act")
            S.emit("vector",
                   lambda e, PG=PG, par=par: e.tensor_tensor(
                       ibf[:, par * 1024:(par + 1) * 1024], PG[:, 0:1024],
                       ubf[:, par * 1024:(par + 1) * 1024], ALU.mult),
                   waits=[("s_act", act_E[j]),
                          ("d_ib%d" % par, ib_st.get(j - 2, 0))],
                   inc="s_dve")
            dve_E[("i", j)] = S.v("s_dve")
            S.dma("scalar", inter_dram[j * 128:(j + 1) * 128, :], ibf[:, par * 1024:(par + 1) * 1024],
                  waits=[("s_dve", dve_E[("i", j)])], sem="d_ib%d" % par)
            ib_st[j] = S.v("d_ib%d" % par)
        ib_done = [("d_ib0", S.v("d_ib0")), ("d_ib1", S.v("d_ib1"))]
        pe_E_done = pe_E[NGC - 1]

        int_sb4 = [B2[:, 16384 + k * 4096:16384 + (k + 1) * 4096].rearrange("p (i t) -> p i t", i=4)
                   for k in range(4)]
        qd = [B2[:, 8192:10240], B2[:, 10240:12288]]
        fs1 = B2[:, 12288:14336]
        fs2 = B2[:, 14336:16384]
        wdf = [F3[:, 0:2048], F3[:, 2048:4096]]
        x1o = [F3[:, 8192:12288].rearrange("p (t o) -> p t o", t=8),
               F3[:, 12288:16384].rearrange("p (t o) -> p t o", t=8)]
        batches = [(i, min(i + 4, NGC)) for i in range(0, NGC, 4)]
        NB = len(batches)
        NGB = 8 * NB
        last_qd = [0]
        pe_gb = {}
        dve_q = {}
        dve_F = {}
        dve_ev = {}
        st_y = {}
        lw_v = {}
        li_v = {}
        x1_ld = {}
        dve_E_all = dve_E[NGC - 1]
        dve_E_int = dve_E[("i", NGC - 1)]
        act_E_all = act_E[NGC - 1]
        isems = ["d_i", "d_i1", "d_i2", "d_i3"]
        ysems = ["d_ys", "d_ys1", "d_ys2", "d_ys3"]

        def emit_F_load(gb):
            oc, bi = gb // NB, gb % NB
            ic0, ic1 = batches[bi]
            n = ic1 - ic0
            bpar = gb % 2
            sl4 = gb % 4
            wdsem = "d_wd" if bpar == 0 else "d_wd1"
            S.dma("sync", wdf[bpar][:, 0:n * 512].rearrange("p (i o) -> p i o", i=n),
                  wd_tiles[oc, ic0:ic1].rearrange("i p o -> p i o"),
                  waits=[("s_dve", dve_q.get(gb - 2, dve_E_all)),
                         ("s_pe", pe_gb.get(gb - 2, pe_E_done if gb < 2 else 0))],
                  sem=wdsem)
            lw_v[gb] = (wdsem, S.v(wdsem))
            S.dma("sync", int_sb4[sl4][:, 0:n, :],
                  inter_dram[ic0 * 128:ic1 * 128, :].rearrange("(i p) t -> p i t", p=128),
                  waits=ib_done + [("s_pe", pe_gb.get(gb - 4, 0)), ("s_dve", dve_E_all)],
                  sem=isems[sl4])
            li_v[gb] = (isems[sl4], S.v(isems[sl4]))

        def emit_F_quant(gb):
            bpar = gb % 2
            ic0, ic1 = batches[gb % NB]
            n = ic1 - ic0
            S.emit("vector", lambda e, bpar=bpar, n=n: e.tensor_scalar(
                       fs1[:, 0:n * 512], wdf[bpar][:, 0:n * 512], thrb[:, 2:3], None, ALU.is_gt),
                   waits=[lw_v[gb], ("s_dve", scale_state["thr"]),
                          ("s_dve", last_qd[0]),
                          ("s_pe", pe_gb.get(gb - 2, 0))])
            S.emit("vector", lambda e, bpar=bpar, n=n: e.tensor_scalar(
                       fs2[:, 0:n * 512], wdf[bpar][:, 0:n * 512], thrb[:, 6:7], None, ALU.is_lt),
                   inc="s_dve")
            S.emit("vector", lambda e, bpar=bpar, n=n: e.tensor_tensor(
                       qd[bpar][:, 0:n * 512], fs1[:, 0:n * 512], fs2[:, 0:n * 512], ALU.subtract),
                   waits=[("s_dve", S.v("s_dve"))], inc="s_dve")
            dve_q[gb] = S.v("s_dve")
            last_qd[0] = S.v("s_dve")

        def emit_F_evac(oc):
            opar = oc % 2
            xfsem = "d_xf" if opar == 0 else "d_xf1"
            for tc in range(8):
                slot = [P0, P1, P2, P3][tc // 2]
                c0 = (tc % 2) * 512
                ys = tc % 4
                yssem = ysems[ys]
                S.emit("vector",
                       lambda e, slot=slot, c0=c0, opar=opar, tc=tc, ys=ys: e.scalar_tensor_tensor(
                           ytile[:, ys * 512:(ys + 1) * 512], slot[:, c0:c0 + 512],
                           thrb[:, 3:4], x1o[opar][:, tc, :], ALU.mult, ALU.add),
                       waits=[("s_pe", pe_gb[oc * NB + NB - 1]), (xfsem, x1_ld[oc]),
                              (yssem,
                               st_y.get((oc, tc - 4), st_y.get((oc - 1, tc + 4), 0)))],
                       inc="s_dve")
                dve_ev[(oc, tc)] = S.v("s_dve")
                dve_F[oc] = S.v("s_dve")
                S.dma("sync", y[tc * 128:(tc + 1) * 128, oc * 512:(oc + 1) * 512],
                      ytile[:, ys * 512:(ys + 1) * 512],
                      waits=[("s_dve", dve_F[oc])], sem=yssem)
                st_y[(oc, tc)] = S.v(yssem)

        emit_F_load(0)
        emit_F_load(1)
        for gb in range(NGB):
            oc, bi = gb // NB, gb % NB
            if bi == 0:
                xfsem = "d_xf" if oc % 2 == 0 else "d_xf1"
                S.dma("scalar", x1o[oc % 2][:, :, :],
                      x1_dram[:, oc * 512:(oc + 1) * 512].rearrange("(t p) o -> p t o", p=128),
                      waits=[("s_dve", dve_F.get(oc - 2, 0))] + x1n_done +
                            [("s_dve", dve_E_all)], sem=xfsem)
                x1_ld[oc] = S.v(xfsem)
            emit_F_quant(gb)
            if gb + 2 < NGB:
                emit_F_load(gb + 2)
            if bi == 0 and oc > 0:
                emit_F_evac(oc - 1)
            ic0, ic1 = batches[bi]
            n = ic1 - ic0
            bpar = gb % 2
            sl4 = gb % 4
            for i in range(n):
                icg = ic0 + i
                for tc in range(8):
                    slot = [P0, P1, P2, P3][tc // 2]
                    c0 = (tc % 2) * 512
                    w = []
                    if i == 0 and tc == 0:
                        w = [("s_dve", dve_q[gb]), li_v[gb]]
                        if gb == 0:
                            w.append(("s_dve", dve_E_int))
                            w.append(("s_act", act_E_all))
                    if i == 0 and bi == 0 and oc > 0:
                        w = w + [("s_dve", dve_ev[(oc - 1, tc)])]
                    S.emit("tensor",
                           lambda e, slot=slot, c0=c0, sl4=sl4, bpar=bpar, i=i, tc=tc, icg=icg: e.matmul(
                               slot[:, c0:c0 + 512], int_sb4[sl4][:, i, tc * 128:(tc + 1) * 128],
                               qd[bpar][:, i * 512:(i + 1) * 512],
                               start=(icg == 0), stop=(icg == NGC - 1)),
                           waits=w,
                           inc=("s_pe" if (i == n - 1 and tc == 7) else None))
            pe_gb[gb] = S.v("s_pe")
        emit_F_evac(7)

        S.emit("sync", lambda e: None,
               waits=[(nm, S.v(nm)) for nm in sems
                      if nm.startswith("d") and S.v(nm) > 0])

        with allow_ici_all_to_all():
            with nc.Block() as block:
                S.run(block, handles)
    return nc


_NC_CACHE = {}


def _get_nc(debug=False):
    key = bool(debug)
    if key not in _NC_CACHE:
        _NC_CACHE[key] = build_nc(debug=key)
    return _NC_CACHE[key]


def make_in_maps(inputs):
    x = np.asarray(inputs["L0_attn_hidden_states"], dtype=np.float32)
    Wq = np.asarray(inputs["L0_attn_Wq"], dtype=np.float32)
    Wk = np.asarray(inputs["L0_attn_Wk"], dtype=np.float32)
    Wv = np.asarray(inputs["L0_attn_Wv"], dtype=np.float32)
    Wo = np.asarray(inputs["L0_attn_Wo"], dtype=np.float32)
    wg = np.asarray(inputs["L0_ffn_w_gate"], dtype=np.float32)
    wu = np.asarray(inputs["L0_ffn_w_up"], dtype=np.float32)
    wd = np.asarray(inputs["L0_ffn_w_down"], dtype=np.float32)

    xr = x.reshape(NT, DM)
    x_pm = np.ascontiguousarray(
        xr.reshape(NTC, 128, NIC, 128).transpose(0, 3, 2, 1)).reshape(NTC, 128, DM)
    woT = np.ascontiguousarray(Wo.T)
    wg_tiles = np.ascontiguousarray(wg.reshape(NGC, 128, NIC, 128).transpose(0, 3, 2, 1))
    wu_tiles = np.ascontiguousarray(wu.reshape(NGC, 128, NIC, 128).transpose(0, 3, 2, 1))
    wd_tiles = np.ascontiguousarray(wd.reshape(8, 512, NGC, 128).transpose(0, 2, 3, 1))
    wg_flat = wg_tiles.reshape(-1)
    wu_flat = wu_tiles.reshape(-1)
    wd_flat = wd_tiles.reshape(-1)
    nsl = wg_flat.size // R
    cos, sinr = rope_tables()
    masks = causal_masks()
    ident = np.eye(128, dtype=np.float32)

    in_maps = []
    for c in range(R):
        wqkv_c = np.ascontiguousarray(np.concatenate(
            [Wq[512 * c:512 * (c + 1), :].T, Wk[128 * c:128 * (c + 1), :].T,
             Wv[128 * c:128 * (c + 1), :].T], axis=1))
        in_maps.append({
            "x_pm": x_pm,
            "x_own": np.ascontiguousarray(xr[TOWN * c:TOWN * (c + 1), :]),
            "wqkv": wqkv_c,
            "woT": woT,
            "wg_tiles": wg_tiles,
            "wu_tiles": wu_tiles,
            "wd_tiles": wd_tiles,
            "wg_s": np.ascontiguousarray(wg_flat[nsl * c:nsl * (c + 1)].reshape(128, 44032)),
            "wu_s": np.ascontiguousarray(wu_flat[nsl * c:nsl * (c + 1)].reshape(128, 44032)),
            "wd_s": np.ascontiguousarray(wd_flat[nsl * c:nsl * (c + 1)].reshape(128, 44032)),
            "cosT": cos,
            "sinrT": sinr,
            "maskT": masks,
            "idf": ident,
        })
    return in_maps


def kernel(**inputs):
    nc = _get_nc(debug=False)
    in_maps = make_in_maps(inputs)
    res = run_bass_kernel_spmd(nc, in_maps, core_ids=list(range(R)))
    out = np.empty((NT, DM), dtype=np.float32)
    for c in range(R):
        out[TOWN * c:TOWN * (c + 1), :] = res.results[c]["y"]
    return out.reshape(NBATCH, SEQ, DM)
